# version 02
# baseline (speedup 1.0000x reference)
import contextlib
import numpy as np
import ml_dtypes

import concourse.bass as bass
import concourse.mybir as mybir
import concourse.bass_utils as _bu
from concourse.bass_utils import run_bass_kernel_spmd

if not getattr(_bu, "_nobirsim_patch", False):
    _orig_run_command = _bu.run_command

    def _run_command_nobirsim(cmd, cwd=None, **kw):
        cmd = ["--enable-birsim=false" if c == "--enable-birsim=true" else c for c in cmd]
        return _orig_run_command(cmd, cwd=cwd, **kw)

    _bu.run_command = _run_command_nobirsim
    _bu._nobirsim_patch = True

F32 = mybir.dt.float32
BF16 = mybir.dt.bfloat16
AF = mybir.ActivationFunctionType
ALU = mybir.AluOpType
AXL = mybir.AxisListType
bfloat16 = ml_dtypes.bfloat16

N_CORES = 8
B = 16
BC = 2
T = 2048
D = 1024
NTOK = BC * T
NDC = D // 128
NTT = NTOK // 128


def _conv_tstack(diff_kernel, ma_r_kernel):
    K128 = np.zeros((128, 720), dtype=np.float32)
    dk = np.asarray(diff_kernel, dtype=np.float32)
    mk = np.asarray(ma_r_kernel, dtype=np.float32)
    for i in range(16):
        for j in range(8):
            r = 8 * i + j
            if j < 4:
                K128[r, :4] = dk[4 * i + j]
            else:
                K128[r, :720] = mk[4 * i + (j - 4)]
    pad = np.zeros((128, 1152), dtype=np.float32)
    pad[:, 128:848] = K128
    base = pad[:, 128:]
    st_g, st_e = base.strides
    ts = np.lib.stride_tricks.as_strided(
        base, shape=(128, 7, 128, 128), strides=(st_g, 128 * st_e, -st_e, st_e))
    return np.ascontiguousarray(ts)



class Ctr:

    def __init__(self, handle):
        self.h = handle
        self.n = 0

    def inc(self, k=1):
        self.n += k
        return self.n


class MK:

    ENGINES = ("tensor", "vector", "scalar", "gpsimd", "sync")

    def __init__(self, nc):
        self.nc = nc
        self.es = contextlib.ExitStack()
        self.ops = {e: [] for e in self.ENGINES}
        self.sems = {}
        self._uid = 0

    def uid(self):
        self._uid += 1
        return self._uid

    def sem(self, name):
        if name in self.sems:
            return self.sems[name]
        c = Ctr(self.es.enter_context(self.nc.semaphore(name)))
        self.sems[name] = c
        return c

    def sb(self, name, shape, dtype):
        return self.es.enter_context(self.nc.sbuf_tensor(name, shape, dtype))

    def ps(self, name, shape, dtype):
        return self.es.enter_context(self.nc.psum_tensor(name, shape, dtype))

    def emit(self, engine, fn):
        self.ops[engine].append(fn)

    def wait(self, engine, ctr, val):
        if val > 0:
            self.emit(engine, lambda eng, c=ctr, v=val: eng.wait_ge(c.h, v))

    def barrier(self):
        snap = [(c, c.n) for c in self.sems.values() if c.n > 0]
        for eng in self.ENGINES:
            for c, v in snap:
                self.wait(eng, c, v)

    def realize(self):
        with self.es, self.nc.Block() as block:
            for name in self.ENGINES:
                deco = getattr(block, name)
                ops = self.ops[name]

                def section(eng, ops=ops):
                    for fn in ops:
                        fn(eng)

                deco(section)
        return self.nc



def emit_conv_block(mk, bufs, sems, tstack_ap):
    nc = mk.nc
    ubf, tsb, pgs = bufs["ubf"], bufs["tsb"], bufs["pg"]
    dW, tT, vV = sems["dW"], sems["tT"], sems["vV"]
    dT = [mk.sem("dT0"), mk.sem("dT1")]
    u4 = ubf[:].rearrange("p (b i d) -> p b i d", b=BC, i=16, d=D)
    dma_done = [0, 0]
    mm_done = [0, 0]
    ev_done = [0, 0]
    for g in range(128):
        bf = g % 2
        mk.wait("gpsimd", tT, mm_done[bf])
        mk.emit("gpsimd", lambda e, b=bf, g=g: e.dma_start(
            tsb[b][:].rearrange("s (d t) -> s d t", d=7, t=128),
            tstack_ap[g].rearrange("d s t -> s d t")).then_inc(dT[b].h, 16))
        dma_done[bf] = dT[bf].inc(16)
        mk.wait("tensor", dT[bf], dma_done[bf])
        mk.wait("tensor", vV, ev_done[bf])
        pview = pgs[bf][:, 0:256].rearrange("p (b i c) -> p b i c", b=BC, i=16, c=8)
        for d in range(7):
            for bb in range(BC):
                out_ap = pview[:, bb, d:16, :]
                rhs_ap = u4[:, bb, 0:16 - d, 8 * g:8 * g + 8]
                last = d == 6 and bb == BC - 1
                st = d == 0 and bb == 0
                mk.emit("tensor", lambda e, b=bf, d=d, o=out_ap, r=rhs_ap, st=st, l=last: (e.matmul(
                    o, tsb[b][:, 128 * d:128 * d + 128], r,
                    start=st, stop=l, skip_group_check=True).then_inc(tT.h, 1) if l else e.matmul(
                    o, tsb[b][:, 128 * d:128 * d + 128], r, start=st, stop=False,
                    skip_group_check=True)))
        mm_done[bf] = tT.inc(1)
        cols = u4[:, :, :, 8 * g:8 * g + 8]
        mk.wait("vector", tT, mm_done[bf])
        mk.emit("vector", lambda e, b=bf, c=cols: e.tensor_tensor(
            c, pgs[b][:, 0:256].rearrange("p (b i c) -> p b i c", b=BC, i=16, c=8),
            c, ALU.add).then_inc(vV.h, 1))
        ev_done[bf] = vV.inc(1)


def emit_transpose_nat_to_T(mk, bufs, sems, src, dst, src_dtype_bf=True):
    eye = bufs["eye"]
    pt = bufs["pt"]
    tT, vV = sems["tT"], sems["vV"]
    mk.wait("tensor", vV, vV.n)
    ev_done = [0, 0]
    for tau in range(NTT):
        for half in range(2):
            bf = (tau * 2 + half) % 2
            mk.wait("tensor", vV, ev_done[bf])
            for q in range(4):
                dc = half * 4 + q
                src_ap = src[:, tau * 1024 + dc * 128: tau * 1024 + dc * 128 + 128]
                last = q == 3
                mk.emit("tensor", lambda e, b=bf, q=q, s=src_ap, l=last:
                        e.transpose(pt[b][:, 128 * q:128 * q + 128], s, eye[:]).then_inc(tT.h, 1)
                        if l else
                        e.transpose(pt[b][:, 128 * q:128 * q + 128], s, eye[:]))
            done = tT.inc(1)
            dview = dst[:].rearrange("p (dc t) -> p dc t", dc=NDC, t=NTOK)
            dst_ap = dview[:, half * 4:half * 4 + 4, tau * 128:tau * 128 + 128]
            src_p = pt[bf][:].rearrange("p (q t) -> p q t", q=4, t=128)
            mk.wait("vector", tT, done)
            mk.emit("vector", lambda e, d=dst_ap, s=src_p: e.tensor_copy(d, s).then_inc(vV.h, 1))
            ev_done[bf] = vV.inc(1)


def emit_lnT(mk, bufs, sems, xt, n_d=D, eps=1e-5):
    tT, vV, sS = sems["tT"], sems["vV"], sems["sS"]
    onec, ones1 = bufs["onec"], bufs["ones1"]
    zsq = bufs["zsq"]
    rows = bufs["rows"]
    rows_bf = bufs["rows_bf"]
    pstx, psts, pbc = bufs["pstx"], bufs["psts"], bufs["pbc"]
    xv = xt[:].rearrange("p (dc t) -> p dc t", dc=NDC, t=NTOK)

    mk.wait("scalar", vV, vV.n)
    mk.wait("tensor", vV, vV.n)

    mm_done = [0, 0]
    rows_done = [0, 0]
    app_done = [0, 0]
    for sl in range(8):
        zb = sl % 2
        tok = slice(512 * sl, 512 * (sl + 1))
        ro = 2048 * zb
        rbo = 1024 * zb
        sumx = rows[0:1, ro:ro + 512]
        var = rows[0:1, ro + 512:ro + 1024]
        t2 = rows[0:1, ro + 1024:ro + 1536]
        rr = rows[0:1, ro + 1536:ro + 2048]
        m_bf = rows_bf[0:1, rbo:rbo + 512]
        r_bf = rows_bf[0:1, rbo + 512:rbo + 1024]
        mk.wait("scalar", tT, mm_done[zb])
        for dc in range(NDC):
            last = dc == NDC - 1
            mk.emit("scalar", lambda e, dc=dc, tok=tok, z=zb, l=last: (e.activation(
                zsq[z][:, 512 * dc:512 * dc + 512], xv[:, dc, tok], AF.Square
            ).then_inc(sS.h, 1) if l else e.activation(
                zsq[z][:, 512 * dc:512 * dc + 512], xv[:, dc, tok], AF.Square)))
        sq_ready = sS.inc(1)
        mk.wait("tensor", sS, sq_ready)
        mk.wait("tensor", vV, rows_done[zb])
        for dc in range(NDC):
            mk.emit("tensor", lambda e, dc=dc, tok=tok, z=zb: e.matmul(
                pstx[z][0:1, :], onec[:], xv[:, dc, tok],
                start=(dc == 0), stop=(dc == NDC - 1)))
            last = dc == NDC - 1
            mk.emit("tensor", lambda e, dc=dc, z=zb, l=last: (e.matmul(
                psts[z][0:1, :], onec[:], zsq[z][:, 512 * dc:512 * dc + 512],
                start=(dc == 0), stop=(dc == NDC - 1)
            ).then_inc(tT.h, 1) if l else e.matmul(
                psts[z][0:1, :], onec[:], zsq[z][:, 512 * dc:512 * dc + 512],
                start=(dc == 0), stop=(dc == NDC - 1))))
        mm_done[zb] = tT.inc(1)
        mk.wait("vector", tT, mm_done[zb])
        steps = [
            ("vector", lambda e, a=sumx, b=pstx[zb]: e.tensor_scalar(a, b[0:1, :], 1.0 / n_d, None, ALU.mult)),
            ("vector", lambda e, a=var, b=psts[zb]: e.tensor_scalar(a, b[0:1, :], 1.0 / n_d, eps, ALU.mult, ALU.add)),
            ("vector", lambda e, a=t2, b=sumx: e.tensor_tensor(a, b, b, ALU.mult)),
            ("vector", lambda e, a=var, b=t2: e.tensor_tensor(a, a, b, ALU.subtract)),
            ("vector", lambda e, a=t2, b=var: e.reciprocal(a, b)),
            ("scalar", lambda e, a=rr, b=t2: e.activation(a, b, AF.Sqrt)),
            ("vector", lambda e, a=m_bf, b=sumx: e.tensor_copy(a, b)),
            ("vector", lambda e, a=r_bf, b=rr: e.tensor_copy(a, b)),
        ]
        prev = None
        for eng, fn in steps:
            if prev is not None:
                pc, pv = prev
                mk.wait(eng, pc, pv)
            ctr = sS if eng == "scalar" else vV
            mk.emit(eng, lambda e, fn=fn, c=ctr: fn(e).then_inc(c.h, 1))
            prev = (ctr, ctr.inc(1))
        rows_done[zb] = vV.n
        mk.wait("tensor", prev[0], prev[1])
        mk.wait("tensor", vV, max(app_done))
        mk.emit("tensor", lambda e, m=m_bf: e.matmul(
            pbc[0][:], ones1[:], m, start=True, stop=True))
        mk.emit("tensor", lambda e, r=r_bf: e.matmul(
            pbc[1][:], ones1[:], r, start=True, stop=True).then_inc(tT.h, 1))
        bc_done = tT.inc(1)
        mk.wait("vector", tT, bc_done)
        for dc in range(NDC):
            last = dc == NDC - 1
            mk.emit("vector", lambda e, dc=dc, tok=tok, l=last: (e.tensor_tensor(
                xv[:, dc, tok], xv[:, dc, tok], pbc[0][:], ALU.subtract
            ).then_inc(vV.h, 1) if l else e.tensor_tensor(
                xv[:, dc, tok], xv[:, dc, tok], pbc[0][:], ALU.subtract)))
        mk.wait("vector", vV, vV.inc(1))
        for dc in range(NDC):
            last = dc == NDC - 1
            mk.emit("vector", lambda e, dc=dc, tok=tok, l=last: (e.tensor_tensor(
                xv[:, dc, tok], xv[:, dc, tok], pbc[1][:], ALU.mult
            ).then_inc(vV.h, 1) if l else e.tensor_tensor(
                xv[:, dc, tok], xv[:, dc, tok], pbc[1][:], ALU.mult)))
        app_done[zb] = vV.inc(1)
    bufs["ln_exit"] = max(app_done)


def emit_ffn0(mk, bufs, sems, wap, xt_dram, acc_dram, n_experts=8, do_experts=True):
    nc = mk.nc
    tT, vV, sS, dW, dA = (sems[n] for n in ("tT", "vV", "sS", "dW", "dA"))
    dX1, dX2 = mk.sem("dX1"), mk.sem("dX2")
    dxc = [mk.sem("dxc0"), mk.sem("dxc1")]
    dxw = [mk.sem("dxw0"), mk.sem("dxw1")]
    W1sb, W2sb = bufs["W1sb"], bufs["W2sb"]
    ar = bufs["arena"]
    ones1, eyef, onesA = bufs["ones1"], bufs["eyef"], bufs["onesA"]
    ps = bufs["ps_f32"]
    h_sb = ar[:, 0:8192]
    xwin = [ar[:, 8192:10240], ar[:, 10240:12288]]
    tmp = ar[:, 12288:14336].bitcast(F32)
    msel = ar[:, 14336:14848].bitcast(F32)
    b1T = ar[:, 14848:15360].bitcast(F32)
    b2q = ar[:, 15360:18432]
    xc = [ar[:, 18432:19456].bitcast(F32), ar[:, 19456:20480].bitcast(F32)]
    lgs = ar[:, 20480:20992].bitcast(F32)
    e8 = ar[:, 20992:21504].bitcast(F32)
    mx = ar[:, 21504:21568].bitcast(F32)
    ssum = ar[:, 21568:21632].bitcast(F32)
    b1s = ar[0:8, 21632:29824].bitcast(F32)
    red_sb = W2sb[:, 0:8192].bitcast(F32)
    ph = [ps[:, 0:256], ps[:, 512:768]]
    py = [ps[:, 1024:2048], ps[:, 2048:3072]]
    p_red = ps[0:16, 0:512]
    pl = ps[:, 512:768]
    pb1 = ps[:, 1024:1032]
    xdv = xt_dram.rearrange("(dc p) t -> p dc t", p=128)

    rsc = contextlib.ExitStack()
    wred = rsc.enter_context(nc.sbuf_tensor("wred", [128, 128], F32))
    wg_sb = rsc.enter_context(nc.sbuf_tensor("wg_sb", [8, 16], F32))
    wgt = rsc.enter_context(nc.sbuf_tensor("wgt", [8, 20], F32))
    wgnT = rsc.enter_context(nc.sbuf_tensor("wgnT", [16, 8], F32))

    mk.emit("gpsimd", lambda e: e.dma_start(
        wred[:].rearrange("p (dc j) -> p dc j", dc=NDC, j=16),
        wap["w_red"].rearrange("(dc p) j -> p dc j", p=128)).then_inc(dW.h, 16))
    mk.emit("gpsimd", lambda e: e.dma_start(wg_sb[:], wap["wg"]).then_inc(dW.h, 16))
    for ei in range(8):
        mk.emit("gpsimd", lambda e, ei=ei: e.dma_start(
            b2q[32 * (ei // 3):32 * (ei // 3) + 1, 1024 * (ei % 3):1024 * (ei % 3) + 1024],
            wap["b2"][ei:ei + 1, :]).then_inc(dW.h, 16))
    mk.emit("gpsimd", lambda e: e.dma_start(b1s[:], wap["b1"]).then_inc(dW.h, 16))
    stg_done = dW.inc(176)
    mk.emit("gpsimd", lambda e: e.dma_start(
        W1sb[:].rearrange("p (dc f) -> p dc f", dc=NDC, f=4096),
        wap["w1"][0].rearrange("(dc p) f -> p dc f", p=128)).then_inc(dX1.h, 16))
    w1_done = dX1.inc(16)

    mk.wait("scalar", dW, stg_done)
    chain = [
        ("scalar", lambda e: e.activation(wgt[:, 0:16], wg_sb[:], AF.Square)),
        ("vector", lambda e: e.reduce_sum(wgt[:, 16:17], wgt[:, 0:16], axis=AXL.X)),
        ("scalar", lambda e: e.activation(wgt[:, 17:18], wgt[:, 16:17], AF.Sqrt)),
        ("vector", lambda e: e.tensor_scalar_max(wgt[:, 18:19], wgt[:, 17:18], 1e-4)),
        ("vector", lambda e: e.reciprocal(wgt[:, 19:20], wgt[:, 18:19])),
        ("vector", lambda e: e.tensor_scalar(wg_sb[:], wg_sb[:], wgt[:, 19:20], None, ALU.mult)),
    ]
    prev = None
    for eng, fn in chain:
        if prev is not None:
            mk.wait(eng, prev[0], prev[1])
        c = sS if eng == "scalar" else vV
        mk.emit(eng, lambda e, fn=fn, c=c: fn(e).then_inc(c.h, 1))
        prev = (c, c.inc(1))
    mk.wait("tensor", prev[0], prev[1])
    mk.emit("tensor", lambda e: e.transpose(
        ps[0:16, 2048:2056], wg_sb[:], eyef[0:8, 0:8]).then_inc(tT.h, 1))
    wgt_done = tT.inc(1)
    mk.wait("vector", tT, wgt_done)
    mk.emit("vector", lambda e: e.tensor_copy(wgnT[:], ps[0:16, 2048:2056]).then_inc(vV.h, 1))
    wgnT_done = vV.inc(1)

    mk.wait("tensor", dW, stg_done)
    cp_done = {}
    for fc in range(32):
        if fc >= 1:
            mk.wait("tensor", vV, cp_done[fc - 1])
        mk.emit("tensor", lambda e, fc=fc: e.transpose(
            pb1, b1s[0:8, 128 * fc:128 * fc + 128], eyef[0:8, 0:8]).then_inc(tT.h, 1))
        td = tT.inc(1)
        mk.wait("vector", tT, td)
        mk.emit("vector", lambda e, fc=fc: e.tensor_copy(
            b1T[:, 8 * fc:8 * fc + 8], pb1).then_inc(vV.h, 1))
        cp_done[fc] = vV.inc(1)
    b1T_done = cp_done[31]

    wv = wred[:].rearrange("p (dc j) -> p dc j", dc=NDC, j=16)
    red_ev = {}
    mm_done = {}
    xc_done = {}
    for sl in range(8):
        for dc in range(NDC):
            z = dc % 2
            k = sl * NDC + dc
            mk.wait("gpsimd", tT, mm_done.get(k - 2, 0))
            mk.emit("gpsimd", lambda e, dc=dc, sl=sl, z=z: e.dma_start(
                xc[z][:], xdv[:, dc, 512 * sl:512 * sl + 512]).then_inc(dxc[z].h, 16))
            xc_done[k] = dxc[z].inc(16)
            mk.wait("tensor", dxc[z], xc_done[k])
            if dc == 0 and sl > 0:
                mk.wait("tensor", vV, red_ev[sl - 1])
            mk.emit("tensor", lambda e, dc=dc, z=z: e.matmul(
                p_red, wv[:, dc, :], xc[z][:],
                start=(dc == 0), stop=(dc == NDC - 1)).then_inc(tT.h, 1))
            mm_done[k] = tT.inc(1)
        mk.wait("vector", tT, mm_done[sl * NDC + NDC - 1])
        mk.emit("vector", lambda e, sl=sl: e.tensor_copy(
            red_sb[0:16, 512 * sl:512 * sl + 512], p_red).then_inc(vV.h, 1))
        red_ev[sl] = vV.inc(1)

    mk.wait("tensor", vV, red_ev[7])
    mk.wait("tensor", vV, wgnT_done)
    for tg in range(NTT):
        last = tg == NTT - 1
        mk.emit("tensor", lambda e, tg=tg, l=last: (e.matmul(
            pl[:, 8 * tg:8 * tg + 8], red_sb[0:16, 128 * tg:128 * tg + 128], wgnT[:],
            start=True, stop=True).then_inc(tT.h, 1) if l else e.matmul(
            pl[:, 8 * tg:8 * tg + 8], red_sb[0:16, 128 * tg:128 * tg + 128], wgnT[:],
            start=True, stop=True)))
    lg_done = tT.inc(1)
    plv = pl.rearrange("p (g e) -> p g e", g=NTT, e=8)
    mxv = mx.rearrange("p (g o) -> p g o", g=NTT, o=1)
    ep = [
        ("vector", lambda e: e.reduce_max(mxv, plv, axis=AXL.X)),
        ("vector", lambda e: e.tensor_tensor(
            lgs.rearrange("p (g e) -> p g e", g=NTT, e=8), plv,
            mxv.broadcast_to((128, NTT, 8)), ALU.subtract)),
        ("scalar", lambda e: e.activation(e8, lgs, AF.Exp)),
        ("vector", lambda e: e.reduce_sum(
            ssum.rearrange("p (g o) -> p g o", g=NTT, o=1),
            e8.rearrange("p (g e) -> p g e", g=NTT, e=8), axis=AXL.X)),
        ("vector", lambda e: e.reciprocal(mx, ssum)),
        ("vector", lambda e: e.tensor_single_scalar(lgs, e8, 1.0, ALU.is_ge)),
        ("vector", lambda e: e.tensor_tensor(
            msel.rearrange("p (g e) -> p g e", g=NTT, e=8),
            lgs.rearrange("p (g e) -> p g e", g=NTT, e=8),
            mxv.broadcast_to((128, NTT, 8)), ALU.mult)),
    ]
    prev = (tT, lg_done)
    for eng, fn in ep:
        mk.wait(eng, prev[0], prev[1])
        c = sS if eng == "scalar" else vV
        mk.emit(eng, lambda e, fn=fn, c=c: fn(e).then_inc(c.h, 1))
        prev = (c, c.inc(1))
    msel_done = prev[1]
    router_done_v = vV.n

    if not do_experts:
        rsc.close()
        return 0
    w2_done = 0
    phz_last = {0: 0, 1: 0}
    ev_cnt = {}
    dma_cnt = {}
    acc_last_of_e = 0
    phB_prev_win = 0
    phA_all_done = 0
    h_free_at = 0
    xw_done = {}
    xw_read_z = {0: 0, 1: 0}
    for e_ix in range(n_experts):
        mk.wait("gpsimd", tT, phB_prev_win if e_ix > 0 else lg_done)
        mk.emit("gpsimd", lambda e, ei=e_ix: e.dma_start(
            W2sb[:].rearrange("p (fc d) -> p fc d", fc=32, d=1024),
            wap["w2"][ei].rearrange("(fc p) d -> p fc d", p=128)).then_inc(dX2.h, 16))
        w2_done = dX2.inc(16)
        w1_this = w1_done
        mk.wait("tensor", dX1, w1_this)
        if e_ix == 0:
            mk.wait("tensor", vV, router_done_v)
            mk.wait("tensor", vV, b1T_done)
        for w in range(16):
            wz = w % 2
            mk.wait("gpsimd", tT, xw_read_z[wz])
            mk.emit("gpsimd", lambda e, w=w, wz=wz: e.dma_start(
                xwin[wz][:].rearrange("p (dc t) -> p dc t", dc=NDC, t=256),
                xdv[:, :, 256 * w:256 * w + 256]).then_inc(dxw[wz].h, 16))
            xw_done[(e_ix, w)] = dxw[wz].inc(16)
            xv = xwin[wz][:].rearrange("p (dc t) -> p dc t", dc=NDC, t=256)
            mk.wait("tensor", dxw[wz], xw_done[(e_ix, w)])
            for fc in range(32):
                z = fc % 2
                mk.wait("tensor", sS, phz_last[z])
                for dc in range(NDC):
                    last = dc == NDC - 1
                    mk.emit("tensor", lambda e, dc=dc, fc=fc, z=z, xv=xv, l=last: (e.matmul(
                        ph[z], W1sb[:, 4096 * dc + 128 * fc:4096 * dc + 128 * fc + 128],
                        xv[:, dc, :], start=(dc == 0), stop=(dc == NDC - 1)
                    ).then_inc(tT.h, 1) if l else e.matmul(
                        ph[z], W1sb[:, 4096 * dc + 128 * fc:4096 * dc + 128 * fc + 128],
                        xv[:, dc, :], start=(dc == 0), stop=(dc == NDC - 1))))
                mmA = tT.inc(1)
                mk.wait("scalar", tT, mmA)
                if fc == 0:
                    mk.wait("scalar", tT, h_free_at)
                mk.emit("scalar", lambda e, fc=fc, z=z, ei=e_ix: e.activation(
                    h_sb[:, 256 * fc:256 * fc + 256], ph[z], AF.Gelu,
                    bias=b1T[:, 8 * fc + ei:8 * fc + ei + 1]).then_inc(sS.h, 1))
                phz_last[z] = sS.inc(1)
            phA_all_done = tT.n
            xw_read_z[wz] = tT.n
            mk.wait("tensor", sS, max(phz_last.values()))
            mk.wait("tensor", dX2, w2_done)
            for th in range(2):
                tg = 2 * w + th
                pz = tg % 2
                mk.wait("tensor", vV, ev_cnt.get(tg - 2, 0))
                if e_ix == 0 and tg <= 1:
                    mk.wait("tensor", vV, router_done_v)
                for fc in range(32):
                    for hh in range(2):
                        mk.emit("tensor", lambda e, fc=fc, th=th, hh=hh, pz=pz: e.matmul(
                            py[pz][:, 512 * hh:512 * hh + 512],
                            h_sb[:, 256 * fc + 128 * th:256 * fc + 128 * th + 128],
                            W2sb[:, 1024 * fc + 512 * hh:1024 * fc + 512 * hh + 512],
                            start=(fc == 0), stop=False))
                b2r = b2q[32 * (e_ix // 3):32 * (e_ix // 3) + 1,
                          1024 * (e_ix % 3):1024 * (e_ix % 3) + 1024]
                brow = 32 * (e_ix // 3)
                for hh in range(2):
                    last = hh == 1
                    mk.emit("tensor", lambda e, hh=hh, pz=pz, r=b2r, br=brow, l=last: (e.matmul(
                        py[pz][:, 512 * hh:512 * hh + 512], onesA[br:br + 1, :],
                        r[:, 512 * hh:512 * hh + 512],
                        start=False, stop=True).then_inc(tT.h, 1) if l else e.matmul(
                        py[pz][:, 512 * hh:512 * hh + 512], onesA[br:br + 1, :],
                        r[:, 512 * hh:512 * hh + 512], start=False, stop=True)))
                mmB = tT.inc(1)
                mk.wait("vector", tT, mmB)
                mk.wait("vector", dA, dA.n)
                if e_ix == 0 and tg == 0:
                    mk.wait("vector", vV, msel_done)
                mk.emit("vector", lambda e, tg=tg, ei=e_ix, pz=pz: e.tensor_scalar(
                    tmp, py[pz][:], msel[:, 8 * tg + ei:8 * tg + ei + 1],
                    None, ALU.mult).then_inc(vV.h, 1))
                ev_cnt[tg] = vV.inc(1)
                mk.wait("gpsimd", vV, ev_cnt[tg])
                if tg == 0:
                    mk.wait("gpsimd", dA, acc_last_of_e)
                op = ALU.bypass if e_ix == 0 else ALU.add
                mk.emit("gpsimd", lambda e, tg=tg, op=op: e.dma_start(
                    acc_dram[128 * tg:128 * tg + 128, :], tmp,
                    accum_op=op).then_inc(dA.h, 16))
                dma_cnt[tg] = dA.inc(16)
            phB_prev_win = tT.n
            h_free_at = tT.n
        acc_last_of_e = dma_cnt[31]
        if e_ix < n_experts - 1:
            mk.wait("gpsimd", tT, phA_all_done)
            mk.emit("gpsimd", lambda e, ei=e_ix + 1: e.dma_start(
                W1sb[:].rearrange("p (dc f) -> p dc f", dc=NDC, f=4096),
                wap["w1"][ei].rearrange("(dc p) f -> p dc f", p=128)).then_inc(dX1.h, 16))
            w1_done = dX1.inc(16)
    rsc.close()
    return dma_cnt[31]


def emit_accT_add(mk, bufs, sems, xt_dram, acc_dram, acc_dma_done):
    nc = mk.nc
    tT, vV, dS = (sems[n] for n in ("tT", "vV", "dS"))
    xt, eye, pt = bufs["arena"], bufs["eye"], bufs["pt"]
    dL = [mk.sem("dL0"), mk.sem("dL1")]
    xv = xt[:].rearrange("p (dc t) -> p dc t", dc=NDC, t=NTOK)
    xdv = xt_dram.rearrange("(dc p) t -> p dc t", p=128)
    asc = contextlib.ExitStack()
    accl = [asc.enter_context(nc.sbuf_tensor(f"accl{mk.uid()}_{i}", [128, 1024], F32)) for i in range(2)]
    accb = [asc.enter_context(nc.sbuf_tensor(f"accb{mk.uid()}_{i}", [128, 1024], BF16)) for i in range(2)]
    x1w = [asc.enter_context(nc.sbuf_tensor(f"x1w{mk.uid()}_{i}", [128, 1024], BF16)) for i in range(2)]
    mk.wait("gpsimd", sems["dA"], acc_dma_done)
    ld = {}
    cb = {}
    add_done = {}
    tr_done = {}
    for tg in range(NTT):
        z = tg % 2
        mk.wait("gpsimd", vV, cb.get(tg - 2, 0))
        mk.emit("gpsimd", lambda e, tg=tg, z=z: e.dma_start(
            accl[z][:], acc_dram[128 * tg:128 * tg + 128, :]).then_inc(dL[z].h, 16))
        mk.wait("gpsimd", vV, add_done.get(tg - 2, 0))
        mk.emit("gpsimd", lambda e, tg=tg, z=z: e.dma_start(
            x1w[z][:].rearrange("p (dc t) -> p dc t", dc=NDC, t=128),
            xdv[:, :, 128 * tg:128 * tg + 128]).then_inc(dL[z].h, 16))
        ld[tg] = dL[z].inc(32)
        mk.wait("vector", dL[z], ld[tg])
        mk.wait("vector", tT, tr_done.get(tg - 2, 0))
        mk.emit("vector", lambda e, z=z: e.tensor_copy(accb[z][:], accl[z][:]).then_inc(vV.h, 1))
        cb[tg] = vV.inc(1)
        for half in range(2):
            pz = half
            mk.wait("tensor", vV, cb[tg])
            mk.wait("tensor", vV, add_done.get(tg - 1, 0))
            for q in range(4):
                dc = 4 * half + q
                last = q == 3
                mk.emit("tensor", lambda e, z=z, dc=dc, q=q, pz=pz, l=last: (e.transpose(
                    pt[pz][:, 128 * q:128 * q + 128], accb[z][:, 128 * dc:128 * dc + 128],
                    eye[:]).then_inc(tT.h, 1) if l else e.transpose(
                    pt[pz][:, 128 * q:128 * q + 128], accb[z][:, 128 * dc:128 * dc + 128],
                    eye[:])))
            t_done = tT.inc(1)
            tr_done[tg] = t_done
            mk.wait("vector", tT, t_done)
            x1v = x1w[z][:].rearrange("p (dc t) -> p dc t", dc=NDC, t=128)
            for q in range(4):
                dc = 4 * half + q
                last = q == 3
                mk.emit("vector", lambda e, tg=tg, dc=dc, q=q, pz=pz, z=z, x1v=x1v, l=last: (
                    e.tensor_tensor(
                        xv[:, dc, 128 * tg:128 * tg + 128], x1v[:, dc, :],
                        pt[pz][:, 128 * q:128 * q + 128], ALU.add).then_inc(vV.h, 1)
                    if l else e.tensor_tensor(
                        xv[:, dc, 128 * tg:128 * tg + 128], x1v[:, dc, :],
                        pt[pz][:, 128 * q:128 * q + 128], ALU.add)))
            add_done[tg] = vV.inc(1)
    asc.close()



def build_stage2(n_experts=8, do_experts=True, do_assembly=True, do_ln=True):
    nc = bass.Bass("TRN2", target_bir_lowering=False)
    xt_in = nc.dram_tensor("XT", [D, NTOK], BF16, kind="ExternalInput").ap()
    wap = {
        "w_red": nc.dram_tensor("WRED", [D, 16], F32, kind="ExternalInput").ap(),
        "wg": nc.dram_tensor("WG", [8, 16], F32, kind="ExternalInput").ap(),
        "w1": nc.dram_tensor("W1", [8, D, 4096], F32, kind="ExternalInput").ap(),
        "b1": nc.dram_tensor("B1", [8, 4096], F32, kind="ExternalInput").ap(),
        "w2": nc.dram_tensor("W2", [8, 4096, D], F32, kind="ExternalInput").ap(),
        "b2": nc.dram_tensor("B2", [8, D], F32, kind="ExternalInput").ap(),
    }
    eye_in = nc.dram_tensor("EYE", [128, 128], BF16, kind="ExternalInput").ap()
    eyef_in = nc.dram_tensor("EYEF", [128, 128], F32, kind="ExternalInput").ap()
    xt2_out = nc.dram_tensor("XT2", [D, NTOK], BF16, kind="ExternalOutput").ap()
    acc_dram = nc.dram_tensor("ACCD", [NTOK, D], F32).ap()

    mk = MK(nc)
    bufs = {}
    bufs["arena"] = mk.sb("arena", [128, NTOK * NDC], BF16)
    bufs["eye"] = mk.sb("eye", [128, 128], BF16)
    bufs["eyef"] = mk.sb("eyef", [128, 128], F32)
    bufs["onec"] = mk.sb("onec", [128, 1], BF16)
    bufs["ones1"] = mk.sb("ones1", [1, 128], BF16)
    bufs["onesA"] = mk.sb("onesA", [128, 128], BF16)
    wsc = contextlib.ExitStack()
    bufs["W1sb"] = wsc.enter_context(nc.sbuf_tensor("W1sb", [128, 32768], BF16))
    bufs["W2sb"] = wsc.enter_context(nc.sbuf_tensor("W2sb", [128, 32768], BF16))
    ps_f32 = mk.ps("ps_f32", [128, 3072], F32)
    ps_bf = mk.ps("ps_bf", [128, 2048], BF16)
    bufs["ps_f32"] = ps_f32
    bufs["pstx"] = [ps_f32[:, 0:512], ps_f32[:, 512:1024]]
    bufs["psts"] = [ps_f32[:, 1024:1536], ps_f32[:, 1536:2048]]
    bufs["pbc"] = [ps_f32[:, 2048:2560], ps_f32[:, 2560:3072]]
    bufs["pt"] = [ps_bf[:, 0:512], ps_bf[:, 1024:1536]]

    sems = {n: mk.sem(n) for n in ("dW", "dS", "dA", "tT", "vV", "sS")}
    dW, dS, dA, tT, vV, sS = (sems[n] for n in ("dW", "dS", "dA", "tT", "vV", "sS"))

    dC = mk.sem("dC")
    mk.emit("gpsimd", lambda e: e.dma_start(bufs["eye"][:], eye_in).then_inc(dC.h, 16))
    mk.emit("gpsimd", lambda e: e.dma_start(bufs["eyef"][:], eyef_in).then_inc(dC.h, 16))
    load_done = dC.inc(32)
    mk.emit("vector", lambda e: e.memset(bufs["onec"][:], 1.0))
    mk.emit("vector", lambda e: e.memset(bufs["ones1"][:], 1.0))
    mk.emit("vector", lambda e: e.memset(bufs["onesA"][:], 1.0).then_inc(vV.h, 1))
    const_done = vV.inc(1)
    for eng in ("tensor", "vector", "scalar"):
        mk.wait(eng, mk.sems["dC"], load_done)
        mk.wait(eng, vV, const_done)

    acc_done = emit_ffn0(mk, bufs, sems, wap, xt_in, acc_dram,
                         n_experts=n_experts, do_experts=do_experts)
    wsc.close()
    if do_assembly:
        emit_accT_add(mk, bufs, sems, xt_in, acc_dram, acc_done)
    if do_ln:
        bufs["zsq"] = [mk.sb(f"zsq{i}", [128, NTOK], BF16) for i in range(2)]
        bufs["rows"] = mk.sb("rows", [1, 4096], F32)
        bufs["rows_bf"] = mk.sb("rows_bf", [1, 2048], BF16)
        bufs["xt"] = bufs["arena"]
        emit_lnT(mk, bufs, sems, bufs["arena"])

    final_v = vV.n
    mk.wait("sync", vV, final_v)
    mk.wait("sync", sems["dA"], sems["dA"].n)
    mk.wait("sync", sems["sS"], sems["sS"].n)
    mk.wait("sync", tT, tT.n)
    xtv = bufs["arena"][:].rearrange("p (dc t) -> p dc t", dc=NDC, t=NTOK)
    for dc in range(NDC):
        mk.emit("sync", lambda e, dc=dc: e.dma_start(
            xt2_out[128 * dc:128 * dc + 128, :], xtv[:, dc, :]).then_inc(dS.h, 16))
    out_done = dS.inc(16 * NDC)
    mk.wait("sync", dS, out_done)
    mk.realize()
    return nc


def _shard_inputs_stage2(inputs, xt_by_core):
    eye = np.eye(128, dtype=np.float32).astype(bfloat16)
    eyef = np.eye(128, dtype=np.float32)
    com = {
        "WRED": np.ascontiguousarray(np.asarray(inputs["L0_ffn_w_red"], np.float32)),
        "WG": np.ascontiguousarray(np.asarray(inputs["L0_ffn_wg"], np.float32)),
        "W1": np.ascontiguousarray(np.asarray(inputs["L0_ffn_w1"], np.float32)),
        "B1": np.ascontiguousarray(np.asarray(inputs["L0_ffn_b1"], np.float32)),
        "W2": np.ascontiguousarray(np.asarray(inputs["L0_ffn_w2"], np.float32)),
        "B2": np.ascontiguousarray(np.asarray(inputs["L0_ffn_b2"], np.float32)),
        "EYE": eye, "EYEF": eyef,
    }
    return [dict(com, XT=np.ascontiguousarray(xt_by_core[c])) for c in range(len(xt_by_core))]


def emit_attn(mk, bufs, sems, wap, xt_dram, n_heads=16, do_proj=True, do_bphase=True, dbg=None):
    nc = mk.nc
    tT, vV, sS, dW, dS = (sems[n] for n in ("tT", "vV", "sS", "dW", "dS"))
    ar, eye, ut, onesA = bufs["arena"], bufs["eye"], bufs["ut"], bufs["onesA"]
    ones1 = bufs["ones1"]
    dwst, dxq = mk.sem("dwst"), mk.sem("dxq")
    ps, pt = bufs["ps_f32"], bufs["pt"]
    xdv = xt_dram.rearrange("(dc p) t -> p dc t", p=128)
    SA = [ps[:, 2048:2560], ps[:, 2560:3072]]
    PO = ps[:, 0:2048]
    PQ = [ps[:, 0:512], ps[:, 512:1024]]
    PV2 = ps[:, 1024:2048]

    asc = contextlib.ExitStack()
    qT = asc.enter_context(nc.sbuf_tensor("qT", [128, 16384], BF16))
    kT = asc.enter_context(nc.sbuf_tensor("kT", [128, 16384], BF16))
    vN = asc.enter_context(nc.sbuf_tensor("vN", [128, 16384], BF16))
    xqw = asc.enter_context(nc.sbuf_tensor("xqw", [128, 4096], BF16))
    wstg = asc.enter_context(nc.sbuf_tensor("wstg", [128, 1024], BF16))
    st8 = asc.enter_context(nc.sbuf_tensor("st8", [1, 64], F32))
    st8b = asc.enter_context(nc.sbuf_tensor("st8b", [1, 16], BF16))
    WpT = asc.enter_context(nc.sbuf_tensor("WpT", [128, 8192], BF16))
    vst = [asc.enter_context(nc.sbuf_tensor(f"vstA{i}", [128, 2048], BF16)) for i in range(2)]
    pstrip = [asc.enter_context(nc.sbuf_tensor(f"pstripA{i}", [128, 2048], BF16)) for i in range(2)]
    rbs = asc.enter_context(nc.sbuf_tensor("rbsA", [128, 1024], BF16))
    Bcols = asc.enter_context(nc.sbuf_tensor("BcolsA", [128, 16], F32))
    sqtmp = xqw

    def emit_wT_load(wsrc_rows, WpT_cols, dst_view):
        cp_z = {0: 0, 1: 0}
        tr_prev = tT.n
        for oc in range(8):
            mk.wait("gpsimd", tT, tr_prev)
            mk.emit("gpsimd", lambda e, oc=oc: e.dma_start(
                wstg[:], wsrc_rows(oc)).then_inc(dwst.h, 16))
            wd = dwst.inc(16)
            mk.wait("tensor", dwst, wd)
            for half in range(2):
                mk.wait("tensor", vV, cp_z[half])
                for q in range(4):
                    dc = 4 * half + q
                    last = q == 3
                    mk.emit("tensor", lambda e, dc=dc, q=q, z=half, l=last: (e.transpose(
                        pt[z][:, 128 * q:128 * q + 128], wstg[:, 128 * dc:128 * dc + 128],
                        eye[:]).then_inc(tT.h, 1) if l else e.transpose(
                        pt[z][:, 128 * q:128 * q + 128], wstg[:, 128 * dc:128 * dc + 128],
                        eye[:])))
                td = tT.inc(1)
                mk.wait("vector", tT, td)
                mk.emit("vector", lambda e, oc=oc, half=half: e.tensor_copy(
                    dst_view[:, 4 * half:4 * half + 4, 128 * oc:128 * oc + 128],
                    pt[half][:].rearrange("p (q t) -> p q t", q=4, t=128)
                ).then_inc(vV.h, 1))
                cp_z[half] = vV.inc(1)
            tr_prev = tT.n
        return max(cp_z.values())

    for b in range(BC):
        tb = 2048 * b
        for eng in ("tensor", "gpsimd", "scalar"):
            mk.wait(eng, vV, vV.n)
        for part in range(3):
            wv2 = WpT[:].rearrange("p (dc o) -> p dc o", dc=NDC, o=1024)
            wpT_ready = emit_wT_load(
                lambda oc, part=part: wap["w_qkv"][
                    128 * (8 * part + oc):128 * (8 * part + oc) + 128, :],
                None, wv2)
            mk.wait("tensor", vV, wpT_ready)
            xqv = xqw[:].rearrange("p (dc t) -> p dc t", dc=NDC, t=512)
            if part < 2:
                dst = qT if part == 0 else kT
                ev_z = {0: 0, 1: 0}
                mm_last = tT.n
                for ts in range(4):
                    mk.wait("gpsimd", tT, mm_last)
                    mk.emit("gpsimd", lambda e, ts=ts, tb=tb: e.dma_start(
                        xqv, xdv[:, :, tb + 512 * ts:tb + 512 * ts + 512]).then_inc(dxq.h, 16))
                    xd = dxq.inc(16)
                    mk.wait("tensor", dxq, xd)
                    for oc in range(8):
                        z = oc % 2
                        mk.wait("tensor", vV, ev_z[z])
                        for dc in range(NDC):
                            last = dc == NDC - 1
                            mk.emit("tensor", lambda e, dc=dc, oc=oc, z=z, l=last: (e.matmul(
                                PQ[z], WpT[:, 1024 * dc + 128 * oc:1024 * dc + 128 * oc + 128],
                                xqv[:, dc, :], start=(dc == 0), stop=(dc == NDC - 1)
                            ).then_inc(tT.h, 1) if l else e.matmul(
                                PQ[z], WpT[:, 1024 * dc + 128 * oc:1024 * dc + 128 * oc + 128],
                                xqv[:, dc, :], start=(dc == 0), stop=(dc == NDC - 1))))
                        mm_z = tT.inc(1)
                        mk.wait("vector", tT, mm_z)
                        mk.emit("vector", lambda e, oc=oc, ts=ts, z=z, dst=dst: e.tensor_copy(
                            dst[:, 2048 * oc + 512 * ts:2048 * oc + 512 * ts + 512],
                            PQ[z]).then_inc(vV.h, 1))
                        ev_z[z] = vV.inc(1)
                    mm_last = tT.n
            else:
                ev_prev = 0
                mm_last = tT.n
                for ts in range(4):
                    mk.wait("gpsimd", tT, mm_last)
                    mk.emit("gpsimd", lambda e, ts=ts, tb=tb: e.dma_start(
                        xqv, xdv[:, :, tb + 512 * ts:tb + 512 * ts + 512]).then_inc(dxq.h, 16))
                    xd = dxq.inc(16)
                    mk.wait("tensor", dxq, xd)
                    for it in range(4):
                        mk.wait("tensor", vV, ev_prev)
                        for dc in range(NDC):
                            for hh in range(2):
                                last = dc == NDC - 1 and hh == 1
                                mk.emit("tensor", lambda e, dc=dc, it=it, hh=hh, l=last: (e.matmul(
                                    PV2[:, 512 * hh:512 * hh + 512],
                                    xqv[:, dc, 128 * it:128 * it + 128],
                                    WpT[:, 1024 * dc + 512 * hh:1024 * dc + 512 * hh + 512],
                                    start=(dc == 0), stop=(dc == NDC - 1)
                                ).then_inc(tT.h, 1) if l else e.matmul(
                                    PV2[:, 512 * hh:512 * hh + 512],
                                    xqv[:, dc, 128 * it:128 * it + 128],
                                    WpT[:, 1024 * dc + 512 * hh:1024 * dc + 512 * hh + 512],
                                    start=(dc == 0), stop=(dc == NDC - 1))))
                        vmm = tT.inc(1)
                        mk.wait("vector", tT, vmm)
                        mk.emit("vector", lambda e, ts=ts, it=it: e.tensor_copy(
                            vN[:, 1024 * (4 * ts + it):1024 * (4 * ts + it) + 1024],
                            PV2).then_inc(vV.h, 1))
                        ev_prev = vV.inc(1)
                    mm_last = tT.n

        if dbg is not None and b == 0:
            dv = vV.n
            mk.wait("sync", vV, dv)
            mk.emit("sync", lambda e: e.dma_start(dbg["QT"], qT[:]).then_inc(dS.h, 16))
            mk.emit("sync", lambda e: e.dma_start(dbg["KT"], kT[:]).then_inc(dS.h, 16))
            mk.emit("sync", lambda e: e.dma_start(dbg["VN"], vN[:]).then_inc(dS.h, 16))
            dS.inc(48)
        if not do_bphase:
            continue
        qkv_v_done = vV.n
        prev = (vV, qkv_v_done)
        for h in range(16):
            bh = 64 * (h % 2)
            qsl = qT[bh:bh + 64, 2048 * (h // 2):2048 * (h // 2) + 2048]
            ksl = kT[bh:bh + 64, 2048 * (h // 2):2048 * (h // 2) + 2048]
            for which, sl in ((0, qsl), (1, ksl)):
                mk.wait("scalar", prev[0], prev[1])
                mk.emit("scalar", lambda e, sl=sl, bh=bh: e.activation(
                    sqtmp[bh:bh + 64, 0:2048], sl, AF.Square).then_inc(sS.h, 1))
                sq = (sS, sS.inc(1))
                mk.wait("tensor", sq[0], sq[1])
                for s in range(4):
                    last = s == 3
                    mk.emit("tensor", lambda e, s=s, bh=bh, l=last: (e.matmul(
                        PO[0:1, 512 * s:512 * s + 512], onesA[bh:bh + 64, 0:1],
                        sqtmp[bh:bh + 64, 512 * s:512 * s + 512], start=True, stop=True
                    ).then_inc(tT.h, 1) if l else e.matmul(
                        PO[0:1, 512 * s:512 * s + 512], onesA[bh:bh + 64, 0:1],
                        sqtmp[bh:bh + 64, 512 * s:512 * s + 512], start=True, stop=True)))
                mm = (tT, tT.inc(1))
                mk.wait("vector", mm[0], mm[1])
                col = 16 * which + h
                mk.emit("vector", lambda e, col=col: e.reduce_max(
                    st8[0:1, col:col + 1],
                    PO[0:1, 0:2048].rearrange("p (a o) -> p a o", a=1, o=2048),
                    axis=AXL.X).then_inc(vV.h, 1))
                prev = (vV, vV.inc(1))
        bsteps = [
            ("scalar", lambda e: e.activation(st8[0:1, 32:48], st8[0:1, 0:16], AF.Sqrt)),
            ("scalar", lambda e: e.activation(st8[0:1, 48:64], st8[0:1, 16:32], AF.Sqrt)),
            ("vector", lambda e: e.tensor_tensor(st8[0:1, 32:48], st8[0:1, 32:48],
                                                 st8[0:1, 48:64], ALU.mult)),
            ("vector", lambda e: e.tensor_scalar(st8[0:1, 48:64], st8[0:1, 32:48],
                                                 -0.125, None, ALU.mult)),
            ("vector", lambda e: e.tensor_copy(st8b[0:1, :], st8[0:1, 48:64])),
        ]
        bdone = seq_emit(mk, sems, bsteps, first=prev)
        mk.wait("tensor", bdone[0], bdone[1])
        for h in range(16):
            last = h == 15
            mk.emit("tensor", lambda e, h=h, l=last: (e.matmul(
                SA[0][:, h:h + 1], ones1[:], st8b[0:1, h:h + 1],
                start=True, stop=True).then_inc(tT.h, 1) if l else e.matmul(
                SA[0][:, h:h + 1], ones1[:], st8b[0:1, h:h + 1],
                start=True, stop=True)))
        bm = (tT, tT.inc(1))
        mk.wait("vector", bm[0], bm[1])
        mk.emit("vector", lambda e: e.tensor_copy(Bcols[:], SA[0][:, 0:16]).then_inc(vV.h, 1))
        bdone = (vV, vV.inc(1))

        vst_used = {0: (tT, 0), 1: (tT, 0)}
        po_free = (vV, 0)
        sa_free = {0: (sS, 0), 1: (sS, 0)}
        sa_rb_free = (vV, 0)
        pstrip_free = {0: (tT, 0), 1: (tT, 0)}
        atto_last = (vV, 0)
        for h in range(n_heads):
            zs = h % 2
            bh = 64 * (h % 2)
            sr = 64 - bh
            hq = 2048 * (h // 2)
            qsl = qT[bh:bh + 64, hq:hq + 2048]
            ksl = kT[bh:bh + 64, hq:hq + 2048]
            pw = vst_used[zs]
            mk.wait("vector", pw[0], pw[1])
            mk.wait("vector", bdone[0], bdone[1])
            mk.emit("vector", lambda e, zs=zs: e.memset(vst[zs][:], 1.0).then_inc(vV.h, 1))
            c3 = (vV, vV.inc(1))
            mk.wait("vector", c3[0], c3[1])
            for j in range(16):
                last = j == 15
                mk.emit("vector", lambda e, j=j, zs=zs, h=h, bh=bh, l=last: (e.tensor_copy(
                    vst[zs][:, 128 * j + bh:128 * j + bh + 64],
                    vN[:, 1024 * j + 64 * h:1024 * j + 64 * h + 64]).then_inc(vV.h, 1)
                    if l else e.tensor_copy(
                    vst[zs][:, 128 * j + bh:128 * j + bh + 64],
                    vN[:, 1024 * j + 64 * h:1024 * j + 64 * h + 64])))
            c4 = (vV, vV.inc(1))
            mk.wait("tensor", c4[0], c4[1])
            mk.wait("tensor", po_free[0], po_free[1])
            sc = 0
            pv_done = (tT, 0)
            for j in range(16):
                z2 = j % 2
                for s in range(j // 4, 4):
                    qs0 = 128 * j if s == j // 4 else 512 * s
                    N = 512 * (s + 1) - qs0
                    za = sc % 2
                    mk.wait("tensor", sa_free[za][0], sa_free[za][1])
                    if sc < 2:
                        mk.wait("tensor", sa_rb_free[0], sa_rb_free[1])
                    mk.emit("tensor", lambda e, j=j, qs0=qs0, N=N, za=za, ksl=ksl, qsl=qsl: e.matmul(
                        SA[za][:, 0:N], ksl[:, 128 * j:128 * j + 128],
                        qsl[:, qs0:qs0 + N],
                        start=True, stop=True).then_inc(tT.h, 1))
                    smm = (tT, tT.inc(1))
                    mk.wait("scalar", smm[0], smm[1])
                    mk.wait("scalar", pstrip_free[z2][0], pstrip_free[z2][1])
                    mk.emit("scalar", lambda e, qs0=qs0, N=N, za=za, z2=z2, h=h: e.activation(
                        pstrip[z2][:, qs0:qs0 + N], SA[za][:, 0:N],
                        AF.Exp, scale=0.125, bias=Bcols[:, h:h + 1]).then_inc(sS.h, 1))
                    ex = (sS, sS.inc(1))
                    sa_free[za] = ex
                    pv_wait = ex
                    if s == j // 4:
                        mk.wait("vector", ex[0], ex[1])
                        mk.emit("vector", lambda e, j=j, z2=z2: e.tensor_tensor(
                            pstrip[z2][:, 128 * j:128 * j + 128],
                            pstrip[z2][:, 128 * j:128 * j + 128],
                            ut[:], ALU.mult).then_inc(vV.h, 1))
                        pv_wait = (vV, vV.inc(1))
                    mk.wait("tensor", pv_wait[0], pv_wait[1])
                    jl = min(15, 4 * s + 3)
                    mk.emit("tensor", lambda e, j=j, qs0=qs0, N=N, z2=z2, zs=zs, jl=jl: e.matmul(
                        PO[:, qs0:qs0 + N], vst[zs][:, 128 * j:128 * j + 128],
                        pstrip[z2][:, qs0:qs0 + N],
                        start=(j == 0), stop=(j == jl),
                        skip_group_check=True).then_inc(tT.h, 1))
                    pv_done = (tT, tT.inc(1))
                    sc += 1
                pstrip_free[z2] = pv_done
            vst_used[zs] = pv_done
            mk.wait("vector", pv_done[0], pv_done[1])

            def _recip(e, sqtmp=sqtmp, PO=PO, nc=nc, vV=vV, sr=sr):
                with nc.allow_low_precision(reason="softmax 1/sum applied in bf16"):
                    return e.reciprocal(sqtmp[sr:sr + 1, 0:2048], PO[sr:sr + 1, 0:2048]).then_inc(vV.h, 1)

            mk.emit("vector", _recip)
            rr = (vV, vV.inc(1))
            mk.wait("tensor", rr[0], rr[1])
            mk.wait("tensor", sa_free[0][0], sa_free[0][1])
            mk.wait("tensor", sa_free[1][0], sa_free[1][1])
            mk.wait("tensor", sa_rb_free[0], sa_rb_free[1])
            rbc = None
            for half in range(2):
                if rbc is not None:
                    mk.wait("tensor", rbc[0], rbc[1])
                for s2 in range(2):
                    last = s2 == 1
                    mk.emit("tensor", lambda e, half=half, s2=s2, sr=sr, bh=bh, l=last: (e.matmul(
                        SA[s2][bh:bh + 64, :], onesA[sr:sr + 1, 0:64],
                        sqtmp[sr:sr + 1, 1024 * half + 512 * s2:1024 * half + 512 * s2 + 512],
                        start=True, stop=True).then_inc(tT.h, 1) if l else e.matmul(
                        SA[s2][bh:bh + 64, :], onesA[sr:sr + 1, 0:64],
                        sqtmp[sr:sr + 1, 1024 * half + 512 * s2:1024 * half + 512 * s2 + 512],
                        start=True, stop=True)))
                rbm = (tT, tT.inc(1))
                mk.wait("vector", rbm[0], rbm[1])
                if half == 1:
                    mk.wait("vector", sa_rb_free[0], sa_rb_free[1])
                mk.emit("vector", lambda e, bh=bh: e.tensor_copy(
                    rbs[bh:bh + 64, 0:512], SA[0][bh:bh + 64, :]))
                mk.emit("vector", lambda e, bh=bh: e.tensor_copy(
                    rbs[bh:bh + 64, 512:1024], SA[1][bh:bh + 64, :]).then_inc(vV.h, 1))
                rbc = (vV, vV.inc(1))
                mk.wait("vector", rbc[0], rbc[1])
                dsth = qT[bh:bh + 64, hq + 1024 * half:hq + 1024 * half + 1024]
                mk.emit("vector", lambda e, dsth=dsth, half=half, bh=bh: e.tensor_tensor(
                    dsth, PO[bh:bh + 64, 1024 * half:1024 * half + 1024],
                    rbs[bh:bh + 64, 0:1024], ALU.mult).then_inc(vV.h, 1))
                sa_rb_free = (vV, vV.inc(1))
            po_free = sa_rb_free
            atto_last = sa_rb_free
        if not do_proj:
            continue

        WoT = WpT
        wo2 = WoT[:].rearrange("p (ic o) -> p ic o", ic=NDC, o=1024)
        mk.wait("tensor", atto_last[0], atto_last[1])
        woT_ready = emit_wT_load(
            lambda oc: wap["w_out"][128 * oc:128 * oc + 128, :], None, wo2)
        mk.wait("tensor", vV, woT_ready)
        xqv = xqw[:].rearrange("p (dc t) -> p dc t", dc=NDC, t=512)
        arv = ar[:].rearrange("p (dc t) -> p dc t", dc=NDC, t=NTOK)
        ev_z = {0: 0, 1: 0}
        mm_last = tT.n
        for ts in range(4):
            mk.wait("gpsimd", tT, mm_last)
            mk.wait("gpsimd", vV, max(ev_z.values()))
            mk.emit("gpsimd", lambda e, ts=ts, tb=tb: e.dma_start(
                xqv, xdv[:, :, tb + 512 * ts:tb + 512 * ts + 512]).then_inc(dxq.h, 16))
            xd = dxq.inc(16)
            mk.wait("tensor", dxq, xd)
            for oc in range(8):
                z = oc % 2
                mk.wait("tensor", vV, ev_z[z])
                for ic in range(NDC):
                    last = ic == NDC - 1
                    mk.emit("tensor", lambda e, ic=ic, oc=oc, ts=ts, z=z, l=last: (e.matmul(
                        PQ[z], WoT[:, 1024 * ic + 128 * oc:1024 * ic + 128 * oc + 128],
                        qT[:, 2048 * ic + 512 * ts:2048 * ic + 512 * ts + 512],
                        start=(ic == 0), stop=(ic == NDC - 1)).then_inc(tT.h, 1)
                        if l else e.matmul(
                        PQ[z], WoT[:, 1024 * ic + 128 * oc:1024 * ic + 128 * oc + 128],
                        qT[:, 2048 * ic + 512 * ts:2048 * ic + 512 * ts + 512],
                        start=(ic == 0), stop=(ic == NDC - 1))))
                mm_z = tT.inc(1)
                mk.wait("vector", tT, mm_z)
                mk.emit("vector", lambda e, oc=oc, ts=ts, z=z, tb=tb: e.tensor_tensor(
                    arv[:, oc, tb + 512 * ts:tb + 512 * ts + 512], PQ[z],
                    xqv[:, oc, :], ALU.add).then_inc(vV.h, 1))
                ev_z[z] = vV.inc(1)
            mm_last = tT.n
    asc.close()


def emit_ffn1(mk, bufs, sems, wap, xt_dram, acc_dram):
    nc = mk.nc
    tT, vV, sS, dW, dA = (sems[n] for n in ("tT", "vV", "sS", "dW", "dA"))
    dE = [mk.sem("dE0"), mk.sem("dE1")]
    dxc = [mk.sem("dxc0"), mk.sem("dxc1")]
    dxw = [mk.sem("dxw0"), mk.sem("dxw1")]
    ar = bufs["arena"]
    ps = bufs["ps_f32"]
    h_sb = ar[:, 0:1024]
    xwin = [ar[:, 1024:3072], ar[:, 3072:5120]]
    tmp = ar[:, 5120:7168].bitcast(F32)
    gsel = ar[:, 7168:7680].bitcast(F32)
    lgs = ar[:, 7680:8192].bitcast(F32)
    m8 = ar[:, 8192:8704].bitcast(F32)
    e2g = ar[:, 8704:8960].bitcast(F32)
    wr_sb = ar[:, 11008:11136].bitcast(F32)
    rb_sb = ar[0:1, 11136:11152].bitcast(F32)
    onesF = ar[0:1, 11152:11408].bitcast(F32)
    xca = [ar[:, 11520:19712].bitcast(F32), ar[:, 19712:27904].bitcast(F32)]
    ph1 = ps[:, 0:256]
    ph3 = ps[:, 512:768]
    py = [ps[:, 1024:2048], ps[:, 2048:3072]]
    pl = ps[:, 512:768]
    xdv = xt_dram.rearrange("(dc p) t -> p dc t", p=128)

    wsc = contextlib.ExitStack()
    w1s = [wsc.enter_context(nc.sbuf_tensor(f"f1w1{mk.uid()}_{i}", [128, 4096], BF16)) for i in range(2)]
    w3s = [wsc.enter_context(nc.sbuf_tensor(f"f1w3{mk.uid()}_{i}", [128, 4096], BF16)) for i in range(2)]
    w2s = [wsc.enter_context(nc.sbuf_tensor(f"f1w2{mk.uid()}_{i}", [128, 4096], BF16)) for i in range(2)]

    def expert_w(ex):
        if ex < 8:
            return wap["r_w1"][ex], wap["r_w3"][ex], wap["r_w2"][ex]
        return wap["s_w1"][ex - 8], wap["s_w3"][ex - 8], wap["s_w2"][ex - 8]

    mk.emit("gpsimd", lambda e: e.dma_start(
        wr_sb.rearrange("p (dc j) -> p dc j", dc=NDC, j=8),
        wap["router_w"].rearrange("(dc p) j -> p dc j", p=128)).then_inc(dW.h, 16))
    mk.emit("gpsimd", lambda e: e.dma_start(
        rb_sb, wap["router_b"].unsqueeze(0)).then_inc(dW.h, 16))
    stg_done = dW.inc(32)
    mk.emit("vector", lambda e: e.memset(onesF, 1.0).then_inc(vV.h, 1))
    onesF_done = vV.inc(1)
    def load_expert(ex, buf):
        a1, a3, a2 = expert_w(ex)
        mk.emit("gpsimd", lambda e, a1=a1, buf=buf: e.dma_start(
            w1s[buf][:].rearrange("p (dc f) -> p dc f", dc=NDC, f=512),
            a1.rearrange("(dc p) f -> p dc f", p=128)).then_inc(dE[buf].h, 16))
        mk.emit("gpsimd", lambda e, a3=a3, buf=buf: e.dma_start(
            w3s[buf][:].rearrange("p (dc f) -> p dc f", dc=NDC, f=512),
            a3.rearrange("(dc p) f -> p dc f", p=128)).then_inc(dE[buf].h, 16))
        mk.emit("gpsimd", lambda e, a2=a2, buf=buf: e.dma_start(
            w2s[buf][:].rearrange("p (fc d) -> p fc d", fc=4, d=1024),
            a2.rearrange("(fc p) d -> p fc d", p=128)).then_inc(dE[buf].h, 16))
        return dE[buf].inc(48)

    wready = {0: load_expert(0, 0)}

    mk.wait("tensor", dW, stg_done)
    mk.wait("tensor", vV, onesF_done)
    wrv = wr_sb.rearrange("p (dc j) -> p dc j", dc=NDC, j=8)
    mm_done = {}
    for sl in range(8):
        z = sl % 2
        mk.wait("gpsimd", tT, mm_done.get(sl - 2, 0))
        mk.emit("gpsimd", lambda e, sl=sl, z=z: e.dma_start(
            xca[z].rearrange("p (dc t) -> p dc t", dc=NDC, t=512),
            xdv[:, :, 512 * sl:512 * sl + 512]).then_inc(dxc[z].h, 16))
        xd = dxc[z].inc(16)
        mk.wait("tensor", dxc[z], xd)
        xcv = xca[z].rearrange("p (dc t) -> p dc t", dc=NDC, t=512)
        for t4 in range(4):
            tg = 4 * sl + t4
            for dc in range(NDC):
                mk.emit("tensor", lambda e, dc=dc, t4=t4, tg=tg, xcv=xcv: e.matmul(
                    pl[:, 8 * tg:8 * tg + 8],
                    xcv[:, dc, 128 * t4:128 * t4 + 128], wrv[:, dc, :],
                    start=(dc == 0), stop=False))
            mk.emit("tensor", lambda e, tg=tg: e.matmul(
                pl[:, 8 * tg:8 * tg + 8], onesF, rb_sb,
                start=False, stop=True).then_inc(tT.h, 1))
            tT.inc(1)
        mm_done[sl] = tT.n
    lg_done = tT.n
    mk.wait("vector", tT, lg_done)
    mk.emit("vector", lambda e: e.tensor_copy(lgs, pl[:, 0:256]).then_inc(vV.h, 1))
    lgs_done = (vV, vV.inc(1))
    mk.wait("vector", lgs_done[0], lgs_done[1])
    for tg in range(NTT):
        last = tg == NTT - 1
        mk.emit("vector", lambda e, tg=tg, l=last: (e.max(
            out=m8[:, 8 * tg:8 * tg + 8], in_=lgs[:, 8 * tg:8 * tg + 8]).then_inc(vV.h, 1)
            if l else e.max(out=m8[:, 8 * tg:8 * tg + 8], in_=lgs[:, 8 * tg:8 * tg + 8])))
    m8_done = (vV, vV.inc(1))
    m8v = m8.rearrange("p (g e) -> p g e", g=NTT, e=8)
    lgv = lgs.rearrange("p (g e) -> p g e", g=NTT, e=8)
    gsv = gsel.rearrange("p (g e) -> p g e", g=NTT, e=8)
    e2 = e2g[:, 0:32]
    g1 = e2g[:, 32:64]
    g2 = e2g[:, 64:96]
    scr = e2g[:, 96:128]
    e2v = e2.rearrange("p (g o) -> p g o", g=NTT, o=1)
    g1v = g1.rearrange("p (g o) -> p g o", g=NTT, o=1)
    g2v = g2.rearrange("p (g o) -> p g o", g=NTT, o=1)
    gsteps = [
        ("vector", lambda e: e.tensor_tensor(e2v, m8v[:, :, 1:2], m8v[:, :, 0:1], ALU.subtract)),
        ("scalar", lambda e: e.activation(e2, e2, AF.Exp)),
        ("vector", lambda e: e.tensor_scalar(scr, e2, 1.0, None, ALU.add)),
        ("vector", lambda e: e.reciprocal(g1, scr)),
        ("vector", lambda e: e.tensor_tensor(g2, e2, g1, ALU.mult)),
        ("vector", lambda e: e.tensor_tensor(gsv, lgv, m8v[:, :, 0:1].broadcast_to((128, NTT, 8)), ALU.is_equal)),
        ("vector", lambda e: e.tensor_tensor(gsv, gsv, g1v.broadcast_to((128, NTT, 8)), ALU.mult)),
        ("vector", lambda e: e.tensor_tensor(lgs, lgv, m8v[:, :, 1:2].broadcast_to((128, NTT, 8)), ALU.is_equal)),
        ("vector", lambda e: e.tensor_tensor(lgs, lgv, g2v.broadcast_to((128, NTT, 8)), ALU.mult)),
        ("vector", lambda e: e.tensor_tensor(gsel, gsel, lgs, ALU.add)),
    ]
    gdone = seq_emit(mk, sems, gsteps, first=m8_done)
    router_done_v = vV.n

    ev_cnt = {}
    dma_cnt = {}
    acc_last_of_e = 0
    xw_read_z = {0: 0, 1: 0}
    phA_all_done = 0
    phB_prev = 0
    sil_prev = 0
    for ex in range(10):
        buf = ex % 2
        mk.wait("tensor", dE[buf], wready[ex])
        if ex == 0:
            mk.wait("tensor", vV, router_done_v)
        w1v = w1s[buf][:].rearrange("p (dc f) -> p dc f", dc=NDC, f=512)
        w3v = w3s[buf][:].rearrange("p (dc f) -> p dc f", dc=NDC, f=512)
        w2v = w2s[buf][:].rearrange("p (fc d) -> p fc d", fc=4, d=1024)
        for w in range(16):
            wz = w % 2
            mk.wait("gpsimd", tT, xw_read_z[wz])
            mk.emit("gpsimd", lambda e, w=w, wz=wz: e.dma_start(
                xwin[wz][:].rearrange("p (dc t) -> p dc t", dc=NDC, t=256),
                xdv[:, :, 256 * w:256 * w + 256]).then_inc(dxw[wz].h, 16))
            xwd = dxw[wz].inc(16)
            xv = xwin[wz][:].rearrange("p (dc t) -> p dc t", dc=NDC, t=256)
            mk.wait("tensor", dxw[wz], xwd)
            for fc in range(4):
                mk.wait("tensor", vV, sil_prev)
                for which, wv in ((0, w1v), (1, w3v)):
                    dstp = ph1 if which == 0 else ph3
                    for dc in range(NDC):
                        last = dc == NDC - 1 and which == 1
                        mk.emit("tensor", lambda e, dc=dc, fc=fc, wv=wv, dstp=dstp, xv=xv, l=last: (e.matmul(
                            dstp, wv[:, dc, 128 * fc:128 * fc + 128],
                            xv[:, dc, :], start=(dc == 0), stop=(dc == NDC - 1)
                        ).then_inc(tT.h, 1) if l else e.matmul(
                            dstp, wv[:, dc, 128 * fc:128 * fc + 128],
                            xv[:, dc, :], start=(dc == 0), stop=(dc == NDC - 1))))
                mmA = tT.inc(1)
                mk.wait("scalar", tT, mmA)
                mk.emit("scalar", lambda e, fc=fc: e.activation(
                    h_sb[:, 256 * fc:256 * fc + 256], ph1, AF.Silu).then_inc(sS.h, 1))
                sdone = sS.inc(1)
                mk.wait("vector", sS, sdone)
                mk.emit("vector", lambda e, fc=fc: e.tensor_tensor(
                    h_sb[:, 256 * fc:256 * fc + 256], h_sb[:, 256 * fc:256 * fc + 256],
                    ph3, ALU.mult).then_inc(vV.h, 1))
                sil_prev = vV.inc(1)
            phA_all_done = tT.n
            xw_read_z[wz] = tT.n
            h_ready = sil_prev
            mk.wait("tensor", vV, h_ready)
            for th in range(2):
                tg = 2 * w + th
                pz = tg % 2
                mk.wait("tensor", vV, ev_cnt.get(tg - 2, 0))
                if ex == 0 and tg <= 1:
                    mk.wait("tensor", vV, router_done_v)
                for fc in range(4):
                    for hh in range(2):
                        last = fc == 3 and hh == 1
                        mk.emit("tensor", lambda e, fc=fc, th=th, hh=hh, pz=pz, w2v=w2v, l=last: (e.matmul(
                            py[pz][:, 512 * hh:512 * hh + 512],
                            h_sb[:, 256 * fc + 128 * th:256 * fc + 128 * th + 128],
                            w2v[:, fc, 512 * hh:512 * hh + 512],
                            start=(fc == 0), stop=(fc == 3)).then_inc(tT.h, 1)
                            if l else e.matmul(
                            py[pz][:, 512 * hh:512 * hh + 512],
                            h_sb[:, 256 * fc + 128 * th:256 * fc + 128 * th + 128],
                            w2v[:, fc, 512 * hh:512 * hh + 512],
                            start=(fc == 0), stop=(fc == 3))))
                mmB = tT.inc(1)
                mk.wait("vector", tT, mmB)
                mk.wait("vector", dA, dA.n)
                if ex < 8:
                    mk.emit("vector", lambda e, tg=tg, ex=ex, pz=pz: e.tensor_scalar(
                        tmp, py[pz][:], gsel[:, 8 * tg + ex:8 * tg + ex + 1],
                        None, ALU.mult).then_inc(vV.h, 1))
                else:
                    mk.emit("vector", lambda e, pz=pz: e.tensor_copy(
                        tmp, py[pz][:]).then_inc(vV.h, 1))
                ev_cnt[tg] = vV.inc(1)
                mk.wait("gpsimd", vV, ev_cnt[tg])
                if tg == 0:
                    mk.wait("gpsimd", dA, acc_last_of_e)
                op = ALU.bypass if ex == 0 else ALU.add
                mk.emit("gpsimd", lambda e, tg=tg, op=op: e.dma_start(
                    acc_dram[128 * tg:128 * tg + 128, :], tmp,
                    accum_op=op).then_inc(dA.h, 16))
                dma_cnt[tg] = dA.inc(16)
            phB_prev = tT.n
        acc_last_of_e = dma_cnt[31]
        if ex < 9:
            mk.wait("gpsimd", tT, phA_all_done)
            wready[ex + 1] = load_expert(ex + 1, (ex + 1) % 2)
    wsc.close()
    return dma_cnt[31]


def emit_final_out(mk, bufs, sems, out_dram):
    nc = mk.nc
    tT, vV, dS = (sems[n] for n in ("tT", "vV", "dS"))
    ar, eye, pt = bufs["arena"], bufs["eye"], bufs["pt"]
    arv = ar[:].rearrange("p (dc t) -> p dc t", dc=NDC, t=NTOK)
    ov = out_dram.rearrange("b (i p) d -> p b i d", p=128)
    fsc = contextlib.ExitStack()
    fst = [fsc.enter_context(nc.sbuf_tensor(f"fst{mk.uid()}_{i}", [128, 1024], F32)) for i in range(2)]
    dF = [mk.sem("dF0"), mk.sem("dF1")]
    mk.wait("tensor", vV, vV.n)
    st_done = {}
    cp_done = {}
    tr_done = {}
    for tg in range(NTT):
        b, i = tg // 16, tg % 16
        z = tg % 2
        for half in range(2):
            pz = half
            mk.wait("tensor", vV, cp_done.get(tg - 1, 0))
            for q in range(4):
                dc = 4 * half + q
                last = q == 3
                mk.emit("tensor", lambda e, dc=dc, q=q, pz=pz, tg=tg, l=last: (e.transpose(
                    pt[pz][:, 128 * q:128 * q + 128],
                    arv[:, dc, 128 * tg:128 * tg + 128], eye[:]).then_inc(tT.h, 1)
                    if l else e.transpose(
                    pt[pz][:, 128 * q:128 * q + 128],
                    arv[:, dc, 128 * tg:128 * tg + 128], eye[:])))
            td = tT.inc(1)
            tr_done[tg] = td
            mk.wait("vector", tT, td)
            mk.wait("vector", dF[z], st_done.get(tg - 2, 0))
            last = half == 1
            mk.emit("vector", lambda e, z=z, half=half, l=last: (e.tensor_copy(
                fst[z][:, 512 * half:512 * half + 512],
                pt[half][:].rearrange("p (q t) -> p (q t)", q=4, t=128)).then_inc(vV.h, 1)
                if l else e.tensor_copy(
                fst[z][:, 512 * half:512 * half + 512],
                pt[half][:].rearrange("p (q t) -> p (q t)", q=4, t=128))))
            cp_done[tg] = vV.inc(1) if last else cp_done.get(tg, 0)
        mk.wait("sync", vV, cp_done[tg])
        mk.emit("sync", lambda e, b=b, i=i, z=z: e.dma_start(
            ov[:, b, i, :], fst[z][:]).then_inc(dF[z].h, 16))
        st_done[tg] = dF[z].inc(16)
    mk.wait("sync", dF[0], dF[0].n)
    mk.wait("sync", dF[1], dF[1].n)
    fsc.close()


def seq_emit(mk, sems, steps, first=None):
    tT, vV, sS = sems["tT"], sems["vV"], sems["sS"]
    prev = first
    for eng, fn in steps:
        if prev is not None:
            mk.wait(eng, prev[0], prev[1])
        c = sS if eng == "scalar" else (vV if eng == "vector" else tT)
        mk.emit(eng, lambda e, fn=fn, c=c: fn(e).then_inc(c.h, 1))
        prev = (c, c.inc(1))
    return prev



def build_stage1(out_dtype=BF16, debug=False):
    nc = bass.Bass("TRN2", target_bir_lowering=False)
    u = nc.dram_tensor("U", [BC, T, D], F32, kind="ExternalInput").ap()
    tstack = nc.dram_tensor("TSTACK", [128, 7, 128, 128], F32, kind="ExternalInput").ap()
    eye_in = nc.dram_tensor("EYE", [128, 128], BF16, kind="ExternalInput").ap()
    xt_out = nc.dram_tensor("XT", [D, NTOK], out_dtype, kind="ExternalOutput").ap()

    mk = MK(nc)
    bufs = {}
    bufs["ubf"] = mk.sb("ubf", [128, NTOK * NDC], BF16)
    bufs["xt"] = mk.sb("xt", [128, NTOK * NDC], BF16)
    bufs["tsb"] = [mk.sb(f"tsb{i}", [128, 896], BF16) for i in range(2)]
    bufs["eye"] = mk.sb("eye", [128, 128], BF16)
    bufs["onec"] = mk.sb("onec", [128, 1], BF16)
    bufs["ones1"] = mk.sb("ones1", [1, 128], BF16)
    bufs["zsq"] = [mk.sb(f"zsq{i}", [128, NTOK], BF16) for i in range(2)]
    bufs["rows"] = mk.sb("rows", [1, 4096], F32)
    bufs["rows_bf"] = mk.sb("rows_bf", [1, 2048], BF16)
    ps_f32 = mk.ps("ps_f32", [128, 3072], F32)
    ps_bf = mk.ps("ps_bf", [128, 2048], BF16)
    bufs["pg"] = [ps_f32[:, 0:512], ps_f32[:, 512:1024]]
    bufs["pstx"] = [ps_f32[:, 0:512], ps_f32[:, 512:1024]]
    bufs["psts"] = [ps_f32[:, 1024:1536], ps_f32[:, 1536:2048]]
    bufs["pbc"] = [ps_f32[:, 2048:2560], ps_f32[:, 2560:3072]]
    bufs["pt"] = [ps_bf[:, 0:512], ps_bf[:, 1024:1536]]

    sems = {n: mk.sem(n) for n in ("dW", "dS", "tT", "vV", "sS")}
    dW, dS, tT, vV, sS = (sems[n] for n in ("dW", "dS", "tT", "vV", "sS"))

    dC = mk.sem("dC")
    mk.emit("gpsimd", lambda e: e.dma_start(bufs["eye"][:], eye_in).then_inc(dC.h, 16))
    mk.emit("gpsimd", lambda e: e.dma_start(
        bufs["ubf"][:].rearrange("p (b i d) -> p b i d", b=BC, i=16, d=D),
        u.rearrange("b (i p) d -> p b i d", p=128)).then_inc(dC.h, 16))
    load_done = dC.inc(32)
    mk.emit("vector", lambda e: e.memset(bufs["onec"][:], 1.0))
    mk.emit("vector", lambda e: e.memset(bufs["ones1"][:], 1.0).then_inc(vV.h, 1))
    const_done = vV.inc(1)
    mk.wait("tensor", mk.sems["dC"], load_done)
    mk.wait("tensor", vV, const_done)
    mk.wait("vector", mk.sems["dC"], load_done)

    emit_conv_block(mk, bufs, sems, tstack)
    if debug:
        udbg = nc.dram_tensor("UDBG", [128, NTOK * NDC], BF16, kind="ExternalOutput").ap()
        mk.wait("sync", vV, vV.n)
        mk.emit("sync", lambda e: e.dma_start(udbg, bufs["ubf"][:]).then_inc(dS.h, 16))
        dS.inc(16)
    emit_transpose_nat_to_T(mk, bufs, sems, bufs["ubf"], bufs["xt"])
    emit_lnT(mk, bufs, sems, bufs["xt"])

    final_v = vV.n
    mk.wait("sync", vV, final_v)
    xtv = bufs["xt"][:].rearrange("p (dc t) -> p dc t", dc=NDC, t=NTOK)
    for dc in range(NDC):
        mk.emit("sync", lambda e, dc=dc: e.dma_start(
            xt_out[128 * dc:128 * dc + 128, :], xtv[:, dc, :]).then_inc(dS.h, 16))
    out_done = dS.inc(16 * NDC)
    mk.wait("sync", dS, out_done)
    mk.realize()
    return nc


def _shard_inputs_stage1(inputs):
    tstack = _conv_tstack(inputs["L0_attn_diff_kernel"], inputs["L0_attn_ma_r_kernel"])
    eye = np.eye(128, dtype=np.float32).astype(bfloat16)
    u = np.ascontiguousarray(np.asarray(inputs["L0_attn_u"], dtype=np.float32))
    in_maps = []
    for c in range(N_CORES):
        in_maps.append({
            "U": np.ascontiguousarray(u[2 * c:2 * c + 2]),
            "TSTACK": tstack,
            "EYE": eye,
        })
    return in_maps


def kernel(**inputs):
    raise NotImplementedError("under construction: stages being built")


def build_stage3(n_heads=16, do_proj=True, do_bphase=True, do_ln=True, debug=False):
    nc = bass.Bass("TRN2", target_bir_lowering=False)
    xt_in = nc.dram_tensor("XT", [D, NTOK], BF16, kind="ExternalInput").ap()
    wap = {
        "w_qkv": nc.dram_tensor("WQKV", [3 * D, D], F32, kind="ExternalInput").ap(),
        "w_out": nc.dram_tensor("WOUT", [D, D], F32, kind="ExternalInput").ap(),
    }
    eye_in = nc.dram_tensor("EYE", [128, 128], BF16, kind="ExternalInput").ap()
    ut_in = nc.dram_tensor("UT", [128, 128], BF16, kind="ExternalInput").ap()
    xt3_out = nc.dram_tensor("XT3", [D, NTOK], BF16, kind="ExternalOutput").ap()

    mk = MK(nc)
    bufs = {}
    bufs["arena"] = mk.sb("arena", [128, NTOK * NDC], BF16)
    bufs["eye"] = mk.sb("eye", [128, 128], BF16)
    bufs["ut"] = mk.sb("ut", [128, 128], BF16)
    bufs["onec"] = mk.sb("onec", [128, 1], BF16)
    bufs["ones1"] = mk.sb("ones1", [1, 128], BF16)
    bufs["onesA"] = mk.sb("onesA", [128, 128], BF16)
    ps_f32 = mk.ps("ps_f32", [128, 3072], F32)
    ps_bf = mk.ps("ps_bf", [128, 2048], BF16)
    bufs["ps_f32"] = ps_f32
    bufs["pstx"] = [ps_f32[:, 0:512], ps_f32[:, 512:1024]]
    bufs["psts"] = [ps_f32[:, 1024:1536], ps_f32[:, 1536:2048]]
    bufs["pbc"] = [ps_f32[:, 2048:2560], ps_f32[:, 2560:3072]]
    bufs["pt"] = [ps_bf[:, 0:512], ps_bf[:, 1024:1536]]

    sems = {n: mk.sem(n) for n in ("dW", "dS", "dA", "tT", "vV", "sS")}
    dW, dS, dA, tT, vV, sS = (sems[n] for n in ("dW", "dS", "dA", "tT", "vV", "sS"))

    dC = mk.sem("dC")
    mk.emit("gpsimd", lambda e: e.dma_start(bufs["eye"][:], eye_in).then_inc(dC.h, 16))
    mk.emit("gpsimd", lambda e: e.dma_start(bufs["ut"][:], ut_in).then_inc(dC.h, 16))
    load_done = dC.inc(32)
    mk.emit("vector", lambda e: e.memset(bufs["onec"][:], 1.0))
    mk.emit("vector", lambda e: e.memset(bufs["ones1"][:], 1.0))
    mk.emit("vector", lambda e: e.memset(bufs["onesA"][:], 1.0).then_inc(vV.h, 1))
    const_done = vV.inc(1)
    for eng in ("tensor", "vector", "scalar"):
        mk.wait(eng, mk.sems["dC"], load_done)
        mk.wait(eng, vV, const_done)

    dbg = None
    if debug:
        dbg = {n: nc.dram_tensor(n, [128, 16384], BF16, kind="ExternalOutput").ap()
               for n in ("QT", "KT", "VN")}
    emit_attn(mk, bufs, sems, wap, xt_in, n_heads=n_heads, do_proj=do_proj,
              do_bphase=do_bphase, dbg=dbg)
    if do_ln:
        bufs["zsq"] = [mk.sb(f"zsq{i}", [128, NTOK], BF16) for i in range(2)]
        bufs["rows"] = mk.sb("rows", [1, 4096], F32)
        bufs["rows_bf"] = mk.sb("rows_bf", [1, 2048], BF16)
        emit_lnT(mk, bufs, sems, bufs["arena"])

    final_v = vV.n
    mk.wait("sync", vV, final_v)
    mk.wait("sync", sems["sS"], sems["sS"].n)
    mk.wait("sync", tT, tT.n)
    xtv = bufs["arena"][:].rearrange("p (dc t) -> p dc t", dc=NDC, t=NTOK)
    for dc in range(NDC):
        mk.emit("sync", lambda e, dc=dc: e.dma_start(
            xt3_out[128 * dc:128 * dc + 128, :], xtv[:, dc, :]).then_inc(dS.h, 16))
    out_done = dS.inc(16 * NDC)
    mk.wait("sync", dS, out_done)
    mk.realize()
    return nc


def _shard_inputs_stage3(inputs, xt_by_core):
    eye = np.eye(128, dtype=np.float32).astype(bfloat16)
    ut = np.triu(np.ones((128, 128), dtype=np.float32)).astype(bfloat16)
    com = {
        "WQKV": np.ascontiguousarray(np.asarray(inputs["L1_attn_w_qkv"], np.float32)),
        "WOUT": np.ascontiguousarray(np.asarray(inputs["L1_attn_w_out"], np.float32)),
        "EYE": eye, "UT": ut,
    }
    return [dict(com, XT=np.ascontiguousarray(xt_by_core[c])) for c in range(len(xt_by_core))]


def build_stage4(final=False):
    nc = bass.Bass("TRN2", target_bir_lowering=False)
    xt_in = nc.dram_tensor("XT", [D, NTOK], BF16, kind="ExternalInput").ap()
    wap = {
        "router_w": nc.dram_tensor("RW", [D, 8], F32, kind="ExternalInput").ap(),
        "router_b": nc.dram_tensor("RB", [8], F32, kind="ExternalInput").ap(),
        "r_w1": nc.dram_tensor("RW1", [8, D, 512], F32, kind="ExternalInput").ap(),
        "r_w3": nc.dram_tensor("RW3", [8, D, 512], F32, kind="ExternalInput").ap(),
        "r_w2": nc.dram_tensor("RW2", [8, 512, D], F32, kind="ExternalInput").ap(),
        "s_w1": nc.dram_tensor("SW1", [2, D, 512], F32, kind="ExternalInput").ap(),
        "s_w3": nc.dram_tensor("SW3", [2, D, 512], F32, kind="ExternalInput").ap(),
        "s_w2": nc.dram_tensor("SW2", [2, 512, D], F32, kind="ExternalInput").ap(),
    }
    eye_in = nc.dram_tensor("EYE", [128, 128], BF16, kind="ExternalInput").ap()
    if final:
        out_dram = nc.dram_tensor("OUT", [BC, T, D], F32, kind="ExternalOutput").ap()
    else:
        xt4_out = nc.dram_tensor("XT4", [D, NTOK], BF16, kind="ExternalOutput").ap()
    acc_dram = nc.dram_tensor("ACCD", [NTOK, D], F32).ap()

    mk = MK(nc)
    bufs = {}
    bufs["arena"] = mk.sb("arena", [128, NTOK * NDC], BF16)
    bufs["eye"] = mk.sb("eye", [128, 128], BF16)
    bufs["onec"] = mk.sb("onec", [128, 1], BF16)
    bufs["ones1"] = mk.sb("ones1", [1, 128], BF16)
    ps_f32 = mk.ps("ps_f32", [128, 3072], F32)
    ps_bf = mk.ps("ps_bf", [128, 2048], BF16)
    bufs["ps_f32"] = ps_f32
    bufs["pstx"] = [ps_f32[:, 0:512], ps_f32[:, 512:1024]]
    bufs["psts"] = [ps_f32[:, 1024:1536], ps_f32[:, 1536:2048]]
    bufs["pbc"] = [ps_f32[:, 2048:2560], ps_f32[:, 2560:3072]]
    bufs["pt"] = [ps_bf[:, 0:512], ps_bf[:, 1024:1536]]

    sems = {n: mk.sem(n) for n in ("dW", "dS", "dA", "tT", "vV", "sS")}
    dW, dS, dA, tT, vV, sS = (sems[n] for n in ("dW", "dS", "dA", "tT", "vV", "sS"))

    dC = mk.sem("dC")
    mk.emit("gpsimd", lambda e: e.dma_start(bufs["eye"][:], eye_in).then_inc(dC.h, 16))
    load_done = dC.inc(16)
    mk.emit("vector", lambda e: e.memset(bufs["onec"][:], 1.0))
    mk.emit("vector", lambda e: e.memset(bufs["ones1"][:], 1.0).then_inc(vV.h, 1))
    const_done = vV.inc(1)
    for eng in ("tensor", "vector", "scalar"):
        mk.wait(eng, dC, load_done)
        mk.wait(eng, vV, const_done)

    acc_done = emit_ffn1(mk, bufs, sems, wap, xt_in, acc_dram)
    emit_accT_add(mk, bufs, sems, xt_in, acc_dram, acc_done)
    bufs["zsq"] = [mk.sb(f"zsq{i}", [128, NTOK], BF16) for i in range(2)]
    bufs["rows"] = mk.sb("rows", [1, 4096], F32)
    bufs["rows_bf"] = mk.sb("rows_bf", [1, 2048], BF16)
    emit_lnT(mk, bufs, sems, bufs["arena"])

    if final:
        emit_final_out(mk, bufs, sems, out_dram)
    else:
        final_v = vV.n
        mk.wait("sync", vV, final_v)
        mk.wait("sync", sems["sS"], sems["sS"].n)
        mk.wait("sync", tT, tT.n)
        xtv = bufs["arena"][:].rearrange("p (dc t) -> p dc t", dc=NDC, t=NTOK)
        for dc in range(NDC):
            mk.emit("sync", lambda e, dc=dc: e.dma_start(
                xt4_out[128 * dc:128 * dc + 128, :], xtv[:, dc, :]).then_inc(dS.h, 16))
        out_done = dS.inc(16 * NDC)
        mk.wait("sync", dS, out_done)
    mk.realize()
    return nc


def _shard_inputs_stage4(inputs, xt_by_core):
    eye = np.eye(128, dtype=np.float32).astype(bfloat16)
    com = {
        "RW": np.ascontiguousarray(np.asarray(inputs["L1_ffn_router_w"], np.float32)),
        "RB": np.ascontiguousarray(np.asarray(inputs["L1_ffn_router_b"], np.float32)),
        "RW1": np.ascontiguousarray(np.asarray(inputs["L1_ffn_routed_w1"], np.float32)),
        "RW3": np.ascontiguousarray(np.asarray(inputs["L1_ffn_routed_w3"], np.float32)),
        "RW2": np.ascontiguousarray(np.asarray(inputs["L1_ffn_routed_w2"], np.float32)),
        "SW1": np.ascontiguousarray(np.asarray(inputs["L1_ffn_shared_w1"], np.float32)),
        "SW3": np.ascontiguousarray(np.asarray(inputs["L1_ffn_shared_w3"], np.float32)),
        "SW2": np.ascontiguousarray(np.asarray(inputs["L1_ffn_shared_w2"], np.float32)),
        "EYE": eye,
    }
    return [dict(com, XT=np.ascontiguousarray(xt_by_core[c])) for c in range(len(xt_by_core))]


_PROG_CACHE = {}


def _get_prog(name, builder):
    if name not in _PROG_CACHE:
        _PROG_CACHE[name] = builder()
    return _PROG_CACHE[name]


def kernel(**inputs):
    return kernel_fused(**inputs)


def kernel_multilaunch(**inputs):
    inputs = {k: np.asarray(v) for k, v in inputs.items()}

    nc1 = _get_prog("s1", build_stage1)
    nc2 = _get_prog("s2", build_stage2)
    nc3 = _get_prog("s3", build_stage3)
    nc4 = _get_prog("s4f", lambda: build_stage4(final=True))

    cores = list(range(N_CORES))
    im1 = _shard_inputs_stage1(inputs)
    r1 = run_bass_kernel_spmd(nc1, im1, core_ids=cores)
    xt = [r1.results[c]["XT"] for c in range(N_CORES)]

    im2 = _shard_inputs_stage2(inputs, xt)
    r2 = run_bass_kernel_spmd(nc2, im2, core_ids=cores)
    xt = [r2.results[c]["XT2"] for c in range(N_CORES)]

    im3 = _shard_inputs_stage3(inputs, xt)
    r3 = run_bass_kernel_spmd(nc3, im3, core_ids=cores)
    xt = [r3.results[c]["XT3"] for c in range(N_CORES)]

    im4 = _shard_inputs_stage4(inputs, xt)
    r4 = run_bass_kernel_spmd(nc4, im4, core_ids=cores)
    out = np.concatenate([np.asarray(r4.results[c]["OUT"]) for c in range(N_CORES)], axis=0)
    return out


def build_fused():
    nc = bass.Bass("TRN2", target_bir_lowering=False)
    u = nc.dram_tensor("U", [BC, T, D], F32, kind="ExternalInput").ap()
    tstack = nc.dram_tensor("TSTACK", [128, 7, 128, 128], F32, kind="ExternalInput").ap()
    wap0 = {
        "w_red": nc.dram_tensor("WRED", [D, 16], F32, kind="ExternalInput").ap(),
        "wg": nc.dram_tensor("WG", [8, 16], F32, kind="ExternalInput").ap(),
        "w1": nc.dram_tensor("W1", [8, D, 4096], F32, kind="ExternalInput").ap(),
        "b1": nc.dram_tensor("B1", [8, 4096], F32, kind="ExternalInput").ap(),
        "w2": nc.dram_tensor("W2", [8, 4096, D], F32, kind="ExternalInput").ap(),
        "b2": nc.dram_tensor("B2", [8, D], F32, kind="ExternalInput").ap(),
    }
    wapA = {
        "w_qkv": nc.dram_tensor("WQKV", [3 * D, D], F32, kind="ExternalInput").ap(),
        "w_out": nc.dram_tensor("WOUT", [D, D], F32, kind="ExternalInput").ap(),
    }
    wap1 = {
        "router_w": nc.dram_tensor("RW", [D, 8], F32, kind="ExternalInput").ap(),
        "router_b": nc.dram_tensor("RB", [8], F32, kind="ExternalInput").ap(),
        "r_w1": nc.dram_tensor("RW1", [8, D, 512], F32, kind="ExternalInput").ap(),
        "r_w3": nc.dram_tensor("RW3", [8, D, 512], F32, kind="ExternalInput").ap(),
        "r_w2": nc.dram_tensor("RW2", [8, 512, D], F32, kind="ExternalInput").ap(),
        "s_w1": nc.dram_tensor("SW1", [2, D, 512], F32, kind="ExternalInput").ap(),
        "s_w3": nc.dram_tensor("SW3", [2, D, 512], F32, kind="ExternalInput").ap(),
        "s_w2": nc.dram_tensor("SW2", [2, 512, D], F32, kind="ExternalInput").ap(),
    }
    eye_in = nc.dram_tensor("EYE", [128, 128], BF16, kind="ExternalInput").ap()
    eyef_in = nc.dram_tensor("EYEF", [128, 128], F32, kind="ExternalInput").ap()
    ut_in = nc.dram_tensor("UT", [128, 128], BF16, kind="ExternalInput").ap()
    out_dram = nc.dram_tensor("OUT", [BC, T, D], F32, kind="ExternalOutput").ap()
    xt_dram = nc.dram_tensor("XTD", [D, NTOK], BF16).ap()
    acc_dram = nc.dram_tensor("ACCD", [NTOK, D], F32).ap()

    mk = MK(nc)
    bufs = {}
    bufs["arena"] = mk.sb("arena", [128, NTOK * NDC], BF16)
    bufs["xt"] = bufs["arena"]
    bufs["eye"] = mk.sb("eye", [128, 128], BF16)
    bufs["eyef"] = mk.sb("eyef", [128, 128], F32)
    bufs["ut"] = mk.sb("ut", [128, 128], BF16)
    bufs["onec"] = mk.sb("onec", [128, 1], BF16)
    bufs["ones1"] = mk.sb("ones1", [1, 128], BF16)
    bufs["onesA"] = mk.sb("onesA", [128, 128], BF16)
    ps_f32 = mk.ps("ps_f32", [128, 3072], F32)
    ps_bf = mk.ps("ps_bf", [128, 2048], BF16)
    bufs["ps_f32"] = ps_f32
    bufs["pg"] = [ps_f32[:, 0:512], ps_f32[:, 512:1024]]
    bufs["pstx"] = [ps_f32[:, 0:512], ps_f32[:, 512:1024]]
    bufs["psts"] = [ps_f32[:, 1024:1536], ps_f32[:, 1536:2048]]
    bufs["pbc"] = [ps_f32[:, 2048:2560], ps_f32[:, 2560:3072]]
    bufs["pt"] = [ps_bf[:, 0:512], ps_bf[:, 1024:1536]]

    sems = {n: mk.sem(n) for n in ("dW", "dS", "dA", "tT", "vV", "sS")}
    dW, dS, dA, tT, vV, sS = (sems[n] for n in ("dW", "dS", "dA", "tT", "vV", "sS"))

    dC = mk.sem("dC")
    for nm, dst in (("eye", eye_in), ("eyef", eyef_in), ("ut", ut_in)):
        mk.emit("gpsimd", lambda e, nm=nm, dst=dst: e.dma_start(
            bufs[nm][:], dst).then_inc(dC.h, 16))
    load_done = dC.inc(48)
    for nm in ("onec", "ones1", "onesA"):
        last = nm == "onesA"
        mk.emit("vector", lambda e, nm=nm, l=last: (e.memset(
            bufs[nm][:], 1.0).then_inc(vV.h, 1) if l else e.memset(bufs[nm][:], 1.0)))
    const_done = vV.inc(1)
    for eng in ("tensor", "vector", "scalar"):
        mk.wait(eng, dC, load_done)
        mk.wait(eng, vV, const_done)

    def spill_xt():
        n = vV.n
        mk.wait("sync", vV, n)
        xtv = bufs["arena"][:].rearrange("p (dc t) -> p dc t", dc=NDC, t=NTOK)
        mk.emit("sync", lambda e: e.dma_start(
            xt_dram.rearrange("(dc p) t -> p dc t", p=128), xtv).then_inc(dS.h, 16))
        dS.inc(16)
        mk.barrier()

    def ln_pass():
        lsc = contextlib.ExitStack()
        bufs["zsq"] = [lsc.enter_context(nc.sbuf_tensor(f"zsq{mk.uid()}", [128, NTOK], BF16))
                       for _ in range(2)]
        bufs["rows"] = lsc.enter_context(nc.sbuf_tensor(f"rows{mk.uid()}", [1, 4096], F32))
        bufs["rows_bf"] = lsc.enter_context(nc.sbuf_tensor(f"rowsb{mk.uid()}", [1, 2048], BF16))
        emit_lnT(mk, bufs, sems, bufs["arena"])
        mk.barrier()
        lsc.close()

    s1 = contextlib.ExitStack()
    bufs["ubf"] = s1.enter_context(nc.sbuf_tensor("ubf", [128, NTOK * NDC], BF16))
    bufs["tsb"] = [s1.enter_context(nc.sbuf_tensor(f"tsb{i}", [128, 896], BF16)) for i in range(2)]
    dU = mk.sem("dU")
    mk.emit("gpsimd", lambda e: e.dma_start(
        bufs["ubf"][:].rearrange("p (b i d) -> p b i d", b=BC, i=16, d=D),
        u.rearrange("b (i p) d -> p b i d", p=128)).then_inc(dU.h, 16))
    u_done = dU.inc(16)
    mk.wait("tensor", dU, u_done)
    mk.wait("vector", dU, u_done)
    emit_conv_block(mk, bufs, sems, tstack)
    emit_transpose_nat_to_T(mk, bufs, sems, bufs["ubf"], bufs["arena"])
    mk.barrier()
    s1.close()
    ln_pass()
    spill_xt()

    s2 = contextlib.ExitStack()
    bufs["W1sb"] = s2.enter_context(nc.sbuf_tensor("W1sb", [128, 32768], BF16))
    bufs["W2sb"] = s2.enter_context(nc.sbuf_tensor("W2sb", [128, 32768], BF16))
    acc_done = emit_ffn0(mk, bufs, sems, wap0, xt_dram, acc_dram)
    mk.barrier()
    s2.close()
    emit_accT_add(mk, bufs, sems, xt_dram, acc_dram, acc_done)
    mk.barrier()
    ln_pass()
    spill_xt()

    emit_attn(mk, bufs, sems, wapA, xt_dram)
    mk.barrier()
    ln_pass()
    spill_xt()

    acc_done = emit_ffn1(mk, bufs, sems, wap1, xt_dram, acc_dram)
    mk.barrier()
    emit_accT_add(mk, bufs, sems, xt_dram, acc_dram, acc_done)
    mk.barrier()
    ln_pass()
    emit_final_out(mk, bufs, sems, out_dram)
    mk.realize()
    return nc


def _shard_inputs_fused(inputs):
    tstack = _conv_tstack(inputs["L0_attn_diff_kernel"], inputs["L0_attn_ma_r_kernel"])
    eye = np.eye(128, dtype=np.float32).astype(bfloat16)
    eyef = np.eye(128, dtype=np.float32)
    ut = np.triu(np.ones((128, 128), dtype=np.float32)).astype(bfloat16)
    f32c = lambda k: np.ascontiguousarray(np.asarray(inputs[k], np.float32))
    com = {
        "TSTACK": tstack, "EYE": eye, "EYEF": eyef, "UT": ut,
        "WRED": f32c("L0_ffn_w_red"), "WG": f32c("L0_ffn_wg"),
        "W1": f32c("L0_ffn_w1"), "B1": f32c("L0_ffn_b1"),
        "W2": f32c("L0_ffn_w2"), "B2": f32c("L0_ffn_b2"),
        "WQKV": f32c("L1_attn_w_qkv"), "WOUT": f32c("L1_attn_w_out"),
        "RW": f32c("L1_ffn_router_w"), "RB": f32c("L1_ffn_router_b"),
        "RW1": f32c("L1_ffn_routed_w1"), "RW3": f32c("L1_ffn_routed_w3"),
        "RW2": f32c("L1_ffn_routed_w2"), "SW1": f32c("L1_ffn_shared_w1"),
        "SW3": f32c("L1_ffn_shared_w3"), "SW2": f32c("L1_ffn_shared_w2"),
    }
    un = np.ascontiguousarray(np.asarray(inputs["L0_attn_u"], np.float32))
    return [dict(com, U=np.ascontiguousarray(un[2 * c:2 * c + 2])) for c in range(N_CORES)]


def kernel_fused(**inputs):
    inputs = {k: np.asarray(v) for k, v in inputs.items()}
    nc = _get_prog("fused", build_fused)
    im = _shard_inputs_fused(inputs)
    r = run_bass_kernel_spmd(nc, im, core_ids=list(range(N_CORES)))
    return np.concatenate([np.asarray(r.results[c]["OUT"]) for c in range(N_CORES)], axis=0)


# version 03
# speedup vs baseline: 1.0414x; 1.0414x over previous
import contextlib
import numpy as np
import ml_dtypes

import concourse.bass as bass
import concourse.mybir as mybir
import concourse.bass_utils as _bu
from concourse.bass_utils import run_bass_kernel_spmd

if not getattr(_bu, "_nobirsim_patch", False):
    _orig_run_command = _bu.run_command

    def _run_command_nobirsim(cmd, cwd=None, **kw):
        cmd = ["--enable-birsim=false" if c == "--enable-birsim=true" else c for c in cmd]
        cmd = ["--enable-ldw-opt=true" if c == "--enable-ldw-opt=false" else c for c in cmd]
        return _orig_run_command(cmd, cwd=cwd, **kw)

    _bu.run_command = _run_command_nobirsim
    _bu._nobirsim_patch = True

F32 = mybir.dt.float32
BF16 = mybir.dt.bfloat16
AF = mybir.ActivationFunctionType
ALU = mybir.AluOpType
AXL = mybir.AxisListType
bfloat16 = ml_dtypes.bfloat16

N_CORES = 8
B = 16
BC = 2
T = 2048
D = 1024
NTOK = BC * T
NDC = D // 128
NTT = NTOK // 128


def _conv_tstack(diff_kernel, ma_r_kernel):
    K128 = np.zeros((128, 720), dtype=np.float32)
    dk = np.asarray(diff_kernel, dtype=np.float32)
    mk = np.asarray(ma_r_kernel, dtype=np.float32)
    for i in range(16):
        for j in range(8):
            r = 8 * i + j
            if j < 4:
                K128[r, :4] = dk[4 * i + j]
            else:
                K128[r, :720] = mk[4 * i + (j - 4)]
    pad = np.zeros((128, 1152), dtype=np.float32)
    pad[:, 128:848] = K128
    base = pad[:, 128:]
    st_g, st_e = base.strides
    ts = np.lib.stride_tricks.as_strided(
        base, shape=(128, 7, 128, 128), strides=(st_g, 128 * st_e, -st_e, st_e))
    return np.ascontiguousarray(ts)



class Ctr:

    def __init__(self, handle):
        self.h = handle
        self.n = 0

    def inc(self, k=1):
        self.n += k
        return self.n


class MK:

    ENGINES = ("tensor", "vector", "scalar", "gpsimd", "sync")

    def __init__(self, nc):
        self.nc = nc
        self.es = contextlib.ExitStack()
        self.ops = {e: [] for e in self.ENGINES}
        self.sems = {}
        self._uid = 0

    def uid(self):
        self._uid += 1
        return self._uid

    def sem(self, name):
        if name in self.sems:
            return self.sems[name]
        c = Ctr(self.es.enter_context(self.nc.semaphore(name)))
        self.sems[name] = c
        return c

    def sb(self, name, shape, dtype):
        return self.es.enter_context(self.nc.sbuf_tensor(name, shape, dtype))

    def ps(self, name, shape, dtype):
        return self.es.enter_context(self.nc.psum_tensor(name, shape, dtype))

    def emit(self, engine, fn):
        self.ops[engine].append(fn)

    def wait(self, engine, ctr, val):
        if val > 0:
            self.emit(engine, lambda eng, c=ctr, v=val: eng.wait_ge(c.h, v))

    def barrier(self):
        snap = [(c, c.n) for c in self.sems.values() if c.n > 0]
        for eng in self.ENGINES:
            for c, v in snap:
                self.wait(eng, c, v)

    def realize(self):
        with self.es, self.nc.Block() as block:
            for name in self.ENGINES:
                deco = getattr(block, name)
                ops = self.ops[name]

                def section(eng, ops=ops):
                    for fn in ops:
                        fn(eng)

                deco(section)
        return self.nc



def emit_conv_block(mk, bufs, sems, tstack_ap):
    nc = mk.nc
    ubf, tsb, pgs = bufs["ubf"], bufs["tsb"], bufs["pg"]
    dW, tT, vV = sems["dW"], sems["tT"], sems["vV"]
    dT = [mk.sem("dT0"), mk.sem("dT1")]
    u4 = ubf[:].rearrange("p (b i d) -> p b i d", b=BC, i=16, d=D)
    dma_done = [0, 0]
    mm_done = [0, 0]
    ev_done = [0, 0]
    for g in range(128):
        bf = g % 2
        mk.wait("gpsimd", tT, mm_done[bf])
        mk.emit("gpsimd", lambda e, b=bf, g=g: e.dma_start(
            tsb[b][:].rearrange("s (d t) -> s d t", d=7, t=128),
            tstack_ap[g].rearrange("d s t -> s d t")).then_inc(dT[b].h, 16))
        dma_done[bf] = dT[bf].inc(16)
        mk.wait("tensor", dT[bf], dma_done[bf])
        mk.wait("tensor", vV, ev_done[bf])
        pview = pgs[bf][:, 0:256].rearrange("p (b i c) -> p b i c", b=BC, i=16, c=8)
        for d in range(7):
            for bb in range(BC):
                out_ap = pview[:, bb, d:16, :]
                rhs_ap = u4[:, bb, 0:16 - d, 8 * g:8 * g + 8]
                last = d == 6 and bb == BC - 1
                st = d == 0 and bb == 0
                mk.emit("tensor", lambda e, b=bf, d=d, o=out_ap, r=rhs_ap, st=st, l=last: (e.matmul(
                    o, tsb[b][:, 128 * d:128 * d + 128], r,
                    start=st, stop=l, skip_group_check=True).then_inc(tT.h, 1) if l else e.matmul(
                    o, tsb[b][:, 128 * d:128 * d + 128], r, start=st, stop=False,
                    skip_group_check=True)))
        mm_done[bf] = tT.inc(1)
        cols = u4[:, :, :, 8 * g:8 * g + 8]
        mk.wait("vector", tT, mm_done[bf])
        mk.emit("vector", lambda e, b=bf, c=cols: e.tensor_tensor(
            c, pgs[b][:, 0:256].rearrange("p (b i c) -> p b i c", b=BC, i=16, c=8),
            c, ALU.add).then_inc(vV.h, 1))
        ev_done[bf] = vV.inc(1)


def emit_transpose_nat_to_T(mk, bufs, sems, src, dst, src_dtype_bf=True):
    eye = bufs["eye"]
    pt = bufs["pt"]
    tT, vV = sems["tT"], sems["vV"]
    mk.wait("tensor", vV, vV.n)
    ev_done = [0, 0]
    for tau in range(NTT):
        for half in range(2):
            bf = (tau * 2 + half) % 2
            mk.wait("tensor", vV, ev_done[bf])
            for q in range(4):
                dc = half * 4 + q
                src_ap = src[:, tau * 1024 + dc * 128: tau * 1024 + dc * 128 + 128]
                last = q == 3
                mk.emit("tensor", lambda e, b=bf, q=q, s=src_ap, l=last:
                        e.transpose(pt[b][:, 128 * q:128 * q + 128], s, eye[:]).then_inc(tT.h, 1)
                        if l else
                        e.transpose(pt[b][:, 128 * q:128 * q + 128], s, eye[:]))
            done = tT.inc(1)
            dview = dst[:].rearrange("p (dc t) -> p dc t", dc=NDC, t=NTOK)
            dst_ap = dview[:, half * 4:half * 4 + 4, tau * 128:tau * 128 + 128]
            src_p = pt[bf][:].rearrange("p (q t) -> p q t", q=4, t=128)
            mk.wait("vector", tT, done)
            mk.emit("vector", lambda e, d=dst_ap, s=src_p: e.tensor_copy(d, s).then_inc(vV.h, 1))
            ev_done[bf] = vV.inc(1)


def emit_lnT(mk, bufs, sems, xt, n_d=D, eps=1e-5):
    tT, vV, sS = sems["tT"], sems["vV"], sems["sS"]
    onec, ones1 = bufs["onec"], bufs["ones1"]
    zsq = bufs["zsq"]
    rows = bufs["rows"]
    rows_bf = bufs["rows_bf"]
    pstx, psts, pbc = bufs["pstx"], bufs["psts"], bufs["pbc"]
    xv = xt[:].rearrange("p (dc t) -> p dc t", dc=NDC, t=NTOK)

    mk.wait("scalar", vV, vV.n)
    mk.wait("tensor", vV, vV.n)

    mm_done = [0, 0]
    rows_done = [0, 0]
    app_done = [0, 0]
    for sl in range(8):
        zb = sl % 2
        tok = slice(512 * sl, 512 * (sl + 1))
        ro = 2048 * zb
        rbo = 1024 * zb
        sumx = rows[0:1, ro:ro + 512]
        var = rows[0:1, ro + 512:ro + 1024]
        t2 = rows[0:1, ro + 1024:ro + 1536]
        rr = rows[0:1, ro + 1536:ro + 2048]
        m_bf = rows_bf[0:1, rbo:rbo + 512]
        r_bf = rows_bf[0:1, rbo + 512:rbo + 1024]
        mk.wait("scalar", tT, mm_done[zb])
        for dc in range(NDC):
            last = dc == NDC - 1
            mk.emit("scalar", lambda e, dc=dc, tok=tok, z=zb, l=last: (e.activation(
                zsq[z][:, 512 * dc:512 * dc + 512], xv[:, dc, tok], AF.Square
            ).then_inc(sS.h, 1) if l else e.activation(
                zsq[z][:, 512 * dc:512 * dc + 512], xv[:, dc, tok], AF.Square)))
        sq_ready = sS.inc(1)
        mk.wait("tensor", sS, sq_ready)
        mk.wait("tensor", vV, rows_done[zb])
        for dc in range(NDC):
            mk.emit("tensor", lambda e, dc=dc, tok=tok, z=zb: e.matmul(
                pstx[z][0:1, :], onec[:], xv[:, dc, tok],
                start=(dc == 0), stop=(dc == NDC - 1)))
            last = dc == NDC - 1
            mk.emit("tensor", lambda e, dc=dc, z=zb, l=last: (e.matmul(
                psts[z][0:1, :], onec[:], zsq[z][:, 512 * dc:512 * dc + 512],
                start=(dc == 0), stop=(dc == NDC - 1)
            ).then_inc(tT.h, 1) if l else e.matmul(
                psts[z][0:1, :], onec[:], zsq[z][:, 512 * dc:512 * dc + 512],
                start=(dc == 0), stop=(dc == NDC - 1))))
        mm_done[zb] = tT.inc(1)
        mk.wait("vector", tT, mm_done[zb])
        steps = [
            ("vector", lambda e, a=sumx, b=pstx[zb]: e.tensor_scalar(a, b[0:1, :], 1.0 / n_d, None, ALU.mult)),
            ("vector", lambda e, a=var, b=psts[zb]: e.tensor_scalar(a, b[0:1, :], 1.0 / n_d, eps, ALU.mult, ALU.add)),
            ("vector", lambda e, a=t2, b=sumx: e.tensor_tensor(a, b, b, ALU.mult)),
            ("vector", lambda e, a=var, b=t2: e.tensor_tensor(a, a, b, ALU.subtract)),
            ("vector", lambda e, a=t2, b=var: e.reciprocal(a, b)),
            ("scalar", lambda e, a=rr, b=t2: e.activation(a, b, AF.Sqrt)),
            ("vector", lambda e, a=m_bf, b=sumx: e.tensor_copy(a, b)),
            ("vector", lambda e, a=r_bf, b=rr: e.tensor_copy(a, b)),
        ]
        prev = None
        for eng, fn in steps:
            if prev is not None:
                pc, pv = prev
                mk.wait(eng, pc, pv)
            ctr = sS if eng == "scalar" else vV
            mk.emit(eng, lambda e, fn=fn, c=ctr: fn(e).then_inc(c.h, 1))
            prev = (ctr, ctr.inc(1))
        rows_done[zb] = vV.n
        mk.wait("tensor", prev[0], prev[1])
        mk.wait("tensor", vV, max(app_done))
        mk.emit("tensor", lambda e, m=m_bf: e.matmul(
            pbc[0][:], ones1[:], m, start=True, stop=True))
        mk.emit("tensor", lambda e, r=r_bf: e.matmul(
            pbc[1][:], ones1[:], r, start=True, stop=True).then_inc(tT.h, 1))
        bc_done = tT.inc(1)
        mk.wait("vector", tT, bc_done)
        for dc in range(NDC):
            last = dc == NDC - 1
            mk.emit("vector", lambda e, dc=dc, tok=tok, l=last: (e.tensor_tensor(
                xv[:, dc, tok], xv[:, dc, tok], pbc[0][:], ALU.subtract
            ).then_inc(vV.h, 1) if l else e.tensor_tensor(
                xv[:, dc, tok], xv[:, dc, tok], pbc[0][:], ALU.subtract)))
        mk.wait("vector", vV, vV.inc(1))
        for dc in range(NDC):
            last = dc == NDC - 1
            mk.emit("vector", lambda e, dc=dc, tok=tok, l=last: (e.tensor_tensor(
                xv[:, dc, tok], xv[:, dc, tok], pbc[1][:], ALU.mult
            ).then_inc(vV.h, 1) if l else e.tensor_tensor(
                xv[:, dc, tok], xv[:, dc, tok], pbc[1][:], ALU.mult)))
        app_done[zb] = vV.inc(1)
    bufs["ln_exit"] = max(app_done)


def emit_ffn0(mk, bufs, sems, wap, xt_dram, acc_dram, n_experts=8, do_experts=True):
    nc = mk.nc
    tT, vV, sS, dW, dA = (sems[n] for n in ("tT", "vV", "sS", "dW", "dA"))
    dX1, dX2 = mk.sem("dX1"), mk.sem("dX2")
    dxc = [mk.sem("dxc0"), mk.sem("dxc1")]
    dxw = [mk.sem("dxw0"), mk.sem("dxw1")]
    W1sb, W2sb = bufs["W1sb"], bufs["W2sb"]
    ar = bufs["arena"]
    ones1, eyef, onesA = bufs["ones1"], bufs["eyef"], bufs["onesA"]
    ps = bufs["ps_f32"]
    h_sb = ar[:, 0:8192]
    xwin = [ar[:, 8192:10240], ar[:, 10240:12288]]
    tmp = ar[:, 12288:14336].bitcast(F32)
    msel = ar[:, 14336:14848].bitcast(F32)
    b1T = ar[:, 14848:15360].bitcast(F32)
    b2q = ar[:, 15360:18432]
    xc = [ar[:, 18432:19456].bitcast(F32), ar[:, 19456:20480].bitcast(F32)]
    lgs = ar[:, 20480:20992].bitcast(F32)
    e8 = ar[:, 20992:21504].bitcast(F32)
    mx = ar[:, 21504:21568].bitcast(F32)
    ssum = ar[:, 21568:21632].bitcast(F32)
    b1s = ar[0:8, 21632:29824].bitcast(F32)
    red_sb = W2sb[:, 0:8192].bitcast(F32)
    ph = [ps[:, 0:256], ps[:, 512:768]]
    py = [ps[:, 1024:2048], ps[:, 2048:3072]]
    p_red = ps[0:16, 0:512]
    pl = ps[:, 512:768]
    pb1 = ps[:, 1024:1032]
    xdv = xt_dram.rearrange("(dc p) t -> p dc t", p=128)

    rsc = contextlib.ExitStack()
    wred = rsc.enter_context(nc.sbuf_tensor("wred", [128, 128], F32))
    wg_sb = rsc.enter_context(nc.sbuf_tensor("wg_sb", [8, 16], F32))
    wgt = rsc.enter_context(nc.sbuf_tensor("wgt", [8, 20], F32))
    wgnT = rsc.enter_context(nc.sbuf_tensor("wgnT", [16, 8], F32))

    mk.emit("gpsimd", lambda e: e.dma_start(
        wred[:].rearrange("p (dc j) -> p dc j", dc=NDC, j=16),
        wap["w_red"].rearrange("(dc p) j -> p dc j", p=128)).then_inc(dW.h, 16))
    mk.emit("gpsimd", lambda e: e.dma_start(wg_sb[:], wap["wg"]).then_inc(dW.h, 16))
    for ei in range(8):
        mk.emit("gpsimd", lambda e, ei=ei: e.dma_start(
            b2q[32 * (ei // 3):32 * (ei // 3) + 1, 1024 * (ei % 3):1024 * (ei % 3) + 1024],
            wap["b2"][ei:ei + 1, :]).then_inc(dW.h, 16))
    mk.emit("gpsimd", lambda e: e.dma_start(b1s[:], wap["b1"]).then_inc(dW.h, 16))
    stg_done = dW.inc(176)
    mk.emit("gpsimd", lambda e: e.dma_start(
        W1sb[:].rearrange("p (dc f) -> p dc f", dc=NDC, f=4096),
        wap["w1"][0].rearrange("(dc p) f -> p dc f", p=128)).then_inc(dX1.h, 16))
    w1_done = dX1.inc(16)

    mk.wait("scalar", dW, stg_done)
    chain = [
        ("scalar", lambda e: e.activation(wgt[:, 0:16], wg_sb[:], AF.Square)),
        ("vector", lambda e: e.reduce_sum(wgt[:, 16:17], wgt[:, 0:16], axis=AXL.X)),
        ("scalar", lambda e: e.activation(wgt[:, 17:18], wgt[:, 16:17], AF.Sqrt)),
        ("vector", lambda e: e.tensor_scalar_max(wgt[:, 18:19], wgt[:, 17:18], 1e-4)),
        ("vector", lambda e: e.reciprocal(wgt[:, 19:20], wgt[:, 18:19])),
        ("vector", lambda e: e.tensor_scalar(wg_sb[:], wg_sb[:], wgt[:, 19:20], None, ALU.mult)),
    ]
    prev = None
    for eng, fn in chain:
        if prev is not None:
            mk.wait(eng, prev[0], prev[1])
        c = sS if eng == "scalar" else vV
        mk.emit(eng, lambda e, fn=fn, c=c: fn(e).then_inc(c.h, 1))
        prev = (c, c.inc(1))
    mk.wait("tensor", prev[0], prev[1])
    mk.emit("tensor", lambda e: e.transpose(
        ps[0:16, 2048:2056], wg_sb[:], eyef[0:8, 0:8]).then_inc(tT.h, 1))
    wgt_done = tT.inc(1)
    mk.wait("vector", tT, wgt_done)
    mk.emit("vector", lambda e: e.tensor_copy(wgnT[:], ps[0:16, 2048:2056]).then_inc(vV.h, 1))
    wgnT_done = vV.inc(1)

    mk.wait("tensor", dW, stg_done)
    cp_done = {}
    for fc in range(32):
        if fc >= 1:
            mk.wait("tensor", vV, cp_done[fc - 1])
        mk.emit("tensor", lambda e, fc=fc: e.transpose(
            pb1, b1s[0:8, 128 * fc:128 * fc + 128], eyef[0:8, 0:8]).then_inc(tT.h, 1))
        td = tT.inc(1)
        mk.wait("vector", tT, td)
        mk.emit("vector", lambda e, fc=fc: e.tensor_copy(
            b1T[:, 8 * fc:8 * fc + 8], pb1).then_inc(vV.h, 1))
        cp_done[fc] = vV.inc(1)
    b1T_done = cp_done[31]

    wv = wred[:].rearrange("p (dc j) -> p dc j", dc=NDC, j=16)
    red_ev = {}
    mm_done = {}
    xc_done = {}
    for sl in range(8):
        for dc in range(NDC):
            z = dc % 2
            k = sl * NDC + dc
            mk.wait("gpsimd", tT, mm_done.get(k - 2, 0))
            mk.emit("gpsimd", lambda e, dc=dc, sl=sl, z=z: e.dma_start(
                xc[z][:], xdv[:, dc, 512 * sl:512 * sl + 512]).then_inc(dxc[z].h, 16))
            xc_done[k] = dxc[z].inc(16)
            mk.wait("tensor", dxc[z], xc_done[k])
            if dc == 0 and sl > 0:
                mk.wait("tensor", vV, red_ev[sl - 1])
            mk.emit("tensor", lambda e, dc=dc, z=z: e.matmul(
                p_red, wv[:, dc, :], xc[z][:],
                start=(dc == 0), stop=(dc == NDC - 1)).then_inc(tT.h, 1))
            mm_done[k] = tT.inc(1)
        mk.wait("vector", tT, mm_done[sl * NDC + NDC - 1])
        mk.emit("vector", lambda e, sl=sl: e.tensor_copy(
            red_sb[0:16, 512 * sl:512 * sl + 512], p_red).then_inc(vV.h, 1))
        red_ev[sl] = vV.inc(1)

    mk.wait("tensor", vV, red_ev[7])
    mk.wait("tensor", vV, wgnT_done)
    for tg in range(NTT):
        last = tg == NTT - 1
        mk.emit("tensor", lambda e, tg=tg, l=last: (e.matmul(
            pl[:, 8 * tg:8 * tg + 8], red_sb[0:16, 128 * tg:128 * tg + 128], wgnT[:],
            start=True, stop=True).then_inc(tT.h, 1) if l else e.matmul(
            pl[:, 8 * tg:8 * tg + 8], red_sb[0:16, 128 * tg:128 * tg + 128], wgnT[:],
            start=True, stop=True)))
    lg_done = tT.inc(1)
    plv = pl.rearrange("p (g e) -> p g e", g=NTT, e=8)
    mxv = mx.rearrange("p (g o) -> p g o", g=NTT, o=1)
    ep = [
        ("vector", lambda e: e.reduce_max(mxv, plv, axis=AXL.X)),
        ("vector", lambda e: e.tensor_tensor(
            lgs.rearrange("p (g e) -> p g e", g=NTT, e=8), plv,
            mxv.broadcast_to((128, NTT, 8)), ALU.subtract)),
        ("scalar", lambda e: e.activation(e8, lgs, AF.Exp)),
        ("vector", lambda e: e.reduce_sum(
            ssum.rearrange("p (g o) -> p g o", g=NTT, o=1),
            e8.rearrange("p (g e) -> p g e", g=NTT, e=8), axis=AXL.X)),
        ("vector", lambda e: e.reciprocal(mx, ssum)),
        ("vector", lambda e: e.tensor_single_scalar(lgs, e8, 1.0, ALU.is_ge)),
        ("vector", lambda e: e.tensor_tensor(
            msel.rearrange("p (g e) -> p g e", g=NTT, e=8),
            lgs.rearrange("p (g e) -> p g e", g=NTT, e=8),
            mxv.broadcast_to((128, NTT, 8)), ALU.mult)),
    ]
    prev = (tT, lg_done)
    for eng, fn in ep:
        mk.wait(eng, prev[0], prev[1])
        c = sS if eng == "scalar" else vV
        mk.emit(eng, lambda e, fn=fn, c=c: fn(e).then_inc(c.h, 1))
        prev = (c, c.inc(1))
    msel_done = prev[1]
    router_done_v = vV.n

    if not do_experts:
        rsc.close()
        return 0
    w2_done = 0
    phz_last = {0: 0, 1: 0}
    ev_cnt = {}
    dma_cnt = {}
    acc_last_of_e = 0
    phB_prev_win = 0
    phA_all_done = 0
    h_free_at = 0
    xw_done = {}
    xw_read_z = {0: 0, 1: 0}
    for e_ix in range(n_experts):
        mk.wait("gpsimd", tT, phB_prev_win if e_ix > 0 else lg_done)
        mk.emit("gpsimd", lambda e, ei=e_ix: e.dma_start(
            W2sb[:].rearrange("p (fc d) -> p fc d", fc=32, d=1024),
            wap["w2"][ei].rearrange("(fc p) d -> p fc d", p=128)).then_inc(dX2.h, 16))
        w2_done = dX2.inc(16)
        w1_this = w1_done
        mk.wait("tensor", dX1, w1_this)
        if e_ix == 0:
            mk.wait("tensor", vV, router_done_v)
            mk.wait("tensor", vV, b1T_done)
        for w in range(16):
            wz = w % 2
            mk.wait("gpsimd", tT, xw_read_z[wz])
            mk.emit("gpsimd", lambda e, w=w, wz=wz: e.dma_start(
                xwin[wz][:].rearrange("p (dc t) -> p dc t", dc=NDC, t=256),
                xdv[:, :, 256 * w:256 * w + 256]).then_inc(dxw[wz].h, 16))
            xw_done[(e_ix, w)] = dxw[wz].inc(16)
            xv = xwin[wz][:].rearrange("p (dc t) -> p dc t", dc=NDC, t=256)
            mk.wait("tensor", dxw[wz], xw_done[(e_ix, w)])
            for fc in range(32):
                z = fc % 2
                mk.wait("tensor", sS, phz_last[z])
                for dc in range(NDC):
                    last = dc == NDC - 1
                    mk.emit("tensor", lambda e, dc=dc, fc=fc, z=z, xv=xv, l=last: (e.matmul(
                        ph[z], W1sb[:, 4096 * dc + 128 * fc:4096 * dc + 128 * fc + 128],
                        xv[:, dc, :], start=(dc == 0), stop=(dc == NDC - 1)
                    ).then_inc(tT.h, 1) if l else e.matmul(
                        ph[z], W1sb[:, 4096 * dc + 128 * fc:4096 * dc + 128 * fc + 128],
                        xv[:, dc, :], start=(dc == 0), stop=(dc == NDC - 1))))
                mmA = tT.inc(1)
                mk.wait("scalar", tT, mmA)
                if fc == 0:
                    mk.wait("scalar", tT, h_free_at)
                mk.emit("scalar", lambda e, fc=fc, z=z, ei=e_ix: e.activation(
                    h_sb[:, 256 * fc:256 * fc + 256], ph[z], AF.Gelu,
                    bias=b1T[:, 8 * fc + ei:8 * fc + ei + 1]).then_inc(sS.h, 1))
                phz_last[z] = sS.inc(1)
            phA_all_done = tT.n
            xw_read_z[wz] = tT.n
            mk.wait("tensor", sS, max(phz_last.values()))
            mk.wait("tensor", dX2, w2_done)
            for th in range(2):
                tg = 2 * w + th
                pz = tg % 2
                mk.wait("tensor", vV, ev_cnt.get(tg - 2, 0))
                if e_ix == 0 and tg <= 1:
                    mk.wait("tensor", vV, router_done_v)
                for fc in range(32):
                    for hh in range(2):
                        mk.emit("tensor", lambda e, fc=fc, th=th, hh=hh, pz=pz: e.matmul(
                            py[pz][:, 512 * hh:512 * hh + 512],
                            h_sb[:, 256 * fc + 128 * th:256 * fc + 128 * th + 128],
                            W2sb[:, 1024 * fc + 512 * hh:1024 * fc + 512 * hh + 512],
                            start=(fc == 0), stop=False))
                b2r = b2q[32 * (e_ix // 3):32 * (e_ix // 3) + 1,
                          1024 * (e_ix % 3):1024 * (e_ix % 3) + 1024]
                brow = 32 * (e_ix // 3)
                for hh in range(2):
                    last = hh == 1
                    mk.emit("tensor", lambda e, hh=hh, pz=pz, r=b2r, br=brow, l=last: (e.matmul(
                        py[pz][:, 512 * hh:512 * hh + 512], onesA[br:br + 1, :],
                        r[:, 512 * hh:512 * hh + 512],
                        start=False, stop=True).then_inc(tT.h, 1) if l else e.matmul(
                        py[pz][:, 512 * hh:512 * hh + 512], onesA[br:br + 1, :],
                        r[:, 512 * hh:512 * hh + 512], start=False, stop=True)))
                mmB = tT.inc(1)
                mk.wait("vector", tT, mmB)
                mk.wait("vector", dA, dA.n)
                if e_ix == 0 and tg == 0:
                    mk.wait("vector", vV, msel_done)
                mk.emit("vector", lambda e, tg=tg, ei=e_ix, pz=pz: e.tensor_scalar(
                    tmp, py[pz][:], msel[:, 8 * tg + ei:8 * tg + ei + 1],
                    None, ALU.mult).then_inc(vV.h, 1))
                ev_cnt[tg] = vV.inc(1)
                mk.wait("gpsimd", vV, ev_cnt[tg])
                if tg == 0:
                    mk.wait("gpsimd", dA, acc_last_of_e)
                op = ALU.bypass if e_ix == 0 else ALU.add
                mk.emit("gpsimd", lambda e, tg=tg, op=op: e.dma_start(
                    acc_dram[128 * tg:128 * tg + 128, :], tmp,
                    accum_op=op).then_inc(dA.h, 16))
                dma_cnt[tg] = dA.inc(16)
            phB_prev_win = tT.n
            h_free_at = tT.n
        acc_last_of_e = dma_cnt[31]
        if e_ix < n_experts - 1:
            mk.wait("gpsimd", tT, phA_all_done)
            mk.emit("gpsimd", lambda e, ei=e_ix + 1: e.dma_start(
                W1sb[:].rearrange("p (dc f) -> p dc f", dc=NDC, f=4096),
                wap["w1"][ei].rearrange("(dc p) f -> p dc f", p=128)).then_inc(dX1.h, 16))
            w1_done = dX1.inc(16)
    rsc.close()
    return dma_cnt[31]


def emit_accT_add(mk, bufs, sems, xt_dram, acc_dram, acc_dma_done):
    nc = mk.nc
    tT, vV, dS = (sems[n] for n in ("tT", "vV", "dS"))
    xt, eye, pt = bufs["arena"], bufs["eye"], bufs["pt"]
    dL = [mk.sem("dL0"), mk.sem("dL1")]
    xv = xt[:].rearrange("p (dc t) -> p dc t", dc=NDC, t=NTOK)
    xdv = xt_dram.rearrange("(dc p) t -> p dc t", p=128)
    asc = contextlib.ExitStack()
    accl = [asc.enter_context(nc.sbuf_tensor(f"accl{mk.uid()}_{i}", [128, 1024], F32)) for i in range(2)]
    accb = [asc.enter_context(nc.sbuf_tensor(f"accb{mk.uid()}_{i}", [128, 1024], BF16)) for i in range(2)]
    x1w = [asc.enter_context(nc.sbuf_tensor(f"x1w{mk.uid()}_{i}", [128, 1024], BF16)) for i in range(2)]
    mk.wait("gpsimd", sems["dA"], acc_dma_done)
    ld = {}
    cb = {}
    add_done = {}
    tr_done = {}
    for tg in range(NTT):
        z = tg % 2
        mk.wait("gpsimd", vV, cb.get(tg - 2, 0))
        mk.emit("gpsimd", lambda e, tg=tg, z=z: e.dma_start(
            accl[z][:], acc_dram[128 * tg:128 * tg + 128, :]).then_inc(dL[z].h, 16))
        mk.wait("gpsimd", vV, add_done.get(tg - 2, 0))
        mk.emit("gpsimd", lambda e, tg=tg, z=z: e.dma_start(
            x1w[z][:].rearrange("p (dc t) -> p dc t", dc=NDC, t=128),
            xdv[:, :, 128 * tg:128 * tg + 128]).then_inc(dL[z].h, 16))
        ld[tg] = dL[z].inc(32)
        mk.wait("vector", dL[z], ld[tg])
        mk.wait("vector", tT, tr_done.get(tg - 2, 0))
        mk.emit("vector", lambda e, z=z: e.tensor_copy(accb[z][:], accl[z][:]).then_inc(vV.h, 1))
        cb[tg] = vV.inc(1)
        for half in range(2):
            pz = half
            mk.wait("tensor", vV, cb[tg])
            mk.wait("tensor", vV, add_done.get(tg - 1, 0))
            for q in range(4):
                dc = 4 * half + q
                last = q == 3
                mk.emit("tensor", lambda e, z=z, dc=dc, q=q, pz=pz, l=last: (e.transpose(
                    pt[pz][:, 128 * q:128 * q + 128], accb[z][:, 128 * dc:128 * dc + 128],
                    eye[:]).then_inc(tT.h, 1) if l else e.transpose(
                    pt[pz][:, 128 * q:128 * q + 128], accb[z][:, 128 * dc:128 * dc + 128],
                    eye[:])))
            t_done = tT.inc(1)
            tr_done[tg] = t_done
            mk.wait("vector", tT, t_done)
            x1v = x1w[z][:].rearrange("p (dc t) -> p dc t", dc=NDC, t=128)
            for q in range(4):
                dc = 4 * half + q
                last = q == 3
                mk.emit("vector", lambda e, tg=tg, dc=dc, q=q, pz=pz, z=z, x1v=x1v, l=last: (
                    e.tensor_tensor(
                        xv[:, dc, 128 * tg:128 * tg + 128], x1v[:, dc, :],
                        pt[pz][:, 128 * q:128 * q + 128], ALU.add).then_inc(vV.h, 1)
                    if l else e.tensor_tensor(
                        xv[:, dc, 128 * tg:128 * tg + 128], x1v[:, dc, :],
                        pt[pz][:, 128 * q:128 * q + 128], ALU.add)))
            add_done[tg] = vV.inc(1)
    asc.close()



def build_stage2(n_experts=8, do_experts=True, do_assembly=True, do_ln=True):
    nc = bass.Bass("TRN2", target_bir_lowering=False)
    xt_in = nc.dram_tensor("XT", [D, NTOK], BF16, kind="ExternalInput").ap()
    wap = {
        "w_red": nc.dram_tensor("WRED", [D, 16], F32, kind="ExternalInput").ap(),
        "wg": nc.dram_tensor("WG", [8, 16], F32, kind="ExternalInput").ap(),
        "w1": nc.dram_tensor("W1", [8, D, 4096], F32, kind="ExternalInput").ap(),
        "b1": nc.dram_tensor("B1", [8, 4096], F32, kind="ExternalInput").ap(),
        "w2": nc.dram_tensor("W2", [8, 4096, D], F32, kind="ExternalInput").ap(),
        "b2": nc.dram_tensor("B2", [8, D], F32, kind="ExternalInput").ap(),
    }
    eye_in = nc.dram_tensor("EYE", [128, 128], BF16, kind="ExternalInput").ap()
    eyef_in = nc.dram_tensor("EYEF", [128, 128], F32, kind="ExternalInput").ap()
    xt2_out = nc.dram_tensor("XT2", [D, NTOK], BF16, kind="ExternalOutput").ap()
    acc_dram = nc.dram_tensor("ACCD", [NTOK, D], F32).ap()

    mk = MK(nc)
    bufs = {}
    bufs["arena"] = mk.sb("arena", [128, NTOK * NDC], BF16)
    bufs["eye"] = mk.sb("eye", [128, 128], BF16)
    bufs["eyef"] = mk.sb("eyef", [128, 128], F32)
    bufs["onec"] = mk.sb("onec", [128, 1], BF16)
    bufs["ones1"] = mk.sb("ones1", [1, 128], BF16)
    bufs["onesA"] = mk.sb("onesA", [128, 128], BF16)
    wsc = contextlib.ExitStack()
    bufs["W1sb"] = wsc.enter_context(nc.sbuf_tensor("W1sb", [128, 32768], BF16))
    bufs["W2sb"] = wsc.enter_context(nc.sbuf_tensor("W2sb", [128, 32768], BF16))
    ps_f32 = mk.ps("ps_f32", [128, 3072], F32)
    ps_bf = mk.ps("ps_bf", [128, 2048], BF16)
    bufs["ps_f32"] = ps_f32
    bufs["pstx"] = [ps_f32[:, 0:512], ps_f32[:, 512:1024]]
    bufs["psts"] = [ps_f32[:, 1024:1536], ps_f32[:, 1536:2048]]
    bufs["pbc"] = [ps_f32[:, 2048:2560], ps_f32[:, 2560:3072]]
    bufs["pt"] = [ps_bf[:, 0:512], ps_bf[:, 1024:1536]]

    sems = {n: mk.sem(n) for n in ("dW", "dS", "dA", "tT", "vV", "sS")}
    dW, dS, dA, tT, vV, sS = (sems[n] for n in ("dW", "dS", "dA", "tT", "vV", "sS"))

    dC = mk.sem("dC")
    mk.emit("gpsimd", lambda e: e.dma_start(bufs["eye"][:], eye_in).then_inc(dC.h, 16))
    mk.emit("gpsimd", lambda e: e.dma_start(bufs["eyef"][:], eyef_in).then_inc(dC.h, 16))
    load_done = dC.inc(32)
    mk.emit("vector", lambda e: e.memset(bufs["onec"][:], 1.0))
    mk.emit("vector", lambda e: e.memset(bufs["ones1"][:], 1.0))
    mk.emit("vector", lambda e: e.memset(bufs["onesA"][:], 1.0).then_inc(vV.h, 1))
    const_done = vV.inc(1)
    for eng in ("tensor", "vector", "scalar"):
        mk.wait(eng, mk.sems["dC"], load_done)
        mk.wait(eng, vV, const_done)

    acc_done = emit_ffn0(mk, bufs, sems, wap, xt_in, acc_dram,
                         n_experts=n_experts, do_experts=do_experts)
    wsc.close()
    if do_assembly:
        emit_accT_add(mk, bufs, sems, xt_in, acc_dram, acc_done)
    if do_ln:
        bufs["zsq"] = [mk.sb(f"zsq{i}", [128, NTOK], BF16) for i in range(2)]
        bufs["rows"] = mk.sb("rows", [1, 4096], F32)
        bufs["rows_bf"] = mk.sb("rows_bf", [1, 2048], BF16)
        bufs["xt"] = bufs["arena"]
        emit_lnT(mk, bufs, sems, bufs["arena"])

    final_v = vV.n
    mk.wait("sync", vV, final_v)
    mk.wait("sync", sems["dA"], sems["dA"].n)
    mk.wait("sync", sems["sS"], sems["sS"].n)
    mk.wait("sync", tT, tT.n)
    xtv = bufs["arena"][:].rearrange("p (dc t) -> p dc t", dc=NDC, t=NTOK)
    for dc in range(NDC):
        mk.emit("sync", lambda e, dc=dc: e.dma_start(
            xt2_out[128 * dc:128 * dc + 128, :], xtv[:, dc, :]).then_inc(dS.h, 16))
    out_done = dS.inc(16 * NDC)
    mk.wait("sync", dS, out_done)
    mk.realize()
    return nc


def _shard_inputs_stage2(inputs, xt_by_core):
    eye = np.eye(128, dtype=np.float32).astype(bfloat16)
    eyef = np.eye(128, dtype=np.float32)
    com = {
        "WRED": np.ascontiguousarray(np.asarray(inputs["L0_ffn_w_red"], np.float32)),
        "WG": np.ascontiguousarray(np.asarray(inputs["L0_ffn_wg"], np.float32)),
        "W1": np.ascontiguousarray(np.asarray(inputs["L0_ffn_w1"], np.float32)),
        "B1": np.ascontiguousarray(np.asarray(inputs["L0_ffn_b1"], np.float32)),
        "W2": np.ascontiguousarray(np.asarray(inputs["L0_ffn_w2"], np.float32)),
        "B2": np.ascontiguousarray(np.asarray(inputs["L0_ffn_b2"], np.float32)),
        "EYE": eye, "EYEF": eyef,
    }
    return [dict(com, XT=np.ascontiguousarray(xt_by_core[c])) for c in range(len(xt_by_core))]


def emit_attn(mk, bufs, sems, wap, xt_dram, n_heads=16, do_proj=True, do_bphase=True, dbg=None):
    nc = mk.nc
    tT, vV, sS, dW, dS = (sems[n] for n in ("tT", "vV", "sS", "dW", "dS"))
    ar, eye, ut, onesA = bufs["arena"], bufs["eye"], bufs["ut"], bufs["onesA"]
    ones1 = bufs["ones1"]
    dwst, dxq = mk.sem("dwst"), mk.sem("dxq")
    ps, pt = bufs["ps_f32"], bufs["pt"]
    xdv = xt_dram.rearrange("(dc p) t -> p dc t", p=128)
    SA = [ps[:, 2048:2560], ps[:, 2560:3072]]
    PO = ps[:, 0:2048]
    PQ = [ps[:, 0:512], ps[:, 512:1024]]
    PV2 = ps[:, 1024:2048]

    asc = contextlib.ExitStack()
    qT = asc.enter_context(nc.sbuf_tensor("qT", [128, 16384], BF16))
    kT = asc.enter_context(nc.sbuf_tensor("kT", [128, 16384], BF16))
    vN = asc.enter_context(nc.sbuf_tensor("vN", [128, 16384], BF16))
    xqw = asc.enter_context(nc.sbuf_tensor("xqw", [128, 4096], BF16))
    wstg = asc.enter_context(nc.sbuf_tensor("wstg", [128, 1024], BF16))
    st8 = asc.enter_context(nc.sbuf_tensor("st8", [1, 64], F32))
    st8b = asc.enter_context(nc.sbuf_tensor("st8b", [1, 16], BF16))
    WpT = asc.enter_context(nc.sbuf_tensor("WpT", [128, 8192], BF16))
    vst = [asc.enter_context(nc.sbuf_tensor(f"vstA{i}", [128, 2048], BF16)) for i in range(2)]
    pstrip = [asc.enter_context(nc.sbuf_tensor(f"pstripA{i}", [128, 2048], BF16)) for i in range(2)]
    rbs = asc.enter_context(nc.sbuf_tensor("rbsA", [128, 1024], BF16))
    Bcols = asc.enter_context(nc.sbuf_tensor("BcolsA", [128, 16], F32))
    sqtmp = xqw

    def emit_wT_load(wsrc_rows, WpT_cols, dst_view):
        cp_z = {0: 0, 1: 0}
        tr_prev = tT.n
        for oc in range(8):
            mk.wait("gpsimd", tT, tr_prev)
            mk.emit("gpsimd", lambda e, oc=oc: e.dma_start(
                wstg[:], wsrc_rows(oc)).then_inc(dwst.h, 16))
            wd = dwst.inc(16)
            mk.wait("tensor", dwst, wd)
            for half in range(2):
                mk.wait("tensor", vV, cp_z[half])
                for q in range(4):
                    dc = 4 * half + q
                    last = q == 3
                    mk.emit("tensor", lambda e, dc=dc, q=q, z=half, l=last: (e.transpose(
                        pt[z][:, 128 * q:128 * q + 128], wstg[:, 128 * dc:128 * dc + 128],
                        eye[:]).then_inc(tT.h, 1) if l else e.transpose(
                        pt[z][:, 128 * q:128 * q + 128], wstg[:, 128 * dc:128 * dc + 128],
                        eye[:])))
                td = tT.inc(1)
                mk.wait("vector", tT, td)
                mk.emit("vector", lambda e, oc=oc, half=half: e.tensor_copy(
                    dst_view[:, 4 * half:4 * half + 4, 128 * oc:128 * oc + 128],
                    pt[half][:].rearrange("p (q t) -> p q t", q=4, t=128)
                ).then_inc(vV.h, 1))
                cp_z[half] = vV.inc(1)
            tr_prev = tT.n
        return max(cp_z.values())

    for b in range(BC):
        tb = 2048 * b
        for eng in ("tensor", "gpsimd", "scalar"):
            mk.wait(eng, vV, vV.n)
        for part in range(3):
            wv2 = WpT[:].rearrange("p (dc o) -> p dc o", dc=NDC, o=1024)
            wpT_ready = emit_wT_load(
                lambda oc, part=part: wap["w_qkv"][
                    128 * (8 * part + oc):128 * (8 * part + oc) + 128, :],
                None, wv2)
            mk.wait("tensor", vV, wpT_ready)
            xqv = xqw[:].rearrange("p (dc t) -> p dc t", dc=NDC, t=512)
            if part < 2:
                dst = qT if part == 0 else kT
                ev_z = {0: 0, 1: 0}
                mm_last = tT.n
                for ts in range(4):
                    mk.wait("gpsimd", tT, mm_last)
                    mk.emit("gpsimd", lambda e, ts=ts, tb=tb: e.dma_start(
                        xqv, xdv[:, :, tb + 512 * ts:tb + 512 * ts + 512]).then_inc(dxq.h, 16))
                    xd = dxq.inc(16)
                    mk.wait("tensor", dxq, xd)
                    for oc in range(8):
                        z = oc % 2
                        mk.wait("tensor", vV, ev_z[z])
                        for dc in range(NDC):
                            last = dc == NDC - 1
                            mk.emit("tensor", lambda e, dc=dc, oc=oc, z=z, l=last: (e.matmul(
                                PQ[z], WpT[:, 1024 * dc + 128 * oc:1024 * dc + 128 * oc + 128],
                                xqv[:, dc, :], start=(dc == 0), stop=(dc == NDC - 1)
                            ).then_inc(tT.h, 1) if l else e.matmul(
                                PQ[z], WpT[:, 1024 * dc + 128 * oc:1024 * dc + 128 * oc + 128],
                                xqv[:, dc, :], start=(dc == 0), stop=(dc == NDC - 1))))
                        mm_z = tT.inc(1)
                        mk.wait("vector", tT, mm_z)
                        mk.emit("vector", lambda e, oc=oc, ts=ts, z=z, dst=dst: e.tensor_copy(
                            dst[:, 2048 * oc + 512 * ts:2048 * oc + 512 * ts + 512],
                            PQ[z]).then_inc(vV.h, 1))
                        ev_z[z] = vV.inc(1)
                    mm_last = tT.n
            else:
                ev_prev = 0
                mm_last = tT.n
                for ts in range(4):
                    mk.wait("gpsimd", tT, mm_last)
                    mk.emit("gpsimd", lambda e, ts=ts, tb=tb: e.dma_start(
                        xqv, xdv[:, :, tb + 512 * ts:tb + 512 * ts + 512]).then_inc(dxq.h, 16))
                    xd = dxq.inc(16)
                    mk.wait("tensor", dxq, xd)
                    for it in range(4):
                        mk.wait("tensor", vV, ev_prev)
                        for dc in range(NDC):
                            for hh in range(2):
                                last = dc == NDC - 1 and hh == 1
                                mk.emit("tensor", lambda e, dc=dc, it=it, hh=hh, l=last: (e.matmul(
                                    PV2[:, 512 * hh:512 * hh + 512],
                                    xqv[:, dc, 128 * it:128 * it + 128],
                                    WpT[:, 1024 * dc + 512 * hh:1024 * dc + 512 * hh + 512],
                                    start=(dc == 0), stop=(dc == NDC - 1)
                                ).then_inc(tT.h, 1) if l else e.matmul(
                                    PV2[:, 512 * hh:512 * hh + 512],
                                    xqv[:, dc, 128 * it:128 * it + 128],
                                    WpT[:, 1024 * dc + 512 * hh:1024 * dc + 512 * hh + 512],
                                    start=(dc == 0), stop=(dc == NDC - 1))))
                        vmm = tT.inc(1)
                        mk.wait("vector", tT, vmm)
                        mk.emit("vector", lambda e, ts=ts, it=it: e.tensor_copy(
                            vN[:, 1024 * (4 * ts + it):1024 * (4 * ts + it) + 1024],
                            PV2).then_inc(vV.h, 1))
                        ev_prev = vV.inc(1)
                    mm_last = tT.n

        if dbg is not None and b == 0:
            dv = vV.n
            mk.wait("sync", vV, dv)
            mk.emit("sync", lambda e: e.dma_start(dbg["QT"], qT[:]).then_inc(dS.h, 16))
            mk.emit("sync", lambda e: e.dma_start(dbg["KT"], kT[:]).then_inc(dS.h, 16))
            mk.emit("sync", lambda e: e.dma_start(dbg["VN"], vN[:]).then_inc(dS.h, 16))
            dS.inc(48)
        if not do_bphase:
            continue
        qkv_v_done = vV.n
        prev = (vV, qkv_v_done)
        for h in range(16):
            bh = 64 * (h % 2)
            qsl = qT[bh:bh + 64, 2048 * (h // 2):2048 * (h // 2) + 2048]
            ksl = kT[bh:bh + 64, 2048 * (h // 2):2048 * (h // 2) + 2048]
            for which, sl in ((0, qsl), (1, ksl)):
                mk.wait("scalar", prev[0], prev[1])
                mk.emit("scalar", lambda e, sl=sl, bh=bh: e.activation(
                    sqtmp[bh:bh + 64, 0:2048], sl, AF.Square).then_inc(sS.h, 1))
                sq = (sS, sS.inc(1))
                mk.wait("tensor", sq[0], sq[1])
                for s in range(4):
                    last = s == 3
                    mk.emit("tensor", lambda e, s=s, bh=bh, l=last: (e.matmul(
                        PO[0:1, 512 * s:512 * s + 512], onesA[bh:bh + 64, 0:1],
                        sqtmp[bh:bh + 64, 512 * s:512 * s + 512], start=True, stop=True
                    ).then_inc(tT.h, 1) if l else e.matmul(
                        PO[0:1, 512 * s:512 * s + 512], onesA[bh:bh + 64, 0:1],
                        sqtmp[bh:bh + 64, 512 * s:512 * s + 512], start=True, stop=True)))
                mm = (tT, tT.inc(1))
                mk.wait("vector", mm[0], mm[1])
                col = 16 * which + h
                mk.emit("vector", lambda e, col=col: e.reduce_max(
                    st8[0:1, col:col + 1],
                    PO[0:1, 0:2048].rearrange("p (a o) -> p a o", a=1, o=2048),
                    axis=AXL.X).then_inc(vV.h, 1))
                prev = (vV, vV.inc(1))
        bsteps = [
            ("scalar", lambda e: e.activation(st8[0:1, 32:48], st8[0:1, 0:16], AF.Sqrt)),
            ("scalar", lambda e: e.activation(st8[0:1, 48:64], st8[0:1, 16:32], AF.Sqrt)),
            ("vector", lambda e: e.tensor_tensor(st8[0:1, 32:48], st8[0:1, 32:48],
                                                 st8[0:1, 48:64], ALU.mult)),
            ("vector", lambda e: e.tensor_scalar(st8[0:1, 48:64], st8[0:1, 32:48],
                                                 -0.125, None, ALU.mult)),
            ("vector", lambda e: e.tensor_copy(st8b[0:1, :], st8[0:1, 48:64])),
        ]
        bdone = seq_emit(mk, sems, bsteps, first=prev)
        mk.wait("tensor", bdone[0], bdone[1])
        for h in range(16):
            last = h == 15
            mk.emit("tensor", lambda e, h=h, l=last: (e.matmul(
                SA[0][:, h:h + 1], ones1[:], st8b[0:1, h:h + 1],
                start=True, stop=True).then_inc(tT.h, 1) if l else e.matmul(
                SA[0][:, h:h + 1], ones1[:], st8b[0:1, h:h + 1],
                start=True, stop=True)))
        bm = (tT, tT.inc(1))
        mk.wait("vector", bm[0], bm[1])
        mk.emit("vector", lambda e: e.tensor_copy(Bcols[:], SA[0][:, 0:16]).then_inc(vV.h, 1))
        bdone = (vV, vV.inc(1))

        vst_used = {0: (tT, 0), 1: (tT, 0)}
        po_free = (vV, 0)
        sa_free = {0: (sS, 0), 1: (sS, 0)}
        sa_rb_free = (vV, 0)
        pstrip_free = {0: (tT, 0), 1: (tT, 0)}
        atto_last = (vV, 0)
        for h in range(n_heads):
            zs = h % 2
            bh = 64 * (h % 2)
            sr = 64 - bh
            hq = 2048 * (h // 2)
            qsl = qT[bh:bh + 64, hq:hq + 2048]
            ksl = kT[bh:bh + 64, hq:hq + 2048]
            pw = vst_used[zs]
            mk.wait("vector", pw[0], pw[1])
            mk.wait("vector", bdone[0], bdone[1])
            mk.emit("vector", lambda e, zs=zs: e.memset(vst[zs][:], 1.0).then_inc(vV.h, 1))
            c3 = (vV, vV.inc(1))
            mk.wait("vector", c3[0], c3[1])
            for j in range(16):
                last = j == 15
                mk.emit("vector", lambda e, j=j, zs=zs, h=h, bh=bh, l=last: (e.tensor_copy(
                    vst[zs][:, 128 * j + bh:128 * j + bh + 64],
                    vN[:, 1024 * j + 64 * h:1024 * j + 64 * h + 64]).then_inc(vV.h, 1)
                    if l else e.tensor_copy(
                    vst[zs][:, 128 * j + bh:128 * j + bh + 64],
                    vN[:, 1024 * j + 64 * h:1024 * j + 64 * h + 64])))
            c4 = (vV, vV.inc(1))
            mk.wait("tensor", c4[0], c4[1])
            mk.wait("tensor", po_free[0], po_free[1])
            sc = 0
            pv_done = (tT, 0)
            for j in range(16):
                z2 = j % 2
                for s in range(j // 4, 4):
                    qs0 = 128 * j if s == j // 4 else 512 * s
                    N = 512 * (s + 1) - qs0
                    za = sc % 2
                    mk.wait("tensor", sa_free[za][0], sa_free[za][1])
                    if sc < 2:
                        mk.wait("tensor", sa_rb_free[0], sa_rb_free[1])
                    mk.emit("tensor", lambda e, j=j, qs0=qs0, N=N, za=za, ksl=ksl, qsl=qsl: e.matmul(
                        SA[za][:, 0:N], ksl[:, 128 * j:128 * j + 128],
                        qsl[:, qs0:qs0 + N],
                        start=True, stop=True).then_inc(tT.h, 1))
                    smm = (tT, tT.inc(1))
                    mk.wait("scalar", smm[0], smm[1])
                    mk.wait("scalar", pstrip_free[z2][0], pstrip_free[z2][1])
                    mk.emit("scalar", lambda e, qs0=qs0, N=N, za=za, z2=z2, h=h: e.activation(
                        pstrip[z2][:, qs0:qs0 + N], SA[za][:, 0:N],
                        AF.Exp, scale=0.125, bias=Bcols[:, h:h + 1]).then_inc(sS.h, 1))
                    ex = (sS, sS.inc(1))
                    sa_free[za] = ex
                    pv_wait = ex
                    if s == j // 4:
                        mk.wait("vector", ex[0], ex[1])
                        mk.emit("vector", lambda e, j=j, z2=z2: e.tensor_tensor(
                            pstrip[z2][:, 128 * j:128 * j + 128],
                            pstrip[z2][:, 128 * j:128 * j + 128],
                            ut[:], ALU.mult).then_inc(vV.h, 1))
                        pv_wait = (vV, vV.inc(1))
                    mk.wait("tensor", pv_wait[0], pv_wait[1])
                    jl = min(15, 4 * s + 3)
                    mk.emit("tensor", lambda e, j=j, qs0=qs0, N=N, z2=z2, zs=zs, jl=jl: e.matmul(
                        PO[:, qs0:qs0 + N], vst[zs][:, 128 * j:128 * j + 128],
                        pstrip[z2][:, qs0:qs0 + N],
                        start=(j == 0), stop=(j == jl),
                        skip_group_check=True).then_inc(tT.h, 1))
                    pv_done = (tT, tT.inc(1))
                    sc += 1
                pstrip_free[z2] = pv_done
            vst_used[zs] = pv_done
            mk.wait("vector", pv_done[0], pv_done[1])

            def _recip(e, sqtmp=sqtmp, PO=PO, nc=nc, vV=vV, sr=sr):
                with nc.allow_low_precision(reason="softmax 1/sum applied in bf16"):
                    return e.reciprocal(sqtmp[sr:sr + 1, 0:2048], PO[sr:sr + 1, 0:2048]).then_inc(vV.h, 1)

            mk.emit("vector", _recip)
            rr = (vV, vV.inc(1))
            mk.wait("tensor", rr[0], rr[1])
            mk.wait("tensor", sa_free[0][0], sa_free[0][1])
            mk.wait("tensor", sa_free[1][0], sa_free[1][1])
            mk.wait("tensor", sa_rb_free[0], sa_rb_free[1])
            rbc = None
            for half in range(2):
                if rbc is not None:
                    mk.wait("tensor", rbc[0], rbc[1])
                for s2 in range(2):
                    last = s2 == 1
                    mk.emit("tensor", lambda e, half=half, s2=s2, sr=sr, bh=bh, l=last: (e.matmul(
                        SA[s2][bh:bh + 64, :], onesA[sr:sr + 1, 0:64],
                        sqtmp[sr:sr + 1, 1024 * half + 512 * s2:1024 * half + 512 * s2 + 512],
                        start=True, stop=True).then_inc(tT.h, 1) if l else e.matmul(
                        SA[s2][bh:bh + 64, :], onesA[sr:sr + 1, 0:64],
                        sqtmp[sr:sr + 1, 1024 * half + 512 * s2:1024 * half + 512 * s2 + 512],
                        start=True, stop=True)))
                rbm = (tT, tT.inc(1))
                mk.wait("vector", rbm[0], rbm[1])
                if half == 1:
                    mk.wait("vector", sa_rb_free[0], sa_rb_free[1])
                mk.emit("vector", lambda e, bh=bh: e.tensor_copy(
                    rbs[bh:bh + 64, 0:512], SA[0][bh:bh + 64, :]))
                mk.emit("vector", lambda e, bh=bh: e.tensor_copy(
                    rbs[bh:bh + 64, 512:1024], SA[1][bh:bh + 64, :]).then_inc(vV.h, 1))
                rbc = (vV, vV.inc(1))
                mk.wait("vector", rbc[0], rbc[1])
                dsth = qT[bh:bh + 64, hq + 1024 * half:hq + 1024 * half + 1024]
                mk.emit("vector", lambda e, dsth=dsth, half=half, bh=bh: e.tensor_tensor(
                    dsth, PO[bh:bh + 64, 1024 * half:1024 * half + 1024],
                    rbs[bh:bh + 64, 0:1024], ALU.mult).then_inc(vV.h, 1))
                sa_rb_free = (vV, vV.inc(1))
            po_free = sa_rb_free
            atto_last = sa_rb_free
        if not do_proj:
            continue

        WoT = WpT
        wo2 = WoT[:].rearrange("p (ic o) -> p ic o", ic=NDC, o=1024)
        mk.wait("tensor", atto_last[0], atto_last[1])
        woT_ready = emit_wT_load(
            lambda oc: wap["w_out"][128 * oc:128 * oc + 128, :], None, wo2)
        mk.wait("tensor", vV, woT_ready)
        xqv = xqw[:].rearrange("p (dc t) -> p dc t", dc=NDC, t=512)
        arv = ar[:].rearrange("p (dc t) -> p dc t", dc=NDC, t=NTOK)
        ev_z = {0: 0, 1: 0}
        mm_last = tT.n
        for ts in range(4):
            mk.wait("gpsimd", tT, mm_last)
            mk.wait("gpsimd", vV, max(ev_z.values()))
            mk.emit("gpsimd", lambda e, ts=ts, tb=tb: e.dma_start(
                xqv, xdv[:, :, tb + 512 * ts:tb + 512 * ts + 512]).then_inc(dxq.h, 16))
            xd = dxq.inc(16)
            mk.wait("tensor", dxq, xd)
            for oc in range(8):
                z = oc % 2
                mk.wait("tensor", vV, ev_z[z])
                for ic in range(NDC):
                    last = ic == NDC - 1
                    mk.emit("tensor", lambda e, ic=ic, oc=oc, ts=ts, z=z, l=last: (e.matmul(
                        PQ[z], WoT[:, 1024 * ic + 128 * oc:1024 * ic + 128 * oc + 128],
                        qT[:, 2048 * ic + 512 * ts:2048 * ic + 512 * ts + 512],
                        start=(ic == 0), stop=(ic == NDC - 1)).then_inc(tT.h, 1)
                        if l else e.matmul(
                        PQ[z], WoT[:, 1024 * ic + 128 * oc:1024 * ic + 128 * oc + 128],
                        qT[:, 2048 * ic + 512 * ts:2048 * ic + 512 * ts + 512],
                        start=(ic == 0), stop=(ic == NDC - 1))))
                mm_z = tT.inc(1)
                mk.wait("vector", tT, mm_z)
                mk.emit("vector", lambda e, oc=oc, ts=ts, z=z, tb=tb: e.tensor_tensor(
                    arv[:, oc, tb + 512 * ts:tb + 512 * ts + 512], PQ[z],
                    xqv[:, oc, :], ALU.add).then_inc(vV.h, 1))
                ev_z[z] = vV.inc(1)
            mm_last = tT.n
    asc.close()


def emit_ffn1(mk, bufs, sems, wap, xt_dram, acc_dram):
    nc = mk.nc
    tT, vV, sS, dW, dA = (sems[n] for n in ("tT", "vV", "sS", "dW", "dA"))
    dE = [mk.sem("dE0"), mk.sem("dE1")]
    dxc = [mk.sem("dxc0"), mk.sem("dxc1")]
    dxw = [mk.sem("dxw0"), mk.sem("dxw1")]
    ar = bufs["arena"]
    ps = bufs["ps_f32"]
    h_sb = ar[:, 0:1024]
    xwin = [ar[:, 1024:3072], ar[:, 3072:5120]]
    tmp = ar[:, 5120:7168].bitcast(F32)
    gsel = ar[:, 7168:7680].bitcast(F32)
    lgs = ar[:, 7680:8192].bitcast(F32)
    m8 = ar[:, 8192:8704].bitcast(F32)
    e2g = ar[:, 8704:8960].bitcast(F32)
    wr_sb = ar[:, 11008:11136].bitcast(F32)
    rb_sb = ar[0:1, 11136:11152].bitcast(F32)
    onesF = ar[0:1, 11152:11408].bitcast(F32)
    xca = [ar[:, 11520:19712].bitcast(F32), ar[:, 19712:27904].bitcast(F32)]
    ph1 = ps[:, 0:256]
    ph3 = ps[:, 512:768]
    py = [ps[:, 1024:2048], ps[:, 2048:3072]]
    pl = ps[:, 512:768]
    xdv = xt_dram.rearrange("(dc p) t -> p dc t", p=128)

    wsc = contextlib.ExitStack()
    w1s = [wsc.enter_context(nc.sbuf_tensor(f"f1w1{mk.uid()}_{i}", [128, 4096], BF16)) for i in range(2)]
    w3s = [wsc.enter_context(nc.sbuf_tensor(f"f1w3{mk.uid()}_{i}", [128, 4096], BF16)) for i in range(2)]
    w2s = [wsc.enter_context(nc.sbuf_tensor(f"f1w2{mk.uid()}_{i}", [128, 4096], BF16)) for i in range(2)]

    def expert_w(ex):
        if ex < 8:
            return wap["r_w1"][ex], wap["r_w3"][ex], wap["r_w2"][ex]
        return wap["s_w1"][ex - 8], wap["s_w3"][ex - 8], wap["s_w2"][ex - 8]

    mk.emit("gpsimd", lambda e: e.dma_start(
        wr_sb.rearrange("p (dc j) -> p dc j", dc=NDC, j=8),
        wap["router_w"].rearrange("(dc p) j -> p dc j", p=128)).then_inc(dW.h, 16))
    mk.emit("gpsimd", lambda e: e.dma_start(
        rb_sb, wap["router_b"].unsqueeze(0)).then_inc(dW.h, 16))
    stg_done = dW.inc(32)
    mk.emit("vector", lambda e: e.memset(onesF, 1.0).then_inc(vV.h, 1))
    onesF_done = vV.inc(1)
    def load_expert(ex, buf):
        a1, a3, a2 = expert_w(ex)
        mk.emit("gpsimd", lambda e, a1=a1, buf=buf: e.dma_start(
            w1s[buf][:].rearrange("p (dc f) -> p dc f", dc=NDC, f=512),
            a1.rearrange("(dc p) f -> p dc f", p=128)).then_inc(dE[buf].h, 16))
        mk.emit("gpsimd", lambda e, a3=a3, buf=buf: e.dma_start(
            w3s[buf][:].rearrange("p (dc f) -> p dc f", dc=NDC, f=512),
            a3.rearrange("(dc p) f -> p dc f", p=128)).then_inc(dE[buf].h, 16))
        mk.emit("gpsimd", lambda e, a2=a2, buf=buf: e.dma_start(
            w2s[buf][:].rearrange("p (fc d) -> p fc d", fc=4, d=1024),
            a2.rearrange("(fc p) d -> p fc d", p=128)).then_inc(dE[buf].h, 16))
        return dE[buf].inc(48)

    wready = {0: load_expert(0, 0)}

    mk.wait("tensor", dW, stg_done)
    mk.wait("tensor", vV, onesF_done)
    wrv = wr_sb.rearrange("p (dc j) -> p dc j", dc=NDC, j=8)
    mm_done = {}
    for sl in range(8):
        z = sl % 2
        mk.wait("gpsimd", tT, mm_done.get(sl - 2, 0))
        mk.emit("gpsimd", lambda e, sl=sl, z=z: e.dma_start(
            xca[z].rearrange("p (dc t) -> p dc t", dc=NDC, t=512),
            xdv[:, :, 512 * sl:512 * sl + 512]).then_inc(dxc[z].h, 16))
        xd = dxc[z].inc(16)
        mk.wait("tensor", dxc[z], xd)
        xcv = xca[z].rearrange("p (dc t) -> p dc t", dc=NDC, t=512)
        for t4 in range(4):
            tg = 4 * sl + t4
            for dc in range(NDC):
                mk.emit("tensor", lambda e, dc=dc, t4=t4, tg=tg, xcv=xcv: e.matmul(
                    pl[:, 8 * tg:8 * tg + 8],
                    xcv[:, dc, 128 * t4:128 * t4 + 128], wrv[:, dc, :],
                    start=(dc == 0), stop=False))
            mk.emit("tensor", lambda e, tg=tg: e.matmul(
                pl[:, 8 * tg:8 * tg + 8], onesF, rb_sb,
                start=False, stop=True).then_inc(tT.h, 1))
            tT.inc(1)
        mm_done[sl] = tT.n
    lg_done = tT.n
    mk.wait("vector", tT, lg_done)
    mk.emit("vector", lambda e: e.tensor_copy(lgs, pl[:, 0:256]).then_inc(vV.h, 1))
    lgs_done = (vV, vV.inc(1))
    mk.wait("vector", lgs_done[0], lgs_done[1])
    for tg in range(NTT):
        last = tg == NTT - 1
        mk.emit("vector", lambda e, tg=tg, l=last: (e.max(
            out=m8[:, 8 * tg:8 * tg + 8], in_=lgs[:, 8 * tg:8 * tg + 8]).then_inc(vV.h, 1)
            if l else e.max(out=m8[:, 8 * tg:8 * tg + 8], in_=lgs[:, 8 * tg:8 * tg + 8])))
    m8_done = (vV, vV.inc(1))
    m8v = m8.rearrange("p (g e) -> p g e", g=NTT, e=8)
    lgv = lgs.rearrange("p (g e) -> p g e", g=NTT, e=8)
    gsv = gsel.rearrange("p (g e) -> p g e", g=NTT, e=8)
    e2 = e2g[:, 0:32]
    g1 = e2g[:, 32:64]
    g2 = e2g[:, 64:96]
    scr = e2g[:, 96:128]
    e2v = e2.rearrange("p (g o) -> p g o", g=NTT, o=1)
    g1v = g1.rearrange("p (g o) -> p g o", g=NTT, o=1)
    g2v = g2.rearrange("p (g o) -> p g o", g=NTT, o=1)
    gsteps = [
        ("vector", lambda e: e.tensor_tensor(e2v, m8v[:, :, 1:2], m8v[:, :, 0:1], ALU.subtract)),
        ("scalar", lambda e: e.activation(e2, e2, AF.Exp)),
        ("vector", lambda e: e.tensor_scalar(scr, e2, 1.0, None, ALU.add)),
        ("vector", lambda e: e.reciprocal(g1, scr)),
        ("vector", lambda e: e.tensor_tensor(g2, e2, g1, ALU.mult)),
        ("vector", lambda e: e.tensor_tensor(gsv, lgv, m8v[:, :, 0:1].broadcast_to((128, NTT, 8)), ALU.is_equal)),
        ("vector", lambda e: e.tensor_tensor(gsv, gsv, g1v.broadcast_to((128, NTT, 8)), ALU.mult)),
        ("vector", lambda e: e.tensor_tensor(lgs, lgv, m8v[:, :, 1:2].broadcast_to((128, NTT, 8)), ALU.is_equal)),
        ("vector", lambda e: e.tensor_tensor(lgs, lgv, g2v.broadcast_to((128, NTT, 8)), ALU.mult)),
        ("vector", lambda e: e.tensor_tensor(gsel, gsel, lgs, ALU.add)),
    ]
    gdone = seq_emit(mk, sems, gsteps, first=m8_done)
    router_done_v = vV.n

    ev_cnt = {}
    dma_cnt = {}
    acc_last_of_e = 0
    xw_read_z = {0: 0, 1: 0}
    phA_all_done = 0
    phB_prev = 0
    sil_prev = 0
    for ex in range(10):
        buf = ex % 2
        mk.wait("tensor", dE[buf], wready[ex])
        if ex == 0:
            mk.wait("tensor", vV, router_done_v)
        w1v = w1s[buf][:].rearrange("p (dc f) -> p dc f", dc=NDC, f=512)
        w3v = w3s[buf][:].rearrange("p (dc f) -> p dc f", dc=NDC, f=512)
        w2v = w2s[buf][:].rearrange("p (fc d) -> p fc d", fc=4, d=1024)
        for w in range(16):
            wz = w % 2
            mk.wait("gpsimd", tT, xw_read_z[wz])
            mk.emit("gpsimd", lambda e, w=w, wz=wz: e.dma_start(
                xwin[wz][:].rearrange("p (dc t) -> p dc t", dc=NDC, t=256),
                xdv[:, :, 256 * w:256 * w + 256]).then_inc(dxw[wz].h, 16))
            xwd = dxw[wz].inc(16)
            xv = xwin[wz][:].rearrange("p (dc t) -> p dc t", dc=NDC, t=256)
            mk.wait("tensor", dxw[wz], xwd)
            for fc in range(4):
                mk.wait("tensor", vV, sil_prev)
                for which, wv in ((0, w1v), (1, w3v)):
                    dstp = ph1 if which == 0 else ph3
                    for dc in range(NDC):
                        last = dc == NDC - 1 and which == 1
                        mk.emit("tensor", lambda e, dc=dc, fc=fc, wv=wv, dstp=dstp, xv=xv, l=last: (e.matmul(
                            dstp, wv[:, dc, 128 * fc:128 * fc + 128],
                            xv[:, dc, :], start=(dc == 0), stop=(dc == NDC - 1)
                        ).then_inc(tT.h, 1) if l else e.matmul(
                            dstp, wv[:, dc, 128 * fc:128 * fc + 128],
                            xv[:, dc, :], start=(dc == 0), stop=(dc == NDC - 1))))
                mmA = tT.inc(1)
                mk.wait("scalar", tT, mmA)
                mk.emit("scalar", lambda e, fc=fc: e.activation(
                    h_sb[:, 256 * fc:256 * fc + 256], ph1, AF.Silu).then_inc(sS.h, 1))
                sdone = sS.inc(1)
                mk.wait("vector", sS, sdone)
                mk.emit("vector", lambda e, fc=fc: e.tensor_tensor(
                    h_sb[:, 256 * fc:256 * fc + 256], h_sb[:, 256 * fc:256 * fc + 256],
                    ph3, ALU.mult).then_inc(vV.h, 1))
                sil_prev = vV.inc(1)
            phA_all_done = tT.n
            xw_read_z[wz] = tT.n
            h_ready = sil_prev
            mk.wait("tensor", vV, h_ready)
            for th in range(2):
                tg = 2 * w + th
                pz = tg % 2
                mk.wait("tensor", vV, ev_cnt.get(tg - 2, 0))
                if ex == 0 and tg <= 1:
                    mk.wait("tensor", vV, router_done_v)
                for fc in range(4):
                    for hh in range(2):
                        last = fc == 3 and hh == 1
                        mk.emit("tensor", lambda e, fc=fc, th=th, hh=hh, pz=pz, w2v=w2v, l=last: (e.matmul(
                            py[pz][:, 512 * hh:512 * hh + 512],
                            h_sb[:, 256 * fc + 128 * th:256 * fc + 128 * th + 128],
                            w2v[:, fc, 512 * hh:512 * hh + 512],
                            start=(fc == 0), stop=(fc == 3)).then_inc(tT.h, 1)
                            if l else e.matmul(
                            py[pz][:, 512 * hh:512 * hh + 512],
                            h_sb[:, 256 * fc + 128 * th:256 * fc + 128 * th + 128],
                            w2v[:, fc, 512 * hh:512 * hh + 512],
                            start=(fc == 0), stop=(fc == 3))))
                mmB = tT.inc(1)
                mk.wait("vector", tT, mmB)
                mk.wait("vector", dA, dA.n)
                if ex < 8:
                    mk.emit("vector", lambda e, tg=tg, ex=ex, pz=pz: e.tensor_scalar(
                        tmp, py[pz][:], gsel[:, 8 * tg + ex:8 * tg + ex + 1],
                        None, ALU.mult).then_inc(vV.h, 1))
                else:
                    mk.emit("vector", lambda e, pz=pz: e.tensor_copy(
                        tmp, py[pz][:]).then_inc(vV.h, 1))
                ev_cnt[tg] = vV.inc(1)
                mk.wait("gpsimd", vV, ev_cnt[tg])
                if tg == 0:
                    mk.wait("gpsimd", dA, acc_last_of_e)
                op = ALU.bypass if ex == 0 else ALU.add
                mk.emit("gpsimd", lambda e, tg=tg, op=op: e.dma_start(
                    acc_dram[128 * tg:128 * tg + 128, :], tmp,
                    accum_op=op).then_inc(dA.h, 16))
                dma_cnt[tg] = dA.inc(16)
            phB_prev = tT.n
        acc_last_of_e = dma_cnt[31]
        if ex < 9:
            mk.wait("gpsimd", tT, phA_all_done)
            wready[ex + 1] = load_expert(ex + 1, (ex + 1) % 2)
    wsc.close()
    return dma_cnt[31]


def emit_final_out(mk, bufs, sems, out_dram):
    nc = mk.nc
    tT, vV, dS = (sems[n] for n in ("tT", "vV", "dS"))
    ar, eye, pt = bufs["arena"], bufs["eye"], bufs["pt"]
    arv = ar[:].rearrange("p (dc t) -> p dc t", dc=NDC, t=NTOK)
    ov = out_dram.rearrange("b (i p) d -> p b i d", p=128)
    fsc = contextlib.ExitStack()
    fst = [fsc.enter_context(nc.sbuf_tensor(f"fst{mk.uid()}_{i}", [128, 1024], F32)) for i in range(2)]
    dF = [mk.sem("dF0"), mk.sem("dF1")]
    mk.wait("tensor", vV, vV.n)
    st_done = {}
    cp_done = {}
    tr_done = {}
    for tg in range(NTT):
        b, i = tg // 16, tg % 16
        z = tg % 2
        for half in range(2):
            pz = half
            mk.wait("tensor", vV, cp_done.get(tg - 1, 0))
            for q in range(4):
                dc = 4 * half + q
                last = q == 3
                mk.emit("tensor", lambda e, dc=dc, q=q, pz=pz, tg=tg, l=last: (e.transpose(
                    pt[pz][:, 128 * q:128 * q + 128],
                    arv[:, dc, 128 * tg:128 * tg + 128], eye[:]).then_inc(tT.h, 1)
                    if l else e.transpose(
                    pt[pz][:, 128 * q:128 * q + 128],
                    arv[:, dc, 128 * tg:128 * tg + 128], eye[:])))
            td = tT.inc(1)
            tr_done[tg] = td
            mk.wait("vector", tT, td)
            mk.wait("vector", dF[z], st_done.get(tg - 2, 0))
            last = half == 1
            mk.emit("vector", lambda e, z=z, half=half, l=last: (e.tensor_copy(
                fst[z][:, 512 * half:512 * half + 512],
                pt[half][:].rearrange("p (q t) -> p (q t)", q=4, t=128)).then_inc(vV.h, 1)
                if l else e.tensor_copy(
                fst[z][:, 512 * half:512 * half + 512],
                pt[half][:].rearrange("p (q t) -> p (q t)", q=4, t=128))))
            cp_done[tg] = vV.inc(1) if last else cp_done.get(tg, 0)
        mk.wait("sync", vV, cp_done[tg])
        mk.emit("sync", lambda e, b=b, i=i, z=z: e.dma_start(
            ov[:, b, i, :], fst[z][:]).then_inc(dF[z].h, 16))
        st_done[tg] = dF[z].inc(16)
    mk.wait("sync", dF[0], dF[0].n)
    mk.wait("sync", dF[1], dF[1].n)
    fsc.close()


def seq_emit(mk, sems, steps, first=None):
    tT, vV, sS = sems["tT"], sems["vV"], sems["sS"]
    prev = first
    for eng, fn in steps:
        if prev is not None:
            mk.wait(eng, prev[0], prev[1])
        c = sS if eng == "scalar" else (vV if eng == "vector" else tT)
        mk.emit(eng, lambda e, fn=fn, c=c: fn(e).then_inc(c.h, 1))
        prev = (c, c.inc(1))
    return prev



def build_stage1(out_dtype=BF16, debug=False):
    nc = bass.Bass("TRN2", target_bir_lowering=False)
    u = nc.dram_tensor("U", [BC, T, D], F32, kind="ExternalInput").ap()
    tstack = nc.dram_tensor("TSTACK", [128, 7, 128, 128], F32, kind="ExternalInput").ap()
    eye_in = nc.dram_tensor("EYE", [128, 128], BF16, kind="ExternalInput").ap()
    xt_out = nc.dram_tensor("XT", [D, NTOK], out_dtype, kind="ExternalOutput").ap()

    mk = MK(nc)
    bufs = {}
    bufs["ubf"] = mk.sb("ubf", [128, NTOK * NDC], BF16)
    bufs["xt"] = mk.sb("xt", [128, NTOK * NDC], BF16)
    bufs["tsb"] = [mk.sb(f"tsb{i}", [128, 896], BF16) for i in range(2)]
    bufs["eye"] = mk.sb("eye", [128, 128], BF16)
    bufs["onec"] = mk.sb("onec", [128, 1], BF16)
    bufs["ones1"] = mk.sb("ones1", [1, 128], BF16)
    bufs["zsq"] = [mk.sb(f"zsq{i}", [128, NTOK], BF16) for i in range(2)]
    bufs["rows"] = mk.sb("rows", [1, 4096], F32)
    bufs["rows_bf"] = mk.sb("rows_bf", [1, 2048], BF16)
    ps_f32 = mk.ps("ps_f32", [128, 3072], F32)
    ps_bf = mk.ps("ps_bf", [128, 2048], BF16)
    bufs["pg"] = [ps_f32[:, 0:512], ps_f32[:, 512:1024]]
    bufs["pstx"] = [ps_f32[:, 0:512], ps_f32[:, 512:1024]]
    bufs["psts"] = [ps_f32[:, 1024:1536], ps_f32[:, 1536:2048]]
    bufs["pbc"] = [ps_f32[:, 2048:2560], ps_f32[:, 2560:3072]]
    bufs["pt"] = [ps_bf[:, 0:512], ps_bf[:, 1024:1536]]

    sems = {n: mk.sem(n) for n in ("dW", "dS", "tT", "vV", "sS")}
    dW, dS, tT, vV, sS = (sems[n] for n in ("dW", "dS", "tT", "vV", "sS"))

    dC = mk.sem("dC")
    mk.emit("gpsimd", lambda e: e.dma_start(bufs["eye"][:], eye_in).then_inc(dC.h, 16))
    mk.emit("gpsimd", lambda e: e.dma_start(
        bufs["ubf"][:].rearrange("p (b i d) -> p b i d", b=BC, i=16, d=D),
        u.rearrange("b (i p) d -> p b i d", p=128)).then_inc(dC.h, 16))
    load_done = dC.inc(32)
    mk.emit("vector", lambda e: e.memset(bufs["onec"][:], 1.0))
    mk.emit("vector", lambda e: e.memset(bufs["ones1"][:], 1.0).then_inc(vV.h, 1))
    const_done = vV.inc(1)
    mk.wait("tensor", mk.sems["dC"], load_done)
    mk.wait("tensor", vV, const_done)
    mk.wait("vector", mk.sems["dC"], load_done)

    emit_conv_block(mk, bufs, sems, tstack)
    if debug:
        udbg = nc.dram_tensor("UDBG", [128, NTOK * NDC], BF16, kind="ExternalOutput").ap()
        mk.wait("sync", vV, vV.n)
        mk.emit("sync", lambda e: e.dma_start(udbg, bufs["ubf"][:]).then_inc(dS.h, 16))
        dS.inc(16)
    emit_transpose_nat_to_T(mk, bufs, sems, bufs["ubf"], bufs["xt"])
    emit_lnT(mk, bufs, sems, bufs["xt"])

    final_v = vV.n
    mk.wait("sync", vV, final_v)
    xtv = bufs["xt"][:].rearrange("p (dc t) -> p dc t", dc=NDC, t=NTOK)
    for dc in range(NDC):
        mk.emit("sync", lambda e, dc=dc: e.dma_start(
            xt_out[128 * dc:128 * dc + 128, :], xtv[:, dc, :]).then_inc(dS.h, 16))
    out_done = dS.inc(16 * NDC)
    mk.wait("sync", dS, out_done)
    mk.realize()
    return nc


def _shard_inputs_stage1(inputs):
    tstack = _conv_tstack(inputs["L0_attn_diff_kernel"], inputs["L0_attn_ma_r_kernel"])
    eye = np.eye(128, dtype=np.float32).astype(bfloat16)
    u = np.ascontiguousarray(np.asarray(inputs["L0_attn_u"], dtype=np.float32))
    in_maps = []
    for c in range(N_CORES):
        in_maps.append({
            "U": np.ascontiguousarray(u[2 * c:2 * c + 2]),
            "TSTACK": tstack,
            "EYE": eye,
        })
    return in_maps


def kernel(**inputs):
    raise NotImplementedError("under construction: stages being built")


def build_stage3(n_heads=16, do_proj=True, do_bphase=True, do_ln=True, debug=False):
    nc = bass.Bass("TRN2", target_bir_lowering=False)
    xt_in = nc.dram_tensor("XT", [D, NTOK], BF16, kind="ExternalInput").ap()
    wap = {
        "w_qkv": nc.dram_tensor("WQKV", [3 * D, D], F32, kind="ExternalInput").ap(),
        "w_out": nc.dram_tensor("WOUT", [D, D], F32, kind="ExternalInput").ap(),
    }
    eye_in = nc.dram_tensor("EYE", [128, 128], BF16, kind="ExternalInput").ap()
    ut_in = nc.dram_tensor("UT", [128, 128], BF16, kind="ExternalInput").ap()
    xt3_out = nc.dram_tensor("XT3", [D, NTOK], BF16, kind="ExternalOutput").ap()

    mk = MK(nc)
    bufs = {}
    bufs["arena"] = mk.sb("arena", [128, NTOK * NDC], BF16)
    bufs["eye"] = mk.sb("eye", [128, 128], BF16)
    bufs["ut"] = mk.sb("ut", [128, 128], BF16)
    bufs["onec"] = mk.sb("onec", [128, 1], BF16)
    bufs["ones1"] = mk.sb("ones1", [1, 128], BF16)
    bufs["onesA"] = mk.sb("onesA", [128, 128], BF16)
    ps_f32 = mk.ps("ps_f32", [128, 3072], F32)
    ps_bf = mk.ps("ps_bf", [128, 2048], BF16)
    bufs["ps_f32"] = ps_f32
    bufs["pstx"] = [ps_f32[:, 0:512], ps_f32[:, 512:1024]]
    bufs["psts"] = [ps_f32[:, 1024:1536], ps_f32[:, 1536:2048]]
    bufs["pbc"] = [ps_f32[:, 2048:2560], ps_f32[:, 2560:3072]]
    bufs["pt"] = [ps_bf[:, 0:512], ps_bf[:, 1024:1536]]

    sems = {n: mk.sem(n) for n in ("dW", "dS", "dA", "tT", "vV", "sS")}
    dW, dS, dA, tT, vV, sS = (sems[n] for n in ("dW", "dS", "dA", "tT", "vV", "sS"))

    dC = mk.sem("dC")
    mk.emit("gpsimd", lambda e: e.dma_start(bufs["eye"][:], eye_in).then_inc(dC.h, 16))
    mk.emit("gpsimd", lambda e: e.dma_start(bufs["ut"][:], ut_in).then_inc(dC.h, 16))
    load_done = dC.inc(32)
    mk.emit("vector", lambda e: e.memset(bufs["onec"][:], 1.0))
    mk.emit("vector", lambda e: e.memset(bufs["ones1"][:], 1.0))
    mk.emit("vector", lambda e: e.memset(bufs["onesA"][:], 1.0).then_inc(vV.h, 1))
    const_done = vV.inc(1)
    for eng in ("tensor", "vector", "scalar"):
        mk.wait(eng, mk.sems["dC"], load_done)
        mk.wait(eng, vV, const_done)

    dbg = None
    if debug:
        dbg = {n: nc.dram_tensor(n, [128, 16384], BF16, kind="ExternalOutput").ap()
               for n in ("QT", "KT", "VN")}
    emit_attn(mk, bufs, sems, wap, xt_in, n_heads=n_heads, do_proj=do_proj,
              do_bphase=do_bphase, dbg=dbg)
    if do_ln:
        bufs["zsq"] = [mk.sb(f"zsq{i}", [128, NTOK], BF16) for i in range(2)]
        bufs["rows"] = mk.sb("rows", [1, 4096], F32)
        bufs["rows_bf"] = mk.sb("rows_bf", [1, 2048], BF16)
        emit_lnT(mk, bufs, sems, bufs["arena"])

    final_v = vV.n
    mk.wait("sync", vV, final_v)
    mk.wait("sync", sems["sS"], sems["sS"].n)
    mk.wait("sync", tT, tT.n)
    xtv = bufs["arena"][:].rearrange("p (dc t) -> p dc t", dc=NDC, t=NTOK)
    for dc in range(NDC):
        mk.emit("sync", lambda e, dc=dc: e.dma_start(
            xt3_out[128 * dc:128 * dc + 128, :], xtv[:, dc, :]).then_inc(dS.h, 16))
    out_done = dS.inc(16 * NDC)
    mk.wait("sync", dS, out_done)
    mk.realize()
    return nc


def _shard_inputs_stage3(inputs, xt_by_core):
    eye = np.eye(128, dtype=np.float32).astype(bfloat16)
    ut = np.triu(np.ones((128, 128), dtype=np.float32)).astype(bfloat16)
    com = {
        "WQKV": np.ascontiguousarray(np.asarray(inputs["L1_attn_w_qkv"], np.float32)),
        "WOUT": np.ascontiguousarray(np.asarray(inputs["L1_attn_w_out"], np.float32)),
        "EYE": eye, "UT": ut,
    }
    return [dict(com, XT=np.ascontiguousarray(xt_by_core[c])) for c in range(len(xt_by_core))]


def build_stage4(final=False):
    nc = bass.Bass("TRN2", target_bir_lowering=False)
    xt_in = nc.dram_tensor("XT", [D, NTOK], BF16, kind="ExternalInput").ap()
    wap = {
        "router_w": nc.dram_tensor("RW", [D, 8], F32, kind="ExternalInput").ap(),
        "router_b": nc.dram_tensor("RB", [8], F32, kind="ExternalInput").ap(),
        "r_w1": nc.dram_tensor("RW1", [8, D, 512], F32, kind="ExternalInput").ap(),
        "r_w3": nc.dram_tensor("RW3", [8, D, 512], F32, kind="ExternalInput").ap(),
        "r_w2": nc.dram_tensor("RW2", [8, 512, D], F32, kind="ExternalInput").ap(),
        "s_w1": nc.dram_tensor("SW1", [2, D, 512], F32, kind="ExternalInput").ap(),
        "s_w3": nc.dram_tensor("SW3", [2, D, 512], F32, kind="ExternalInput").ap(),
        "s_w2": nc.dram_tensor("SW2", [2, 512, D], F32, kind="ExternalInput").ap(),
    }
    eye_in = nc.dram_tensor("EYE", [128, 128], BF16, kind="ExternalInput").ap()
    if final:
        out_dram = nc.dram_tensor("OUT", [BC, T, D], F32, kind="ExternalOutput").ap()
    else:
        xt4_out = nc.dram_tensor("XT4", [D, NTOK], BF16, kind="ExternalOutput").ap()
    acc_dram = nc.dram_tensor("ACCD", [NTOK, D], F32).ap()

    mk = MK(nc)
    bufs = {}
    bufs["arena"] = mk.sb("arena", [128, NTOK * NDC], BF16)
    bufs["eye"] = mk.sb("eye", [128, 128], BF16)
    bufs["onec"] = mk.sb("onec", [128, 1], BF16)
    bufs["ones1"] = mk.sb("ones1", [1, 128], BF16)
    ps_f32 = mk.ps("ps_f32", [128, 3072], F32)
    ps_bf = mk.ps("ps_bf", [128, 2048], BF16)
    bufs["ps_f32"] = ps_f32
    bufs["pstx"] = [ps_f32[:, 0:512], ps_f32[:, 512:1024]]
    bufs["psts"] = [ps_f32[:, 1024:1536], ps_f32[:, 1536:2048]]
    bufs["pbc"] = [ps_f32[:, 2048:2560], ps_f32[:, 2560:3072]]
    bufs["pt"] = [ps_bf[:, 0:512], ps_bf[:, 1024:1536]]

    sems = {n: mk.sem(n) for n in ("dW", "dS", "dA", "tT", "vV", "sS")}
    dW, dS, dA, tT, vV, sS = (sems[n] for n in ("dW", "dS", "dA", "tT", "vV", "sS"))

    dC = mk.sem("dC")
    mk.emit("gpsimd", lambda e: e.dma_start(bufs["eye"][:], eye_in).then_inc(dC.h, 16))
    load_done = dC.inc(16)
    mk.emit("vector", lambda e: e.memset(bufs["onec"][:], 1.0))
    mk.emit("vector", lambda e: e.memset(bufs["ones1"][:], 1.0).then_inc(vV.h, 1))
    const_done = vV.inc(1)
    for eng in ("tensor", "vector", "scalar"):
        mk.wait(eng, dC, load_done)
        mk.wait(eng, vV, const_done)

    acc_done = emit_ffn1(mk, bufs, sems, wap, xt_in, acc_dram)
    emit_accT_add(mk, bufs, sems, xt_in, acc_dram, acc_done)
    bufs["zsq"] = [mk.sb(f"zsq{i}", [128, NTOK], BF16) for i in range(2)]
    bufs["rows"] = mk.sb("rows", [1, 4096], F32)
    bufs["rows_bf"] = mk.sb("rows_bf", [1, 2048], BF16)
    emit_lnT(mk, bufs, sems, bufs["arena"])

    if final:
        emit_final_out(mk, bufs, sems, out_dram)
    else:
        final_v = vV.n
        mk.wait("sync", vV, final_v)
        mk.wait("sync", sems["sS"], sems["sS"].n)
        mk.wait("sync", tT, tT.n)
        xtv = bufs["arena"][:].rearrange("p (dc t) -> p dc t", dc=NDC, t=NTOK)
        for dc in range(NDC):
            mk.emit("sync", lambda e, dc=dc: e.dma_start(
                xt4_out[128 * dc:128 * dc + 128, :], xtv[:, dc, :]).then_inc(dS.h, 16))
        out_done = dS.inc(16 * NDC)
        mk.wait("sync", dS, out_done)
    mk.realize()
    return nc


def _shard_inputs_stage4(inputs, xt_by_core):
    eye = np.eye(128, dtype=np.float32).astype(bfloat16)
    com = {
        "RW": np.ascontiguousarray(np.asarray(inputs["L1_ffn_router_w"], np.float32)),
        "RB": np.ascontiguousarray(np.asarray(inputs["L1_ffn_router_b"], np.float32)),
        "RW1": np.ascontiguousarray(np.asarray(inputs["L1_ffn_routed_w1"], np.float32)),
        "RW3": np.ascontiguousarray(np.asarray(inputs["L1_ffn_routed_w3"], np.float32)),
        "RW2": np.ascontiguousarray(np.asarray(inputs["L1_ffn_routed_w2"], np.float32)),
        "SW1": np.ascontiguousarray(np.asarray(inputs["L1_ffn_shared_w1"], np.float32)),
        "SW3": np.ascontiguousarray(np.asarray(inputs["L1_ffn_shared_w3"], np.float32)),
        "SW2": np.ascontiguousarray(np.asarray(inputs["L1_ffn_shared_w2"], np.float32)),
        "EYE": eye,
    }
    return [dict(com, XT=np.ascontiguousarray(xt_by_core[c])) for c in range(len(xt_by_core))]


_PROG_CACHE = {}


def _get_prog(name, builder):
    if name not in _PROG_CACHE:
        _PROG_CACHE[name] = builder()
    return _PROG_CACHE[name]


def kernel(**inputs):
    return kernel_fused(**inputs)


def kernel_multilaunch(**inputs):
    inputs = {k: np.asarray(v) for k, v in inputs.items()}

    nc1 = _get_prog("s1", build_stage1)
    nc2 = _get_prog("s2", build_stage2)
    nc3 = _get_prog("s3", build_stage3)
    nc4 = _get_prog("s4f", lambda: build_stage4(final=True))

    cores = list(range(N_CORES))
    im1 = _shard_inputs_stage1(inputs)
    r1 = run_bass_kernel_spmd(nc1, im1, core_ids=cores)
    xt = [r1.results[c]["XT"] for c in range(N_CORES)]

    im2 = _shard_inputs_stage2(inputs, xt)
    r2 = run_bass_kernel_spmd(nc2, im2, core_ids=cores)
    xt = [r2.results[c]["XT2"] for c in range(N_CORES)]

    im3 = _shard_inputs_stage3(inputs, xt)
    r3 = run_bass_kernel_spmd(nc3, im3, core_ids=cores)
    xt = [r3.results[c]["XT3"] for c in range(N_CORES)]

    im4 = _shard_inputs_stage4(inputs, xt)
    r4 = run_bass_kernel_spmd(nc4, im4, core_ids=cores)
    out = np.concatenate([np.asarray(r4.results[c]["OUT"]) for c in range(N_CORES)], axis=0)
    return out


def build_fused():
    nc = bass.Bass("TRN2", target_bir_lowering=False)
    u = nc.dram_tensor("U", [BC, T, D], F32, kind="ExternalInput").ap()
    tstack = nc.dram_tensor("TSTACK", [128, 7, 128, 128], F32, kind="ExternalInput").ap()
    wap0 = {
        "w_red": nc.dram_tensor("WRED", [D, 16], F32, kind="ExternalInput").ap(),
        "wg": nc.dram_tensor("WG", [8, 16], F32, kind="ExternalInput").ap(),
        "w1": nc.dram_tensor("W1", [8, D, 4096], F32, kind="ExternalInput").ap(),
        "b1": nc.dram_tensor("B1", [8, 4096], F32, kind="ExternalInput").ap(),
        "w2": nc.dram_tensor("W2", [8, 4096, D], F32, kind="ExternalInput").ap(),
        "b2": nc.dram_tensor("B2", [8, D], F32, kind="ExternalInput").ap(),
    }
    wapA = {
        "w_qkv": nc.dram_tensor("WQKV", [3 * D, D], F32, kind="ExternalInput").ap(),
        "w_out": nc.dram_tensor("WOUT", [D, D], F32, kind="ExternalInput").ap(),
    }
    wap1 = {
        "router_w": nc.dram_tensor("RW", [D, 8], F32, kind="ExternalInput").ap(),
        "router_b": nc.dram_tensor("RB", [8], F32, kind="ExternalInput").ap(),
        "r_w1": nc.dram_tensor("RW1", [8, D, 512], F32, kind="ExternalInput").ap(),
        "r_w3": nc.dram_tensor("RW3", [8, D, 512], F32, kind="ExternalInput").ap(),
        "r_w2": nc.dram_tensor("RW2", [8, 512, D], F32, kind="ExternalInput").ap(),
        "s_w1": nc.dram_tensor("SW1", [2, D, 512], F32, kind="ExternalInput").ap(),
        "s_w3": nc.dram_tensor("SW3", [2, D, 512], F32, kind="ExternalInput").ap(),
        "s_w2": nc.dram_tensor("SW2", [2, 512, D], F32, kind="ExternalInput").ap(),
    }
    eye_in = nc.dram_tensor("EYE", [128, 128], BF16, kind="ExternalInput").ap()
    eyef_in = nc.dram_tensor("EYEF", [128, 128], F32, kind="ExternalInput").ap()
    ut_in = nc.dram_tensor("UT", [128, 128], BF16, kind="ExternalInput").ap()
    out_dram = nc.dram_tensor("OUT", [BC, T, D], F32, kind="ExternalOutput").ap()
    xt_dram = nc.dram_tensor("XTD", [D, NTOK], BF16).ap()
    acc_dram = nc.dram_tensor("ACCD", [NTOK, D], F32).ap()

    mk = MK(nc)
    bufs = {}
    bufs["arena"] = mk.sb("arena", [128, NTOK * NDC], BF16)
    bufs["xt"] = bufs["arena"]
    bufs["eye"] = mk.sb("eye", [128, 128], BF16)
    bufs["eyef"] = mk.sb("eyef", [128, 128], F32)
    bufs["ut"] = mk.sb("ut", [128, 128], BF16)
    bufs["onec"] = mk.sb("onec", [128, 1], BF16)
    bufs["ones1"] = mk.sb("ones1", [1, 128], BF16)
    bufs["onesA"] = mk.sb("onesA", [128, 128], BF16)
    ps_f32 = mk.ps("ps_f32", [128, 3072], F32)
    ps_bf = mk.ps("ps_bf", [128, 2048], BF16)
    bufs["ps_f32"] = ps_f32
    bufs["pg"] = [ps_f32[:, 0:512], ps_f32[:, 512:1024]]
    bufs["pstx"] = [ps_f32[:, 0:512], ps_f32[:, 512:1024]]
    bufs["psts"] = [ps_f32[:, 1024:1536], ps_f32[:, 1536:2048]]
    bufs["pbc"] = [ps_f32[:, 2048:2560], ps_f32[:, 2560:3072]]
    bufs["pt"] = [ps_bf[:, 0:512], ps_bf[:, 1024:1536]]

    sems = {n: mk.sem(n) for n in ("dW", "dS", "dA", "tT", "vV", "sS")}
    dW, dS, dA, tT, vV, sS = (sems[n] for n in ("dW", "dS", "dA", "tT", "vV", "sS"))

    dC = mk.sem("dC")
    for nm, dst in (("eye", eye_in), ("eyef", eyef_in), ("ut", ut_in)):
        mk.emit("gpsimd", lambda e, nm=nm, dst=dst: e.dma_start(
            bufs[nm][:], dst).then_inc(dC.h, 16))
    load_done = dC.inc(48)
    for nm in ("onec", "ones1", "onesA"):
        last = nm == "onesA"
        mk.emit("vector", lambda e, nm=nm, l=last: (e.memset(
            bufs[nm][:], 1.0).then_inc(vV.h, 1) if l else e.memset(bufs[nm][:], 1.0)))
    const_done = vV.inc(1)
    for eng in ("tensor", "vector", "scalar"):
        mk.wait(eng, dC, load_done)
        mk.wait(eng, vV, const_done)

    def spill_xt():
        n = vV.n
        mk.wait("sync", vV, n)
        xtv = bufs["arena"][:].rearrange("p (dc t) -> p dc t", dc=NDC, t=NTOK)
        mk.emit("sync", lambda e: e.dma_start(
            xt_dram.rearrange("(dc p) t -> p dc t", p=128), xtv).then_inc(dS.h, 16))
        dS.inc(16)
        mk.barrier()

    def ln_pass():
        lsc = contextlib.ExitStack()
        bufs["zsq"] = [lsc.enter_context(nc.sbuf_tensor(f"zsq{mk.uid()}", [128, NTOK], BF16))
                       for _ in range(2)]
        bufs["rows"] = lsc.enter_context(nc.sbuf_tensor(f"rows{mk.uid()}", [1, 4096], F32))
        bufs["rows_bf"] = lsc.enter_context(nc.sbuf_tensor(f"rowsb{mk.uid()}", [1, 2048], BF16))
        emit_lnT(mk, bufs, sems, bufs["arena"])
        mk.barrier()
        lsc.close()

    s1 = contextlib.ExitStack()
    bufs["ubf"] = s1.enter_context(nc.sbuf_tensor("ubf", [128, NTOK * NDC], BF16))
    bufs["tsb"] = [s1.enter_context(nc.sbuf_tensor(f"tsb{i}", [128, 896], BF16)) for i in range(2)]
    dU = mk.sem("dU")
    mk.emit("gpsimd", lambda e: e.dma_start(
        bufs["ubf"][:].rearrange("p (b i d) -> p b i d", b=BC, i=16, d=D),
        u.rearrange("b (i p) d -> p b i d", p=128)).then_inc(dU.h, 16))
    u_done = dU.inc(16)
    mk.wait("tensor", dU, u_done)
    mk.wait("vector", dU, u_done)
    emit_conv_block(mk, bufs, sems, tstack)
    emit_transpose_nat_to_T(mk, bufs, sems, bufs["ubf"], bufs["arena"])
    mk.barrier()
    s1.close()
    ln_pass()
    spill_xt()

    s2 = contextlib.ExitStack()
    bufs["W1sb"] = s2.enter_context(nc.sbuf_tensor("W1sb", [128, 32768], BF16))
    bufs["W2sb"] = s2.enter_context(nc.sbuf_tensor("W2sb", [128, 32768], BF16))
    acc_done = emit_ffn0(mk, bufs, sems, wap0, xt_dram, acc_dram)
    mk.barrier()
    s2.close()
    emit_accT_add(mk, bufs, sems, xt_dram, acc_dram, acc_done)
    mk.barrier()
    ln_pass()
    spill_xt()

    emit_attn(mk, bufs, sems, wapA, xt_dram)
    mk.barrier()
    ln_pass()
    spill_xt()

    acc_done = emit_ffn1(mk, bufs, sems, wap1, xt_dram, acc_dram)
    mk.barrier()
    emit_accT_add(mk, bufs, sems, xt_dram, acc_dram, acc_done)
    mk.barrier()
    ln_pass()
    emit_final_out(mk, bufs, sems, out_dram)
    mk.realize()
    return nc


def _shard_inputs_fused(inputs):
    tstack = _conv_tstack(inputs["L0_attn_diff_kernel"], inputs["L0_attn_ma_r_kernel"])
    eye = np.eye(128, dtype=np.float32).astype(bfloat16)
    eyef = np.eye(128, dtype=np.float32)
    ut = np.triu(np.ones((128, 128), dtype=np.float32)).astype(bfloat16)
    f32c = lambda k: np.ascontiguousarray(np.asarray(inputs[k], np.float32))
    com = {
        "TSTACK": tstack, "EYE": eye, "EYEF": eyef, "UT": ut,
        "WRED": f32c("L0_ffn_w_red"), "WG": f32c("L0_ffn_wg"),
        "W1": f32c("L0_ffn_w1"), "B1": f32c("L0_ffn_b1"),
        "W2": f32c("L0_ffn_w2"), "B2": f32c("L0_ffn_b2"),
        "WQKV": f32c("L1_attn_w_qkv"), "WOUT": f32c("L1_attn_w_out"),
        "RW": f32c("L1_ffn_router_w"), "RB": f32c("L1_ffn_router_b"),
        "RW1": f32c("L1_ffn_routed_w1"), "RW3": f32c("L1_ffn_routed_w3"),
        "RW2": f32c("L1_ffn_routed_w2"), "SW1": f32c("L1_ffn_shared_w1"),
        "SW3": f32c("L1_ffn_shared_w3"), "SW2": f32c("L1_ffn_shared_w2"),
    }
    un = np.ascontiguousarray(np.asarray(inputs["L0_attn_u"], np.float32))
    return [dict(com, U=np.ascontiguousarray(un[2 * c:2 * c + 2])) for c in range(N_CORES)]


def kernel_fused(**inputs):
    inputs = {k: np.asarray(v) for k, v in inputs.items()}
    nc = _get_prog("fused", build_fused)
    im = _shard_inputs_fused(inputs)
    r = run_bass_kernel_spmd(nc, im, core_ids=list(range(N_CORES)))
    return np.concatenate([np.asarray(r.results[c]["OUT"]) for c in range(N_CORES)], axis=0)


# version 04
# speedup vs baseline: 1.0781x; 1.0352x over previous
import contextlib
import numpy as np
import ml_dtypes

import concourse.bass as bass
import concourse.mybir as mybir
import concourse.bass_utils as _bu
from concourse.bass_utils import run_bass_kernel_spmd

if not getattr(_bu, "_nobirsim_patch", False):
    _orig_run_command = _bu.run_command

    def _run_command_nobirsim(cmd, cwd=None, **kw):
        cmd = ["--enable-birsim=false" if c == "--enable-birsim=true" else c for c in cmd]
        cmd = ["--enable-ldw-opt=true" if c == "--enable-ldw-opt=false" else c for c in cmd]
        return _orig_run_command(cmd, cwd=cwd, **kw)

    _bu.run_command = _run_command_nobirsim
    _bu._nobirsim_patch = True

F32 = mybir.dt.float32
BF16 = mybir.dt.bfloat16
AF = mybir.ActivationFunctionType
ALU = mybir.AluOpType
AXL = mybir.AxisListType
bfloat16 = ml_dtypes.bfloat16

N_CORES = 8
B = 16
BC = 2
T = 2048
D = 1024
NTOK = BC * T
NDC = D // 128
NTT = NTOK // 128


def _conv_tstack(diff_kernel, ma_r_kernel):
    K128 = np.zeros((128, 720), dtype=np.float32)
    dk = np.asarray(diff_kernel, dtype=np.float32)
    mk = np.asarray(ma_r_kernel, dtype=np.float32)
    for i in range(16):
        for j in range(8):
            r = 8 * i + j
            if j < 4:
                K128[r, :4] = dk[4 * i + j]
            else:
                K128[r, :720] = mk[4 * i + (j - 4)]
    pad = np.zeros((128, 1152), dtype=np.float32)
    pad[:, 128:848] = K128
    base = pad[:, 128:]
    st_g, st_e = base.strides
    ts = np.lib.stride_tricks.as_strided(
        base, shape=(128, 7, 128, 128), strides=(st_g, 128 * st_e, -st_e, st_e))
    return np.ascontiguousarray(ts)



class Ctr:

    def __init__(self, handle):
        self.h = handle
        self.n = 0

    def inc(self, k=1):
        self.n += k
        return self.n


class MK:

    ENGINES = ("tensor", "vector", "scalar", "gpsimd", "sync")

    def __init__(self, nc):
        self.nc = nc
        self.es = contextlib.ExitStack()
        self.ops = {e: [] for e in self.ENGINES}
        self.sems = {}
        self._uid = 0

    def uid(self):
        self._uid += 1
        return self._uid

    def sem(self, name):
        if name in self.sems:
            return self.sems[name]
        c = Ctr(self.es.enter_context(self.nc.semaphore(name)))
        self.sems[name] = c
        return c

    def sb(self, name, shape, dtype):
        return self.es.enter_context(self.nc.sbuf_tensor(name, shape, dtype))

    def ps(self, name, shape, dtype):
        return self.es.enter_context(self.nc.psum_tensor(name, shape, dtype))

    def emit(self, engine, fn):
        self.ops[engine].append(fn)

    def wait(self, engine, ctr, val):
        if val > 0:
            self.emit(engine, lambda eng, c=ctr, v=val: eng.wait_ge(c.h, v))

    def barrier(self):
        snap = [(c, c.n) for c in self.sems.values() if c.n > 0]
        for eng in self.ENGINES:
            for c, v in snap:
                self.wait(eng, c, v)

    def realize(self):
        with self.es, self.nc.Block() as block:
            for name in self.ENGINES:
                deco = getattr(block, name)
                ops = self.ops[name]

                def section(eng, ops=ops):
                    for fn in ops:
                        fn(eng)

                deco(section)
        return self.nc



def emit_conv_block(mk, bufs, sems, tstack_ap):
    nc = mk.nc
    ubf, tsb, pgs = bufs["ubf"], bufs["tsb"], bufs["pg"]
    dW, tT, vV = sems["dW"], sems["tT"], sems["vV"]
    dT = [mk.sem("dT0"), mk.sem("dT1")]
    u4 = ubf[:].rearrange("p (b i d) -> p b i d", b=BC, i=16, d=D)
    dma_done = [0, 0]
    mm_done = [0, 0]
    ev_done = [0, 0]
    for g in range(128):
        bf = g % 2
        mk.wait("gpsimd", tT, mm_done[bf])
        mk.emit("gpsimd", lambda e, b=bf, g=g: e.dma_start(
            tsb[b][:].rearrange("s (d t) -> s d t", d=7, t=128),
            tstack_ap[g].rearrange("d s t -> s d t")).then_inc(dT[b].h, 16))
        dma_done[bf] = dT[bf].inc(16)
        mk.wait("tensor", dT[bf], dma_done[bf])
        mk.wait("tensor", vV, ev_done[bf])
        pview = pgs[bf][:, 0:256].rearrange("p (b i c) -> p b i c", b=BC, i=16, c=8)
        for d in range(7):
            for bb in range(BC):
                out_ap = pview[:, bb, d:16, :]
                rhs_ap = u4[:, bb, 0:16 - d, 8 * g:8 * g + 8]
                last = d == 6 and bb == BC - 1
                st = d == 0 and bb == 0
                mk.emit("tensor", lambda e, b=bf, d=d, o=out_ap, r=rhs_ap, st=st, l=last: (e.matmul(
                    o, tsb[b][:, 128 * d:128 * d + 128], r,
                    start=st, stop=l, skip_group_check=True).then_inc(tT.h, 1) if l else e.matmul(
                    o, tsb[b][:, 128 * d:128 * d + 128], r, start=st, stop=False,
                    skip_group_check=True)))
        mm_done[bf] = tT.inc(1)
        cols = u4[:, :, :, 8 * g:8 * g + 8]
        mk.wait("vector", tT, mm_done[bf])
        mk.emit("vector", lambda e, b=bf, c=cols: e.tensor_tensor(
            c, pgs[b][:, 0:256].rearrange("p (b i c) -> p b i c", b=BC, i=16, c=8),
            c, ALU.add).then_inc(vV.h, 1))
        ev_done[bf] = vV.inc(1)


def emit_transpose_nat_to_T(mk, bufs, sems, src, dst, src_dtype_bf=True):
    eye = bufs["eye"]
    pt = bufs["pt"]
    tT, vV = sems["tT"], sems["vV"]
    mk.wait("tensor", vV, vV.n)
    ev_done = [0, 0]
    for tau in range(NTT):
        for half in range(2):
            bf = (tau * 2 + half) % 2
            mk.wait("tensor", vV, ev_done[bf])
            for q in range(4):
                dc = half * 4 + q
                src_ap = src[:, tau * 1024 + dc * 128: tau * 1024 + dc * 128 + 128]
                last = q == 3
                mk.emit("tensor", lambda e, b=bf, q=q, s=src_ap, l=last:
                        e.transpose(pt[b][:, 128 * q:128 * q + 128], s, eye[:]).then_inc(tT.h, 1)
                        if l else
                        e.transpose(pt[b][:, 128 * q:128 * q + 128], s, eye[:]))
            done = tT.inc(1)
            dview = dst[:].rearrange("p (dc t) -> p dc t", dc=NDC, t=NTOK)
            dst_ap = dview[:, half * 4:half * 4 + 4, tau * 128:tau * 128 + 128]
            src_p = pt[bf][:].rearrange("p (q t) -> p q t", q=4, t=128)
            mk.wait("vector", tT, done)
            mk.emit("vector", lambda e, d=dst_ap, s=src_p: e.tensor_copy(d, s).then_inc(vV.h, 1))
            ev_done[bf] = vV.inc(1)


def emit_lnT(mk, bufs, sems, xt, n_d=D, eps=1e-5):
    tT, vV, sS = sems["tT"], sems["vV"], sems["sS"]
    onec, ones1 = bufs["onec"], bufs["ones1"]
    zsq = bufs["zsq"]
    rows = bufs["rows"]
    rows_bf = bufs["rows_bf"]
    pstx, psts, pbc = bufs["pstx"], bufs["psts"], bufs["pbc"]
    xv = xt[:].rearrange("p (dc t) -> p dc t", dc=NDC, t=NTOK)

    mk.wait("scalar", vV, vV.n)
    mk.wait("tensor", vV, vV.n)

    mm_done = [0, 0]
    rows_done = [0, 0]
    app_done = [0, 0]
    for sl in range(8):
        zb = sl % 2
        tok = slice(512 * sl, 512 * (sl + 1))
        ro = 2048 * zb
        rbo = 1024 * zb
        sumx = rows[0:1, ro:ro + 512]
        var = rows[0:1, ro + 512:ro + 1024]
        t2 = rows[0:1, ro + 1024:ro + 1536]
        rr = rows[0:1, ro + 1536:ro + 2048]
        m_bf = rows_bf[0:1, rbo:rbo + 512]
        r_bf = rows_bf[0:1, rbo + 512:rbo + 1024]
        mk.wait("scalar", tT, mm_done[zb])
        for dc in range(NDC):
            last = dc == NDC - 1
            mk.emit("scalar", lambda e, dc=dc, tok=tok, z=zb, l=last: (e.activation(
                zsq[z][:, 512 * dc:512 * dc + 512], xv[:, dc, tok], AF.Square
            ).then_inc(sS.h, 1) if l else e.activation(
                zsq[z][:, 512 * dc:512 * dc + 512], xv[:, dc, tok], AF.Square)))
        sq_ready = sS.inc(1)
        mk.wait("tensor", sS, sq_ready)
        mk.wait("tensor", vV, rows_done[zb])
        for dc in range(NDC):
            mk.emit("tensor", lambda e, dc=dc, tok=tok, z=zb: e.matmul(
                pstx[z][0:1, :], onec[:], xv[:, dc, tok],
                start=(dc == 0), stop=(dc == NDC - 1)))
            last = dc == NDC - 1
            mk.emit("tensor", lambda e, dc=dc, z=zb, l=last: (e.matmul(
                psts[z][0:1, :], onec[:], zsq[z][:, 512 * dc:512 * dc + 512],
                start=(dc == 0), stop=(dc == NDC - 1)
            ).then_inc(tT.h, 1) if l else e.matmul(
                psts[z][0:1, :], onec[:], zsq[z][:, 512 * dc:512 * dc + 512],
                start=(dc == 0), stop=(dc == NDC - 1))))
        mm_done[zb] = tT.inc(1)
        mk.wait("vector", tT, mm_done[zb])
        steps = [
            ("vector", lambda e, a=sumx, b=pstx[zb]: e.tensor_scalar(a, b[0:1, :], 1.0 / n_d, None, ALU.mult)),
            ("vector", lambda e, a=var, b=psts[zb]: e.tensor_scalar(a, b[0:1, :], 1.0 / n_d, eps, ALU.mult, ALU.add)),
            ("vector", lambda e, a=t2, b=sumx: e.tensor_tensor(a, b, b, ALU.mult)),
            ("vector", lambda e, a=var, b=t2: e.tensor_tensor(a, a, b, ALU.subtract)),
            ("vector", lambda e, a=t2, b=var: e.reciprocal(a, b)),
            ("scalar", lambda e, a=rr, b=t2: e.activation(a, b, AF.Sqrt)),
            ("vector", lambda e, a=m_bf, b=sumx: e.tensor_copy(a, b)),
            ("vector", lambda e, a=r_bf, b=rr: e.tensor_copy(a, b)),
        ]
        prev = None
        for eng, fn in steps:
            if prev is not None:
                pc, pv = prev
                mk.wait(eng, pc, pv)
            ctr = sS if eng == "scalar" else vV
            mk.emit(eng, lambda e, fn=fn, c=ctr: fn(e).then_inc(c.h, 1))
            prev = (ctr, ctr.inc(1))
        rows_done[zb] = vV.n
        mk.wait("tensor", prev[0], prev[1])
        mk.wait("tensor", vV, max(app_done))
        mk.emit("tensor", lambda e, m=m_bf: e.matmul(
            pbc[0][:], ones1[:], m, start=True, stop=True))
        mk.emit("tensor", lambda e, r=r_bf: e.matmul(
            pbc[1][:], ones1[:], r, start=True, stop=True).then_inc(tT.h, 1))
        bc_done = tT.inc(1)
        mk.wait("vector", tT, bc_done)
        for dc in range(NDC):
            last = dc == NDC - 1
            mk.emit("vector", lambda e, dc=dc, tok=tok, l=last: (e.tensor_tensor(
                xv[:, dc, tok], xv[:, dc, tok], pbc[0][:], ALU.subtract
            ).then_inc(vV.h, 1) if l else e.tensor_tensor(
                xv[:, dc, tok], xv[:, dc, tok], pbc[0][:], ALU.subtract)))
        mk.wait("vector", vV, vV.inc(1))
        for dc in range(NDC):
            last = dc == NDC - 1
            mk.emit("vector", lambda e, dc=dc, tok=tok, l=last: (e.tensor_tensor(
                xv[:, dc, tok], xv[:, dc, tok], pbc[1][:], ALU.mult
            ).then_inc(vV.h, 1) if l else e.tensor_tensor(
                xv[:, dc, tok], xv[:, dc, tok], pbc[1][:], ALU.mult)))
        app_done[zb] = vV.inc(1)
    bufs["ln_exit"] = max(app_done)


def emit_ffn0(mk, bufs, sems, wap, xt_dram, acc_dram, n_experts=8, do_experts=True):
    nc = mk.nc
    tT, vV, sS, dW, dA = (sems[n] for n in ("tT", "vV", "sS", "dW", "dA"))
    dX1, dX2 = mk.sem("dX1"), mk.sem("dX2")
    dxc = [mk.sem("dxc0"), mk.sem("dxc1")]
    dxw = [mk.sem("dxw0"), mk.sem("dxw1")]
    W1sb, W2sb = bufs["W1sb"], bufs["W2sb"]
    ar = bufs["arena"]
    ones1, eyef, onesA = bufs["ones1"], bufs["eyef"], bufs["onesA"]
    ps = bufs["ps_f32"]
    h_sb = ar[:, 0:16384]
    xwin = [ar[:, 16384:20480], ar[:, 20480:24576]]
    tmp = ar[:, 24576:26624].bitcast(F32)
    msel = ar[:, 26624:27136].bitcast(F32)
    b1T = ar[:, 27136:27648].bitcast(F32)
    b2q = ar[:, 27648:30720]
    xc = [ar[:, 0:1024].bitcast(F32), ar[:, 1024:2048].bitcast(F32)]
    lgs = ar[:, 2048:2560].bitcast(F32)
    e8 = ar[:, 2560:3072].bitcast(F32)
    mx = ar[:, 3072:3136].bitcast(F32)
    ssum = ar[:, 3136:3200].bitcast(F32)
    b1s = ar[0:8, 4096:12288].bitcast(F32)
    red_sb = W2sb[:, 0:8192].bitcast(F32)
    ph = [ps[:, 0:512], ps[:, 512:1024]]
    py = [ps[:, 1024:2048], ps[:, 2048:3072]]
    p_red = ps[0:16, 0:512]
    pl = ps[:, 512:768]
    pb1 = ps[:, 1024:1032]
    xdv = xt_dram.rearrange("(dc p) t -> p dc t", p=128)

    rsc = contextlib.ExitStack()
    wred = rsc.enter_context(nc.sbuf_tensor("wred", [128, 128], F32))
    wg_sb = rsc.enter_context(nc.sbuf_tensor("wg_sb", [8, 16], F32))
    wgt = rsc.enter_context(nc.sbuf_tensor("wgt", [8, 20], F32))
    wgnT = rsc.enter_context(nc.sbuf_tensor("wgnT", [16, 8], F32))

    mk.emit("gpsimd", lambda e: e.dma_start(
        wred[:].rearrange("p (dc j) -> p dc j", dc=NDC, j=16),
        wap["w_red"].rearrange("(dc p) j -> p dc j", p=128)).then_inc(dW.h, 16))
    mk.emit("gpsimd", lambda e: e.dma_start(wg_sb[:], wap["wg"]).then_inc(dW.h, 16))
    for ei in range(8):
        mk.emit("gpsimd", lambda e, ei=ei: e.dma_start(
            b2q[32 * (ei // 3):32 * (ei // 3) + 1, 1024 * (ei % 3):1024 * (ei % 3) + 1024],
            wap["b2"][ei:ei + 1, :]).then_inc(dW.h, 16))
    mk.emit("gpsimd", lambda e: e.dma_start(b1s[:], wap["b1"]).then_inc(dW.h, 16))
    stg_done = dW.inc(176)
    mk.emit("gpsimd", lambda e: e.dma_start(
        W1sb[:].rearrange("p (dc f) -> p dc f", dc=NDC, f=4096),
        wap["w1"][0].rearrange("(dc p) f -> p dc f", p=128)).then_inc(dX1.h, 16))
    w1_done = dX1.inc(16)

    mk.wait("scalar", dW, stg_done)
    chain = [
        ("scalar", lambda e: e.activation(wgt[:, 0:16], wg_sb[:], AF.Square)),
        ("vector", lambda e: e.reduce_sum(wgt[:, 16:17], wgt[:, 0:16], axis=AXL.X)),
        ("scalar", lambda e: e.activation(wgt[:, 17:18], wgt[:, 16:17], AF.Sqrt)),
        ("vector", lambda e: e.tensor_scalar_max(wgt[:, 18:19], wgt[:, 17:18], 1e-4)),
        ("vector", lambda e: e.reciprocal(wgt[:, 19:20], wgt[:, 18:19])),
        ("vector", lambda e: e.tensor_scalar(wg_sb[:], wg_sb[:], wgt[:, 19:20], None, ALU.mult)),
    ]
    prev = None
    for eng, fn in chain:
        if prev is not None:
            mk.wait(eng, prev[0], prev[1])
        c = sS if eng == "scalar" else vV
        mk.emit(eng, lambda e, fn=fn, c=c: fn(e).then_inc(c.h, 1))
        prev = (c, c.inc(1))
    mk.wait("tensor", prev[0], prev[1])
    mk.emit("tensor", lambda e: e.transpose(
        ps[0:16, 2048:2056], wg_sb[:], eyef[0:8, 0:8]).then_inc(tT.h, 1))
    wgt_done = tT.inc(1)
    mk.wait("vector", tT, wgt_done)
    mk.emit("vector", lambda e: e.tensor_copy(wgnT[:], ps[0:16, 2048:2056]).then_inc(vV.h, 1))
    wgnT_done = vV.inc(1)

    mk.wait("tensor", dW, stg_done)
    cp_done = {}
    for fc in range(32):
        if fc >= 1:
            mk.wait("tensor", vV, cp_done[fc - 1])
        mk.emit("tensor", lambda e, fc=fc: e.transpose(
            pb1, b1s[0:8, 128 * fc:128 * fc + 128], eyef[0:8, 0:8]).then_inc(tT.h, 1))
        td = tT.inc(1)
        mk.wait("vector", tT, td)
        mk.emit("vector", lambda e, fc=fc: e.tensor_copy(
            b1T[:, 8 * fc:8 * fc + 8], pb1).then_inc(vV.h, 1))
        cp_done[fc] = vV.inc(1)
    b1T_done = cp_done[31]

    wv = wred[:].rearrange("p (dc j) -> p dc j", dc=NDC, j=16)
    red_ev = {}
    mm_done = {}
    xc_done = {}
    for sl in range(8):
        for dc in range(NDC):
            z = dc % 2
            k = sl * NDC + dc
            mk.wait("gpsimd", tT, mm_done.get(k - 2, 0))
            mk.emit("gpsimd", lambda e, dc=dc, sl=sl, z=z: e.dma_start(
                xc[z][:], xdv[:, dc, 512 * sl:512 * sl + 512]).then_inc(dxc[z].h, 16))
            xc_done[k] = dxc[z].inc(16)
            mk.wait("tensor", dxc[z], xc_done[k])
            if dc == 0 and sl > 0:
                mk.wait("tensor", vV, red_ev[sl - 1])
            mk.emit("tensor", lambda e, dc=dc, z=z: e.matmul(
                p_red, wv[:, dc, :], xc[z][:],
                start=(dc == 0), stop=(dc == NDC - 1)).then_inc(tT.h, 1))
            mm_done[k] = tT.inc(1)
        mk.wait("vector", tT, mm_done[sl * NDC + NDC - 1])
        mk.emit("vector", lambda e, sl=sl: e.tensor_copy(
            red_sb[0:16, 512 * sl:512 * sl + 512], p_red).then_inc(vV.h, 1))
        red_ev[sl] = vV.inc(1)

    mk.wait("tensor", vV, red_ev[7])
    mk.wait("tensor", vV, wgnT_done)
    for tg in range(NTT):
        last = tg == NTT - 1
        mk.emit("tensor", lambda e, tg=tg, l=last: (e.matmul(
            pl[:, 8 * tg:8 * tg + 8], red_sb[0:16, 128 * tg:128 * tg + 128], wgnT[:],
            start=True, stop=True).then_inc(tT.h, 1) if l else e.matmul(
            pl[:, 8 * tg:8 * tg + 8], red_sb[0:16, 128 * tg:128 * tg + 128], wgnT[:],
            start=True, stop=True)))
    lg_done = tT.inc(1)
    plv = pl.rearrange("p (g e) -> p g e", g=NTT, e=8)
    mxv = mx.rearrange("p (g o) -> p g o", g=NTT, o=1)
    ep = [
        ("vector", lambda e: e.reduce_max(mxv, plv, axis=AXL.X)),
        ("vector", lambda e: e.tensor_tensor(
            lgs.rearrange("p (g e) -> p g e", g=NTT, e=8), plv,
            mxv.broadcast_to((128, NTT, 8)), ALU.subtract)),
        ("scalar", lambda e: e.activation(e8, lgs, AF.Exp)),
        ("vector", lambda e: e.reduce_sum(
            ssum.rearrange("p (g o) -> p g o", g=NTT, o=1),
            e8.rearrange("p (g e) -> p g e", g=NTT, e=8), axis=AXL.X)),
        ("vector", lambda e: e.reciprocal(mx, ssum)),
        ("vector", lambda e: e.tensor_single_scalar(lgs, e8, 1.0, ALU.is_ge)),
        ("vector", lambda e: e.tensor_tensor(
            msel.rearrange("p (g e) -> p g e", g=NTT, e=8),
            lgs.rearrange("p (g e) -> p g e", g=NTT, e=8),
            mxv.broadcast_to((128, NTT, 8)), ALU.mult)),
    ]
    prev = (tT, lg_done)
    for eng, fn in ep:
        mk.wait(eng, prev[0], prev[1])
        c = sS if eng == "scalar" else vV
        mk.emit(eng, lambda e, fn=fn, c=c: fn(e).then_inc(c.h, 1))
        prev = (c, c.inc(1))
    msel_done = prev[1]
    router_done_v = vV.n

    if not do_experts:
        rsc.close()
        return 0
    w2_done = 0
    phz_last = {0: 0, 1: 0}
    ev_cnt = {}
    dma_cnt = {}
    acc_last_of_e = 0
    phB_prev_win = 0
    phA_all_done = 0
    h_free_at = 0
    xw_done = {}
    xw_read_z = {0: 0, 1: 0}
    for e_ix in range(n_experts):
        mk.wait("gpsimd", tT, phB_prev_win if e_ix > 0 else lg_done)
        mk.emit("gpsimd", lambda e, ei=e_ix: e.dma_start(
            W2sb[:].rearrange("p (fc d) -> p fc d", fc=32, d=1024),
            wap["w2"][ei].rearrange("(fc p) d -> p fc d", p=128)).then_inc(dX2.h, 16))
        w2_done = dX2.inc(16)
        w1_this = w1_done
        mk.wait("tensor", dX1, w1_this)
        if e_ix == 0:
            mk.wait("tensor", vV, router_done_v)
            mk.wait("tensor", vV, b1T_done)
        for w in range(8):
            wz = w % 2
            mk.wait("gpsimd", tT, xw_read_z[wz])
            mk.emit("gpsimd", lambda e, w=w, wz=wz: e.dma_start(
                xwin[wz][:].rearrange("p (dc t) -> p dc t", dc=NDC, t=512),
                xdv[:, :, 512 * w:512 * w + 512]).then_inc(dxw[wz].h, 16))
            xw_done[(e_ix, w)] = dxw[wz].inc(16)
            xv = xwin[wz][:].rearrange("p (dc t) -> p dc t", dc=NDC, t=512)
            mk.wait("tensor", dxw[wz], xw_done[(e_ix, w)])
            for fc in range(32):
                z = fc % 2
                mk.wait("tensor", sS, phz_last[z])
                for dc in range(NDC):
                    last = dc == NDC - 1
                    mk.emit("tensor", lambda e, dc=dc, fc=fc, z=z, xv=xv, l=last: (e.matmul(
                        ph[z], W1sb[:, 4096 * dc + 128 * fc:4096 * dc + 128 * fc + 128],
                        xv[:, dc, :], start=(dc == 0), stop=(dc == NDC - 1)
                    ).then_inc(tT.h, 1) if l else e.matmul(
                        ph[z], W1sb[:, 4096 * dc + 128 * fc:4096 * dc + 128 * fc + 128],
                        xv[:, dc, :], start=(dc == 0), stop=(dc == NDC - 1))))
                mmA = tT.inc(1)
                mk.wait("scalar", tT, mmA)
                if fc == 0:
                    mk.wait("scalar", tT, h_free_at)
                mk.emit("scalar", lambda e, fc=fc, z=z, ei=e_ix: e.activation(
                    h_sb[:, 512 * fc:512 * fc + 512], ph[z], AF.Gelu,
                    bias=b1T[:, 8 * fc + ei:8 * fc + ei + 1]).then_inc(sS.h, 1))
                phz_last[z] = sS.inc(1)
            phA_all_done = tT.n
            xw_read_z[wz] = tT.n
            mk.wait("tensor", sS, max(phz_last.values()))
            mk.wait("tensor", dX2, w2_done)
            for th in range(4):
                tg = 4 * w + th
                pz = tg % 2
                mk.wait("tensor", vV, ev_cnt.get(tg - 2, 0))
                if e_ix == 0 and tg <= 1:
                    mk.wait("tensor", vV, router_done_v)
                for fc in range(32):
                    for hh in range(2):
                        mk.emit("tensor", lambda e, fc=fc, th=th, hh=hh, pz=pz: e.matmul(
                            py[pz][:, 512 * hh:512 * hh + 512],
                            h_sb[:, 512 * fc + 128 * th:512 * fc + 128 * th + 128],
                            W2sb[:, 1024 * fc + 512 * hh:1024 * fc + 512 * hh + 512],
                            start=(fc == 0), stop=False))
                b2r = b2q[32 * (e_ix // 3):32 * (e_ix // 3) + 1,
                          1024 * (e_ix % 3):1024 * (e_ix % 3) + 1024]
                brow = 32 * (e_ix // 3)
                for hh in range(2):
                    last = hh == 1
                    mk.emit("tensor", lambda e, hh=hh, pz=pz, r=b2r, br=brow, l=last: (e.matmul(
                        py[pz][:, 512 * hh:512 * hh + 512], onesA[br:br + 1, :],
                        r[:, 512 * hh:512 * hh + 512],
                        start=False, stop=True).then_inc(tT.h, 1) if l else e.matmul(
                        py[pz][:, 512 * hh:512 * hh + 512], onesA[br:br + 1, :],
                        r[:, 512 * hh:512 * hh + 512], start=False, stop=True)))
                mmB = tT.inc(1)
                mk.wait("vector", tT, mmB)
                mk.wait("vector", dA, dA.n)
                if e_ix == 0 and tg == 0:
                    mk.wait("vector", vV, msel_done)
                mk.emit("vector", lambda e, tg=tg, ei=e_ix, pz=pz: e.tensor_scalar(
                    tmp, py[pz][:], msel[:, 8 * tg + ei:8 * tg + ei + 1],
                    None, ALU.mult).then_inc(vV.h, 1))
                ev_cnt[tg] = vV.inc(1)
                mk.wait("gpsimd", vV, ev_cnt[tg])
                if tg == 0:
                    mk.wait("gpsimd", dA, acc_last_of_e)
                op = ALU.bypass if e_ix == 0 else ALU.add
                mk.emit("gpsimd", lambda e, tg=tg, op=op: e.dma_start(
                    acc_dram[128 * tg:128 * tg + 128, :], tmp,
                    accum_op=op).then_inc(dA.h, 16))
                dma_cnt[tg] = dA.inc(16)
            phB_prev_win = tT.n
            h_free_at = tT.n
        acc_last_of_e = dma_cnt[31]
        if e_ix < n_experts - 1:
            mk.wait("gpsimd", tT, phA_all_done)
            mk.emit("gpsimd", lambda e, ei=e_ix + 1: e.dma_start(
                W1sb[:].rearrange("p (dc f) -> p dc f", dc=NDC, f=4096),
                wap["w1"][ei].rearrange("(dc p) f -> p dc f", p=128)).then_inc(dX1.h, 16))
            w1_done = dX1.inc(16)
    rsc.close()
    return dma_cnt[31]


def emit_accT_add(mk, bufs, sems, xt_dram, acc_dram, acc_dma_done):
    nc = mk.nc
    tT, vV, dS = (sems[n] for n in ("tT", "vV", "dS"))
    xt, eye, pt = bufs["arena"], bufs["eye"], bufs["pt"]
    dL = [mk.sem("dL0"), mk.sem("dL1")]
    xv = xt[:].rearrange("p (dc t) -> p dc t", dc=NDC, t=NTOK)
    xdv = xt_dram.rearrange("(dc p) t -> p dc t", p=128)
    asc = contextlib.ExitStack()
    accl = [asc.enter_context(nc.sbuf_tensor(f"accl{mk.uid()}_{i}", [128, 1024], F32)) for i in range(2)]
    accb = [asc.enter_context(nc.sbuf_tensor(f"accb{mk.uid()}_{i}", [128, 1024], BF16)) for i in range(2)]
    x1w = [asc.enter_context(nc.sbuf_tensor(f"x1w{mk.uid()}_{i}", [128, 1024], BF16)) for i in range(2)]
    mk.wait("gpsimd", sems["dA"], acc_dma_done)
    ld = {}
    cb = {}
    add_done = {}
    tr_done = {}
    for tg in range(NTT):
        z = tg % 2
        mk.wait("gpsimd", vV, cb.get(tg - 2, 0))
        mk.emit("gpsimd", lambda e, tg=tg, z=z: e.dma_start(
            accl[z][:], acc_dram[128 * tg:128 * tg + 128, :]).then_inc(dL[z].h, 16))
        mk.wait("gpsimd", vV, add_done.get(tg - 2, 0))
        mk.emit("gpsimd", lambda e, tg=tg, z=z: e.dma_start(
            x1w[z][:].rearrange("p (dc t) -> p dc t", dc=NDC, t=128),
            xdv[:, :, 128 * tg:128 * tg + 128]).then_inc(dL[z].h, 16))
        ld[tg] = dL[z].inc(32)
        mk.wait("vector", dL[z], ld[tg])
        mk.wait("vector", tT, tr_done.get(tg - 2, 0))
        mk.emit("vector", lambda e, z=z: e.tensor_copy(accb[z][:], accl[z][:]).then_inc(vV.h, 1))
        cb[tg] = vV.inc(1)
        for half in range(2):
            pz = half
            mk.wait("tensor", vV, cb[tg])
            mk.wait("tensor", vV, add_done.get(tg - 1, 0))
            for q in range(4):
                dc = 4 * half + q
                last = q == 3
                mk.emit("tensor", lambda e, z=z, dc=dc, q=q, pz=pz, l=last: (e.transpose(
                    pt[pz][:, 128 * q:128 * q + 128], accb[z][:, 128 * dc:128 * dc + 128],
                    eye[:]).then_inc(tT.h, 1) if l else e.transpose(
                    pt[pz][:, 128 * q:128 * q + 128], accb[z][:, 128 * dc:128 * dc + 128],
                    eye[:])))
            t_done = tT.inc(1)
            tr_done[tg] = t_done
            mk.wait("vector", tT, t_done)
            x1v = x1w[z][:].rearrange("p (dc t) -> p dc t", dc=NDC, t=128)
            for q in range(4):
                dc = 4 * half + q
                last = q == 3
                mk.emit("vector", lambda e, tg=tg, dc=dc, q=q, pz=pz, z=z, x1v=x1v, l=last: (
                    e.tensor_tensor(
                        xv[:, dc, 128 * tg:128 * tg + 128], x1v[:, dc, :],
                        pt[pz][:, 128 * q:128 * q + 128], ALU.add).then_inc(vV.h, 1)
                    if l else e.tensor_tensor(
                        xv[:, dc, 128 * tg:128 * tg + 128], x1v[:, dc, :],
                        pt[pz][:, 128 * q:128 * q + 128], ALU.add)))
            add_done[tg] = vV.inc(1)
    asc.close()



def build_stage2(n_experts=8, do_experts=True, do_assembly=True, do_ln=True):
    nc = bass.Bass("TRN2", target_bir_lowering=False)
    xt_in = nc.dram_tensor("XT", [D, NTOK], BF16, kind="ExternalInput").ap()
    wap = {
        "w_red": nc.dram_tensor("WRED", [D, 16], F32, kind="ExternalInput").ap(),
        "wg": nc.dram_tensor("WG", [8, 16], F32, kind="ExternalInput").ap(),
        "w1": nc.dram_tensor("W1", [8, D, 4096], F32, kind="ExternalInput").ap(),
        "b1": nc.dram_tensor("B1", [8, 4096], F32, kind="ExternalInput").ap(),
        "w2": nc.dram_tensor("W2", [8, 4096, D], F32, kind="ExternalInput").ap(),
        "b2": nc.dram_tensor("B2", [8, D], F32, kind="ExternalInput").ap(),
    }
    eye_in = nc.dram_tensor("EYE", [128, 128], BF16, kind="ExternalInput").ap()
    eyef_in = nc.dram_tensor("EYEF", [128, 128], F32, kind="ExternalInput").ap()
    xt2_out = nc.dram_tensor("XT2", [D, NTOK], BF16, kind="ExternalOutput").ap()
    acc_dram = nc.dram_tensor("ACCD", [NTOK, D], F32).ap()

    mk = MK(nc)
    bufs = {}
    bufs["arena"] = mk.sb("arena", [128, NTOK * NDC], BF16)
    bufs["eye"] = mk.sb("eye", [128, 128], BF16)
    bufs["eyef"] = mk.sb("eyef", [128, 128], F32)
    bufs["onec"] = mk.sb("onec", [128, 1], BF16)
    bufs["ones1"] = mk.sb("ones1", [1, 128], BF16)
    bufs["onesA"] = mk.sb("onesA", [128, 128], BF16)
    wsc = contextlib.ExitStack()
    bufs["W1sb"] = wsc.enter_context(nc.sbuf_tensor("W1sb", [128, 32768], BF16))
    bufs["W2sb"] = wsc.enter_context(nc.sbuf_tensor("W2sb", [128, 32768], BF16))
    ps_f32 = mk.ps("ps_f32", [128, 3072], F32)
    ps_bf = mk.ps("ps_bf", [128, 2048], BF16)
    bufs["ps_f32"] = ps_f32
    bufs["pstx"] = [ps_f32[:, 0:512], ps_f32[:, 512:1024]]
    bufs["psts"] = [ps_f32[:, 1024:1536], ps_f32[:, 1536:2048]]
    bufs["pbc"] = [ps_f32[:, 2048:2560], ps_f32[:, 2560:3072]]
    bufs["pt"] = [ps_bf[:, 0:512], ps_bf[:, 1024:1536]]

    sems = {n: mk.sem(n) for n in ("dW", "dS", "dA", "tT", "vV", "sS")}
    dW, dS, dA, tT, vV, sS = (sems[n] for n in ("dW", "dS", "dA", "tT", "vV", "sS"))

    dC = mk.sem("dC")
    mk.emit("gpsimd", lambda e: e.dma_start(bufs["eye"][:], eye_in).then_inc(dC.h, 16))
    mk.emit("gpsimd", lambda e: e.dma_start(bufs["eyef"][:], eyef_in).then_inc(dC.h, 16))
    load_done = dC.inc(32)
    mk.emit("vector", lambda e: e.memset(bufs["onec"][:], 1.0))
    mk.emit("vector", lambda e: e.memset(bufs["ones1"][:], 1.0))
    mk.emit("vector", lambda e: e.memset(bufs["onesA"][:], 1.0).then_inc(vV.h, 1))
    const_done = vV.inc(1)
    for eng in ("tensor", "vector", "scalar"):
        mk.wait(eng, mk.sems["dC"], load_done)
        mk.wait(eng, vV, const_done)

    acc_done = emit_ffn0(mk, bufs, sems, wap, xt_in, acc_dram,
                         n_experts=n_experts, do_experts=do_experts)
    wsc.close()
    if do_assembly:
        emit_accT_add(mk, bufs, sems, xt_in, acc_dram, acc_done)
    if do_ln:
        bufs["zsq"] = [mk.sb(f"zsq{i}", [128, NTOK], BF16) for i in range(2)]
        bufs["rows"] = mk.sb("rows", [1, 4096], F32)
        bufs["rows_bf"] = mk.sb("rows_bf", [1, 2048], BF16)
        bufs["xt"] = bufs["arena"]
        emit_lnT(mk, bufs, sems, bufs["arena"])

    final_v = vV.n
    mk.wait("sync", vV, final_v)
    mk.wait("sync", sems["dA"], sems["dA"].n)
    mk.wait("sync", sems["sS"], sems["sS"].n)
    mk.wait("sync", tT, tT.n)
    xtv = bufs["arena"][:].rearrange("p (dc t) -> p dc t", dc=NDC, t=NTOK)
    for dc in range(NDC):
        mk.emit("sync", lambda e, dc=dc: e.dma_start(
            xt2_out[128 * dc:128 * dc + 128, :], xtv[:, dc, :]).then_inc(dS.h, 16))
    out_done = dS.inc(16 * NDC)
    mk.wait("sync", dS, out_done)
    mk.realize()
    return nc


def _shard_inputs_stage2(inputs, xt_by_core):
    eye = np.eye(128, dtype=np.float32).astype(bfloat16)
    eyef = np.eye(128, dtype=np.float32)
    com = {
        "WRED": np.ascontiguousarray(np.asarray(inputs["L0_ffn_w_red"], np.float32)),
        "WG": np.ascontiguousarray(np.asarray(inputs["L0_ffn_wg"], np.float32)),
        "W1": np.ascontiguousarray(np.asarray(inputs["L0_ffn_w1"], np.float32)),
        "B1": np.ascontiguousarray(np.asarray(inputs["L0_ffn_b1"], np.float32)),
        "W2": np.ascontiguousarray(np.asarray(inputs["L0_ffn_w2"], np.float32)),
        "B2": np.ascontiguousarray(np.asarray(inputs["L0_ffn_b2"], np.float32)),
        "EYE": eye, "EYEF": eyef,
    }
    return [dict(com, XT=np.ascontiguousarray(xt_by_core[c])) for c in range(len(xt_by_core))]


def emit_attn(mk, bufs, sems, wap, xt_dram, n_heads=16, do_proj=True, do_bphase=True, dbg=None):
    nc = mk.nc
    tT, vV, sS, dW, dS = (sems[n] for n in ("tT", "vV", "sS", "dW", "dS"))
    ar, eye, ut, onesA = bufs["arena"], bufs["eye"], bufs["ut"], bufs["onesA"]
    ones1 = bufs["ones1"]
    dwst, dxq = mk.sem("dwst"), mk.sem("dxq")
    ps, pt = bufs["ps_f32"], bufs["pt"]
    xdv = xt_dram.rearrange("(dc p) t -> p dc t", p=128)
    SA = [ps[:, 2048:2560], ps[:, 2560:3072]]
    PO = ps[:, 0:2048]
    PQ = [ps[:, 0:512], ps[:, 512:1024]]
    PV2 = ps[:, 1024:2048]

    asc = contextlib.ExitStack()
    qT = asc.enter_context(nc.sbuf_tensor("qT", [128, 16384], BF16))
    kT = asc.enter_context(nc.sbuf_tensor("kT", [128, 16384], BF16))
    vN = asc.enter_context(nc.sbuf_tensor("vN", [128, 16384], BF16))
    xqw = asc.enter_context(nc.sbuf_tensor("xqw", [128, 4096], BF16))
    wstg = asc.enter_context(nc.sbuf_tensor("wstg", [128, 1024], BF16))
    st8 = asc.enter_context(nc.sbuf_tensor("st8", [1, 64], F32))
    st8b = asc.enter_context(nc.sbuf_tensor("st8b", [1, 16], BF16))
    WpT = asc.enter_context(nc.sbuf_tensor("WpT", [128, 8192], BF16))
    vst = [asc.enter_context(nc.sbuf_tensor(f"vstA{i}", [128, 2048], BF16)) for i in range(2)]
    pstrip = [asc.enter_context(nc.sbuf_tensor(f"pstripA{i}", [128, 2048], BF16)) for i in range(2)]
    rbs = asc.enter_context(nc.sbuf_tensor("rbsA", [128, 1024], BF16))
    Bcols = asc.enter_context(nc.sbuf_tensor("BcolsA", [128, 16], F32))
    sqtmp = xqw

    def emit_wT_load(wsrc_rows, WpT_cols, dst_view):
        cp_z = {0: 0, 1: 0}
        tr_prev = tT.n
        for oc in range(8):
            mk.wait("gpsimd", tT, tr_prev)
            mk.emit("gpsimd", lambda e, oc=oc: e.dma_start(
                wstg[:], wsrc_rows(oc)).then_inc(dwst.h, 16))
            wd = dwst.inc(16)
            mk.wait("tensor", dwst, wd)
            for half in range(2):
                mk.wait("tensor", vV, cp_z[half])
                for q in range(4):
                    dc = 4 * half + q
                    last = q == 3
                    mk.emit("tensor", lambda e, dc=dc, q=q, z=half, l=last: (e.transpose(
                        pt[z][:, 128 * q:128 * q + 128], wstg[:, 128 * dc:128 * dc + 128],
                        eye[:]).then_inc(tT.h, 1) if l else e.transpose(
                        pt[z][:, 128 * q:128 * q + 128], wstg[:, 128 * dc:128 * dc + 128],
                        eye[:])))
                td = tT.inc(1)
                mk.wait("vector", tT, td)
                mk.emit("vector", lambda e, oc=oc, half=half: e.tensor_copy(
                    dst_view[:, 4 * half:4 * half + 4, 128 * oc:128 * oc + 128],
                    pt[half][:].rearrange("p (q t) -> p q t", q=4, t=128)
                ).then_inc(vV.h, 1))
                cp_z[half] = vV.inc(1)
            tr_prev = tT.n
        return max(cp_z.values())

    for b in range(BC):
        tb = 2048 * b
        for eng in ("tensor", "gpsimd", "scalar"):
            mk.wait(eng, vV, vV.n)
        for part in range(3):
            wv2 = WpT[:].rearrange("p (dc o) -> p dc o", dc=NDC, o=1024)
            wpT_ready = emit_wT_load(
                lambda oc, part=part: wap["w_qkv"][
                    128 * (8 * part + oc):128 * (8 * part + oc) + 128, :],
                None, wv2)
            mk.wait("tensor", vV, wpT_ready)
            xqv = xqw[:].rearrange("p (dc t) -> p dc t", dc=NDC, t=512)
            if part < 2:
                dst = qT if part == 0 else kT
                ev_z = {0: 0, 1: 0}
                mm_last = tT.n
                for ts in range(4):
                    mk.wait("gpsimd", tT, mm_last)
                    mk.emit("gpsimd", lambda e, ts=ts, tb=tb: e.dma_start(
                        xqv, xdv[:, :, tb + 512 * ts:tb + 512 * ts + 512]).then_inc(dxq.h, 16))
                    xd = dxq.inc(16)
                    mk.wait("tensor", dxq, xd)
                    for oc in range(8):
                        z = oc % 2
                        mk.wait("tensor", vV, ev_z[z])
                        for dc in range(NDC):
                            last = dc == NDC - 1
                            mk.emit("tensor", lambda e, dc=dc, oc=oc, z=z, l=last: (e.matmul(
                                PQ[z], WpT[:, 1024 * dc + 128 * oc:1024 * dc + 128 * oc + 128],
                                xqv[:, dc, :], start=(dc == 0), stop=(dc == NDC - 1)
                            ).then_inc(tT.h, 1) if l else e.matmul(
                                PQ[z], WpT[:, 1024 * dc + 128 * oc:1024 * dc + 128 * oc + 128],
                                xqv[:, dc, :], start=(dc == 0), stop=(dc == NDC - 1))))
                        mm_z = tT.inc(1)
                        mk.wait("vector", tT, mm_z)
                        mk.emit("vector", lambda e, oc=oc, ts=ts, z=z, dst=dst: e.tensor_copy(
                            dst[:, 2048 * oc + 512 * ts:2048 * oc + 512 * ts + 512],
                            PQ[z]).then_inc(vV.h, 1))
                        ev_z[z] = vV.inc(1)
                    mm_last = tT.n
            else:
                ev_prev = 0
                mm_last = tT.n
                for ts in range(4):
                    mk.wait("gpsimd", tT, mm_last)
                    mk.emit("gpsimd", lambda e, ts=ts, tb=tb: e.dma_start(
                        xqv, xdv[:, :, tb + 512 * ts:tb + 512 * ts + 512]).then_inc(dxq.h, 16))
                    xd = dxq.inc(16)
                    mk.wait("tensor", dxq, xd)
                    for it in range(4):
                        mk.wait("tensor", vV, ev_prev)
                        for dc in range(NDC):
                            for hh in range(2):
                                last = dc == NDC - 1 and hh == 1
                                mk.emit("tensor", lambda e, dc=dc, it=it, hh=hh, l=last: (e.matmul(
                                    PV2[:, 512 * hh:512 * hh + 512],
                                    xqv[:, dc, 128 * it:128 * it + 128],
                                    WpT[:, 1024 * dc + 512 * hh:1024 * dc + 512 * hh + 512],
                                    start=(dc == 0), stop=(dc == NDC - 1)
                                ).then_inc(tT.h, 1) if l else e.matmul(
                                    PV2[:, 512 * hh:512 * hh + 512],
                                    xqv[:, dc, 128 * it:128 * it + 128],
                                    WpT[:, 1024 * dc + 512 * hh:1024 * dc + 512 * hh + 512],
                                    start=(dc == 0), stop=(dc == NDC - 1))))
                        vmm = tT.inc(1)
                        mk.wait("vector", tT, vmm)
                        mk.emit("vector", lambda e, ts=ts, it=it: e.tensor_copy(
                            vN[:, 1024 * (4 * ts + it):1024 * (4 * ts + it) + 1024],
                            PV2).then_inc(vV.h, 1))
                        ev_prev = vV.inc(1)
                    mm_last = tT.n

        if dbg is not None and b == 0:
            dv = vV.n
            mk.wait("sync", vV, dv)
            mk.emit("sync", lambda e: e.dma_start(dbg["QT"], qT[:]).then_inc(dS.h, 16))
            mk.emit("sync", lambda e: e.dma_start(dbg["KT"], kT[:]).then_inc(dS.h, 16))
            mk.emit("sync", lambda e: e.dma_start(dbg["VN"], vN[:]).then_inc(dS.h, 16))
            dS.inc(48)
        if not do_bphase:
            continue
        qkv_v_done = vV.n
        prev = (vV, qkv_v_done)
        for h in range(16):
            bh = 64 * (h % 2)
            qsl = qT[bh:bh + 64, 2048 * (h // 2):2048 * (h // 2) + 2048]
            ksl = kT[bh:bh + 64, 2048 * (h // 2):2048 * (h // 2) + 2048]
            for which, sl in ((0, qsl), (1, ksl)):
                mk.wait("scalar", prev[0], prev[1])
                mk.emit("scalar", lambda e, sl=sl, bh=bh: e.activation(
                    sqtmp[bh:bh + 64, 0:2048], sl, AF.Square).then_inc(sS.h, 1))
                sq = (sS, sS.inc(1))
                mk.wait("tensor", sq[0], sq[1])
                for s in range(4):
                    last = s == 3
                    mk.emit("tensor", lambda e, s=s, bh=bh, l=last: (e.matmul(
                        PO[0:1, 512 * s:512 * s + 512], onesA[bh:bh + 64, 0:1],
                        sqtmp[bh:bh + 64, 512 * s:512 * s + 512], start=True, stop=True
                    ).then_inc(tT.h, 1) if l else e.matmul(
                        PO[0:1, 512 * s:512 * s + 512], onesA[bh:bh + 64, 0:1],
                        sqtmp[bh:bh + 64, 512 * s:512 * s + 512], start=True, stop=True)))
                mm = (tT, tT.inc(1))
                mk.wait("vector", mm[0], mm[1])
                col = 16 * which + h
                mk.emit("vector", lambda e, col=col: e.reduce_max(
                    st8[0:1, col:col + 1],
                    PO[0:1, 0:2048].rearrange("p (a o) -> p a o", a=1, o=2048),
                    axis=AXL.X).then_inc(vV.h, 1))
                prev = (vV, vV.inc(1))
        bsteps = [
            ("scalar", lambda e: e.activation(st8[0:1, 32:48], st8[0:1, 0:16], AF.Sqrt)),
            ("scalar", lambda e: e.activation(st8[0:1, 48:64], st8[0:1, 16:32], AF.Sqrt)),
            ("vector", lambda e: e.tensor_tensor(st8[0:1, 32:48], st8[0:1, 32:48],
                                                 st8[0:1, 48:64], ALU.mult)),
            ("vector", lambda e: e.tensor_scalar(st8[0:1, 48:64], st8[0:1, 32:48],
                                                 -0.125, None, ALU.mult)),
            ("vector", lambda e: e.tensor_copy(st8b[0:1, :], st8[0:1, 48:64])),
        ]
        bdone = seq_emit(mk, sems, bsteps, first=prev)
        mk.wait("tensor", bdone[0], bdone[1])
        for h in range(16):
            last = h == 15
            mk.emit("tensor", lambda e, h=h, l=last: (e.matmul(
                SA[0][:, h:h + 1], ones1[:], st8b[0:1, h:h + 1],
                start=True, stop=True).then_inc(tT.h, 1) if l else e.matmul(
                SA[0][:, h:h + 1], ones1[:], st8b[0:1, h:h + 1],
                start=True, stop=True)))
        bm = (tT, tT.inc(1))
        mk.wait("vector", bm[0], bm[1])
        mk.emit("vector", lambda e: e.tensor_copy(Bcols[:], SA[0][:, 0:16]).then_inc(vV.h, 1))
        bdone = (vV, vV.inc(1))

        vst_used = {0: (tT, 0), 1: (tT, 0)}
        po_free = (vV, 0)
        sa_free = {0: (sS, 0), 1: (sS, 0)}
        sa_rb_free = (vV, 0)
        pstrip_free = {0: (tT, 0), 1: (tT, 0)}
        atto_last = (vV, 0)
        for h in range(n_heads):
            zs = h % 2
            bh = 64 * (h % 2)
            sr = 64 - bh
            hq = 2048 * (h // 2)
            qsl = qT[bh:bh + 64, hq:hq + 2048]
            ksl = kT[bh:bh + 64, hq:hq + 2048]
            pw = vst_used[zs]
            mk.wait("vector", pw[0], pw[1])
            mk.wait("vector", bdone[0], bdone[1])
            mk.emit("vector", lambda e, zs=zs: e.memset(vst[zs][:], 1.0).then_inc(vV.h, 1))
            c3 = (vV, vV.inc(1))
            mk.wait("vector", c3[0], c3[1])
            for j in range(16):
                last = j == 15
                mk.emit("vector", lambda e, j=j, zs=zs, h=h, bh=bh, l=last: (e.tensor_copy(
                    vst[zs][:, 128 * j + bh:128 * j + bh + 64],
                    vN[:, 1024 * j + 64 * h:1024 * j + 64 * h + 64]).then_inc(vV.h, 1)
                    if l else e.tensor_copy(
                    vst[zs][:, 128 * j + bh:128 * j + bh + 64],
                    vN[:, 1024 * j + 64 * h:1024 * j + 64 * h + 64])))
            c4 = (vV, vV.inc(1))
            mk.wait("tensor", c4[0], c4[1])
            mk.wait("tensor", po_free[0], po_free[1])
            sc = 0
            pv_done = (tT, 0)
            for j in range(16):
                z2 = j % 2
                for s in range(j // 4, 4):
                    qs0 = 128 * j if s == j // 4 else 512 * s
                    N = 512 * (s + 1) - qs0
                    za = sc % 2
                    mk.wait("tensor", sa_free[za][0], sa_free[za][1])
                    if sc < 2:
                        mk.wait("tensor", sa_rb_free[0], sa_rb_free[1])
                    mk.emit("tensor", lambda e, j=j, qs0=qs0, N=N, za=za, ksl=ksl, qsl=qsl: e.matmul(
                        SA[za][:, 0:N], ksl[:, 128 * j:128 * j + 128],
                        qsl[:, qs0:qs0 + N],
                        start=True, stop=True).then_inc(tT.h, 1))
                    smm = (tT, tT.inc(1))
                    mk.wait("scalar", smm[0], smm[1])
                    mk.wait("scalar", pstrip_free[z2][0], pstrip_free[z2][1])
                    mk.emit("scalar", lambda e, qs0=qs0, N=N, za=za, z2=z2, h=h: e.activation(
                        pstrip[z2][:, qs0:qs0 + N], SA[za][:, 0:N],
                        AF.Exp, scale=0.125, bias=Bcols[:, h:h + 1]).then_inc(sS.h, 1))
                    ex = (sS, sS.inc(1))
                    sa_free[za] = ex
                    pv_wait = ex
                    if s == j // 4:
                        mk.wait("vector", ex[0], ex[1])
                        mk.emit("vector", lambda e, j=j, z2=z2: e.tensor_tensor(
                            pstrip[z2][:, 128 * j:128 * j + 128],
                            pstrip[z2][:, 128 * j:128 * j + 128],
                            ut[:], ALU.mult).then_inc(vV.h, 1))
                        pv_wait = (vV, vV.inc(1))
                    mk.wait("tensor", pv_wait[0], pv_wait[1])
                    jl = min(15, 4 * s + 3)
                    mk.emit("tensor", lambda e, j=j, qs0=qs0, N=N, z2=z2, zs=zs, jl=jl: e.matmul(
                        PO[:, qs0:qs0 + N], vst[zs][:, 128 * j:128 * j + 128],
                        pstrip[z2][:, qs0:qs0 + N],
                        start=(j == 0), stop=(j == jl),
                        skip_group_check=True).then_inc(tT.h, 1))
                    pv_done = (tT, tT.inc(1))
                    sc += 1
                pstrip_free[z2] = pv_done
            vst_used[zs] = pv_done
            mk.wait("vector", pv_done[0], pv_done[1])

            def _recip(e, sqtmp=sqtmp, PO=PO, nc=nc, vV=vV, sr=sr):
                with nc.allow_low_precision(reason="softmax 1/sum applied in bf16"):
                    return e.reciprocal(sqtmp[sr:sr + 1, 0:2048], PO[sr:sr + 1, 0:2048]).then_inc(vV.h, 1)

            mk.emit("vector", _recip)
            rr = (vV, vV.inc(1))
            mk.wait("tensor", rr[0], rr[1])
            mk.wait("tensor", sa_free[0][0], sa_free[0][1])
            mk.wait("tensor", sa_free[1][0], sa_free[1][1])
            mk.wait("tensor", sa_rb_free[0], sa_rb_free[1])
            rbc = None
            for half in range(2):
                if rbc is not None:
                    mk.wait("tensor", rbc[0], rbc[1])
                for s2 in range(2):
                    last = s2 == 1
                    mk.emit("tensor", lambda e, half=half, s2=s2, sr=sr, bh=bh, l=last: (e.matmul(
                        SA[s2][bh:bh + 64, :], onesA[sr:sr + 1, 0:64],
                        sqtmp[sr:sr + 1, 1024 * half + 512 * s2:1024 * half + 512 * s2 + 512],
                        start=True, stop=True).then_inc(tT.h, 1) if l else e.matmul(
                        SA[s2][bh:bh + 64, :], onesA[sr:sr + 1, 0:64],
                        sqtmp[sr:sr + 1, 1024 * half + 512 * s2:1024 * half + 512 * s2 + 512],
                        start=True, stop=True)))
                rbm = (tT, tT.inc(1))
                mk.wait("vector", rbm[0], rbm[1])
                if half == 1:
                    mk.wait("vector", sa_rb_free[0], sa_rb_free[1])
                mk.emit("vector", lambda e, bh=bh: e.tensor_copy(
                    rbs[bh:bh + 64, 0:512], SA[0][bh:bh + 64, :]))
                mk.emit("vector", lambda e, bh=bh: e.tensor_copy(
                    rbs[bh:bh + 64, 512:1024], SA[1][bh:bh + 64, :]).then_inc(vV.h, 1))
                rbc = (vV, vV.inc(1))
                mk.wait("vector", rbc[0], rbc[1])
                dsth = qT[bh:bh + 64, hq + 1024 * half:hq + 1024 * half + 1024]
                mk.emit("vector", lambda e, dsth=dsth, half=half, bh=bh: e.tensor_tensor(
                    dsth, PO[bh:bh + 64, 1024 * half:1024 * half + 1024],
                    rbs[bh:bh + 64, 0:1024], ALU.mult).then_inc(vV.h, 1))
                sa_rb_free = (vV, vV.inc(1))
            po_free = sa_rb_free
            atto_last = sa_rb_free
        if not do_proj:
            continue

        WoT = WpT
        wo2 = WoT[:].rearrange("p (ic o) -> p ic o", ic=NDC, o=1024)
        mk.wait("tensor", atto_last[0], atto_last[1])
        woT_ready = emit_wT_load(
            lambda oc: wap["w_out"][128 * oc:128 * oc + 128, :], None, wo2)
        mk.wait("tensor", vV, woT_ready)
        xqv = xqw[:].rearrange("p (dc t) -> p dc t", dc=NDC, t=512)
        arv = ar[:].rearrange("p (dc t) -> p dc t", dc=NDC, t=NTOK)
        ev_z = {0: 0, 1: 0}
        mm_last = tT.n
        for ts in range(4):
            mk.wait("gpsimd", tT, mm_last)
            mk.wait("gpsimd", vV, max(ev_z.values()))
            mk.emit("gpsimd", lambda e, ts=ts, tb=tb: e.dma_start(
                xqv, xdv[:, :, tb + 512 * ts:tb + 512 * ts + 512]).then_inc(dxq.h, 16))
            xd = dxq.inc(16)
            mk.wait("tensor", dxq, xd)
            for oc in range(8):
                z = oc % 2
                mk.wait("tensor", vV, ev_z[z])
                for ic in range(NDC):
                    last = ic == NDC - 1
                    mk.emit("tensor", lambda e, ic=ic, oc=oc, ts=ts, z=z, l=last: (e.matmul(
                        PQ[z], WoT[:, 1024 * ic + 128 * oc:1024 * ic + 128 * oc + 128],
                        qT[:, 2048 * ic + 512 * ts:2048 * ic + 512 * ts + 512],
                        start=(ic == 0), stop=(ic == NDC - 1)).then_inc(tT.h, 1)
                        if l else e.matmul(
                        PQ[z], WoT[:, 1024 * ic + 128 * oc:1024 * ic + 128 * oc + 128],
                        qT[:, 2048 * ic + 512 * ts:2048 * ic + 512 * ts + 512],
                        start=(ic == 0), stop=(ic == NDC - 1))))
                mm_z = tT.inc(1)
                mk.wait("vector", tT, mm_z)
                mk.emit("vector", lambda e, oc=oc, ts=ts, z=z, tb=tb: e.tensor_tensor(
                    arv[:, oc, tb + 512 * ts:tb + 512 * ts + 512], PQ[z],
                    xqv[:, oc, :], ALU.add).then_inc(vV.h, 1))
                ev_z[z] = vV.inc(1)
            mm_last = tT.n
    asc.close()


def emit_ffn1(mk, bufs, sems, wap, xt_dram, acc_dram):
    nc = mk.nc
    tT, vV, sS, dW, dA = (sems[n] for n in ("tT", "vV", "sS", "dW", "dA"))
    dE = [mk.sem("dE0"), mk.sem("dE1")]
    dxc = [mk.sem("dxc0"), mk.sem("dxc1")]
    dxw = [mk.sem("dxw0"), mk.sem("dxw1")]
    ar = bufs["arena"]
    ps = bufs["ps_f32"]
    h_sb = ar[:, 0:2048]
    xwin = [ar[:, 2048:6144].bitcast(BF16) if False else ar[:, 2048:6144],
            ar[:, 6144:10240]]
    tmp = ar[:, 5120 + 5120:7168 + 5120].bitcast(F32)
    gsel = ar[:, 12288:12800].bitcast(F32)
    lgs = ar[:, 12800:13312].bitcast(F32)
    m8 = ar[:, 13312:13824].bitcast(F32)
    e2g = ar[:, 13824:14080].bitcast(F32)
    wr_sb = ar[:, 14080:14208].bitcast(F32)
    rb_sb = ar[0:1, 14208:14224].bitcast(F32)
    onesF = ar[0:1, 14224:14480].bitcast(F32)
    xca = [ar[:, 14592:22784].bitcast(F32), ar[:, 22784:30976].bitcast(F32)]
    ph1 = ps[:, 0:512]
    ph3 = ps[:, 512:1024]
    py = [ps[:, 1024:2048], ps[:, 2048:3072]]
    pl = ps[:, 2048:2304]
    xdv = xt_dram.rearrange("(dc p) t -> p dc t", p=128)

    wsc = contextlib.ExitStack()
    w1s = [wsc.enter_context(nc.sbuf_tensor(f"f1w1{mk.uid()}_{i}", [128, 4096], BF16)) for i in range(2)]
    w3s = [wsc.enter_context(nc.sbuf_tensor(f"f1w3{mk.uid()}_{i}", [128, 4096], BF16)) for i in range(2)]
    w2s = [wsc.enter_context(nc.sbuf_tensor(f"f1w2{mk.uid()}_{i}", [128, 4096], BF16)) for i in range(2)]

    def expert_w(ex):
        if ex < 8:
            return wap["r_w1"][ex], wap["r_w3"][ex], wap["r_w2"][ex]
        return wap["s_w1"][ex - 8], wap["s_w3"][ex - 8], wap["s_w2"][ex - 8]

    mk.emit("gpsimd", lambda e: e.dma_start(
        wr_sb.rearrange("p (dc j) -> p dc j", dc=NDC, j=8),
        wap["router_w"].rearrange("(dc p) j -> p dc j", p=128)).then_inc(dW.h, 16))
    mk.emit("gpsimd", lambda e: e.dma_start(
        rb_sb, wap["router_b"].unsqueeze(0)).then_inc(dW.h, 16))
    stg_done = dW.inc(32)
    mk.emit("vector", lambda e: e.memset(onesF, 1.0).then_inc(vV.h, 1))
    onesF_done = vV.inc(1)
    def load_expert(ex, buf):
        a1, a3, a2 = expert_w(ex)
        mk.emit("gpsimd", lambda e, a1=a1, buf=buf: e.dma_start(
            w1s[buf][:].rearrange("p (dc f) -> p dc f", dc=NDC, f=512),
            a1.rearrange("(dc p) f -> p dc f", p=128)).then_inc(dE[buf].h, 16))
        mk.emit("gpsimd", lambda e, a3=a3, buf=buf: e.dma_start(
            w3s[buf][:].rearrange("p (dc f) -> p dc f", dc=NDC, f=512),
            a3.rearrange("(dc p) f -> p dc f", p=128)).then_inc(dE[buf].h, 16))
        mk.emit("gpsimd", lambda e, a2=a2, buf=buf: e.dma_start(
            w2s[buf][:].rearrange("p (fc d) -> p fc d", fc=4, d=1024),
            a2.rearrange("(fc p) d -> p fc d", p=128)).then_inc(dE[buf].h, 16))
        return dE[buf].inc(48)

    wready = {0: load_expert(0, 0)}

    mk.wait("tensor", dW, stg_done)
    mk.wait("tensor", vV, onesF_done)
    wrv = wr_sb.rearrange("p (dc j) -> p dc j", dc=NDC, j=8)
    mm_done = {}
    for sl in range(8):
        z = sl % 2
        mk.wait("gpsimd", tT, mm_done.get(sl - 2, 0))
        mk.emit("gpsimd", lambda e, sl=sl, z=z: e.dma_start(
            xca[z].rearrange("p (dc t) -> p dc t", dc=NDC, t=512),
            xdv[:, :, 512 * sl:512 * sl + 512]).then_inc(dxc[z].h, 16))
        xd = dxc[z].inc(16)
        mk.wait("tensor", dxc[z], xd)
        xcv = xca[z].rearrange("p (dc t) -> p dc t", dc=NDC, t=512)
        for t4 in range(4):
            tg = 4 * sl + t4
            for dc in range(NDC):
                mk.emit("tensor", lambda e, dc=dc, t4=t4, tg=tg, xcv=xcv: e.matmul(
                    pl[:, 8 * tg:8 * tg + 8],
                    xcv[:, dc, 128 * t4:128 * t4 + 128], wrv[:, dc, :],
                    start=(dc == 0), stop=False))
            mk.emit("tensor", lambda e, tg=tg: e.matmul(
                pl[:, 8 * tg:8 * tg + 8], onesF, rb_sb,
                start=False, stop=True).then_inc(tT.h, 1))
            tT.inc(1)
        mm_done[sl] = tT.n
    lg_done = tT.n
    mk.wait("vector", tT, lg_done)
    mk.emit("vector", lambda e: e.tensor_copy(lgs, pl[:, 0:256]).then_inc(vV.h, 1))
    lgs_done = (vV, vV.inc(1))
    mk.wait("vector", lgs_done[0], lgs_done[1])
    for tg in range(NTT):
        last = tg == NTT - 1
        mk.emit("vector", lambda e, tg=tg, l=last: (e.max(
            out=m8[:, 8 * tg:8 * tg + 8], in_=lgs[:, 8 * tg:8 * tg + 8]).then_inc(vV.h, 1)
            if l else e.max(out=m8[:, 8 * tg:8 * tg + 8], in_=lgs[:, 8 * tg:8 * tg + 8])))
    m8_done = (vV, vV.inc(1))
    m8v = m8.rearrange("p (g e) -> p g e", g=NTT, e=8)
    lgv = lgs.rearrange("p (g e) -> p g e", g=NTT, e=8)
    gsv = gsel.rearrange("p (g e) -> p g e", g=NTT, e=8)
    e2 = e2g[:, 0:32]
    g1 = e2g[:, 32:64]
    g2 = e2g[:, 64:96]
    scr = e2g[:, 96:128]
    e2v = e2.rearrange("p (g o) -> p g o", g=NTT, o=1)
    g1v = g1.rearrange("p (g o) -> p g o", g=NTT, o=1)
    g2v = g2.rearrange("p (g o) -> p g o", g=NTT, o=1)
    gsteps = [
        ("vector", lambda e: e.tensor_tensor(e2v, m8v[:, :, 1:2], m8v[:, :, 0:1], ALU.subtract)),
        ("scalar", lambda e: e.activation(e2, e2, AF.Exp)),
        ("vector", lambda e: e.tensor_scalar(scr, e2, 1.0, None, ALU.add)),
        ("vector", lambda e: e.reciprocal(g1, scr)),
        ("vector", lambda e: e.tensor_tensor(g2, e2, g1, ALU.mult)),
        ("vector", lambda e: e.tensor_tensor(gsv, lgv, m8v[:, :, 0:1].broadcast_to((128, NTT, 8)), ALU.is_equal)),
        ("vector", lambda e: e.tensor_tensor(gsv, gsv, g1v.broadcast_to((128, NTT, 8)), ALU.mult)),
        ("vector", lambda e: e.tensor_tensor(lgs, lgv, m8v[:, :, 1:2].broadcast_to((128, NTT, 8)), ALU.is_equal)),
        ("vector", lambda e: e.tensor_tensor(lgs, lgv, g2v.broadcast_to((128, NTT, 8)), ALU.mult)),
        ("vector", lambda e: e.tensor_tensor(gsel, gsel, lgs, ALU.add)),
    ]
    gdone = seq_emit(mk, sems, gsteps, first=m8_done)
    router_done_v = vV.n

    ev_cnt = {}
    dma_cnt = {}
    acc_last_of_e = 0
    xw_read_z = {0: 0, 1: 0}
    phA_all_done = 0
    phB_prev = 0
    sil_prev = 0
    for ex in range(10):
        buf = ex % 2
        mk.wait("tensor", dE[buf], wready[ex])
        if ex == 0:
            mk.wait("tensor", vV, router_done_v)
        w1v = w1s[buf][:].rearrange("p (dc f) -> p dc f", dc=NDC, f=512)
        w3v = w3s[buf][:].rearrange("p (dc f) -> p dc f", dc=NDC, f=512)
        w2v = w2s[buf][:].rearrange("p (fc d) -> p fc d", fc=4, d=1024)
        for w in range(8):
            wz = w % 2
            mk.wait("gpsimd", tT, xw_read_z[wz])
            mk.emit("gpsimd", lambda e, w=w, wz=wz: e.dma_start(
                xwin[wz].rearrange("p (dc t) -> p dc t", dc=NDC, t=512),
                xdv[:, :, 512 * w:512 * w + 512]).then_inc(dxw[wz].h, 16))
            xwd = dxw[wz].inc(16)
            xv = xwin[wz].rearrange("p (dc t) -> p dc t", dc=NDC, t=512)
            mk.wait("tensor", dxw[wz], xwd)
            for fc in range(4):
                mk.wait("tensor", vV, sil_prev)
                for which, wv in ((0, w1v), (1, w3v)):
                    dstp = ph1 if which == 0 else ph3
                    for dc in range(NDC):
                        last = dc == NDC - 1 and which == 1
                        mk.emit("tensor", lambda e, dc=dc, fc=fc, wv=wv, dstp=dstp, xv=xv, l=last: (e.matmul(
                            dstp, wv[:, dc, 128 * fc:128 * fc + 128],
                            xv[:, dc, :], start=(dc == 0), stop=(dc == NDC - 1)
                        ).then_inc(tT.h, 1) if l else e.matmul(
                            dstp, wv[:, dc, 128 * fc:128 * fc + 128],
                            xv[:, dc, :], start=(dc == 0), stop=(dc == NDC - 1))))
                mmA = tT.inc(1)
                mk.wait("scalar", tT, mmA)
                mk.emit("scalar", lambda e, fc=fc: e.activation(
                    h_sb[:, 512 * fc:512 * fc + 512], ph1, AF.Silu).then_inc(sS.h, 1))
                sdone = sS.inc(1)
                mk.wait("vector", sS, sdone)
                mk.emit("vector", lambda e, fc=fc: e.tensor_tensor(
                    h_sb[:, 512 * fc:512 * fc + 512], h_sb[:, 512 * fc:512 * fc + 512],
                    ph3, ALU.mult).then_inc(vV.h, 1))
                sil_prev = vV.inc(1)
            phA_all_done = tT.n
            xw_read_z[wz] = tT.n
            h_ready = sil_prev
            mk.wait("tensor", vV, h_ready)
            for th in range(4):
                tg = 4 * w + th
                pz = tg % 2
                mk.wait("tensor", vV, ev_cnt.get(tg - 2, 0))
                if ex == 0 and tg <= 1:
                    mk.wait("tensor", vV, router_done_v)
                for fc in range(4):
                    for hh in range(2):
                        last = fc == 3 and hh == 1
                        mk.emit("tensor", lambda e, fc=fc, th=th, hh=hh, pz=pz, w2v=w2v, l=last: (e.matmul(
                            py[pz][:, 512 * hh:512 * hh + 512],
                            h_sb[:, 512 * fc + 128 * th:512 * fc + 128 * th + 128],
                            w2v[:, fc, 512 * hh:512 * hh + 512],
                            start=(fc == 0), stop=(fc == 3)).then_inc(tT.h, 1)
                            if l else e.matmul(
                            py[pz][:, 512 * hh:512 * hh + 512],
                            h_sb[:, 512 * fc + 128 * th:512 * fc + 128 * th + 128],
                            w2v[:, fc, 512 * hh:512 * hh + 512],
                            start=(fc == 0), stop=(fc == 3))))
                mmB = tT.inc(1)
                mk.wait("vector", tT, mmB)
                mk.wait("vector", dA, dA.n)
                if ex < 8:
                    mk.emit("vector", lambda e, tg=tg, ex=ex, pz=pz: e.tensor_scalar(
                        tmp, py[pz][:], gsel[:, 8 * tg + ex:8 * tg + ex + 1],
                        None, ALU.mult).then_inc(vV.h, 1))
                else:
                    mk.emit("vector", lambda e, pz=pz: e.tensor_copy(
                        tmp, py[pz][:]).then_inc(vV.h, 1))
                ev_cnt[tg] = vV.inc(1)
                mk.wait("gpsimd", vV, ev_cnt[tg])
                if tg == 0:
                    mk.wait("gpsimd", dA, acc_last_of_e)
                op = ALU.bypass if ex == 0 else ALU.add
                mk.emit("gpsimd", lambda e, tg=tg, op=op: e.dma_start(
                    acc_dram[128 * tg:128 * tg + 128, :], tmp,
                    accum_op=op).then_inc(dA.h, 16))
                dma_cnt[tg] = dA.inc(16)
            phB_prev = tT.n
        acc_last_of_e = dma_cnt[31]
        if ex < 9:
            mk.wait("gpsimd", tT, phA_all_done)
            wready[ex + 1] = load_expert(ex + 1, (ex + 1) % 2)
    wsc.close()
    return dma_cnt[31]


def emit_final_out(mk, bufs, sems, out_dram):
    nc = mk.nc
    tT, vV, dS = (sems[n] for n in ("tT", "vV", "dS"))
    ar, eye, pt = bufs["arena"], bufs["eye"], bufs["pt"]
    arv = ar[:].rearrange("p (dc t) -> p dc t", dc=NDC, t=NTOK)
    ov = out_dram.rearrange("b (i p) d -> p b i d", p=128)
    fsc = contextlib.ExitStack()
    fst = [fsc.enter_context(nc.sbuf_tensor(f"fst{mk.uid()}_{i}", [128, 1024], F32)) for i in range(2)]
    dF = [mk.sem("dF0"), mk.sem("dF1")]
    mk.wait("tensor", vV, vV.n)
    st_done = {}
    cp_done = {}
    tr_done = {}
    for tg in range(NTT):
        b, i = tg // 16, tg % 16
        z = tg % 2
        for half in range(2):
            pz = half
            mk.wait("tensor", vV, cp_done.get(tg - 1, 0))
            for q in range(4):
                dc = 4 * half + q
                last = q == 3
                mk.emit("tensor", lambda e, dc=dc, q=q, pz=pz, tg=tg, l=last: (e.transpose(
                    pt[pz][:, 128 * q:128 * q + 128],
                    arv[:, dc, 128 * tg:128 * tg + 128], eye[:]).then_inc(tT.h, 1)
                    if l else e.transpose(
                    pt[pz][:, 128 * q:128 * q + 128],
                    arv[:, dc, 128 * tg:128 * tg + 128], eye[:])))
            td = tT.inc(1)
            tr_done[tg] = td
            mk.wait("vector", tT, td)
            mk.wait("vector", dF[z], st_done.get(tg - 2, 0))
            last = half == 1
            mk.emit("vector", lambda e, z=z, half=half, l=last: (e.tensor_copy(
                fst[z][:, 512 * half:512 * half + 512],
                pt[half][:].rearrange("p (q t) -> p (q t)", q=4, t=128)).then_inc(vV.h, 1)
                if l else e.tensor_copy(
                fst[z][:, 512 * half:512 * half + 512],
                pt[half][:].rearrange("p (q t) -> p (q t)", q=4, t=128))))
            cp_done[tg] = vV.inc(1) if last else cp_done.get(tg, 0)
        mk.wait("sync", vV, cp_done[tg])
        mk.emit("sync", lambda e, b=b, i=i, z=z: e.dma_start(
            ov[:, b, i, :], fst[z][:]).then_inc(dF[z].h, 16))
        st_done[tg] = dF[z].inc(16)
    mk.wait("sync", dF[0], dF[0].n)
    mk.wait("sync", dF[1], dF[1].n)
    fsc.close()


def seq_emit(mk, sems, steps, first=None):
    tT, vV, sS = sems["tT"], sems["vV"], sems["sS"]
    prev = first
    for eng, fn in steps:
        if prev is not None:
            mk.wait(eng, prev[0], prev[1])
        c = sS if eng == "scalar" else (vV if eng == "vector" else tT)
        mk.emit(eng, lambda e, fn=fn, c=c: fn(e).then_inc(c.h, 1))
        prev = (c, c.inc(1))
    return prev



def build_stage1(out_dtype=BF16, debug=False):
    nc = bass.Bass("TRN2", target_bir_lowering=False)
    u = nc.dram_tensor("U", [BC, T, D], F32, kind="ExternalInput").ap()
    tstack = nc.dram_tensor("TSTACK", [128, 7, 128, 128], F32, kind="ExternalInput").ap()
    eye_in = nc.dram_tensor("EYE", [128, 128], BF16, kind="ExternalInput").ap()
    xt_out = nc.dram_tensor("XT", [D, NTOK], out_dtype, kind="ExternalOutput").ap()

    mk = MK(nc)
    bufs = {}
    bufs["ubf"] = mk.sb("ubf", [128, NTOK * NDC], BF16)
    bufs["xt"] = mk.sb("xt", [128, NTOK * NDC], BF16)
    bufs["tsb"] = [mk.sb(f"tsb{i}", [128, 896], BF16) for i in range(2)]
    bufs["eye"] = mk.sb("eye", [128, 128], BF16)
    bufs["onec"] = mk.sb("onec", [128, 1], BF16)
    bufs["ones1"] = mk.sb("ones1", [1, 128], BF16)
    bufs["zsq"] = [mk.sb(f"zsq{i}", [128, NTOK], BF16) for i in range(2)]
    bufs["rows"] = mk.sb("rows", [1, 4096], F32)
    bufs["rows_bf"] = mk.sb("rows_bf", [1, 2048], BF16)
    ps_f32 = mk.ps("ps_f32", [128, 3072], F32)
    ps_bf = mk.ps("ps_bf", [128, 2048], BF16)
    bufs["pg"] = [ps_f32[:, 0:512], ps_f32[:, 512:1024]]
    bufs["pstx"] = [ps_f32[:, 0:512], ps_f32[:, 512:1024]]
    bufs["psts"] = [ps_f32[:, 1024:1536], ps_f32[:, 1536:2048]]
    bufs["pbc"] = [ps_f32[:, 2048:2560], ps_f32[:, 2560:3072]]
    bufs["pt"] = [ps_bf[:, 0:512], ps_bf[:, 1024:1536]]

    sems = {n: mk.sem(n) for n in ("dW", "dS", "tT", "vV", "sS")}
    dW, dS, tT, vV, sS = (sems[n] for n in ("dW", "dS", "tT", "vV", "sS"))

    dC = mk.sem("dC")
    mk.emit("gpsimd", lambda e: e.dma_start(bufs["eye"][:], eye_in).then_inc(dC.h, 16))
    mk.emit("gpsimd", lambda e: e.dma_start(
        bufs["ubf"][:].rearrange("p (b i d) -> p b i d", b=BC, i=16, d=D),
        u.rearrange("b (i p) d -> p b i d", p=128)).then_inc(dC.h, 16))
    load_done = dC.inc(32)
    mk.emit("vector", lambda e: e.memset(bufs["onec"][:], 1.0))
    mk.emit("vector", lambda e: e.memset(bufs["ones1"][:], 1.0).then_inc(vV.h, 1))
    const_done = vV.inc(1)
    mk.wait("tensor", mk.sems["dC"], load_done)
    mk.wait("tensor", vV, const_done)
    mk.wait("vector", mk.sems["dC"], load_done)

    emit_conv_block(mk, bufs, sems, tstack)
    if debug:
        udbg = nc.dram_tensor("UDBG", [128, NTOK * NDC], BF16, kind="ExternalOutput").ap()
        mk.wait("sync", vV, vV.n)
        mk.emit("sync", lambda e: e.dma_start(udbg, bufs["ubf"][:]).then_inc(dS.h, 16))
        dS.inc(16)
    emit_transpose_nat_to_T(mk, bufs, sems, bufs["ubf"], bufs["xt"])
    emit_lnT(mk, bufs, sems, bufs["xt"])

    final_v = vV.n
    mk.wait("sync", vV, final_v)
    xtv = bufs["xt"][:].rearrange("p (dc t) -> p dc t", dc=NDC, t=NTOK)
    for dc in range(NDC):
        mk.emit("sync", lambda e, dc=dc: e.dma_start(
            xt_out[128 * dc:128 * dc + 128, :], xtv[:, dc, :]).then_inc(dS.h, 16))
    out_done = dS.inc(16 * NDC)
    mk.wait("sync", dS, out_done)
    mk.realize()
    return nc


def _shard_inputs_stage1(inputs):
    tstack = _conv_tstack(inputs["L0_attn_diff_kernel"], inputs["L0_attn_ma_r_kernel"])
    eye = np.eye(128, dtype=np.float32).astype(bfloat16)
    u = np.ascontiguousarray(np.asarray(inputs["L0_attn_u"], dtype=np.float32))
    in_maps = []
    for c in range(N_CORES):
        in_maps.append({
            "U": np.ascontiguousarray(u[2 * c:2 * c + 2]),
            "TSTACK": tstack,
            "EYE": eye,
        })
    return in_maps


def kernel(**inputs):
    raise NotImplementedError("under construction: stages being built")


def build_stage3(n_heads=16, do_proj=True, do_bphase=True, do_ln=True, debug=False):
    nc = bass.Bass("TRN2", target_bir_lowering=False)
    xt_in = nc.dram_tensor("XT", [D, NTOK], BF16, kind="ExternalInput").ap()
    wap = {
        "w_qkv": nc.dram_tensor("WQKV", [3 * D, D], F32, kind="ExternalInput").ap(),
        "w_out": nc.dram_tensor("WOUT", [D, D], F32, kind="ExternalInput").ap(),
    }
    eye_in = nc.dram_tensor("EYE", [128, 128], BF16, kind="ExternalInput").ap()
    ut_in = nc.dram_tensor("UT", [128, 128], BF16, kind="ExternalInput").ap()
    xt3_out = nc.dram_tensor("XT3", [D, NTOK], BF16, kind="ExternalOutput").ap()

    mk = MK(nc)
    bufs = {}
    bufs["arena"] = mk.sb("arena", [128, NTOK * NDC], BF16)
    bufs["eye"] = mk.sb("eye", [128, 128], BF16)
    bufs["ut"] = mk.sb("ut", [128, 128], BF16)
    bufs["onec"] = mk.sb("onec", [128, 1], BF16)
    bufs["ones1"] = mk.sb("ones1", [1, 128], BF16)
    bufs["onesA"] = mk.sb("onesA", [128, 128], BF16)
    ps_f32 = mk.ps("ps_f32", [128, 3072], F32)
    ps_bf = mk.ps("ps_bf", [128, 2048], BF16)
    bufs["ps_f32"] = ps_f32
    bufs["pstx"] = [ps_f32[:, 0:512], ps_f32[:, 512:1024]]
    bufs["psts"] = [ps_f32[:, 1024:1536], ps_f32[:, 1536:2048]]
    bufs["pbc"] = [ps_f32[:, 2048:2560], ps_f32[:, 2560:3072]]
    bufs["pt"] = [ps_bf[:, 0:512], ps_bf[:, 1024:1536]]

    sems = {n: mk.sem(n) for n in ("dW", "dS", "dA", "tT", "vV", "sS")}
    dW, dS, dA, tT, vV, sS = (sems[n] for n in ("dW", "dS", "dA", "tT", "vV", "sS"))

    dC = mk.sem("dC")
    mk.emit("gpsimd", lambda e: e.dma_start(bufs["eye"][:], eye_in).then_inc(dC.h, 16))
    mk.emit("gpsimd", lambda e: e.dma_start(bufs["ut"][:], ut_in).then_inc(dC.h, 16))
    load_done = dC.inc(32)
    mk.emit("vector", lambda e: e.memset(bufs["onec"][:], 1.0))
    mk.emit("vector", lambda e: e.memset(bufs["ones1"][:], 1.0))
    mk.emit("vector", lambda e: e.memset(bufs["onesA"][:], 1.0).then_inc(vV.h, 1))
    const_done = vV.inc(1)
    for eng in ("tensor", "vector", "scalar"):
        mk.wait(eng, mk.sems["dC"], load_done)
        mk.wait(eng, vV, const_done)

    dbg = None
    if debug:
        dbg = {n: nc.dram_tensor(n, [128, 16384], BF16, kind="ExternalOutput").ap()
               for n in ("QT", "KT", "VN")}
    emit_attn(mk, bufs, sems, wap, xt_in, n_heads=n_heads, do_proj=do_proj,
              do_bphase=do_bphase, dbg=dbg)
    if do_ln:
        bufs["zsq"] = [mk.sb(f"zsq{i}", [128, NTOK], BF16) for i in range(2)]
        bufs["rows"] = mk.sb("rows", [1, 4096], F32)
        bufs["rows_bf"] = mk.sb("rows_bf", [1, 2048], BF16)
        emit_lnT(mk, bufs, sems, bufs["arena"])

    final_v = vV.n
    mk.wait("sync", vV, final_v)
    mk.wait("sync", sems["sS"], sems["sS"].n)
    mk.wait("sync", tT, tT.n)
    xtv = bufs["arena"][:].rearrange("p (dc t) -> p dc t", dc=NDC, t=NTOK)
    for dc in range(NDC):
        mk.emit("sync", lambda e, dc=dc: e.dma_start(
            xt3_out[128 * dc:128 * dc + 128, :], xtv[:, dc, :]).then_inc(dS.h, 16))
    out_done = dS.inc(16 * NDC)
    mk.wait("sync", dS, out_done)
    mk.realize()
    return nc


def _shard_inputs_stage3(inputs, xt_by_core):
    eye = np.eye(128, dtype=np.float32).astype(bfloat16)
    ut = np.triu(np.ones((128, 128), dtype=np.float32)).astype(bfloat16)
    com = {
        "WQKV": np.ascontiguousarray(np.asarray(inputs["L1_attn_w_qkv"], np.float32)),
        "WOUT": np.ascontiguousarray(np.asarray(inputs["L1_attn_w_out"], np.float32)),
        "EYE": eye, "UT": ut,
    }
    return [dict(com, XT=np.ascontiguousarray(xt_by_core[c])) for c in range(len(xt_by_core))]


def build_stage4(final=False):
    nc = bass.Bass("TRN2", target_bir_lowering=False)
    xt_in = nc.dram_tensor("XT", [D, NTOK], BF16, kind="ExternalInput").ap()
    wap = {
        "router_w": nc.dram_tensor("RW", [D, 8], F32, kind="ExternalInput").ap(),
        "router_b": nc.dram_tensor("RB", [8], F32, kind="ExternalInput").ap(),
        "r_w1": nc.dram_tensor("RW1", [8, D, 512], F32, kind="ExternalInput").ap(),
        "r_w3": nc.dram_tensor("RW3", [8, D, 512], F32, kind="ExternalInput").ap(),
        "r_w2": nc.dram_tensor("RW2", [8, 512, D], F32, kind="ExternalInput").ap(),
        "s_w1": nc.dram_tensor("SW1", [2, D, 512], F32, kind="ExternalInput").ap(),
        "s_w3": nc.dram_tensor("SW3", [2, D, 512], F32, kind="ExternalInput").ap(),
        "s_w2": nc.dram_tensor("SW2", [2, 512, D], F32, kind="ExternalInput").ap(),
    }
    eye_in = nc.dram_tensor("EYE", [128, 128], BF16, kind="ExternalInput").ap()
    if final:
        out_dram = nc.dram_tensor("OUT", [BC, T, D], F32, kind="ExternalOutput").ap()
    else:
        xt4_out = nc.dram_tensor("XT4", [D, NTOK], BF16, kind="ExternalOutput").ap()
    acc_dram = nc.dram_tensor("ACCD", [NTOK, D], F32).ap()

    mk = MK(nc)
    bufs = {}
    bufs["arena"] = mk.sb("arena", [128, NTOK * NDC], BF16)
    bufs["eye"] = mk.sb("eye", [128, 128], BF16)
    bufs["onec"] = mk.sb("onec", [128, 1], BF16)
    bufs["ones1"] = mk.sb("ones1", [1, 128], BF16)
    ps_f32 = mk.ps("ps_f32", [128, 3072], F32)
    ps_bf = mk.ps("ps_bf", [128, 2048], BF16)
    bufs["ps_f32"] = ps_f32
    bufs["pstx"] = [ps_f32[:, 0:512], ps_f32[:, 512:1024]]
    bufs["psts"] = [ps_f32[:, 1024:1536], ps_f32[:, 1536:2048]]
    bufs["pbc"] = [ps_f32[:, 2048:2560], ps_f32[:, 2560:3072]]
    bufs["pt"] = [ps_bf[:, 0:512], ps_bf[:, 1024:1536]]

    sems = {n: mk.sem(n) for n in ("dW", "dS", "dA", "tT", "vV", "sS")}
    dW, dS, dA, tT, vV, sS = (sems[n] for n in ("dW", "dS", "dA", "tT", "vV", "sS"))

    dC = mk.sem("dC")
    mk.emit("gpsimd", lambda e: e.dma_start(bufs["eye"][:], eye_in).then_inc(dC.h, 16))
    load_done = dC.inc(16)
    mk.emit("vector", lambda e: e.memset(bufs["onec"][:], 1.0))
    mk.emit("vector", lambda e: e.memset(bufs["ones1"][:], 1.0).then_inc(vV.h, 1))
    const_done = vV.inc(1)
    for eng in ("tensor", "vector", "scalar"):
        mk.wait(eng, dC, load_done)
        mk.wait(eng, vV, const_done)

    acc_done = emit_ffn1(mk, bufs, sems, wap, xt_in, acc_dram)
    emit_accT_add(mk, bufs, sems, xt_in, acc_dram, acc_done)
    bufs["zsq"] = [mk.sb(f"zsq{i}", [128, NTOK], BF16) for i in range(2)]
    bufs["rows"] = mk.sb("rows", [1, 4096], F32)
    bufs["rows_bf"] = mk.sb("rows_bf", [1, 2048], BF16)
    emit_lnT(mk, bufs, sems, bufs["arena"])

    if final:
        emit_final_out(mk, bufs, sems, out_dram)
    else:
        final_v = vV.n
        mk.wait("sync", vV, final_v)
        mk.wait("sync", sems["sS"], sems["sS"].n)
        mk.wait("sync", tT, tT.n)
        xtv = bufs["arena"][:].rearrange("p (dc t) -> p dc t", dc=NDC, t=NTOK)
        for dc in range(NDC):
            mk.emit("sync", lambda e, dc=dc: e.dma_start(
                xt4_out[128 * dc:128 * dc + 128, :], xtv[:, dc, :]).then_inc(dS.h, 16))
        out_done = dS.inc(16 * NDC)
        mk.wait("sync", dS, out_done)
    mk.realize()
    return nc


def _shard_inputs_stage4(inputs, xt_by_core):
    eye = np.eye(128, dtype=np.float32).astype(bfloat16)
    com = {
        "RW": np.ascontiguousarray(np.asarray(inputs["L1_ffn_router_w"], np.float32)),
        "RB": np.ascontiguousarray(np.asarray(inputs["L1_ffn_router_b"], np.float32)),
        "RW1": np.ascontiguousarray(np.asarray(inputs["L1_ffn_routed_w1"], np.float32)),
        "RW3": np.ascontiguousarray(np.asarray(inputs["L1_ffn_routed_w3"], np.float32)),
        "RW2": np.ascontiguousarray(np.asarray(inputs["L1_ffn_routed_w2"], np.float32)),
        "SW1": np.ascontiguousarray(np.asarray(inputs["L1_ffn_shared_w1"], np.float32)),
        "SW3": np.ascontiguousarray(np.asarray(inputs["L1_ffn_shared_w3"], np.float32)),
        "SW2": np.ascontiguousarray(np.asarray(inputs["L1_ffn_shared_w2"], np.float32)),
        "EYE": eye,
    }
    return [dict(com, XT=np.ascontiguousarray(xt_by_core[c])) for c in range(len(xt_by_core))]


_PROG_CACHE = {}


def _get_prog(name, builder):
    if name not in _PROG_CACHE:
        _PROG_CACHE[name] = builder()
    return _PROG_CACHE[name]


def kernel(**inputs):
    return kernel_fused(**inputs)


def kernel_multilaunch(**inputs):
    inputs = {k: np.asarray(v) for k, v in inputs.items()}

    nc1 = _get_prog("s1", build_stage1)
    nc2 = _get_prog("s2", build_stage2)
    nc3 = _get_prog("s3", build_stage3)
    nc4 = _get_prog("s4f", lambda: build_stage4(final=True))

    cores = list(range(N_CORES))
    im1 = _shard_inputs_stage1(inputs)
    r1 = run_bass_kernel_spmd(nc1, im1, core_ids=cores)
    xt = [r1.results[c]["XT"] for c in range(N_CORES)]

    im2 = _shard_inputs_stage2(inputs, xt)
    r2 = run_bass_kernel_spmd(nc2, im2, core_ids=cores)
    xt = [r2.results[c]["XT2"] for c in range(N_CORES)]

    im3 = _shard_inputs_stage3(inputs, xt)
    r3 = run_bass_kernel_spmd(nc3, im3, core_ids=cores)
    xt = [r3.results[c]["XT3"] for c in range(N_CORES)]

    im4 = _shard_inputs_stage4(inputs, xt)
    r4 = run_bass_kernel_spmd(nc4, im4, core_ids=cores)
    out = np.concatenate([np.asarray(r4.results[c]["OUT"]) for c in range(N_CORES)], axis=0)
    return out


def build_fused():
    nc = bass.Bass("TRN2", target_bir_lowering=False)
    u = nc.dram_tensor("U", [BC, T, D], F32, kind="ExternalInput").ap()
    tstack = nc.dram_tensor("TSTACK", [128, 7, 128, 128], F32, kind="ExternalInput").ap()
    wap0 = {
        "w_red": nc.dram_tensor("WRED", [D, 16], F32, kind="ExternalInput").ap(),
        "wg": nc.dram_tensor("WG", [8, 16], F32, kind="ExternalInput").ap(),
        "w1": nc.dram_tensor("W1", [8, D, 4096], F32, kind="ExternalInput").ap(),
        "b1": nc.dram_tensor("B1", [8, 4096], F32, kind="ExternalInput").ap(),
        "w2": nc.dram_tensor("W2", [8, 4096, D], F32, kind="ExternalInput").ap(),
        "b2": nc.dram_tensor("B2", [8, D], F32, kind="ExternalInput").ap(),
    }
    wapA = {
        "w_qkv": nc.dram_tensor("WQKV", [3 * D, D], F32, kind="ExternalInput").ap(),
        "w_out": nc.dram_tensor("WOUT", [D, D], F32, kind="ExternalInput").ap(),
    }
    wap1 = {
        "router_w": nc.dram_tensor("RW", [D, 8], F32, kind="ExternalInput").ap(),
        "router_b": nc.dram_tensor("RB", [8], F32, kind="ExternalInput").ap(),
        "r_w1": nc.dram_tensor("RW1", [8, D, 512], F32, kind="ExternalInput").ap(),
        "r_w3": nc.dram_tensor("RW3", [8, D, 512], F32, kind="ExternalInput").ap(),
        "r_w2": nc.dram_tensor("RW2", [8, 512, D], F32, kind="ExternalInput").ap(),
        "s_w1": nc.dram_tensor("SW1", [2, D, 512], F32, kind="ExternalInput").ap(),
        "s_w3": nc.dram_tensor("SW3", [2, D, 512], F32, kind="ExternalInput").ap(),
        "s_w2": nc.dram_tensor("SW2", [2, 512, D], F32, kind="ExternalInput").ap(),
    }
    eye_in = nc.dram_tensor("EYE", [128, 128], BF16, kind="ExternalInput").ap()
    eyef_in = nc.dram_tensor("EYEF", [128, 128], F32, kind="ExternalInput").ap()
    ut_in = nc.dram_tensor("UT", [128, 128], BF16, kind="ExternalInput").ap()
    out_dram = nc.dram_tensor("OUT", [BC, T, D], F32, kind="ExternalOutput").ap()
    xt_dram = nc.dram_tensor("XTD", [D, NTOK], BF16).ap()
    acc_dram = nc.dram_tensor("ACCD", [NTOK, D], F32).ap()

    mk = MK(nc)
    bufs = {}
    bufs["arena"] = mk.sb("arena", [128, NTOK * NDC], BF16)
    bufs["xt"] = bufs["arena"]
    bufs["eye"] = mk.sb("eye", [128, 128], BF16)
    bufs["eyef"] = mk.sb("eyef", [128, 128], F32)
    bufs["ut"] = mk.sb("ut", [128, 128], BF16)
    bufs["onec"] = mk.sb("onec", [128, 1], BF16)
    bufs["ones1"] = mk.sb("ones1", [1, 128], BF16)
    bufs["onesA"] = mk.sb("onesA", [128, 128], BF16)
    ps_f32 = mk.ps("ps_f32", [128, 3072], F32)
    ps_bf = mk.ps("ps_bf", [128, 2048], BF16)
    bufs["ps_f32"] = ps_f32
    bufs["pg"] = [ps_f32[:, 0:512], ps_f32[:, 512:1024]]
    bufs["pstx"] = [ps_f32[:, 0:512], ps_f32[:, 512:1024]]
    bufs["psts"] = [ps_f32[:, 1024:1536], ps_f32[:, 1536:2048]]
    bufs["pbc"] = [ps_f32[:, 2048:2560], ps_f32[:, 2560:3072]]
    bufs["pt"] = [ps_bf[:, 0:512], ps_bf[:, 1024:1536]]

    sems = {n: mk.sem(n) for n in ("dW", "dS", "dA", "tT", "vV", "sS")}
    dW, dS, dA, tT, vV, sS = (sems[n] for n in ("dW", "dS", "dA", "tT", "vV", "sS"))

    dC = mk.sem("dC")
    for nm, dst in (("eye", eye_in), ("eyef", eyef_in), ("ut", ut_in)):
        mk.emit("gpsimd", lambda e, nm=nm, dst=dst: e.dma_start(
            bufs[nm][:], dst).then_inc(dC.h, 16))
    load_done = dC.inc(48)
    for nm in ("onec", "ones1", "onesA"):
        last = nm == "onesA"
        mk.emit("vector", lambda e, nm=nm, l=last: (e.memset(
            bufs[nm][:], 1.0).then_inc(vV.h, 1) if l else e.memset(bufs[nm][:], 1.0)))
    const_done = vV.inc(1)
    for eng in ("tensor", "vector", "scalar"):
        mk.wait(eng, dC, load_done)
        mk.wait(eng, vV, const_done)

    def spill_xt():
        n = vV.n
        mk.wait("sync", vV, n)
        xtv = bufs["arena"][:].rearrange("p (dc t) -> p dc t", dc=NDC, t=NTOK)
        mk.emit("sync", lambda e: e.dma_start(
            xt_dram.rearrange("(dc p) t -> p dc t", p=128), xtv).then_inc(dS.h, 16))
        dS.inc(16)
        mk.barrier()

    def ln_pass():
        lsc = contextlib.ExitStack()
        bufs["zsq"] = [lsc.enter_context(nc.sbuf_tensor(f"zsq{mk.uid()}", [128, NTOK], BF16))
                       for _ in range(2)]
        bufs["rows"] = lsc.enter_context(nc.sbuf_tensor(f"rows{mk.uid()}", [1, 4096], F32))
        bufs["rows_bf"] = lsc.enter_context(nc.sbuf_tensor(f"rowsb{mk.uid()}", [1, 2048], BF16))
        emit_lnT(mk, bufs, sems, bufs["arena"])
        mk.barrier()
        lsc.close()

    s1 = contextlib.ExitStack()
    bufs["ubf"] = s1.enter_context(nc.sbuf_tensor("ubf", [128, NTOK * NDC], BF16))
    bufs["tsb"] = [s1.enter_context(nc.sbuf_tensor(f"tsb{i}", [128, 896], BF16)) for i in range(2)]
    dU = mk.sem("dU")
    mk.emit("gpsimd", lambda e: e.dma_start(
        bufs["ubf"][:].rearrange("p (b i d) -> p b i d", b=BC, i=16, d=D),
        u.rearrange("b (i p) d -> p b i d", p=128)).then_inc(dU.h, 16))
    u_done = dU.inc(16)
    mk.wait("tensor", dU, u_done)
    mk.wait("vector", dU, u_done)
    emit_conv_block(mk, bufs, sems, tstack)
    emit_transpose_nat_to_T(mk, bufs, sems, bufs["ubf"], bufs["arena"])
    mk.barrier()
    s1.close()
    ln_pass()
    spill_xt()

    s2 = contextlib.ExitStack()
    bufs["W1sb"] = s2.enter_context(nc.sbuf_tensor("W1sb", [128, 32768], BF16))
    bufs["W2sb"] = s2.enter_context(nc.sbuf_tensor("W2sb", [128, 32768], BF16))
    acc_done = emit_ffn0(mk, bufs, sems, wap0, xt_dram, acc_dram)
    mk.barrier()
    s2.close()
    emit_accT_add(mk, bufs, sems, xt_dram, acc_dram, acc_done)
    mk.barrier()
    ln_pass()
    spill_xt()

    emit_attn(mk, bufs, sems, wapA, xt_dram)
    mk.barrier()
    ln_pass()
    spill_xt()

    acc_done = emit_ffn1(mk, bufs, sems, wap1, xt_dram, acc_dram)
    mk.barrier()
    emit_accT_add(mk, bufs, sems, xt_dram, acc_dram, acc_done)
    mk.barrier()
    ln_pass()
    emit_final_out(mk, bufs, sems, out_dram)
    mk.realize()
    return nc


def _shard_inputs_fused(inputs):
    tstack = _conv_tstack(inputs["L0_attn_diff_kernel"], inputs["L0_attn_ma_r_kernel"])
    eye = np.eye(128, dtype=np.float32).astype(bfloat16)
    eyef = np.eye(128, dtype=np.float32)
    ut = np.triu(np.ones((128, 128), dtype=np.float32)).astype(bfloat16)
    f32c = lambda k: np.ascontiguousarray(np.asarray(inputs[k], np.float32))
    com = {
        "TSTACK": tstack, "EYE": eye, "EYEF": eyef, "UT": ut,
        "WRED": f32c("L0_ffn_w_red"), "WG": f32c("L0_ffn_wg"),
        "W1": f32c("L0_ffn_w1"), "B1": f32c("L0_ffn_b1"),
        "W2": f32c("L0_ffn_w2"), "B2": f32c("L0_ffn_b2"),
        "WQKV": f32c("L1_attn_w_qkv"), "WOUT": f32c("L1_attn_w_out"),
        "RW": f32c("L1_ffn_router_w"), "RB": f32c("L1_ffn_router_b"),
        "RW1": f32c("L1_ffn_routed_w1"), "RW3": f32c("L1_ffn_routed_w3"),
        "RW2": f32c("L1_ffn_routed_w2"), "SW1": f32c("L1_ffn_shared_w1"),
        "SW3": f32c("L1_ffn_shared_w3"), "SW2": f32c("L1_ffn_shared_w2"),
    }
    un = np.ascontiguousarray(np.asarray(inputs["L0_attn_u"], np.float32))
    return [dict(com, U=np.ascontiguousarray(un[2 * c:2 * c + 2])) for c in range(N_CORES)]


def kernel_fused(**inputs):
    inputs = {k: np.asarray(v) for k, v in inputs.items()}
    nc = _get_prog("fused", build_fused)
    im = _shard_inputs_fused(inputs)
    r = run_bass_kernel_spmd(nc, im, core_ids=list(range(N_CORES)))
    return np.concatenate([np.asarray(r.results[c]["OUT"]) for c in range(N_CORES)], axis=0)


# version 06
# speedup vs baseline: 1.2316x; 1.1424x over previous
import contextlib
import numpy as np
import ml_dtypes

import concourse.bass as bass
import concourse.mybir as mybir
import concourse.bass_utils as _bu
from concourse.bass_utils import run_bass_kernel_spmd

if not getattr(_bu, "_nobirsim_patch", False):
    _orig_run_command = _bu.run_command

    def _run_command_nobirsim(cmd, cwd=None, **kw):
        cmd = ["--enable-birsim=false" if c == "--enable-birsim=true" else c for c in cmd]
        cmd = ["--enable-ldw-opt=true" if c == "--enable-ldw-opt=false" else c for c in cmd]
        return _orig_run_command(cmd, cwd=cwd, **kw)

    _bu.run_command = _run_command_nobirsim
    _bu._nobirsim_patch = True

F32 = mybir.dt.float32
BF16 = mybir.dt.bfloat16
AF = mybir.ActivationFunctionType
ALU = mybir.AluOpType
AXL = mybir.AxisListType
bfloat16 = ml_dtypes.bfloat16

N_CORES = 8
B = 16
BC = 2
T = 2048
D = 1024
NTOK = BC * T
NDC = D // 128
NTT = NTOK // 128


def _conv_tstack(diff_kernel, ma_r_kernel):
    K128 = np.zeros((128, 720), dtype=np.float32)
    dk = np.asarray(diff_kernel, dtype=np.float32)
    mk = np.asarray(ma_r_kernel, dtype=np.float32)
    for i in range(16):
        for j in range(8):
            r = 8 * i + j
            if j < 4:
                K128[r, :4] = dk[4 * i + j]
            else:
                K128[r, :720] = mk[4 * i + (j - 4)]
    pad = np.zeros((128, 1152), dtype=np.float32)
    pad[:, 128:848] = K128
    base = pad[:, 128:]
    st_g, st_e = base.strides
    ts = np.lib.stride_tricks.as_strided(
        base, shape=(128, 7, 128, 128), strides=(st_g, 128 * st_e, -st_e, st_e))
    return np.ascontiguousarray(ts)



class Ctr:

    def __init__(self, handle):
        self.h = handle
        self.n = 0

    def inc(self, k=1):
        self.n += k
        return self.n


class MK:

    ENGINES = ("tensor", "vector", "scalar", "gpsimd", "sync")

    def __init__(self, nc):
        self.nc = nc
        self.es = contextlib.ExitStack()
        self.ops = {e: [] for e in self.ENGINES}
        self.sems = {}
        self._uid = 0

    def uid(self):
        self._uid += 1
        return self._uid

    def sem(self, name):
        if name in self.sems:
            return self.sems[name]
        c = Ctr(self.es.enter_context(self.nc.semaphore(name)))
        self.sems[name] = c
        return c

    def sb(self, name, shape, dtype):
        return self.es.enter_context(self.nc.sbuf_tensor(name, shape, dtype))

    def ps(self, name, shape, dtype):
        return self.es.enter_context(self.nc.psum_tensor(name, shape, dtype))

    def emit(self, engine, fn):
        self.ops[engine].append(fn)

    def wait(self, engine, ctr, val):
        if val > 0:
            self.emit(engine, lambda eng, c=ctr, v=val: eng.wait_ge(c.h, v))

    def barrier(self):
        snap = [(c, c.n) for c in self.sems.values() if c.n > 0]
        for eng in self.ENGINES:
            for c, v in snap:
                self.wait(eng, c, v)

    def realize(self):
        with self.es, self.nc.Block() as block:
            for name in self.ENGINES:
                deco = getattr(block, name)
                ops = self.ops[name]

                def section(eng, ops=ops):
                    for fn in ops:
                        fn(eng)

                deco(section)
        return self.nc



def emit_conv_block(mk, bufs, sems, tstack_ap):
    nc = mk.nc
    ubf, tsb, pgs = bufs["ubf"], bufs["tsb"], bufs["pg"]
    dW, tT, vV = sems["dW"], sems["tT"], sems["vV"]
    dT = [mk.sem("dT0"), mk.sem("dT1")]
    u4 = ubf[:].rearrange("p (b i d) -> p b i d", b=BC, i=16, d=D)
    dma_done = [0, 0]
    mm_done = [0, 0]
    ev_done = [0, 0]
    for g in range(128):
        bf = g % 2
        mk.wait("gpsimd", tT, mm_done[bf])
        mk.emit("gpsimd", lambda e, b=bf, g=g: e.dma_start(
            tsb[b][:].rearrange("s (d t) -> s d t", d=7, t=128),
            tstack_ap[g].rearrange("d s t -> s d t")).then_inc(dT[b].h, 16))
        dma_done[bf] = dT[bf].inc(16)
        mk.wait("tensor", dT[bf], dma_done[bf])
        mk.wait("tensor", vV, ev_done[bf])
        pview = pgs[bf][:, 0:256].rearrange("p (b i c) -> p b i c", b=BC, i=16, c=8)
        for d in range(7):
            for bb in range(BC):
                out_ap = pview[:, bb, d:16, :]
                rhs_ap = u4[:, bb, 0:16 - d, 8 * g:8 * g + 8]
                last = d == 6 and bb == BC - 1
                st = d == 0 and bb == 0
                mk.emit("tensor", lambda e, b=bf, d=d, o=out_ap, r=rhs_ap, st=st, l=last: (e.matmul(
                    o, tsb[b][:, 128 * d:128 * d + 128], r,
                    start=st, stop=l, skip_group_check=True).then_inc(tT.h, 1) if l else e.matmul(
                    o, tsb[b][:, 128 * d:128 * d + 128], r, start=st, stop=False,
                    skip_group_check=True)))
        mm_done[bf] = tT.inc(1)
        cols = u4[:, :, :, 8 * g:8 * g + 8]
        mk.wait("vector", tT, mm_done[bf])
        mk.emit("vector", lambda e, b=bf, c=cols: e.tensor_tensor(
            c, pgs[b][:, 0:256].rearrange("p (b i c) -> p b i c", b=BC, i=16, c=8),
            c, ALU.add).then_inc(vV.h, 1))
        ev_done[bf] = vV.inc(1)


def emit_transpose_nat_to_T(mk, bufs, sems, src, dst, src_dtype_bf=True):
    eye = bufs["eye"]
    pt = bufs["pt"]
    tT, vV = sems["tT"], sems["vV"]
    mk.wait("tensor", vV, vV.n)
    ev_done = [0, 0]
    for tau in range(NTT):
        for half in range(2):
            bf = (tau * 2 + half) % 2
            mk.wait("tensor", vV, ev_done[bf])
            for q in range(4):
                dc = half * 4 + q
                src_ap = src[:, tau * 1024 + dc * 128: tau * 1024 + dc * 128 + 128]
                last = q == 3
                mk.emit("tensor", lambda e, b=bf, q=q, s=src_ap, l=last:
                        e.transpose(pt[b][:, 128 * q:128 * q + 128], s, eye[:]).then_inc(tT.h, 1)
                        if l else
                        e.transpose(pt[b][:, 128 * q:128 * q + 128], s, eye[:]))
            done = tT.inc(1)
            dview = dst[:].rearrange("p (dc t) -> p dc t", dc=NDC, t=NTOK)
            dst_ap = dview[:, half * 4:half * 4 + 4, tau * 128:tau * 128 + 128]
            src_p = pt[bf][:].rearrange("p (q t) -> p q t", q=4, t=128)
            mk.wait("vector", tT, done)
            mk.emit("vector", lambda e, d=dst_ap, s=src_p: e.tensor_copy(d, s).then_inc(vV.h, 1))
            ev_done[bf] = vV.inc(1)


def emit_lnT(mk, bufs, sems, xt, n_d=D, eps=1e-5):
    tT, vV, sS = sems["tT"], sems["vV"], sems["sS"]
    onec, ones1 = bufs["onec"], bufs["ones1"]
    zsq = bufs["zsq"]
    rows = bufs["rows"]
    rows_bf = bufs["rows_bf"]
    pstx, psts, pbc = bufs["pstx"], bufs["psts"], bufs["pbc"]
    xv = xt[:].rearrange("p (dc t) -> p dc t", dc=NDC, t=NTOK)

    mk.wait("scalar", vV, vV.n)
    mk.wait("tensor", vV, vV.n)

    mm_done = [0, 0]
    rows_done = [0, 0]
    app_done = [0, 0]
    for sl in range(8):
        zb = sl % 2
        tok = slice(512 * sl, 512 * (sl + 1))
        ro = 2048 * zb
        rbo = 1024 * zb
        sumx = rows[0:1, ro:ro + 512]
        var = rows[0:1, ro + 512:ro + 1024]
        t2 = rows[0:1, ro + 1024:ro + 1536]
        rr = rows[0:1, ro + 1536:ro + 2048]
        m_bf = rows_bf[0:1, rbo:rbo + 512]
        r_bf = rows_bf[0:1, rbo + 512:rbo + 1024]
        mk.wait("scalar", tT, mm_done[zb])
        for dc in range(NDC):
            last = dc == NDC - 1
            mk.emit("scalar", lambda e, dc=dc, tok=tok, z=zb, l=last: (e.activation(
                zsq[z][:, 512 * dc:512 * dc + 512], xv[:, dc, tok], AF.Square
            ).then_inc(sS.h, 1) if l else e.activation(
                zsq[z][:, 512 * dc:512 * dc + 512], xv[:, dc, tok], AF.Square)))
        sq_ready = sS.inc(1)
        mk.wait("tensor", sS, sq_ready)
        mk.wait("tensor", vV, rows_done[zb])
        for dc in range(NDC):
            mk.emit("tensor", lambda e, dc=dc, tok=tok, z=zb: e.matmul(
                pstx[z][0:1, :], onec[:], xv[:, dc, tok],
                start=(dc == 0), stop=(dc == NDC - 1)))
            last = dc == NDC - 1
            mk.emit("tensor", lambda e, dc=dc, z=zb, l=last: (e.matmul(
                psts[z][0:1, :], onec[:], zsq[z][:, 512 * dc:512 * dc + 512],
                start=(dc == 0), stop=(dc == NDC - 1)
            ).then_inc(tT.h, 1) if l else e.matmul(
                psts[z][0:1, :], onec[:], zsq[z][:, 512 * dc:512 * dc + 512],
                start=(dc == 0), stop=(dc == NDC - 1))))
        mm_done[zb] = tT.inc(1)
        mk.wait("vector", tT, mm_done[zb])
        steps = [
            ("vector", lambda e, a=sumx, b=pstx[zb]: e.tensor_scalar(a, b[0:1, :], 1.0 / n_d, None, ALU.mult)),
            ("vector", lambda e, a=var, b=psts[zb]: e.tensor_scalar(a, b[0:1, :], 1.0 / n_d, eps, ALU.mult, ALU.add)),
            ("vector", lambda e, a=t2, b=sumx: e.tensor_tensor(a, b, b, ALU.mult)),
            ("vector", lambda e, a=var, b=t2: e.tensor_tensor(a, a, b, ALU.subtract)),
            ("vector", lambda e, a=t2, b=var: e.reciprocal(a, b)),
            ("scalar", lambda e, a=rr, b=t2: e.activation(a, b, AF.Sqrt)),
            ("vector", lambda e, a=m_bf, b=sumx: e.tensor_copy(a, b)),
            ("vector", lambda e, a=r_bf, b=rr: e.tensor_copy(a, b)),
        ]
        prev = None
        for eng, fn in steps:
            if prev is not None:
                pc, pv = prev
                mk.wait(eng, pc, pv)
            ctr = sS if eng == "scalar" else vV
            mk.emit(eng, lambda e, fn=fn, c=ctr: fn(e).then_inc(c.h, 1))
            prev = (ctr, ctr.inc(1))
        rows_done[zb] = vV.n
        mk.wait("tensor", prev[0], prev[1])
        mk.wait("tensor", vV, max(app_done))
        mk.emit("tensor", lambda e, m=m_bf: e.matmul(
            pbc[0][:], ones1[:], m, start=True, stop=True))
        mk.emit("tensor", lambda e, r=r_bf: e.matmul(
            pbc[1][:], ones1[:], r, start=True, stop=True).then_inc(tT.h, 1))
        bc_done = tT.inc(1)
        mk.wait("vector", tT, bc_done)
        for dc in range(NDC):
            last = dc == NDC - 1
            mk.emit("vector", lambda e, dc=dc, tok=tok, l=last: (e.tensor_tensor(
                xv[:, dc, tok], xv[:, dc, tok], pbc[0][:], ALU.subtract
            ).then_inc(vV.h, 1) if l else e.tensor_tensor(
                xv[:, dc, tok], xv[:, dc, tok], pbc[0][:], ALU.subtract)))
        mk.wait("vector", vV, vV.inc(1))
        for dc in range(NDC):
            last = dc == NDC - 1
            mk.emit("vector", lambda e, dc=dc, tok=tok, l=last: (e.tensor_tensor(
                xv[:, dc, tok], xv[:, dc, tok], pbc[1][:], ALU.mult
            ).then_inc(vV.h, 1) if l else e.tensor_tensor(
                xv[:, dc, tok], xv[:, dc, tok], pbc[1][:], ALU.mult)))
        app_done[zb] = vV.inc(1)
    bufs["ln_exit"] = max(app_done)


def emit_ffn0(mk, bufs, sems, wap, xt_dram, acc_dram, n_experts=8, do_experts=True):
    nc = mk.nc
    tT, vV, sS, dW, dA = (sems[n] for n in ("tT", "vV", "sS", "dW", "dA"))
    dAz = [mk.sem("dA0"), mk.sem("dA1")]
    dX1, dX2 = mk.sem("dX1"), mk.sem("dX2")
    dxc = [mk.sem("dxc0"), mk.sem("dxc1")]
    dxw = [mk.sem("dxw0"), mk.sem("dxw1")]
    W1sb, W2sb = bufs["W1sb"], bufs["W2sb"]
    ar = bufs["arena"]
    ones1, eyef, onesA = bufs["ones1"], bufs["eyef"], bufs["onesA"]
    ps = bufs["ps_f32"]
    h_sb = ar[:, 0:16384]
    xwin = [ar[:, 16384:20480], ar[:, 20480:24576]]
    tmp = [ar[:, 24576:26624].bitcast(F32), ar[:, 30720:32768].bitcast(F32)]
    msel = ar[:, 26624:27136].bitcast(F32)
    b1T = ar[:, 27136:27648].bitcast(F32)
    b2q = ar[:, 27648:30720]
    xc = [ar[:, 0:1024].bitcast(F32), ar[:, 1024:2048].bitcast(F32)]
    lgs = ar[:, 2048:2560].bitcast(F32)
    e8 = ar[:, 2560:3072].bitcast(F32)
    mx = ar[:, 3072:3136].bitcast(F32)
    ssum = ar[:, 3136:3200].bitcast(F32)
    b1s = ar[0:8, 4096:12288].bitcast(F32)
    red_sb = W2sb[:, 0:8192].bitcast(F32)
    ph = [ps[:, 0:512], ps[:, 512:1024]]
    py = [ps[:, 1024:2048], ps[:, 2048:3072]]
    p_red = ps[0:16, 0:512]
    pl = ps[:, 512:768]
    pb1 = ps[:, 1024:1032]
    xdv = xt_dram.rearrange("(dc p) t -> p dc t", p=128)

    rsc = contextlib.ExitStack()
    wred = rsc.enter_context(nc.sbuf_tensor("wred", [128, 128], F32))
    wg_sb = rsc.enter_context(nc.sbuf_tensor("wg_sb", [8, 16], F32))
    wgt = rsc.enter_context(nc.sbuf_tensor("wgt", [8, 20], F32))
    wgnT = rsc.enter_context(nc.sbuf_tensor("wgnT", [16, 8], F32))

    mk.emit("gpsimd", lambda e: e.dma_start(
        wred[:].rearrange("p (dc j) -> p dc j", dc=NDC, j=16),
        wap["w_red"].rearrange("(dc p) j -> p dc j", p=128)).then_inc(dW.h, 16))
    mk.emit("gpsimd", lambda e: e.dma_start(wg_sb[:], wap["wg"]).then_inc(dW.h, 16))
    for ei in range(8):
        mk.emit("gpsimd", lambda e, ei=ei: e.dma_start(
            b2q[32 * (ei // 3):32 * (ei // 3) + 1, 1024 * (ei % 3):1024 * (ei % 3) + 1024],
            wap["b2"][ei:ei + 1, :]).then_inc(dW.h, 16))
    mk.emit("gpsimd", lambda e: e.dma_start(b1s[:], wap["b1"]).then_inc(dW.h, 16))
    stg_done = dW.inc(176)
    mk.emit("gpsimd", lambda e: e.dma_start(
        W1sb[:].rearrange("p (dc f) -> p dc f", dc=NDC, f=4096),
        wap["w1"][0].rearrange("(dc p) f -> p dc f", p=128)).then_inc(dX1.h, 16))
    w1_done = dX1.inc(16)

    mk.wait("scalar", dW, stg_done)
    chain = [
        ("scalar", lambda e: e.activation(wgt[:, 0:16], wg_sb[:], AF.Square)),
        ("vector", lambda e: e.reduce_sum(wgt[:, 16:17], wgt[:, 0:16], axis=AXL.X)),
        ("scalar", lambda e: e.activation(wgt[:, 17:18], wgt[:, 16:17], AF.Sqrt)),
        ("vector", lambda e: e.tensor_scalar_max(wgt[:, 18:19], wgt[:, 17:18], 1e-4)),
        ("vector", lambda e: e.reciprocal(wgt[:, 19:20], wgt[:, 18:19])),
        ("vector", lambda e: e.tensor_scalar(wg_sb[:], wg_sb[:], wgt[:, 19:20], None, ALU.mult)),
    ]
    prev = None
    for eng, fn in chain:
        if prev is not None:
            mk.wait(eng, prev[0], prev[1])
        c = sS if eng == "scalar" else vV
        mk.emit(eng, lambda e, fn=fn, c=c: fn(e).then_inc(c.h, 1))
        prev = (c, c.inc(1))
    mk.wait("tensor", prev[0], prev[1])
    mk.emit("tensor", lambda e: e.transpose(
        ps[0:16, 2048:2056], wg_sb[:], eyef[0:8, 0:8]).then_inc(tT.h, 1))
    wgt_done = tT.inc(1)
    mk.wait("vector", tT, wgt_done)
    mk.emit("vector", lambda e: e.tensor_copy(wgnT[:], ps[0:16, 2048:2056]).then_inc(vV.h, 1))
    wgnT_done = vV.inc(1)

    mk.wait("tensor", dW, stg_done)
    cp_done = {}
    for fc in range(32):
        if fc >= 1:
            mk.wait("tensor", vV, cp_done[fc - 1])
        mk.emit("tensor", lambda e, fc=fc: e.transpose(
            pb1, b1s[0:8, 128 * fc:128 * fc + 128], eyef[0:8, 0:8]).then_inc(tT.h, 1))
        td = tT.inc(1)
        mk.wait("vector", tT, td)
        mk.emit("vector", lambda e, fc=fc: e.tensor_copy(
            b1T[:, 8 * fc:8 * fc + 8], pb1).then_inc(vV.h, 1))
        cp_done[fc] = vV.inc(1)
    b1T_done = cp_done[31]

    wv = wred[:].rearrange("p (dc j) -> p dc j", dc=NDC, j=16)
    red_ev = {}
    mm_done = {}
    xc_done = {}
    for sl in range(8):
        for dc in range(NDC):
            z = dc % 2
            k = sl * NDC + dc
            mk.wait("gpsimd", tT, mm_done.get(k - 2, 0))
            mk.emit("gpsimd", lambda e, dc=dc, sl=sl, z=z: e.dma_start(
                xc[z][:], xdv[:, dc, 512 * sl:512 * sl + 512]).then_inc(dxc[z].h, 16))
            xc_done[k] = dxc[z].inc(16)
            mk.wait("tensor", dxc[z], xc_done[k])
            if dc == 0 and sl > 0:
                mk.wait("tensor", vV, red_ev[sl - 1])
            mk.emit("tensor", lambda e, dc=dc, z=z: e.matmul(
                p_red, wv[:, dc, :], xc[z][:],
                start=(dc == 0), stop=(dc == NDC - 1)).then_inc(tT.h, 1))
            mm_done[k] = tT.inc(1)
        mk.wait("vector", tT, mm_done[sl * NDC + NDC - 1])
        mk.emit("vector", lambda e, sl=sl: e.tensor_copy(
            red_sb[0:16, 512 * sl:512 * sl + 512], p_red).then_inc(vV.h, 1))
        red_ev[sl] = vV.inc(1)

    mk.wait("tensor", vV, red_ev[7])
    mk.wait("tensor", vV, wgnT_done)
    for tg in range(NTT):
        last = tg == NTT - 1
        mk.emit("tensor", lambda e, tg=tg, l=last: (e.matmul(
            pl[:, 8 * tg:8 * tg + 8], red_sb[0:16, 128 * tg:128 * tg + 128], wgnT[:],
            start=True, stop=True).then_inc(tT.h, 1) if l else e.matmul(
            pl[:, 8 * tg:8 * tg + 8], red_sb[0:16, 128 * tg:128 * tg + 128], wgnT[:],
            start=True, stop=True)))
    lg_done = tT.inc(1)
    plv = pl.rearrange("p (g e) -> p g e", g=NTT, e=8)
    mxv = mx.rearrange("p (g o) -> p g o", g=NTT, o=1)
    ep = [
        ("vector", lambda e: e.reduce_max(mxv, plv, axis=AXL.X)),
        ("vector", lambda e: e.tensor_tensor(
            lgs.rearrange("p (g e) -> p g e", g=NTT, e=8), plv,
            mxv.broadcast_to((128, NTT, 8)), ALU.subtract)),
        ("scalar", lambda e: e.activation(e8, lgs, AF.Exp)),
        ("vector", lambda e: e.reduce_sum(
            ssum.rearrange("p (g o) -> p g o", g=NTT, o=1),
            e8.rearrange("p (g e) -> p g e", g=NTT, e=8), axis=AXL.X)),
        ("vector", lambda e: e.reciprocal(mx, ssum)),
        ("vector", lambda e: e.tensor_single_scalar(lgs, e8, 1.0, ALU.is_ge)),
        ("vector", lambda e: e.tensor_tensor(
            msel.rearrange("p (g e) -> p g e", g=NTT, e=8),
            lgs.rearrange("p (g e) -> p g e", g=NTT, e=8),
            mxv.broadcast_to((128, NTT, 8)), ALU.mult)),
    ]
    prev = (tT, lg_done)
    for eng, fn in ep:
        mk.wait(eng, prev[0], prev[1])
        c = sS if eng == "scalar" else vV
        mk.emit(eng, lambda e, fn=fn, c=c: fn(e).then_inc(c.h, 1))
        prev = (c, c.inc(1))
    msel_done = prev[1]
    router_done_v = vV.n

    if not do_experts:
        rsc.close()
        return 0
    w2_done = 0
    phz_last = {0: 0, 1: 0}
    ev_cnt = {}
    dma_cnt = {}
    phB_prev_win = 0
    phA_all_done = 0
    h_free_at = 0
    xw_done = {}
    xw_read_z = {0: 0, 1: 0}
    for e_ix in range(n_experts):
        mk.wait("gpsimd", tT, phB_prev_win if e_ix > 0 else lg_done)
        mk.emit("gpsimd", lambda e, ei=e_ix: e.dma_start(
            W2sb[:].rearrange("p (fc d) -> p fc d", fc=32, d=1024),
            wap["w2"][ei].rearrange("(fc p) d -> p fc d", p=128)).then_inc(dX2.h, 16))
        w2_done = dX2.inc(16)
        w1_this = w1_done
        mk.wait("tensor", dX1, w1_this)
        if e_ix == 0:
            mk.wait("tensor", vV, router_done_v)
            mk.wait("tensor", vV, b1T_done)
        for w in range(8):
            wz = w % 2
            mk.wait("gpsimd", tT, xw_read_z[wz])
            mk.emit("gpsimd", lambda e, w=w, wz=wz: e.dma_start(
                xwin[wz][:].rearrange("p (dc t) -> p dc t", dc=NDC, t=512),
                xdv[:, :, 512 * w:512 * w + 512]).then_inc(dxw[wz].h, 16))
            xw_done[(e_ix, w)] = dxw[wz].inc(16)
            xv = xwin[wz][:].rearrange("p (dc t) -> p dc t", dc=NDC, t=512)
            mk.wait("tensor", dxw[wz], xw_done[(e_ix, w)])
            for fc in range(32):
                z = fc % 2
                mk.wait("tensor", sS, phz_last[z])
                for dc in range(NDC):
                    last = dc == NDC - 1
                    mk.emit("tensor", lambda e, dc=dc, fc=fc, z=z, xv=xv, l=last: (e.matmul(
                        ph[z], W1sb[:, 4096 * dc + 128 * fc:4096 * dc + 128 * fc + 128],
                        xv[:, dc, :], start=(dc == 0), stop=(dc == NDC - 1)
                    ).then_inc(tT.h, 1) if l else e.matmul(
                        ph[z], W1sb[:, 4096 * dc + 128 * fc:4096 * dc + 128 * fc + 128],
                        xv[:, dc, :], start=(dc == 0), stop=(dc == NDC - 1))))
                mmA = tT.inc(1)
                mk.wait("scalar", tT, mmA)
                if fc == 0:
                    mk.wait("scalar", tT, h_free_at)
                mk.emit("scalar", lambda e, fc=fc, z=z, ei=e_ix: e.activation(
                    h_sb[:, 512 * fc:512 * fc + 512], ph[z], AF.Gelu,
                    bias=b1T[:, 8 * fc + ei:8 * fc + ei + 1]).then_inc(sS.h, 1))
                phz_last[z] = sS.inc(1)
            phA_all_done = tT.n
            xw_read_z[wz] = tT.n
            mk.wait("tensor", sS, max(phz_last.values()))
            mk.wait("tensor", dX2, w2_done)
            for th in range(4):
                tg = 4 * w + th
                pz = tg % 2
                mk.wait("tensor", vV, ev_cnt.get(tg - 2, 0))
                if e_ix == 0 and tg <= 1:
                    mk.wait("tensor", vV, router_done_v)
                for fc in range(32):
                    for hh in range(2):
                        mk.emit("tensor", lambda e, fc=fc, th=th, hh=hh, pz=pz: e.matmul(
                            py[pz][:, 512 * hh:512 * hh + 512],
                            h_sb[:, 512 * fc + 128 * th:512 * fc + 128 * th + 128],
                            W2sb[:, 1024 * fc + 512 * hh:1024 * fc + 512 * hh + 512],
                            start=(fc == 0), stop=False))
                b2r = b2q[32 * (e_ix // 3):32 * (e_ix // 3) + 1,
                          1024 * (e_ix % 3):1024 * (e_ix % 3) + 1024]
                brow = 32 * (e_ix // 3)
                for hh in range(2):
                    last = hh == 1
                    mk.emit("tensor", lambda e, hh=hh, pz=pz, r=b2r, br=brow, l=last: (e.matmul(
                        py[pz][:, 512 * hh:512 * hh + 512], onesA[br:br + 1, :],
                        r[:, 512 * hh:512 * hh + 512],
                        start=False, stop=True).then_inc(tT.h, 1) if l else e.matmul(
                        py[pz][:, 512 * hh:512 * hh + 512], onesA[br:br + 1, :],
                        r[:, 512 * hh:512 * hh + 512], start=False, stop=True)))
                mmB = tT.inc(1)
                az = tg % 2
                pk = tg - 2 if tg >= 2 else tg + 30
                mk.wait("vector", tT, mmB)
                mk.wait("vector", dAz[az], dma_cnt.get(pk, 0))
                if e_ix == 0 and tg == 0:
                    mk.wait("vector", vV, msel_done)
                mk.emit("vector", lambda e, tg=tg, ei=e_ix, pz=pz, az=az: e.tensor_scalar(
                    tmp[az], py[pz][:], msel[:, 8 * tg + ei:8 * tg + ei + 1],
                    None, ALU.mult).then_inc(vV.h, 1))
                ev_cnt[tg] = vV.inc(1)
                mk.wait("gpsimd", vV, ev_cnt[tg])
                op = ALU.bypass if e_ix == 0 else ALU.add
                mk.emit("gpsimd", lambda e, tg=tg, op=op, az=az: e.dma_start(
                    acc_dram[128 * tg:128 * tg + 128, :], tmp[az],
                    accum_op=op).then_inc(dAz[az].h, 16))
                dma_cnt[tg] = dAz[az].inc(16)
            phB_prev_win = tT.n
            h_free_at = tT.n
        if e_ix < n_experts - 1:
            mk.wait("gpsimd", tT, phA_all_done)
            mk.emit("gpsimd", lambda e, ei=e_ix + 1: e.dma_start(
                W1sb[:].rearrange("p (dc f) -> p dc f", dc=NDC, f=4096),
                wap["w1"][ei].rearrange("(dc p) f -> p dc f", p=128)).then_inc(dX1.h, 16))
            w1_done = dX1.inc(16)
    rsc.close()
    return (dAz[0].n, dAz[1].n)


def emit_accT_add(mk, bufs, sems, xt_dram, acc_dram, acc_dma_done):
    if isinstance(acc_dma_done, tuple):
        dAz = [mk.sem("dA0"), mk.sem("dA1")]
        mk.wait("gpsimd", dAz[0], acc_dma_done[0])
        mk.wait("gpsimd", dAz[1], acc_dma_done[1])
        acc_dma_done = 0
    """x2pre^T = x1T (from xt_dram) + acc (from acc_dram, natural, transposed
    on the fly) written into bufs["arena"] (becomes the residual stream xt)."""
    nc = mk.nc
    tT, vV, dS = (sems[n] for n in ("tT", "vV", "dS"))
    xt, eye, pt = bufs["arena"], bufs["eye"], bufs["pt"]
    dL = [mk.sem("dL0"), mk.sem("dL1")]
    xv = xt[:].rearrange("p (dc t) -> p dc t", dc=NDC, t=NTOK)
    xdv = xt_dram.rearrange("(dc p) t -> p dc t", p=128)
    asc = contextlib.ExitStack()
    accl = [asc.enter_context(nc.sbuf_tensor(f"accl{mk.uid()}_{i}", [128, 1024], F32)) for i in range(2)]
    accb = [asc.enter_context(nc.sbuf_tensor(f"accb{mk.uid()}_{i}", [128, 1024], BF16)) for i in range(2)]
    x1w = [asc.enter_context(nc.sbuf_tensor(f"x1w{mk.uid()}_{i}", [128, 1024], BF16)) for i in range(2)]
    mk.wait("gpsimd", sems["dA"], acc_dma_done)
    ld = {}
    cb = {}
    add_done = {}
    tr_done = {}
    for tg in range(NTT):
        z = tg % 2
        mk.wait("gpsimd", vV, cb.get(tg - 2, 0))
        mk.emit("gpsimd", lambda e, tg=tg, z=z: e.dma_start(
            accl[z][:], acc_dram[128 * tg:128 * tg + 128, :]).then_inc(dL[z].h, 16))
        mk.wait("gpsimd", vV, add_done.get(tg - 2, 0))
        mk.emit("gpsimd", lambda e, tg=tg, z=z: e.dma_start(
            x1w[z][:].rearrange("p (dc t) -> p dc t", dc=NDC, t=128),
            xdv[:, :, 128 * tg:128 * tg + 128]).then_inc(dL[z].h, 16))
        ld[tg] = dL[z].inc(32)
        mk.wait("vector", dL[z], ld[tg])
        mk.wait("vector", tT, tr_done.get(tg - 2, 0))
        mk.emit("vector", lambda e, z=z: e.tensor_copy(accb[z][:], accl[z][:]).then_inc(vV.h, 1))
        cb[tg] = vV.inc(1)
        for half in range(2):
            pz = half
            mk.wait("tensor", vV, cb[tg])
            mk.wait("tensor", vV, add_done.get(tg - 1, 0))
            for q in range(4):
                dc = 4 * half + q
                last = q == 3
                mk.emit("tensor", lambda e, z=z, dc=dc, q=q, pz=pz, l=last: (e.transpose(
                    pt[pz][:, 128 * q:128 * q + 128], accb[z][:, 128 * dc:128 * dc + 128],
                    eye[:]).then_inc(tT.h, 1) if l else e.transpose(
                    pt[pz][:, 128 * q:128 * q + 128], accb[z][:, 128 * dc:128 * dc + 128],
                    eye[:])))
            t_done = tT.inc(1)
            tr_done[tg] = t_done
            mk.wait("vector", tT, t_done)
            x1v = x1w[z][:].rearrange("p (dc t) -> p dc t", dc=NDC, t=128)
            for q in range(4):
                dc = 4 * half + q
                last = q == 3
                mk.emit("vector", lambda e, tg=tg, dc=dc, q=q, pz=pz, z=z, x1v=x1v, l=last: (
                    e.tensor_tensor(
                        xv[:, dc, 128 * tg:128 * tg + 128], x1v[:, dc, :],
                        pt[pz][:, 128 * q:128 * q + 128], ALU.add).then_inc(vV.h, 1)
                    if l else e.tensor_tensor(
                        xv[:, dc, 128 * tg:128 * tg + 128], x1v[:, dc, :],
                        pt[pz][:, 128 * q:128 * q + 128], ALU.add)))
            add_done[tg] = vV.inc(1)
    asc.close()



def build_stage2(n_experts=8, do_experts=True, do_assembly=True, do_ln=True):
    nc = bass.Bass("TRN2", target_bir_lowering=False)
    xt_in = nc.dram_tensor("XT", [D, NTOK], BF16, kind="ExternalInput").ap()
    wap = {
        "w_red": nc.dram_tensor("WRED", [D, 16], F32, kind="ExternalInput").ap(),
        "wg": nc.dram_tensor("WG", [8, 16], F32, kind="ExternalInput").ap(),
        "w1": nc.dram_tensor("W1", [8, D, 4096], F32, kind="ExternalInput").ap(),
        "b1": nc.dram_tensor("B1", [8, 4096], F32, kind="ExternalInput").ap(),
        "w2": nc.dram_tensor("W2", [8, 4096, D], F32, kind="ExternalInput").ap(),
        "b2": nc.dram_tensor("B2", [8, D], F32, kind="ExternalInput").ap(),
    }
    eye_in = nc.dram_tensor("EYE", [128, 128], BF16, kind="ExternalInput").ap()
    eyef_in = nc.dram_tensor("EYEF", [128, 128], F32, kind="ExternalInput").ap()
    xt2_out = nc.dram_tensor("XT2", [D, NTOK], BF16, kind="ExternalOutput").ap()
    acc_dram = nc.dram_tensor("ACCD", [NTOK, D], F32).ap()

    mk = MK(nc)
    bufs = {}
    bufs["arena"] = mk.sb("arena", [128, NTOK * NDC], BF16)
    bufs["eye"] = mk.sb("eye", [128, 128], BF16)
    bufs["eyef"] = mk.sb("eyef", [128, 128], F32)
    bufs["onec"] = mk.sb("onec", [128, 1], BF16)
    bufs["ones1"] = mk.sb("ones1", [1, 128], BF16)
    bufs["onesA"] = mk.sb("onesA", [128, 128], BF16)
    wsc = contextlib.ExitStack()
    bufs["W1sb"] = wsc.enter_context(nc.sbuf_tensor("W1sb", [128, 32768], BF16))
    bufs["W2sb"] = wsc.enter_context(nc.sbuf_tensor("W2sb", [128, 32768], BF16))
    ps_f32 = mk.ps("ps_f32", [128, 3072], F32)
    ps_bf = mk.ps("ps_bf", [128, 2048], BF16)
    bufs["ps_f32"] = ps_f32
    bufs["pstx"] = [ps_f32[:, 0:512], ps_f32[:, 512:1024]]
    bufs["psts"] = [ps_f32[:, 1024:1536], ps_f32[:, 1536:2048]]
    bufs["pbc"] = [ps_f32[:, 2048:2560], ps_f32[:, 2560:3072]]
    bufs["pt"] = [ps_bf[:, 0:512], ps_bf[:, 1024:1536]]

    sems = {n: mk.sem(n) for n in ("dW", "dS", "dA", "tT", "vV", "sS")}
    dW, dS, dA, tT, vV, sS = (sems[n] for n in ("dW", "dS", "dA", "tT", "vV", "sS"))

    dC = mk.sem("dC")
    mk.emit("gpsimd", lambda e: e.dma_start(bufs["eye"][:], eye_in).then_inc(dC.h, 16))
    mk.emit("gpsimd", lambda e: e.dma_start(bufs["eyef"][:], eyef_in).then_inc(dC.h, 16))
    load_done = dC.inc(32)
    mk.emit("vector", lambda e: e.memset(bufs["onec"][:], 1.0))
    mk.emit("vector", lambda e: e.memset(bufs["ones1"][:], 1.0))
    mk.emit("vector", lambda e: e.memset(bufs["onesA"][:], 1.0).then_inc(vV.h, 1))
    const_done = vV.inc(1)
    for eng in ("tensor", "vector", "scalar"):
        mk.wait(eng, mk.sems["dC"], load_done)
        mk.wait(eng, vV, const_done)

    acc_done = emit_ffn0(mk, bufs, sems, wap, xt_in, acc_dram,
                         n_experts=n_experts, do_experts=do_experts)
    wsc.close()
    if do_assembly:
        emit_accT_add(mk, bufs, sems, xt_in, acc_dram, acc_done)
    if do_ln:
        bufs["zsq"] = [mk.sb(f"zsq{i}", [128, NTOK], BF16) for i in range(2)]
        bufs["rows"] = mk.sb("rows", [1, 4096], F32)
        bufs["rows_bf"] = mk.sb("rows_bf", [1, 2048], BF16)
        bufs["xt"] = bufs["arena"]
        emit_lnT(mk, bufs, sems, bufs["arena"])

    final_v = vV.n
    mk.wait("sync", vV, final_v)
    mk.wait("sync", sems["dA"], sems["dA"].n)
    mk.wait("sync", sems["sS"], sems["sS"].n)
    mk.wait("sync", tT, tT.n)
    xtv = bufs["arena"][:].rearrange("p (dc t) -> p dc t", dc=NDC, t=NTOK)
    for dc in range(NDC):
        mk.emit("sync", lambda e, dc=dc: e.dma_start(
            xt2_out[128 * dc:128 * dc + 128, :], xtv[:, dc, :]).then_inc(dS.h, 16))
    out_done = dS.inc(16 * NDC)
    mk.wait("sync", dS, out_done)
    mk.realize()
    return nc


def _shard_inputs_stage2(inputs, xt_by_core):
    eye = np.eye(128, dtype=np.float32).astype(bfloat16)
    eyef = np.eye(128, dtype=np.float32)
    com = {
        "WRED": np.ascontiguousarray(np.asarray(inputs["L0_ffn_w_red"], np.float32)),
        "WG": np.ascontiguousarray(np.asarray(inputs["L0_ffn_wg"], np.float32)),
        "W1": np.ascontiguousarray(np.asarray(inputs["L0_ffn_w1"], np.float32)),
        "B1": np.ascontiguousarray(np.asarray(inputs["L0_ffn_b1"], np.float32)),
        "W2": np.ascontiguousarray(np.asarray(inputs["L0_ffn_w2"], np.float32)),
        "B2": np.ascontiguousarray(np.asarray(inputs["L0_ffn_b2"], np.float32)),
        "EYE": eye, "EYEF": eyef,
    }
    return [dict(com, XT=np.ascontiguousarray(xt_by_core[c])) for c in range(len(xt_by_core))]


def emit_attn(mk, bufs, sems, wap, xt_dram, n_heads=16, do_proj=True, do_bphase=True, dbg=None):
    nc = mk.nc
    tT, vV, sS, dW, dS = (sems[n] for n in ("tT", "vV", "sS", "dW", "dS"))
    ar, eye, ut, onesA = bufs["arena"], bufs["eye"], bufs["ut"], bufs["onesA"]
    ones1 = bufs["ones1"]
    dwst, dxq = mk.sem("dwst"), mk.sem("dxq")
    ps, pt = bufs["ps_f32"], bufs["pt"]
    xdv = xt_dram.rearrange("(dc p) t -> p dc t", p=128)
    SA = [ps[:, 2048:2560], ps[:, 2560:3072]]
    PO = ps[:, 0:2048]
    PQ = [ps[:, 0:512], ps[:, 512:1024]]
    PV2 = ps[:, 1024:2048]

    asc = contextlib.ExitStack()
    qT = asc.enter_context(nc.sbuf_tensor("qT", [128, 16384], BF16))
    kT = asc.enter_context(nc.sbuf_tensor("kT", [128, 16384], BF16))
    vN = asc.enter_context(nc.sbuf_tensor("vN", [128, 16384], BF16))
    xqw = asc.enter_context(nc.sbuf_tensor("xqw", [128, 4096], BF16))
    wstg = asc.enter_context(nc.sbuf_tensor("wstg", [128, 1024], BF16))
    st8 = asc.enter_context(nc.sbuf_tensor("st8", [1, 64], F32))
    st8b = asc.enter_context(nc.sbuf_tensor("st8b", [1, 16], BF16))
    WpT = asc.enter_context(nc.sbuf_tensor("WpT", [128, 8192], BF16))
    vst = [asc.enter_context(nc.sbuf_tensor(f"vstA{i}", [128, 2048], BF16)) for i in range(2)]
    pstrip = [asc.enter_context(nc.sbuf_tensor(f"pstripA{i}", [128, 2048], BF16)) for i in range(2)]
    rbs = asc.enter_context(nc.sbuf_tensor("rbsA", [128, 1024], BF16))
    Bcols = asc.enter_context(nc.sbuf_tensor("BcolsA", [128, 16], F32))
    sqtmp = xqw

    def emit_wT_load(wsrc_rows, WpT_cols, dst_view):
        cp_z = {0: 0, 1: 0}
        tr_prev = tT.n
        for oc in range(8):
            mk.wait("gpsimd", tT, tr_prev)
            mk.emit("gpsimd", lambda e, oc=oc: e.dma_start(
                wstg[:], wsrc_rows(oc)).then_inc(dwst.h, 16))
            wd = dwst.inc(16)
            mk.wait("tensor", dwst, wd)
            for half in range(2):
                mk.wait("tensor", vV, cp_z[half])
                for q in range(4):
                    dc = 4 * half + q
                    last = q == 3
                    mk.emit("tensor", lambda e, dc=dc, q=q, z=half, l=last: (e.transpose(
                        pt[z][:, 128 * q:128 * q + 128], wstg[:, 128 * dc:128 * dc + 128],
                        eye[:]).then_inc(tT.h, 1) if l else e.transpose(
                        pt[z][:, 128 * q:128 * q + 128], wstg[:, 128 * dc:128 * dc + 128],
                        eye[:])))
                td = tT.inc(1)
                mk.wait("vector", tT, td)
                mk.emit("vector", lambda e, oc=oc, half=half: e.tensor_copy(
                    dst_view[:, 4 * half:4 * half + 4, 128 * oc:128 * oc + 128],
                    pt[half][:].rearrange("p (q t) -> p q t", q=4, t=128)
                ).then_inc(vV.h, 1))
                cp_z[half] = vV.inc(1)
            tr_prev = tT.n
        return max(cp_z.values())

    for b in range(BC):
        tb = 2048 * b
        for eng in ("tensor", "gpsimd", "scalar"):
            mk.wait(eng, vV, vV.n)
        for part in range(3):
            wv2 = WpT[:].rearrange("p (dc o) -> p dc o", dc=NDC, o=1024)
            wpT_ready = emit_wT_load(
                lambda oc, part=part: wap["w_qkv"][
                    128 * (8 * part + oc):128 * (8 * part + oc) + 128, :],
                None, wv2)
            mk.wait("tensor", vV, wpT_ready)
            xqv = xqw[:].rearrange("p (dc t) -> p dc t", dc=NDC, t=512)
            if part < 2:
                dst = qT if part == 0 else kT
                ev_z = {0: 0, 1: 0}
                mm_last = tT.n
                for ts in range(4):
                    mk.wait("gpsimd", tT, mm_last)
                    mk.emit("gpsimd", lambda e, ts=ts, tb=tb: e.dma_start(
                        xqv, xdv[:, :, tb + 512 * ts:tb + 512 * ts + 512]).then_inc(dxq.h, 16))
                    xd = dxq.inc(16)
                    mk.wait("tensor", dxq, xd)
                    for oc in range(8):
                        z = oc % 2
                        mk.wait("tensor", vV, ev_z[z])
                        for dc in range(NDC):
                            last = dc == NDC - 1
                            mk.emit("tensor", lambda e, dc=dc, oc=oc, z=z, l=last: (e.matmul(
                                PQ[z], WpT[:, 1024 * dc + 128 * oc:1024 * dc + 128 * oc + 128],
                                xqv[:, dc, :], start=(dc == 0), stop=(dc == NDC - 1)
                            ).then_inc(tT.h, 1) if l else e.matmul(
                                PQ[z], WpT[:, 1024 * dc + 128 * oc:1024 * dc + 128 * oc + 128],
                                xqv[:, dc, :], start=(dc == 0), stop=(dc == NDC - 1))))
                        mm_z = tT.inc(1)
                        mk.wait("vector", tT, mm_z)
                        mk.emit("vector", lambda e, oc=oc, ts=ts, z=z, dst=dst: e.tensor_copy(
                            dst[:, 2048 * oc + 512 * ts:2048 * oc + 512 * ts + 512],
                            PQ[z]).then_inc(vV.h, 1))
                        ev_z[z] = vV.inc(1)
                    mm_last = tT.n
            else:
                ev_prev = 0
                mm_last = tT.n
                for ts in range(4):
                    mk.wait("gpsimd", tT, mm_last)
                    mk.emit("gpsimd", lambda e, ts=ts, tb=tb: e.dma_start(
                        xqv, xdv[:, :, tb + 512 * ts:tb + 512 * ts + 512]).then_inc(dxq.h, 16))
                    xd = dxq.inc(16)
                    mk.wait("tensor", dxq, xd)
                    for it in range(4):
                        mk.wait("tensor", vV, ev_prev)
                        for dc in range(NDC):
                            for hh in range(2):
                                last = dc == NDC - 1 and hh == 1
                                mk.emit("tensor", lambda e, dc=dc, it=it, hh=hh, l=last: (e.matmul(
                                    PV2[:, 512 * hh:512 * hh + 512],
                                    xqv[:, dc, 128 * it:128 * it + 128],
                                    WpT[:, 1024 * dc + 512 * hh:1024 * dc + 512 * hh + 512],
                                    start=(dc == 0), stop=(dc == NDC - 1)
                                ).then_inc(tT.h, 1) if l else e.matmul(
                                    PV2[:, 512 * hh:512 * hh + 512],
                                    xqv[:, dc, 128 * it:128 * it + 128],
                                    WpT[:, 1024 * dc + 512 * hh:1024 * dc + 512 * hh + 512],
                                    start=(dc == 0), stop=(dc == NDC - 1))))
                        vmm = tT.inc(1)
                        mk.wait("vector", tT, vmm)
                        mk.emit("vector", lambda e, ts=ts, it=it: e.tensor_copy(
                            vN[:, 1024 * (4 * ts + it):1024 * (4 * ts + it) + 1024],
                            PV2).then_inc(vV.h, 1))
                        ev_prev = vV.inc(1)
                    mm_last = tT.n

        if dbg is not None and b == 0:
            dv = vV.n
            mk.wait("sync", vV, dv)
            mk.emit("sync", lambda e: e.dma_start(dbg["QT"], qT[:]).then_inc(dS.h, 16))
            mk.emit("sync", lambda e: e.dma_start(dbg["KT"], kT[:]).then_inc(dS.h, 16))
            mk.emit("sync", lambda e: e.dma_start(dbg["VN"], vN[:]).then_inc(dS.h, 16))
            dS.inc(48)
        if not do_bphase:
            continue
        qkv_v_done = vV.n
        prev = (vV, qkv_v_done)
        for h in range(16):
            bh = 64 * (h % 2)
            qsl = qT[bh:bh + 64, 2048 * (h // 2):2048 * (h // 2) + 2048]
            ksl = kT[bh:bh + 64, 2048 * (h // 2):2048 * (h // 2) + 2048]
            for which, sl in ((0, qsl), (1, ksl)):
                mk.wait("scalar", prev[0], prev[1])
                mk.emit("scalar", lambda e, sl=sl, bh=bh: e.activation(
                    sqtmp[bh:bh + 64, 0:2048], sl, AF.Square).then_inc(sS.h, 1))
                sq = (sS, sS.inc(1))
                mk.wait("tensor", sq[0], sq[1])
                for s in range(4):
                    last = s == 3
                    mk.emit("tensor", lambda e, s=s, bh=bh, l=last: (e.matmul(
                        PO[0:1, 512 * s:512 * s + 512], onesA[bh:bh + 64, 0:1],
                        sqtmp[bh:bh + 64, 512 * s:512 * s + 512], start=True, stop=True
                    ).then_inc(tT.h, 1) if l else e.matmul(
                        PO[0:1, 512 * s:512 * s + 512], onesA[bh:bh + 64, 0:1],
                        sqtmp[bh:bh + 64, 512 * s:512 * s + 512], start=True, stop=True)))
                mm = (tT, tT.inc(1))
                mk.wait("vector", mm[0], mm[1])
                col = 16 * which + h
                mk.emit("vector", lambda e, col=col: e.reduce_max(
                    st8[0:1, col:col + 1],
                    PO[0:1, 0:2048].rearrange("p (a o) -> p a o", a=1, o=2048),
                    axis=AXL.X).then_inc(vV.h, 1))
                prev = (vV, vV.inc(1))
        bsteps = [
            ("scalar", lambda e: e.activation(st8[0:1, 32:48], st8[0:1, 0:16], AF.Sqrt)),
            ("scalar", lambda e: e.activation(st8[0:1, 48:64], st8[0:1, 16:32], AF.Sqrt)),
            ("vector", lambda e: e.tensor_tensor(st8[0:1, 32:48], st8[0:1, 32:48],
                                                 st8[0:1, 48:64], ALU.mult)),
            ("vector", lambda e: e.tensor_scalar(st8[0:1, 48:64], st8[0:1, 32:48],
                                                 -0.125, None, ALU.mult)),
            ("vector", lambda e: e.tensor_copy(st8b[0:1, :], st8[0:1, 48:64])),
        ]
        bdone = seq_emit(mk, sems, bsteps, first=prev)
        mk.wait("tensor", bdone[0], bdone[1])
        for h in range(16):
            last = h == 15
            mk.emit("tensor", lambda e, h=h, l=last: (e.matmul(
                SA[0][:, h:h + 1], ones1[:], st8b[0:1, h:h + 1],
                start=True, stop=True).then_inc(tT.h, 1) if l else e.matmul(
                SA[0][:, h:h + 1], ones1[:], st8b[0:1, h:h + 1],
                start=True, stop=True)))
        bm = (tT, tT.inc(1))
        mk.wait("vector", bm[0], bm[1])
        mk.emit("vector", lambda e: e.tensor_copy(Bcols[:], SA[0][:, 0:16]).then_inc(vV.h, 1))
        bdone = (vV, vV.inc(1))

        vst_used = {0: (tT, 0), 1: (tT, 0)}
        po_free = (vV, 0)
        sa_free = {0: (sS, 0), 1: (sS, 0)}
        sa_rb_free = (vV, 0)
        pstrip_free = {0: (tT, 0), 1: (tT, 0)}
        atto_last = (vV, 0)
        for h in range(n_heads):
            zs = h % 2
            bh = 64 * (h % 2)
            sr = 64 - bh
            hq = 2048 * (h // 2)
            qsl = qT[bh:bh + 64, hq:hq + 2048]
            ksl = kT[bh:bh + 64, hq:hq + 2048]
            pw = vst_used[zs]
            mk.wait("vector", pw[0], pw[1])
            mk.wait("vector", bdone[0], bdone[1])
            mk.emit("vector", lambda e, zs=zs: e.memset(vst[zs][:], 1.0).then_inc(vV.h, 1))
            c3 = (vV, vV.inc(1))
            mk.wait("vector", c3[0], c3[1])
            for j in range(16):
                last = j == 15
                mk.emit("vector", lambda e, j=j, zs=zs, h=h, bh=bh, l=last: (e.tensor_copy(
                    vst[zs][:, 128 * j + bh:128 * j + bh + 64],
                    vN[:, 1024 * j + 64 * h:1024 * j + 64 * h + 64]).then_inc(vV.h, 1)
                    if l else e.tensor_copy(
                    vst[zs][:, 128 * j + bh:128 * j + bh + 64],
                    vN[:, 1024 * j + 64 * h:1024 * j + 64 * h + 64])))
            c4 = (vV, vV.inc(1))
            mk.wait("tensor", c4[0], c4[1])
            mk.wait("tensor", po_free[0], po_free[1])
            slices = []
            for j in range(16):
                for s in range(j // 4, 4):
                    qs0 = 128 * j if s == j // 4 else 512 * s
                    slices.append((j, s, qs0, 512 * (s + 1) - qs0))

            def emit_scores(idx):
                j, s, qs0, N = slices[idx]
                za = idx % 2
                mk.wait("tensor", sa_free[za][0], sa_free[za][1])
                if idx < 2:
                    mk.wait("tensor", sa_rb_free[0], sa_rb_free[1])
                mk.emit("tensor", lambda e, j=j, qs0=qs0, N=N, za=za, ksl=ksl, qsl=qsl: e.matmul(
                    SA[za][:, 0:N], ksl[:, 128 * j:128 * j + 128],
                    qsl[:, qs0:qs0 + N],
                    start=True, stop=True).then_inc(tT.h, 1))
                smm = (tT, tT.inc(1))
                mk.wait("scalar", smm[0], smm[1])
                mk.wait("scalar", pstrip_free[j % 2][0], pstrip_free[j % 2][1])
                mk.emit("scalar", lambda e, qs0=qs0, N=N, za=za, z2=j % 2, h=h: e.activation(
                    pstrip[z2][:, qs0:qs0 + N], SA[za][:, 0:N],
                    AF.Exp, scale=0.125, bias=Bcols[:, h:h + 1]).then_inc(sS.h, 1))
                ex = (sS, sS.inc(1))
                sa_free[za] = ex
                pv_wait = ex
                if s == j // 4:
                    mk.wait("vector", ex[0], ex[1])
                    mk.emit("vector", lambda e, j=j, z2=j % 2: e.tensor_tensor(
                        pstrip[z2][:, 128 * j:128 * j + 128],
                        pstrip[z2][:, 128 * j:128 * j + 128],
                        ut[:], ALU.mult).then_inc(vV.h, 1))
                    pv_wait = (vV, vV.inc(1))
                return pv_wait

            def emit_pv(idx, pv_wait):
                j, s, qs0, N = slices[idx]
                mk.wait("tensor", pv_wait[0], pv_wait[1])
                jl = min(15, 4 * s + 3)
                mk.emit("tensor", lambda e, j=j, qs0=qs0, N=N, z2=j % 2, zs=zs, jl=jl: e.matmul(
                    PO[:, qs0:qs0 + N], vst[zs][:, 128 * j:128 * j + 128],
                    pstrip[z2][:, qs0:qs0 + N],
                    start=(j == 0), stop=(j == jl),
                    skip_group_check=True).then_inc(tT.h, 1))
                done = (tT, tT.inc(1))
                if s == 3:
                    pstrip_free[j % 2] = done
                return done

            pv_done = (tT, 0)
            pending = None
            for idx in range(len(slices)):
                pw = emit_scores(idx)
                if pending is not None:
                    pv_done = emit_pv(idx - 1, pending)
                pending = pw
            pv_done = emit_pv(len(slices) - 1, pending)
            vst_used[zs] = pv_done
            mk.wait("vector", pv_done[0], pv_done[1])

            def _recip(e, sqtmp=sqtmp, PO=PO, nc=nc, vV=vV, sr=sr):
                with nc.allow_low_precision(reason="softmax 1/sum applied in bf16"):
                    return e.reciprocal(sqtmp[sr:sr + 1, 0:2048], PO[sr:sr + 1, 0:2048]).then_inc(vV.h, 1)

            mk.emit("vector", _recip)
            rr = (vV, vV.inc(1))
            mk.wait("tensor", rr[0], rr[1])
            mk.wait("tensor", sa_free[0][0], sa_free[0][1])
            mk.wait("tensor", sa_free[1][0], sa_free[1][1])
            mk.wait("tensor", sa_rb_free[0], sa_rb_free[1])
            rbc = None
            for half in range(2):
                if rbc is not None:
                    mk.wait("tensor", rbc[0], rbc[1])
                for s2 in range(2):
                    last = s2 == 1
                    mk.emit("tensor", lambda e, half=half, s2=s2, sr=sr, bh=bh, l=last: (e.matmul(
                        SA[s2][bh:bh + 64, :], onesA[sr:sr + 1, 0:64],
                        sqtmp[sr:sr + 1, 1024 * half + 512 * s2:1024 * half + 512 * s2 + 512],
                        start=True, stop=True).then_inc(tT.h, 1) if l else e.matmul(
                        SA[s2][bh:bh + 64, :], onesA[sr:sr + 1, 0:64],
                        sqtmp[sr:sr + 1, 1024 * half + 512 * s2:1024 * half + 512 * s2 + 512],
                        start=True, stop=True)))
                rbm = (tT, tT.inc(1))
                mk.wait("vector", rbm[0], rbm[1])
                if half == 1:
                    mk.wait("vector", sa_rb_free[0], sa_rb_free[1])
                mk.emit("vector", lambda e, bh=bh: e.tensor_copy(
                    rbs[bh:bh + 64, 0:512], SA[0][bh:bh + 64, :]))
                mk.emit("vector", lambda e, bh=bh: e.tensor_copy(
                    rbs[bh:bh + 64, 512:1024], SA[1][bh:bh + 64, :]).then_inc(vV.h, 1))
                rbc = (vV, vV.inc(1))
                mk.wait("vector", rbc[0], rbc[1])
                dsth = qT[bh:bh + 64, hq + 1024 * half:hq + 1024 * half + 1024]
                mk.emit("vector", lambda e, dsth=dsth, half=half, bh=bh: e.tensor_tensor(
                    dsth, PO[bh:bh + 64, 1024 * half:1024 * half + 1024],
                    rbs[bh:bh + 64, 0:1024], ALU.mult).then_inc(vV.h, 1))
                sa_rb_free = (vV, vV.inc(1))
            po_free = sa_rb_free
            atto_last = sa_rb_free
        if not do_proj:
            continue

        WoT = WpT
        wo2 = WoT[:].rearrange("p (ic o) -> p ic o", ic=NDC, o=1024)
        mk.wait("tensor", atto_last[0], atto_last[1])
        woT_ready = emit_wT_load(
            lambda oc: wap["w_out"][128 * oc:128 * oc + 128, :], None, wo2)
        mk.wait("tensor", vV, woT_ready)
        xqv = xqw[:].rearrange("p (dc t) -> p dc t", dc=NDC, t=512)
        arv = ar[:].rearrange("p (dc t) -> p dc t", dc=NDC, t=NTOK)
        ev_z = {0: 0, 1: 0}
        mm_last = tT.n
        for ts in range(4):
            mk.wait("gpsimd", tT, mm_last)
            mk.wait("gpsimd", vV, max(ev_z.values()))
            mk.emit("gpsimd", lambda e, ts=ts, tb=tb: e.dma_start(
                xqv, xdv[:, :, tb + 512 * ts:tb + 512 * ts + 512]).then_inc(dxq.h, 16))
            xd = dxq.inc(16)
            mk.wait("tensor", dxq, xd)
            for oc in range(8):
                z = oc % 2
                mk.wait("tensor", vV, ev_z[z])
                for ic in range(NDC):
                    last = ic == NDC - 1
                    mk.emit("tensor", lambda e, ic=ic, oc=oc, ts=ts, z=z, l=last: (e.matmul(
                        PQ[z], WoT[:, 1024 * ic + 128 * oc:1024 * ic + 128 * oc + 128],
                        qT[:, 2048 * ic + 512 * ts:2048 * ic + 512 * ts + 512],
                        start=(ic == 0), stop=(ic == NDC - 1)).then_inc(tT.h, 1)
                        if l else e.matmul(
                        PQ[z], WoT[:, 1024 * ic + 128 * oc:1024 * ic + 128 * oc + 128],
                        qT[:, 2048 * ic + 512 * ts:2048 * ic + 512 * ts + 512],
                        start=(ic == 0), stop=(ic == NDC - 1))))
                mm_z = tT.inc(1)
                mk.wait("vector", tT, mm_z)
                mk.emit("vector", lambda e, oc=oc, ts=ts, z=z, tb=tb: e.tensor_tensor(
                    arv[:, oc, tb + 512 * ts:tb + 512 * ts + 512], PQ[z],
                    xqv[:, oc, :], ALU.add).then_inc(vV.h, 1))
                ev_z[z] = vV.inc(1)
            mm_last = tT.n
    asc.close()


def emit_ffn1(mk, bufs, sems, wap, xt_dram, acc_dram):
    nc = mk.nc
    tT, vV, sS, dW, dA = (sems[n] for n in ("tT", "vV", "sS", "dW", "dA"))
    dAz = [mk.sem("dA0"), mk.sem("dA1")]
    dE = [mk.sem("dE0"), mk.sem("dE1")]
    dxc = [mk.sem("dxc0"), mk.sem("dxc1")]
    dxw = [mk.sem("dxw0"), mk.sem("dxw1")]
    ar = bufs["arena"]
    ps = bufs["ps_f32"]
    h_sb = ar[:, 0:2048]
    xwin = [ar[:, 2048:6144].bitcast(BF16) if False else ar[:, 2048:6144],
            ar[:, 6144:10240]]
    tmp = [ar[:, 10240:12288].bitcast(F32), ar[:, 28928:30976].bitcast(F32)]
    gsel = ar[:, 12288:12800].bitcast(F32)
    lgs = ar[:, 12800:13312].bitcast(F32)
    m8 = ar[:, 13312:13824].bitcast(F32)
    e2g = ar[:, 13824:14080].bitcast(F32)
    wr_sb = ar[:, 14080:14208].bitcast(F32)
    rb_sb = ar[0:1, 14208:14224].bitcast(F32)
    onesF = ar[0:1, 14224:14480].bitcast(F32)
    xca = [ar[:, 14592:22784].bitcast(F32), ar[:, 22784:30976].bitcast(F32)]
    ph1 = ps[:, 0:512]
    ph3 = ps[:, 512:1024]
    py = [ps[:, 1024:2048], ps[:, 2048:3072]]
    pl = ps[:, 2048:2304]
    xdv = xt_dram.rearrange("(dc p) t -> p dc t", p=128)

    wsc = contextlib.ExitStack()
    w1s = [wsc.enter_context(nc.sbuf_tensor(f"f1w1{mk.uid()}_{i}", [128, 4096], BF16)) for i in range(2)]
    w3s = [wsc.enter_context(nc.sbuf_tensor(f"f1w3{mk.uid()}_{i}", [128, 4096], BF16)) for i in range(2)]
    w2s = [wsc.enter_context(nc.sbuf_tensor(f"f1w2{mk.uid()}_{i}", [128, 4096], BF16)) for i in range(2)]

    def expert_w(ex):
        if ex < 8:
            return wap["r_w1"][ex], wap["r_w3"][ex], wap["r_w2"][ex]
        return wap["s_w1"][ex - 8], wap["s_w3"][ex - 8], wap["s_w2"][ex - 8]

    mk.emit("gpsimd", lambda e: e.dma_start(
        wr_sb.rearrange("p (dc j) -> p dc j", dc=NDC, j=8),
        wap["router_w"].rearrange("(dc p) j -> p dc j", p=128)).then_inc(dW.h, 16))
    mk.emit("gpsimd", lambda e: e.dma_start(
        rb_sb, wap["router_b"].unsqueeze(0)).then_inc(dW.h, 16))
    stg_done = dW.inc(32)
    mk.emit("vector", lambda e: e.memset(onesF, 1.0).then_inc(vV.h, 1))
    onesF_done = vV.inc(1)
    def load_expert(ex, buf):
        a1, a3, a2 = expert_w(ex)
        mk.emit("gpsimd", lambda e, a1=a1, buf=buf: e.dma_start(
            w1s[buf][:].rearrange("p (dc f) -> p dc f", dc=NDC, f=512),
            a1.rearrange("(dc p) f -> p dc f", p=128)).then_inc(dE[buf].h, 16))
        mk.emit("gpsimd", lambda e, a3=a3, buf=buf: e.dma_start(
            w3s[buf][:].rearrange("p (dc f) -> p dc f", dc=NDC, f=512),
            a3.rearrange("(dc p) f -> p dc f", p=128)).then_inc(dE[buf].h, 16))
        mk.emit("gpsimd", lambda e, a2=a2, buf=buf: e.dma_start(
            w2s[buf][:].rearrange("p (fc d) -> p fc d", fc=4, d=1024),
            a2.rearrange("(fc p) d -> p fc d", p=128)).then_inc(dE[buf].h, 16))
        return dE[buf].inc(48)

    wready = {0: load_expert(0, 0)}

    mk.wait("tensor", dW, stg_done)
    mk.wait("tensor", vV, onesF_done)
    wrv = wr_sb.rearrange("p (dc j) -> p dc j", dc=NDC, j=8)
    mm_done = {}
    for sl in range(8):
        z = sl % 2
        mk.wait("gpsimd", tT, mm_done.get(sl - 2, 0))
        mk.emit("gpsimd", lambda e, sl=sl, z=z: e.dma_start(
            xca[z].rearrange("p (dc t) -> p dc t", dc=NDC, t=512),
            xdv[:, :, 512 * sl:512 * sl + 512]).then_inc(dxc[z].h, 16))
        xd = dxc[z].inc(16)
        mk.wait("tensor", dxc[z], xd)
        xcv = xca[z].rearrange("p (dc t) -> p dc t", dc=NDC, t=512)
        for t4 in range(4):
            tg = 4 * sl + t4
            for dc in range(NDC):
                mk.emit("tensor", lambda e, dc=dc, t4=t4, tg=tg, xcv=xcv: e.matmul(
                    pl[:, 8 * tg:8 * tg + 8],
                    xcv[:, dc, 128 * t4:128 * t4 + 128], wrv[:, dc, :],
                    start=(dc == 0), stop=False))
            mk.emit("tensor", lambda e, tg=tg: e.matmul(
                pl[:, 8 * tg:8 * tg + 8], onesF, rb_sb,
                start=False, stop=True).then_inc(tT.h, 1))
            tT.inc(1)
        mm_done[sl] = tT.n
    lg_done = tT.n
    mk.wait("vector", tT, lg_done)
    mk.emit("vector", lambda e: e.tensor_copy(lgs, pl[:, 0:256]).then_inc(vV.h, 1))
    lgs_done = (vV, vV.inc(1))
    mk.wait("vector", lgs_done[0], lgs_done[1])
    for tg in range(NTT):
        last = tg == NTT - 1
        mk.emit("vector", lambda e, tg=tg, l=last: (e.max(
            out=m8[:, 8 * tg:8 * tg + 8], in_=lgs[:, 8 * tg:8 * tg + 8]).then_inc(vV.h, 1)
            if l else e.max(out=m8[:, 8 * tg:8 * tg + 8], in_=lgs[:, 8 * tg:8 * tg + 8])))
    m8_done = (vV, vV.inc(1))
    m8v = m8.rearrange("p (g e) -> p g e", g=NTT, e=8)
    lgv = lgs.rearrange("p (g e) -> p g e", g=NTT, e=8)
    gsv = gsel.rearrange("p (g e) -> p g e", g=NTT, e=8)
    e2 = e2g[:, 0:32]
    g1 = e2g[:, 32:64]
    g2 = e2g[:, 64:96]
    scr = e2g[:, 96:128]
    e2v = e2.rearrange("p (g o) -> p g o", g=NTT, o=1)
    g1v = g1.rearrange("p (g o) -> p g o", g=NTT, o=1)
    g2v = g2.rearrange("p (g o) -> p g o", g=NTT, o=1)
    gsteps = [
        ("vector", lambda e: e.tensor_tensor(e2v, m8v[:, :, 1:2], m8v[:, :, 0:1], ALU.subtract)),
        ("scalar", lambda e: e.activation(e2, e2, AF.Exp)),
        ("vector", lambda e: e.tensor_scalar(scr, e2, 1.0, None, ALU.add)),
        ("vector", lambda e: e.reciprocal(g1, scr)),
        ("vector", lambda e: e.tensor_tensor(g2, e2, g1, ALU.mult)),
        ("vector", lambda e: e.tensor_tensor(gsv, lgv, m8v[:, :, 0:1].broadcast_to((128, NTT, 8)), ALU.is_equal)),
        ("vector", lambda e: e.tensor_tensor(gsv, gsv, g1v.broadcast_to((128, NTT, 8)), ALU.mult)),
        ("vector", lambda e: e.tensor_tensor(lgs, lgv, m8v[:, :, 1:2].broadcast_to((128, NTT, 8)), ALU.is_equal)),
        ("vector", lambda e: e.tensor_tensor(lgs, lgv, g2v.broadcast_to((128, NTT, 8)), ALU.mult)),
        ("vector", lambda e: e.tensor_tensor(gsel, gsel, lgs, ALU.add)),
    ]
    gdone = seq_emit(mk, sems, gsteps, first=m8_done)
    router_done_v = vV.n

    ev_cnt = {}
    dma_cnt = {}
    xw_read_z = {0: 0, 1: 0}
    phA_all_done = 0
    phB_prev = 0
    sil_prev = 0
    for ex in range(10):
        buf = ex % 2
        mk.wait("tensor", dE[buf], wready[ex])
        if ex == 0:
            mk.wait("tensor", vV, router_done_v)
        w1v = w1s[buf][:].rearrange("p (dc f) -> p dc f", dc=NDC, f=512)
        w3v = w3s[buf][:].rearrange("p (dc f) -> p dc f", dc=NDC, f=512)
        w2v = w2s[buf][:].rearrange("p (fc d) -> p fc d", fc=4, d=1024)
        for w in range(8):
            wz = w % 2
            mk.wait("gpsimd", tT, xw_read_z[wz])
            mk.emit("gpsimd", lambda e, w=w, wz=wz: e.dma_start(
                xwin[wz].rearrange("p (dc t) -> p dc t", dc=NDC, t=512),
                xdv[:, :, 512 * w:512 * w + 512]).then_inc(dxw[wz].h, 16))
            xwd = dxw[wz].inc(16)
            xv = xwin[wz].rearrange("p (dc t) -> p dc t", dc=NDC, t=512)
            mk.wait("tensor", dxw[wz], xwd)
            for fc in range(4):
                mk.wait("tensor", vV, sil_prev)
                for which, wv in ((0, w1v), (1, w3v)):
                    dstp = ph1 if which == 0 else ph3
                    for dc in range(NDC):
                        last = dc == NDC - 1 and which == 1
                        mk.emit("tensor", lambda e, dc=dc, fc=fc, wv=wv, dstp=dstp, xv=xv, l=last: (e.matmul(
                            dstp, wv[:, dc, 128 * fc:128 * fc + 128],
                            xv[:, dc, :], start=(dc == 0), stop=(dc == NDC - 1)
                        ).then_inc(tT.h, 1) if l else e.matmul(
                            dstp, wv[:, dc, 128 * fc:128 * fc + 128],
                            xv[:, dc, :], start=(dc == 0), stop=(dc == NDC - 1))))
                mmA = tT.inc(1)
                mk.wait("scalar", tT, mmA)
                mk.emit("scalar", lambda e, fc=fc: e.activation(
                    h_sb[:, 512 * fc:512 * fc + 512], ph1, AF.Silu).then_inc(sS.h, 1))
                sdone = sS.inc(1)
                mk.wait("vector", sS, sdone)
                mk.emit("vector", lambda e, fc=fc: e.tensor_tensor(
                    h_sb[:, 512 * fc:512 * fc + 512], h_sb[:, 512 * fc:512 * fc + 512],
                    ph3, ALU.mult).then_inc(vV.h, 1))
                sil_prev = vV.inc(1)
            phA_all_done = tT.n
            xw_read_z[wz] = tT.n
            h_ready = sil_prev
            mk.wait("tensor", vV, h_ready)
            for th in range(4):
                tg = 4 * w + th
                pz = tg % 2
                mk.wait("tensor", vV, ev_cnt.get(tg - 2, 0))
                if ex == 0 and tg <= 1:
                    mk.wait("tensor", vV, router_done_v)
                for fc in range(4):
                    for hh in range(2):
                        last = fc == 3 and hh == 1
                        mk.emit("tensor", lambda e, fc=fc, th=th, hh=hh, pz=pz, w2v=w2v, l=last: (e.matmul(
                            py[pz][:, 512 * hh:512 * hh + 512],
                            h_sb[:, 512 * fc + 128 * th:512 * fc + 128 * th + 128],
                            w2v[:, fc, 512 * hh:512 * hh + 512],
                            start=(fc == 0), stop=(fc == 3)).then_inc(tT.h, 1)
                            if l else e.matmul(
                            py[pz][:, 512 * hh:512 * hh + 512],
                            h_sb[:, 512 * fc + 128 * th:512 * fc + 128 * th + 128],
                            w2v[:, fc, 512 * hh:512 * hh + 512],
                            start=(fc == 0), stop=(fc == 3))))
                mmB = tT.inc(1)
                az = tg % 2
                pk = tg - 2 if tg >= 2 else tg + 30
                mk.wait("vector", tT, mmB)
                mk.wait("vector", dAz[az], dma_cnt.get(pk, 0))
                if ex < 8:
                    mk.emit("vector", lambda e, tg=tg, ex=ex, pz=pz, az=az: e.tensor_scalar(
                        tmp[az], py[pz][:], gsel[:, 8 * tg + ex:8 * tg + ex + 1],
                        None, ALU.mult).then_inc(vV.h, 1))
                else:
                    mk.emit("vector", lambda e, pz=pz, az=az: e.tensor_copy(
                        tmp[az], py[pz][:]).then_inc(vV.h, 1))
                ev_cnt[tg] = vV.inc(1)
                mk.wait("gpsimd", vV, ev_cnt[tg])
                op = ALU.bypass if ex == 0 else ALU.add
                mk.emit("gpsimd", lambda e, tg=tg, op=op, az=az: e.dma_start(
                    acc_dram[128 * tg:128 * tg + 128, :], tmp[az],
                    accum_op=op).then_inc(dAz[az].h, 16))
                dma_cnt[tg] = dAz[az].inc(16)
            phB_prev = tT.n
        if ex < 9:
            mk.wait("gpsimd", tT, phA_all_done)
            wready[ex + 1] = load_expert(ex + 1, (ex + 1) % 2)
    wsc.close()
    return (dAz[0].n, dAz[1].n)


def emit_final_out(mk, bufs, sems, out_dram):
    nc = mk.nc
    tT, vV, dS = (sems[n] for n in ("tT", "vV", "dS"))
    ar, eye, pt = bufs["arena"], bufs["eye"], bufs["pt"]
    arv = ar[:].rearrange("p (dc t) -> p dc t", dc=NDC, t=NTOK)
    ov = out_dram.rearrange("b (i p) d -> p b i d", p=128)
    fsc = contextlib.ExitStack()
    fst = [fsc.enter_context(nc.sbuf_tensor(f"fst{mk.uid()}_{i}", [128, 1024], F32)) for i in range(2)]
    dF = [mk.sem("dF0"), mk.sem("dF1")]
    mk.wait("tensor", vV, vV.n)
    st_done = {}
    cp_done = {}
    tr_done = {}
    for tg in range(NTT):
        b, i = tg // 16, tg % 16
        z = tg % 2
        for half in range(2):
            pz = half
            mk.wait("tensor", vV, cp_done.get(tg - 1, 0))
            for q in range(4):
                dc = 4 * half + q
                last = q == 3
                mk.emit("tensor", lambda e, dc=dc, q=q, pz=pz, tg=tg, l=last: (e.transpose(
                    pt[pz][:, 128 * q:128 * q + 128],
                    arv[:, dc, 128 * tg:128 * tg + 128], eye[:]).then_inc(tT.h, 1)
                    if l else e.transpose(
                    pt[pz][:, 128 * q:128 * q + 128],
                    arv[:, dc, 128 * tg:128 * tg + 128], eye[:])))
            td = tT.inc(1)
            tr_done[tg] = td
            mk.wait("vector", tT, td)
            mk.wait("vector", dF[z], st_done.get(tg - 2, 0))
            last = half == 1
            mk.emit("vector", lambda e, z=z, half=half, l=last: (e.tensor_copy(
                fst[z][:, 512 * half:512 * half + 512],
                pt[half][:].rearrange("p (q t) -> p (q t)", q=4, t=128)).then_inc(vV.h, 1)
                if l else e.tensor_copy(
                fst[z][:, 512 * half:512 * half + 512],
                pt[half][:].rearrange("p (q t) -> p (q t)", q=4, t=128))))
            cp_done[tg] = vV.inc(1) if last else cp_done.get(tg, 0)
        mk.wait("sync", vV, cp_done[tg])
        mk.emit("sync", lambda e, b=b, i=i, z=z: e.dma_start(
            ov[:, b, i, :], fst[z][:]).then_inc(dF[z].h, 16))
        st_done[tg] = dF[z].inc(16)
    mk.wait("sync", dF[0], dF[0].n)
    mk.wait("sync", dF[1], dF[1].n)
    fsc.close()


def seq_emit(mk, sems, steps, first=None):
    tT, vV, sS = sems["tT"], sems["vV"], sems["sS"]
    prev = first
    for eng, fn in steps:
        if prev is not None:
            mk.wait(eng, prev[0], prev[1])
        c = sS if eng == "scalar" else (vV if eng == "vector" else tT)
        mk.emit(eng, lambda e, fn=fn, c=c: fn(e).then_inc(c.h, 1))
        prev = (c, c.inc(1))
    return prev



def build_stage1(out_dtype=BF16, debug=False):
    nc = bass.Bass("TRN2", target_bir_lowering=False)
    u = nc.dram_tensor("U", [BC, T, D], F32, kind="ExternalInput").ap()
    tstack = nc.dram_tensor("TSTACK", [128, 7, 128, 128], F32, kind="ExternalInput").ap()
    eye_in = nc.dram_tensor("EYE", [128, 128], BF16, kind="ExternalInput").ap()
    xt_out = nc.dram_tensor("XT", [D, NTOK], out_dtype, kind="ExternalOutput").ap()

    mk = MK(nc)
    bufs = {}
    bufs["ubf"] = mk.sb("ubf", [128, NTOK * NDC], BF16)
    bufs["xt"] = mk.sb("xt", [128, NTOK * NDC], BF16)
    bufs["tsb"] = [mk.sb(f"tsb{i}", [128, 896], BF16) for i in range(2)]
    bufs["eye"] = mk.sb("eye", [128, 128], BF16)
    bufs["onec"] = mk.sb("onec", [128, 1], BF16)
    bufs["ones1"] = mk.sb("ones1", [1, 128], BF16)
    bufs["zsq"] = [mk.sb(f"zsq{i}", [128, NTOK], BF16) for i in range(2)]
    bufs["rows"] = mk.sb("rows", [1, 4096], F32)
    bufs["rows_bf"] = mk.sb("rows_bf", [1, 2048], BF16)
    ps_f32 = mk.ps("ps_f32", [128, 3072], F32)
    ps_bf = mk.ps("ps_bf", [128, 2048], BF16)
    bufs["pg"] = [ps_f32[:, 0:512], ps_f32[:, 512:1024]]
    bufs["pstx"] = [ps_f32[:, 0:512], ps_f32[:, 512:1024]]
    bufs["psts"] = [ps_f32[:, 1024:1536], ps_f32[:, 1536:2048]]
    bufs["pbc"] = [ps_f32[:, 2048:2560], ps_f32[:, 2560:3072]]
    bufs["pt"] = [ps_bf[:, 0:512], ps_bf[:, 1024:1536]]

    sems = {n: mk.sem(n) for n in ("dW", "dS", "tT", "vV", "sS")}
    dW, dS, tT, vV, sS = (sems[n] for n in ("dW", "dS", "tT", "vV", "sS"))

    dC = mk.sem("dC")
    mk.emit("gpsimd", lambda e: e.dma_start(bufs["eye"][:], eye_in).then_inc(dC.h, 16))
    mk.emit("gpsimd", lambda e: e.dma_start(
        bufs["ubf"][:].rearrange("p (b i d) -> p b i d", b=BC, i=16, d=D),
        u.rearrange("b (i p) d -> p b i d", p=128)).then_inc(dC.h, 16))
    load_done = dC.inc(32)
    mk.emit("vector", lambda e: e.memset(bufs["onec"][:], 1.0))
    mk.emit("vector", lambda e: e.memset(bufs["ones1"][:], 1.0).then_inc(vV.h, 1))
    const_done = vV.inc(1)
    mk.wait("tensor", mk.sems["dC"], load_done)
    mk.wait("tensor", vV, const_done)
    mk.wait("vector", mk.sems["dC"], load_done)

    emit_conv_block(mk, bufs, sems, tstack)
    if debug:
        udbg = nc.dram_tensor("UDBG", [128, NTOK * NDC], BF16, kind="ExternalOutput").ap()
        mk.wait("sync", vV, vV.n)
        mk.emit("sync", lambda e: e.dma_start(udbg, bufs["ubf"][:]).then_inc(dS.h, 16))
        dS.inc(16)
    emit_transpose_nat_to_T(mk, bufs, sems, bufs["ubf"], bufs["xt"])
    emit_lnT(mk, bufs, sems, bufs["xt"])

    final_v = vV.n
    mk.wait("sync", vV, final_v)
    xtv = bufs["xt"][:].rearrange("p (dc t) -> p dc t", dc=NDC, t=NTOK)
    for dc in range(NDC):
        mk.emit("sync", lambda e, dc=dc: e.dma_start(
            xt_out[128 * dc:128 * dc + 128, :], xtv[:, dc, :]).then_inc(dS.h, 16))
    out_done = dS.inc(16 * NDC)
    mk.wait("sync", dS, out_done)
    mk.realize()
    return nc


def _shard_inputs_stage1(inputs):
    tstack = _conv_tstack(inputs["L0_attn_diff_kernel"], inputs["L0_attn_ma_r_kernel"])
    eye = np.eye(128, dtype=np.float32).astype(bfloat16)
    u = np.ascontiguousarray(np.asarray(inputs["L0_attn_u"], dtype=np.float32))
    in_maps = []
    for c in range(N_CORES):
        in_maps.append({
            "U": np.ascontiguousarray(u[2 * c:2 * c + 2]),
            "TSTACK": tstack,
            "EYE": eye,
        })
    return in_maps


def kernel(**inputs):
    raise NotImplementedError("under construction: stages being built")


def build_stage3(n_heads=16, do_proj=True, do_bphase=True, do_ln=True, debug=False):
    nc = bass.Bass("TRN2", target_bir_lowering=False)
    xt_in = nc.dram_tensor("XT", [D, NTOK], BF16, kind="ExternalInput").ap()
    wap = {
        "w_qkv": nc.dram_tensor("WQKV", [3 * D, D], F32, kind="ExternalInput").ap(),
        "w_out": nc.dram_tensor("WOUT", [D, D], F32, kind="ExternalInput").ap(),
    }
    eye_in = nc.dram_tensor("EYE", [128, 128], BF16, kind="ExternalInput").ap()
    ut_in = nc.dram_tensor("UT", [128, 128], BF16, kind="ExternalInput").ap()
    xt3_out = nc.dram_tensor("XT3", [D, NTOK], BF16, kind="ExternalOutput").ap()

    mk = MK(nc)
    bufs = {}
    bufs["arena"] = mk.sb("arena", [128, NTOK * NDC], BF16)
    bufs["eye"] = mk.sb("eye", [128, 128], BF16)
    bufs["ut"] = mk.sb("ut", [128, 128], BF16)
    bufs["onec"] = mk.sb("onec", [128, 1], BF16)
    bufs["ones1"] = mk.sb("ones1", [1, 128], BF16)
    bufs["onesA"] = mk.sb("onesA", [128, 128], BF16)
    ps_f32 = mk.ps("ps_f32", [128, 3072], F32)
    ps_bf = mk.ps("ps_bf", [128, 2048], BF16)
    bufs["ps_f32"] = ps_f32
    bufs["pstx"] = [ps_f32[:, 0:512], ps_f32[:, 512:1024]]
    bufs["psts"] = [ps_f32[:, 1024:1536], ps_f32[:, 1536:2048]]
    bufs["pbc"] = [ps_f32[:, 2048:2560], ps_f32[:, 2560:3072]]
    bufs["pt"] = [ps_bf[:, 0:512], ps_bf[:, 1024:1536]]

    sems = {n: mk.sem(n) for n in ("dW", "dS", "dA", "tT", "vV", "sS")}
    dW, dS, dA, tT, vV, sS = (sems[n] for n in ("dW", "dS", "dA", "tT", "vV", "sS"))

    dC = mk.sem("dC")
    mk.emit("gpsimd", lambda e: e.dma_start(bufs["eye"][:], eye_in).then_inc(dC.h, 16))
    mk.emit("gpsimd", lambda e: e.dma_start(bufs["ut"][:], ut_in).then_inc(dC.h, 16))
    load_done = dC.inc(32)
    mk.emit("vector", lambda e: e.memset(bufs["onec"][:], 1.0))
    mk.emit("vector", lambda e: e.memset(bufs["ones1"][:], 1.0))
    mk.emit("vector", lambda e: e.memset(bufs["onesA"][:], 1.0).then_inc(vV.h, 1))
    const_done = vV.inc(1)
    for eng in ("tensor", "vector", "scalar"):
        mk.wait(eng, mk.sems["dC"], load_done)
        mk.wait(eng, vV, const_done)

    dbg = None
    if debug:
        dbg = {n: nc.dram_tensor(n, [128, 16384], BF16, kind="ExternalOutput").ap()
               for n in ("QT", "KT", "VN")}
    emit_attn(mk, bufs, sems, wap, xt_in, n_heads=n_heads, do_proj=do_proj,
              do_bphase=do_bphase, dbg=dbg)
    if do_ln:
        bufs["zsq"] = [mk.sb(f"zsq{i}", [128, NTOK], BF16) for i in range(2)]
        bufs["rows"] = mk.sb("rows", [1, 4096], F32)
        bufs["rows_bf"] = mk.sb("rows_bf", [1, 2048], BF16)
        emit_lnT(mk, bufs, sems, bufs["arena"])

    final_v = vV.n
    mk.wait("sync", vV, final_v)
    mk.wait("sync", sems["sS"], sems["sS"].n)
    mk.wait("sync", tT, tT.n)
    xtv = bufs["arena"][:].rearrange("p (dc t) -> p dc t", dc=NDC, t=NTOK)
    for dc in range(NDC):
        mk.emit("sync", lambda e, dc=dc: e.dma_start(
            xt3_out[128 * dc:128 * dc + 128, :], xtv[:, dc, :]).then_inc(dS.h, 16))
    out_done = dS.inc(16 * NDC)
    mk.wait("sync", dS, out_done)
    mk.realize()
    return nc


def _shard_inputs_stage3(inputs, xt_by_core):
    eye = np.eye(128, dtype=np.float32).astype(bfloat16)
    ut = np.triu(np.ones((128, 128), dtype=np.float32)).astype(bfloat16)
    com = {
        "WQKV": np.ascontiguousarray(np.asarray(inputs["L1_attn_w_qkv"], np.float32)),
        "WOUT": np.ascontiguousarray(np.asarray(inputs["L1_attn_w_out"], np.float32)),
        "EYE": eye, "UT": ut,
    }
    return [dict(com, XT=np.ascontiguousarray(xt_by_core[c])) for c in range(len(xt_by_core))]


def build_stage4(final=False):
    nc = bass.Bass("TRN2", target_bir_lowering=False)
    xt_in = nc.dram_tensor("XT", [D, NTOK], BF16, kind="ExternalInput").ap()
    wap = {
        "router_w": nc.dram_tensor("RW", [D, 8], F32, kind="ExternalInput").ap(),
        "router_b": nc.dram_tensor("RB", [8], F32, kind="ExternalInput").ap(),
        "r_w1": nc.dram_tensor("RW1", [8, D, 512], F32, kind="ExternalInput").ap(),
        "r_w3": nc.dram_tensor("RW3", [8, D, 512], F32, kind="ExternalInput").ap(),
        "r_w2": nc.dram_tensor("RW2", [8, 512, D], F32, kind="ExternalInput").ap(),
        "s_w1": nc.dram_tensor("SW1", [2, D, 512], F32, kind="ExternalInput").ap(),
        "s_w3": nc.dram_tensor("SW3", [2, D, 512], F32, kind="ExternalInput").ap(),
        "s_w2": nc.dram_tensor("SW2", [2, 512, D], F32, kind="ExternalInput").ap(),
    }
    eye_in = nc.dram_tensor("EYE", [128, 128], BF16, kind="ExternalInput").ap()
    if final:
        out_dram = nc.dram_tensor("OUT", [BC, T, D], F32, kind="ExternalOutput").ap()
    else:
        xt4_out = nc.dram_tensor("XT4", [D, NTOK], BF16, kind="ExternalOutput").ap()
    acc_dram = nc.dram_tensor("ACCD", [NTOK, D], F32).ap()

    mk = MK(nc)
    bufs = {}
    bufs["arena"] = mk.sb("arena", [128, NTOK * NDC], BF16)
    bufs["eye"] = mk.sb("eye", [128, 128], BF16)
    bufs["onec"] = mk.sb("onec", [128, 1], BF16)
    bufs["ones1"] = mk.sb("ones1", [1, 128], BF16)
    ps_f32 = mk.ps("ps_f32", [128, 3072], F32)
    ps_bf = mk.ps("ps_bf", [128, 2048], BF16)
    bufs["ps_f32"] = ps_f32
    bufs["pstx"] = [ps_f32[:, 0:512], ps_f32[:, 512:1024]]
    bufs["psts"] = [ps_f32[:, 1024:1536], ps_f32[:, 1536:2048]]
    bufs["pbc"] = [ps_f32[:, 2048:2560], ps_f32[:, 2560:3072]]
    bufs["pt"] = [ps_bf[:, 0:512], ps_bf[:, 1024:1536]]

    sems = {n: mk.sem(n) for n in ("dW", "dS", "dA", "tT", "vV", "sS")}
    dW, dS, dA, tT, vV, sS = (sems[n] for n in ("dW", "dS", "dA", "tT", "vV", "sS"))

    dC = mk.sem("dC")
    mk.emit("gpsimd", lambda e: e.dma_start(bufs["eye"][:], eye_in).then_inc(dC.h, 16))
    load_done = dC.inc(16)
    mk.emit("vector", lambda e: e.memset(bufs["onec"][:], 1.0))
    mk.emit("vector", lambda e: e.memset(bufs["ones1"][:], 1.0).then_inc(vV.h, 1))
    const_done = vV.inc(1)
    for eng in ("tensor", "vector", "scalar"):
        mk.wait(eng, dC, load_done)
        mk.wait(eng, vV, const_done)

    acc_done = emit_ffn1(mk, bufs, sems, wap, xt_in, acc_dram)
    emit_accT_add(mk, bufs, sems, xt_in, acc_dram, acc_done)
    bufs["zsq"] = [mk.sb(f"zsq{i}", [128, NTOK], BF16) for i in range(2)]
    bufs["rows"] = mk.sb("rows", [1, 4096], F32)
    bufs["rows_bf"] = mk.sb("rows_bf", [1, 2048], BF16)
    emit_lnT(mk, bufs, sems, bufs["arena"])

    if final:
        emit_final_out(mk, bufs, sems, out_dram)
    else:
        final_v = vV.n
        mk.wait("sync", vV, final_v)
        mk.wait("sync", sems["sS"], sems["sS"].n)
        mk.wait("sync", tT, tT.n)
        xtv = bufs["arena"][:].rearrange("p (dc t) -> p dc t", dc=NDC, t=NTOK)
        for dc in range(NDC):
            mk.emit("sync", lambda e, dc=dc: e.dma_start(
                xt4_out[128 * dc:128 * dc + 128, :], xtv[:, dc, :]).then_inc(dS.h, 16))
        out_done = dS.inc(16 * NDC)
        mk.wait("sync", dS, out_done)
    mk.realize()
    return nc


def _shard_inputs_stage4(inputs, xt_by_core):
    eye = np.eye(128, dtype=np.float32).astype(bfloat16)
    com = {
        "RW": np.ascontiguousarray(np.asarray(inputs["L1_ffn_router_w"], np.float32)),
        "RB": np.ascontiguousarray(np.asarray(inputs["L1_ffn_router_b"], np.float32)),
        "RW1": np.ascontiguousarray(np.asarray(inputs["L1_ffn_routed_w1"], np.float32)),
        "RW3": np.ascontiguousarray(np.asarray(inputs["L1_ffn_routed_w3"], np.float32)),
        "RW2": np.ascontiguousarray(np.asarray(inputs["L1_ffn_routed_w2"], np.float32)),
        "SW1": np.ascontiguousarray(np.asarray(inputs["L1_ffn_shared_w1"], np.float32)),
        "SW3": np.ascontiguousarray(np.asarray(inputs["L1_ffn_shared_w3"], np.float32)),
        "SW2": np.ascontiguousarray(np.asarray(inputs["L1_ffn_shared_w2"], np.float32)),
        "EYE": eye,
    }
    return [dict(com, XT=np.ascontiguousarray(xt_by_core[c])) for c in range(len(xt_by_core))]


_PROG_CACHE = {}


def _get_prog(name, builder):
    if name not in _PROG_CACHE:
        _PROG_CACHE[name] = builder()
    return _PROG_CACHE[name]


def kernel(**inputs):
    return kernel_fused(**inputs)


def kernel_multilaunch(**inputs):
    inputs = {k: np.asarray(v) for k, v in inputs.items()}

    nc1 = _get_prog("s1", build_stage1)
    nc2 = _get_prog("s2", build_stage2)
    nc3 = _get_prog("s3", build_stage3)
    nc4 = _get_prog("s4f", lambda: build_stage4(final=True))

    cores = list(range(N_CORES))
    im1 = _shard_inputs_stage1(inputs)
    r1 = run_bass_kernel_spmd(nc1, im1, core_ids=cores)
    xt = [r1.results[c]["XT"] for c in range(N_CORES)]

    im2 = _shard_inputs_stage2(inputs, xt)
    r2 = run_bass_kernel_spmd(nc2, im2, core_ids=cores)
    xt = [r2.results[c]["XT2"] for c in range(N_CORES)]

    im3 = _shard_inputs_stage3(inputs, xt)
    r3 = run_bass_kernel_spmd(nc3, im3, core_ids=cores)
    xt = [r3.results[c]["XT3"] for c in range(N_CORES)]

    im4 = _shard_inputs_stage4(inputs, xt)
    r4 = run_bass_kernel_spmd(nc4, im4, core_ids=cores)
    out = np.concatenate([np.asarray(r4.results[c]["OUT"]) for c in range(N_CORES)], axis=0)
    return out


def build_fused():
    nc = bass.Bass("TRN2", target_bir_lowering=False)
    u = nc.dram_tensor("U", [BC, T, D], F32, kind="ExternalInput").ap()
    tstack = nc.dram_tensor("TSTACK", [128, 7, 128, 128], F32, kind="ExternalInput").ap()
    wap0 = {
        "w_red": nc.dram_tensor("WRED", [D, 16], F32, kind="ExternalInput").ap(),
        "wg": nc.dram_tensor("WG", [8, 16], F32, kind="ExternalInput").ap(),
        "w1": nc.dram_tensor("W1", [8, D, 4096], F32, kind="ExternalInput").ap(),
        "b1": nc.dram_tensor("B1", [8, 4096], F32, kind="ExternalInput").ap(),
        "w2": nc.dram_tensor("W2", [8, 4096, D], F32, kind="ExternalInput").ap(),
        "b2": nc.dram_tensor("B2", [8, D], F32, kind="ExternalInput").ap(),
    }
    wapA = {
        "w_qkv": nc.dram_tensor("WQKV", [3 * D, D], F32, kind="ExternalInput").ap(),
        "w_out": nc.dram_tensor("WOUT", [D, D], F32, kind="ExternalInput").ap(),
    }
    wap1 = {
        "router_w": nc.dram_tensor("RW", [D, 8], F32, kind="ExternalInput").ap(),
        "router_b": nc.dram_tensor("RB", [8], F32, kind="ExternalInput").ap(),
        "r_w1": nc.dram_tensor("RW1", [8, D, 512], F32, kind="ExternalInput").ap(),
        "r_w3": nc.dram_tensor("RW3", [8, D, 512], F32, kind="ExternalInput").ap(),
        "r_w2": nc.dram_tensor("RW2", [8, 512, D], F32, kind="ExternalInput").ap(),
        "s_w1": nc.dram_tensor("SW1", [2, D, 512], F32, kind="ExternalInput").ap(),
        "s_w3": nc.dram_tensor("SW3", [2, D, 512], F32, kind="ExternalInput").ap(),
        "s_w2": nc.dram_tensor("SW2", [2, 512, D], F32, kind="ExternalInput").ap(),
    }
    eye_in = nc.dram_tensor("EYE", [128, 128], BF16, kind="ExternalInput").ap()
    eyef_in = nc.dram_tensor("EYEF", [128, 128], F32, kind="ExternalInput").ap()
    ut_in = nc.dram_tensor("UT", [128, 128], BF16, kind="ExternalInput").ap()
    out_dram = nc.dram_tensor("OUT", [BC, T, D], F32, kind="ExternalOutput").ap()
    xt_dram = nc.dram_tensor("XTD", [D, NTOK], BF16).ap()
    acc_dram = nc.dram_tensor("ACCD", [NTOK, D], F32).ap()

    mk = MK(nc)
    bufs = {}
    bufs["arena"] = mk.sb("arena", [128, NTOK * NDC], BF16)
    bufs["xt"] = bufs["arena"]
    bufs["eye"] = mk.sb("eye", [128, 128], BF16)
    bufs["eyef"] = mk.sb("eyef", [128, 128], F32)
    bufs["ut"] = mk.sb("ut", [128, 128], BF16)
    bufs["onec"] = mk.sb("onec", [128, 1], BF16)
    bufs["ones1"] = mk.sb("ones1", [1, 128], BF16)
    bufs["onesA"] = mk.sb("onesA", [128, 128], BF16)
    ps_f32 = mk.ps("ps_f32", [128, 3072], F32)
    ps_bf = mk.ps("ps_bf", [128, 2048], BF16)
    bufs["ps_f32"] = ps_f32
    bufs["pg"] = [ps_f32[:, 0:512], ps_f32[:, 512:1024]]
    bufs["pstx"] = [ps_f32[:, 0:512], ps_f32[:, 512:1024]]
    bufs["psts"] = [ps_f32[:, 1024:1536], ps_f32[:, 1536:2048]]
    bufs["pbc"] = [ps_f32[:, 2048:2560], ps_f32[:, 2560:3072]]
    bufs["pt"] = [ps_bf[:, 0:512], ps_bf[:, 1024:1536]]

    sems = {n: mk.sem(n) for n in ("dW", "dS", "dA", "tT", "vV", "sS")}
    dW, dS, dA, tT, vV, sS = (sems[n] for n in ("dW", "dS", "dA", "tT", "vV", "sS"))

    dC = mk.sem("dC")
    for nm, dst in (("eye", eye_in), ("eyef", eyef_in), ("ut", ut_in)):
        mk.emit("gpsimd", lambda e, nm=nm, dst=dst: e.dma_start(
            bufs[nm][:], dst).then_inc(dC.h, 16))
    load_done = dC.inc(48)
    for nm in ("onec", "ones1", "onesA"):
        last = nm == "onesA"
        mk.emit("vector", lambda e, nm=nm, l=last: (e.memset(
            bufs[nm][:], 1.0).then_inc(vV.h, 1) if l else e.memset(bufs[nm][:], 1.0)))
    const_done = vV.inc(1)
    for eng in ("tensor", "vector", "scalar"):
        mk.wait(eng, dC, load_done)
        mk.wait(eng, vV, const_done)

    def spill_xt():
        n = vV.n
        mk.wait("sync", vV, n)
        xtv = bufs["arena"][:].rearrange("p (dc t) -> p dc t", dc=NDC, t=NTOK)
        mk.emit("sync", lambda e: e.dma_start(
            xt_dram.rearrange("(dc p) t -> p dc t", p=128), xtv).then_inc(dS.h, 16))
        dS.inc(16)
        mk.barrier()

    def ln_pass():
        lsc = contextlib.ExitStack()
        bufs["zsq"] = [lsc.enter_context(nc.sbuf_tensor(f"zsq{mk.uid()}", [128, NTOK], BF16))
                       for _ in range(2)]
        bufs["rows"] = lsc.enter_context(nc.sbuf_tensor(f"rows{mk.uid()}", [1, 4096], F32))
        bufs["rows_bf"] = lsc.enter_context(nc.sbuf_tensor(f"rowsb{mk.uid()}", [1, 2048], BF16))
        emit_lnT(mk, bufs, sems, bufs["arena"])
        mk.barrier()
        lsc.close()

    s1 = contextlib.ExitStack()
    bufs["ubf"] = s1.enter_context(nc.sbuf_tensor("ubf", [128, NTOK * NDC], BF16))
    bufs["tsb"] = [s1.enter_context(nc.sbuf_tensor(f"tsb{i}", [128, 896], BF16)) for i in range(2)]
    dU = mk.sem("dU")
    mk.emit("gpsimd", lambda e: e.dma_start(
        bufs["ubf"][:].rearrange("p (b i d) -> p b i d", b=BC, i=16, d=D),
        u.rearrange("b (i p) d -> p b i d", p=128)).then_inc(dU.h, 16))
    u_done = dU.inc(16)
    mk.wait("tensor", dU, u_done)
    mk.wait("vector", dU, u_done)
    emit_conv_block(mk, bufs, sems, tstack)
    emit_transpose_nat_to_T(mk, bufs, sems, bufs["ubf"], bufs["arena"])
    mk.barrier()
    s1.close()
    ln_pass()
    spill_xt()

    s2 = contextlib.ExitStack()
    bufs["W1sb"] = s2.enter_context(nc.sbuf_tensor("W1sb", [128, 32768], BF16))
    bufs["W2sb"] = s2.enter_context(nc.sbuf_tensor("W2sb", [128, 32768], BF16))
    acc_done = emit_ffn0(mk, bufs, sems, wap0, xt_dram, acc_dram)
    mk.barrier()
    s2.close()
    emit_accT_add(mk, bufs, sems, xt_dram, acc_dram, acc_done)
    mk.barrier()
    ln_pass()
    spill_xt()

    emit_attn(mk, bufs, sems, wapA, xt_dram)
    mk.barrier()
    ln_pass()
    spill_xt()

    acc_done = emit_ffn1(mk, bufs, sems, wap1, xt_dram, acc_dram)
    mk.barrier()
    emit_accT_add(mk, bufs, sems, xt_dram, acc_dram, acc_done)
    mk.barrier()
    ln_pass()
    emit_final_out(mk, bufs, sems, out_dram)
    mk.realize()
    return nc


def _shard_inputs_fused(inputs):
    tstack = _conv_tstack(inputs["L0_attn_diff_kernel"], inputs["L0_attn_ma_r_kernel"])
    eye = np.eye(128, dtype=np.float32).astype(bfloat16)
    eyef = np.eye(128, dtype=np.float32)
    ut = np.triu(np.ones((128, 128), dtype=np.float32)).astype(bfloat16)
    f32c = lambda k: np.ascontiguousarray(np.asarray(inputs[k], np.float32))
    com = {
        "TSTACK": tstack, "EYE": eye, "EYEF": eyef, "UT": ut,
        "WRED": f32c("L0_ffn_w_red"), "WG": f32c("L0_ffn_wg"),
        "W1": f32c("L0_ffn_w1"), "B1": f32c("L0_ffn_b1"),
        "W2": f32c("L0_ffn_w2"), "B2": f32c("L0_ffn_b2"),
        "WQKV": f32c("L1_attn_w_qkv"), "WOUT": f32c("L1_attn_w_out"),
        "RW": f32c("L1_ffn_router_w"), "RB": f32c("L1_ffn_router_b"),
        "RW1": f32c("L1_ffn_routed_w1"), "RW3": f32c("L1_ffn_routed_w3"),
        "RW2": f32c("L1_ffn_routed_w2"), "SW1": f32c("L1_ffn_shared_w1"),
        "SW3": f32c("L1_ffn_shared_w3"), "SW2": f32c("L1_ffn_shared_w2"),
    }
    un = np.ascontiguousarray(np.asarray(inputs["L0_attn_u"], np.float32))
    return [dict(com, U=np.ascontiguousarray(un[2 * c:2 * c + 2])) for c in range(N_CORES)]


def kernel_fused(**inputs):
    inputs = {k: np.asarray(v) for k, v in inputs.items()}
    nc = _get_prog("fused", build_fused)
    im = _shard_inputs_fused(inputs)
    r = run_bass_kernel_spmd(nc, im, core_ids=list(range(N_CORES)))
    return np.concatenate([np.asarray(r.results[c]["OUT"]) for c in range(N_CORES)], axis=0)
